# version 02
# baseline (speedup 1.0000x reference)
import numpy as np
from contextlib import ExitStack

import concourse.bass as bass
import concourse.mybir as mybir
from concourse.bass_utils import run_bass_kernel_spmd

F32 = mybir.dt.float32
BF16 = mybir.dt.bfloat16
AF = mybir.ActivationFunctionType
OP = mybir.AluOpType
AX = mybir.AxisListType

D = 1024
PB = 8
NE = 16
NEX = 18
HEAD = 16
HD = 64
FFN_CHUNKS = ((0, 1, 2, 3), (4, 5, 6, 7), (8, 9, 10, 11), (12, 13, 14, 15),
              (16, 17))


class _Rec:
    __slots__ = ("src", "cnt")

    def __init__(self, src, cnt):
        self.src = src
        self.cnt = cnt


class TK:
    STREAMS = ("pe", "act", "dve", "pool", "sp")
    ENGS = ("pe", "act", "dve", "pool")
    NDMA = 8

    def __init__(self, nc, es):
        self.nc = nc
        self.psum_ids = set()
        self.ops = {s: [] for s in self.STREAMS}
        self.sem = {s: es.enter_context(nc.semaphore("sem_" + s)) for s in self.ENGS}
        for q in ("gq", "sq"):
            for i in range(self.NDMA):
                self.sem[f"{q}{i}"] = es.enter_context(nc.semaphore(f"sem_{q}{i}"))
        self.cnt = {s: 0 for s in self.sem}
        self.rr = {"gq": 0, "sq": 0}
        self.waited = {c: {s: 0 for s in self.sem} for c in self.STREAMS}
        self.floor = {c: {} for c in self.STREAMS}
        self.bufs = {}
        self.pe_pending = []

    def _deps(self, h, tag, readers):
        recs = self.bufs.get(id(h))
        if not recs:
            return ()
        out = []
        for t2, r in recs.items():
            if tag is None or t2 is None or t2 == tag:
                if r["w"] is not None:
                    out.append(r["w"])
                if readers:
                    out.extend(r["r"].values())
        return out

    def emit(self, stream, fn, reads=(), writes=(), inc=True, dma=None):
        need = dict(self.floor[stream])
        self.floor[stream] = {}
        for h, tag in reads:
            psum_rr = id(h) in self.psum_ids
            for r in self._deps(h, tag, psum_rr):
                if r.src == "pe" and stream == "pe":
                    continue
                if psum_rr and r.src == stream:
                    continue
                assert r.cnt is not None, "dep on open PE group"
                if r.cnt > need.get(r.src, 0):
                    need[r.src] = r.cnt
        for h, tag in writes:
            for r in self._deps(h, tag, True):
                if r.src == "pe" and stream == "pe":
                    continue
                assert r.cnt is not None, "dep on open PE group"
                if r.cnt > need.get(r.src, 0):
                    need[r.src] = r.cnt
        sname = None
        if dma is not None:
            sname = f"{dma}{self.rr[dma]}"
            self.rr[dma] = (self.rr[dma] + 1) % self.NDMA
            if self.cnt[sname] > need.get(sname, 0):
                need[sname] = self.cnt[sname]
        waits = []
        wt = self.waited[stream]
        for src in sorted(need):
            if need[src] > wt[src]:
                waits.append((self.sem[src], need[src]))
                wt[src] = need[src]

        if dma is not None:
            self.cnt[sname] += 16
            rec = _Rec(sname, self.cnt[sname])
            inc_sem, inc_n = self.sem[sname], 16
        elif inc:
            self.cnt[stream] += 1
            rec = _Rec(stream, self.cnt[stream])
            inc_sem, inc_n = self.sem[stream], 1
            if stream == "pe":
                for p in self.pe_pending:
                    p.cnt = rec.cnt
                self.pe_pending = []
        else:
            assert stream == "pe"
            rec = _Rec("pe", None)
            self.pe_pending.append(rec)
            inc_sem, inc_n = None, 0

        def closure(eng, _w=tuple(waits), _f=fn, _s=inc_sem, _n=inc_n):
            for sem, cnt in _w:
                eng.wait_ge(sem, cnt)
            inst = _f(eng)
            if _s is not None:
                inst.then_inc(_s, _n)

        self.ops[stream].append(closure)

        for h, tag in writes:
            self.bufs.setdefault(id(h), {})[tag] = {"w": rec, "r": {}}
        for h, tag in reads:
            recs = self.bufs.setdefault(id(h), {})
            recs.setdefault(tag, {"w": None, "r": {}})["r"][rec.src] = rec
        return rec

    def barrier(self):
        for c in ("pe", "act", "dve"):
            f = self.floor[c]
            for s in ("pe", "act", "dve"):
                f[s] = max(f.get(s, 0), self.cnt[s])

    def drop(self, handles):
        for h in handles:
            self.bufs.pop(id(h), None)


def build_nc(T=2048, NB=2, taps=False):
    nc = bass.Bass("TRN2", target_bir_lowering=False)
    n5 = T // 512
    n2 = T // 256
    LOGT = float(np.log(T))

    dx = nc.dram_tensor("x", [NB, PB, 128, T], F32, kind="ExternalInput").ap()
    dout = nc.dram_tensor("out", [NB, PB, 128, T], F32, kind="ExternalOutput").ap()
    dW = {}
    for nm in ("wq", "wk", "wv", "wo", "wkey", "wval", "wrec", "wout"):
        dW[nm] = nc.dram_tensor(nm, [128, PB * D], F32, kind="ExternalInput").ap()
    dw13 = nc.dram_tensor("w13", [NEX, 128, PB * D], F32, kind="ExternalInput").ap()
    dw2 = nc.dram_tensor("w2", [NEX, 128, 4 * D], F32, kind="ExternalInput").ap()
    dgw = nc.dram_tensor("gw", [128, PB * NE], F32, kind="ExternalInput").ap()
    dpv = nc.dram_tensor("pvec", [128, 80], F32, kind="ExternalInput").ap()
    dbrow = nc.dram_tensor("brow", [1, 2 * D], F32, kind="ExternalInput").ap()
    dones = nc.dram_tensor("ones_k1", [1, 128], F32, kind="ExternalInput").ap()
    donesb = nc.dram_tensor("onescol_b", [128, 1], BF16, kind="ExternalInput").ap()
    dbd = nc.dram_tensor("bd", [128, PB * HEAD], BF16, kind="ExternalInput").ap()
    dselh = nc.dram_tensor("selh", [16, PB * 128], BF16, kind="ExternalInput").ap()
    dsele = nc.dram_tensor("sel_e", [16, NE * 128], BF16, kind="ExternalInput").ap()
    dident = nc.dram_tensor("ident_b", [128, 128], BF16, kind="ExternalInput").ap()
    dcv = nc.dram_tensor("cvals", [128, 8], F32, kind="ExternalInput").ap()
    dtap = {}
    if taps:
        for nm in ("xn1", "attna", "ffn1", "attnb", "ffn2", "wkv"):
            dtap[nm] = nc.dram_tensor("tap_" + nm, [NB, PB, 128, T], F32,
                                      kind="ExternalOutput").ap()

    es = ExitStack()
    _uid = [0]

    def _nm(name):
        _uid[0] += 1
        return f"{name}_{_uid[0]}"

    sb = lambda name, shape, dt=F32: es.enter_context(nc.sbuf_tensor(name, shape, dt))
    tk = TK(nc, es)

    x8 = [sb(f"x{i}", [128, T]) for i in range(PB)]
    c_ones = sb("c_ones", [1, 128])
    c_onesb = sb("c_onesb", [128, 1], BF16)
    c_bd = sb("c_bd", [128, PB * HEAD], BF16)
    c_selh = sb("c_selh", [16, PB * 128], BF16)
    c_sele = sb("c_sele", [16, NE * 128], BF16)
    c_id = sb("c_id", [128, 128], BF16)
    pv = sb("pv", [128, 80])
    dpar = sb("dpar", [128, 32])
    sgate = sb("sgate", [16, 1])
    btm = sb("btm", [128, 2 * D])
    gwS = sb("gwS", [128, PB * NE], BF16)
    W0 = sb("W0", [128, PB * D], BF16)
    W1 = sb("W1", [128, PB * D], BF16)
    stats = sb("stats", [1, 1024])
    cvals = sb("cvals_s", [128, 8])
    msb = sb("msb", [128, 512], BF16)
    ps8 = [es.enter_context(nc.psum_tensor(f"ps{i}", [128, 512], F32))
           for i in range(8)]
    tk.psum_ids = {id(p) for p in ps8}

    BQ, BK, BV, BO, UU, TD, TMK, TMV, TMR, MCOL = 0, 8, 16, 24, 32, 40, 48, 56, 64, 72
    OTMK, OTMV, OTMR, EW = 0, 8, 16, 24

    pe = lambda fn, r=(), w=(), inc=True: tk.emit("pe", fn, r, w, inc)
    acte = lambda fn, r=(), w=(): tk.emit("act", fn, r, w)
    dve = lambda fn, r=(), w=(): tk.emit("dve", fn, r, w)
    gq = lambda fn, r=(), w=(): tk.emit("pool", fn, r, w, dma="gq")
    sq = lambda fn, r=(), w=(): tk.emit("sp", fn, r, w, dma="sq")

    def ldw(dst, src_ap, r0=0, r1=None):
        n = dst.shape[1] if r1 is None else (r1 - r0)
        gq(lambda g, d=dst[:, r0:r0 + n], s=src_ap[:, r0:r0 + n]: g.dma_start(d, s),
           w=[(dst, ("ld", r0))])

    CV = [1e-5, -0.5, -1.0, -LOGT]
    for i, v in enumerate(CV):
        nc.const_aps.register(F32, v, cvals[:, i:i + 1])
    sq(lambda e: e.dma_start(cvals[:, :], dcv), w=[(cvals, None)])
    sq(lambda e: e.dma_start(c_ones[:, :], dones), w=[(c_ones, None)])
    sq(lambda e: e.dma_start(c_onesb[:, :], donesb), w=[(c_onesb, None)])
    sq(lambda e: e.dma_start(c_bd[:, :], dbd), w=[(c_bd, None)])
    sq(lambda e: e.dma_start(c_selh[:, :], dselh), w=[(c_selh, None)])
    sq(lambda e: e.dma_start(c_sele[:, :], dsele), w=[(c_sele, None)])
    sq(lambda e: e.dma_start(c_id[:, :], dident), w=[(c_id, None)])
    sq(lambda e: e.dma_start(pv[:, :], dpv), w=[(pv, None)])
    gq(lambda e: e.dma_start(gwS[:, :], dgw), w=[(gwS, None)])
    for p in range(PB):
        sq(lambda e, p=p: e.dma_start(x8[p][:, :], dx[0, p]), w=[(x8[p], None)])

    for c in ("pe", "act", "dve"):
        for i in range(tk.NDMA):
            for q in ("sq", "gq"):
                if tk.cnt[f"{q}{i}"]:
                    tk.floor[c][f"{q}{i}"] = tk.cnt[f"{q}{i}"]
    for col, src in ((OTMK, TMK), (OTMV, TMV), (OTMR, TMR)):
        dve(lambda v, c=col, s=src: v.tensor_scalar(
            dpar[:, c:c + 8], pv[:, s:s + 8], -1.0, 1.0, OP.mult, OP.add),
            r=[(pv, None)], w=[(dpar, ("c", col))])
    acte(lambda s: s.activation(dpar[:, EW:EW + 8], pv[:, TD:TD + 8], AF.Exp),
         r=[(pv, None)], w=[(dpar, ("c", EW))])
    dve(lambda v: v.tensor_scalar(dpar[:, EW:EW + 8], dpar[:, EW:EW + 8],
                                  -1.0, 0.0, OP.mult, OP.bypass),
        r=[(dpar, ("c", EW))], w=[(dpar, ("c", EW))])
    acte(lambda s: s.activation(dpar[:, EW:EW + 8], dpar[:, EW:EW + 8], AF.Exp),
         r=[(dpar, ("c", EW))], w=[(dpar, ("c", EW))])
    acte(lambda s: s.activation(sgate[:, :], pv[0:16, MCOL:MCOL + 1], AF.Sigmoid),
         r=[(pv, None)], w=[(sgate, None)])
    acte(lambda s: s.activation(sgate[:, :], sgate[:, :], AF.Exp, scale=-LOGT),
         r=[(sgate, None)], w=[(sgate, None)])
    with nc.sbuf_tensor("browS", [1, 2 * D], F32) as browS:
        sq(lambda e: e.dma_start(browS[:, :], dbrow), w=[(browS, None)])
        for j in range(4):
            pe(lambda t, j=j: t.matmul(ps8[j][:, :], c_ones[:, :],
                                       browS[:, 512 * j:512 * (j + 1)],
                                       start=True, stop=True),
               r=[(c_ones, None), (browS, None)], w=[(ps8[j], None)])
            dve(lambda v, j=j: v.tensor_copy(btm[:, 512 * j:512 * (j + 1)],
                                             ps8[j][:, :]),
                r=[(ps8[j], None)], w=[(btm, ("c", j))])
        tk.barrier()
    tk.drop([])

    def ln_tile(xsrc, tt, dst_fn, scratch, tag):
        t0 = tt * 512
        xc, xq = scratch["xc"], scratch["xq"]
        for hh in range(4):
            for p in range(PB):
                s0 = t0 + hh * 128
                acte(lambda s, p=p, s0=s0: s.activation(
                    xc[:, p * 128:(p + 1) * 128], xsrc[p][:, s0:s0 + 128], AF.Identity),
                    r=[(xsrc[p], None)], w=[(xc, ("p", p))])
                acte(lambda s, p=p, s0=s0: s.activation(
                    xq[:, p * 128:(p + 1) * 128], xsrc[p][:, s0:s0 + 128], AF.Square),
                    r=[(xsrc[p], None)], w=[(xq, ("p", p))])
            for p in range(PB):
                pe(lambda t, p=p, hh=hh: t.matmul(
                    ps8[0][0:1, hh * 128:(hh + 1) * 128], c_onesb[:, :],
                    xc[:, p * 128:(p + 1) * 128], start=(p == 0), stop=(p == PB - 1)),
                   r=[(c_onesb, None), (xc, ("p", p))], w=[(ps8[0], None)],
                   inc=(p == PB - 1))
                pe(lambda t, p=p, hh=hh: t.matmul(
                    ps8[1][0:1, hh * 128:(hh + 1) * 128], c_onesb[:, :],
                    xq[:, p * 128:(p + 1) * 128], start=(p == 0), stop=(p == PB - 1)),
                   r=[(c_onesb, None), (xq, ("p", p))], w=[(ps8[1], None)],
                   inc=(p == PB - 1))
        dve(lambda v: v.tensor_copy(stats[:, 0:512], ps8[0][0:1, :]),
            r=[(ps8[0], None)], w=[(stats, 0)])
        dve(lambda v: v.tensor_copy(stats[:, 512:1024], ps8[1][0:1, :]),
            r=[(ps8[1], None)], w=[(stats, 1)])
        pe(lambda t: t.matmul(ps8[2][:, :], c_ones[:, :], stats[:, 0:512],
                              start=True, stop=True),
           r=[(c_ones, None), (stats, 0)], w=[(ps8[2], None)])
        pe(lambda t: t.matmul(ps8[3][:, :], c_ones[:, :], stats[:, 512:1024],
                              start=True, stop=True),
           r=[(c_ones, None), (stats, 1)], w=[(ps8[3], None)])
        mean, ex2, var, istd = (scratch["mean"], scratch["ex2"], scratch["var"],
                                scratch["istd"])
        dve(lambda v: v.tensor_scalar(mean[:, :], ps8[2][:, :], 1.0 / D, 0.0,
                                      OP.mult, OP.add),
            r=[(ps8[2], None)], w=[(mean, None)])
        dve(lambda v: v.tensor_scalar(ex2[:, :], ps8[3][:, :], 1.0 / D, 0.0,
                                      OP.mult, OP.add),
            r=[(ps8[3], None)], w=[(ex2, None)])
        dve(lambda v: v.tensor_tensor(var[:, :], mean[:, :], mean[:, :], OP.mult),
            r=[(mean, None)], w=[(var, None)])
        dve(lambda v: v.tensor_tensor(var[:, :], ex2[:, :], var[:, :], OP.subtract),
            r=[(ex2, None), (var, None)], w=[(var, None)])
        acte(lambda s: s.activation(var[:, :], var[:, :], AF.Ln, bias=1e-5),
             r=[(var, None)], w=[(var, None)])
        acte(lambda s: s.activation(istd[:, :], var[:, :], AF.Exp, scale=-0.5),
             r=[(var, None)], w=[(istd, None)])
        dlt = scratch["dlt"]
        for p in range(PB):
            dap, dh, dtag = dst_fn(p)
            dve(lambda v, p=p: v.tensor_tensor(dlt[:, :], xsrc[p][:, t0:t0 + 512],
                                               mean[:, :], OP.subtract),
                r=[(xsrc[p], None), (mean, None)], w=[(dlt, None)])
            dve(lambda v, a=dap: v.tensor_tensor(a, dlt[:, :], istd[:, :], OP.mult),
                r=[(dlt, None), (istd, None)], w=[(dh, dtag)])

    def mk_ln_scratch(sbp):
        return dict(xc=sbp("ln_xc", [128, PB * 128], BF16),
                    xq=sbp("ln_xq", [128, PB * 128], BF16),
                    mean=sbp("ln_mean", [128, 512]), ex2=sbp("ln_ex2", [128, 512]),
                    var=sbp("ln_var", [128, 512]), istd=sbp("ln_istd", [128, 512]),
                    dlt=sbp("ln_dlt", [128, 512]))

    def attn_a(b):
        with ExitStack() as ph:
            sbp = lambda n, s, dt=F32: ph.enter_context(
                nc.sbuf_tensor(_nm(n), s, dt))
            lns = mk_ln_scratch(sbp)
            xn1 = sbp("xn1", [128, PB * 512], BF16)
            knt = [sbp(f"knt{i}", [128, D], BF16) for i in range(2)]
            vt = [sbp(f"vt{i}", [128, D], BF16) for i in range(2)]
            kb = sbp("kb", [128, D])
            ksq = sbp("ksq", [128, D], BF16)
            ksc = sbp("ksc", [128, 16])
            ldw(W0, dW["wk"])
            ldw(W1, dW["wv"])
            for tt in range(n5):
                ln_tile(x8, tt, lambda p: (xn1[:, p * 512:(p + 1) * 512], xn1,
                                           ("p", p)), lns, "ln1")
                for tbl in range(4):
                    tb = tt * 4 + tbl
                    kn, vv = knt[tb % 2], vt[tb % 2]
                    for half in range(2):
                        pk, pv_ = ps8[half], ps8[2 + half]
                        for cb in range(PB):
                            lh = xn1[:, cb * 512 + tbl * 128: cb * 512 + (tbl + 1) * 128]
                            pe(lambda t, o=pk[:, :], l=lh,
                               r_=W0[:, cb * D + half * 512: cb * D + (half + 1) * 512],
                               st=(cb == 0), sp=(cb == PB - 1): t.matmul(
                                   o, l, r_, start=st, stop=sp),
                               r=[(xn1, ("p", cb)), (W0, ("ld", 0))],
                               w=[(pk, None)], inc=(cb == PB - 1))
                            pe(lambda t, o=pv_[:, :], l=lh,
                               r_=W1[:, cb * D + half * 512: cb * D + (half + 1) * 512],
                               st=(cb == 0), sp=(cb == PB - 1): t.matmul(
                                   o, l, r_, start=st, stop=sp),
                               r=[(xn1, ("p", cb)), (W1, ("ld", 0))],
                               w=[(pv_, None)], inc=(cb == PB - 1))
                        dve(lambda v, h=half: v.tensor_tensor(
                            kb[:, h * 512:(h + 1) * 512], ps8[h][:, :],
                            btm[:, h * 512:(h + 1) * 512], OP.add),
                            r=[(ps8[half], None), (btm, ("c", half))],
                            w=[(kb, ("h", half))])
                        acte(lambda s, h=half: s.activation(
                            ksq[:, h * 512:(h + 1) * 512], kb[:, h * 512:(h + 1) * 512],
                            AF.Square),
                            r=[(kb, ("h", half))], w=[(ksq, ("h", half))])
                        dve(lambda v, h=half, vv=vv: v.tensor_tensor(
                            vv[:, h * 512:(h + 1) * 512], ps8[2 + h][:, :],
                            btm[:, D + h * 512: D + (h + 1) * 512], OP.add),
                            r=[(ps8[2 + half], None), (btm, ("c", 2 + half))],
                            w=[(vv, ("h", half))])
                    dve(lambda v: v.reduce_sum(
                        ksc[:, :], ksq[:, :].rearrange("p (h d) -> p h d", d=HD),
                        axis=AX.X),
                        r=[(ksq, ("h", 0)), (ksq, ("h", 1))], w=[(ksc, None)])
                    dve(lambda v: v.tensor_scalar(ksc[:, :], ksc[:, :], 1e-24, 0.0,
                                                  OP.max, OP.add),
                        r=[(ksc, None)], w=[(ksc, None)])
                    acte(lambda s: s.activation(ksc[:, :], ksc[:, :], AF.Ln),
                         r=[(ksc, None)], w=[(ksc, None)])
                    acte(lambda s: s.activation(ksc[:, :], ksc[:, :], AF.Exp,
                                                scale=-0.5),
                         r=[(ksc, None)], w=[(ksc, None)])
                    for half in range(2):
                        dve(lambda v, h=half, kn=kn: v.tensor_tensor(
                            kn[:, h * 512:(h + 1) * 512].rearrange(
                                "p (h d) -> p h d", d=HD),
                            kb[:, h * 512:(h + 1) * 512].rearrange(
                                "p (h d) -> p h d", d=HD),
                            ksc[:, h * 8:(h + 1) * 8].unsqueeze(2).broadcast_to(
                                (128, 8, HD)),
                            OP.mult),
                            r=[(kb, ("h", half)), (ksc, None)], w=[(kn, ("h", half))])
                    for h in range(HEAD):
                        r0, c0 = 64 * (h % 2), 64 * (h // 2)
                        pe(lambda t, kn=kn, vv=vv, h=h, r0=r0, c0=c0,
                           st=(tb == 0 and h < 2),
                           sp=(tb == 4 * n5 - 1 and h == HEAD - 1):
                           t.matmul(ps8[7][r0:r0 + 64, c0:c0 + 64],
                                    kn[:, h * 64:(h + 1) * 64],
                                    vv[:, h * 64:(h + 1) * 64],
                                    start=st, stop=sp, skip_group_check=True),
                           r=[(kn, ("h", 0)), (kn, ("h", 1)), (vv, ("h", 0)),
                              (vv, ("h", 1))],
                           w=[(ps8[7], None)], inc=(h == HEAD - 1))
            dve(lambda v: v.tensor_copy(msb[:, :], ps8[7][:, :]),
                r=[(ps8[7], None)], w=[(msb, None)])
            ldw(W0, dW["wq"])
            ldw(W1, dW["wo"])
            qb = sbp("qb", [128, PB * 512], BF16)
            qsq = sbp("qsq", [128, PB * 512], BF16)
            qn = sbp("qn", [128, PB * 512], BF16)
            ao = sbp("ao", [128, PB * 512], BF16)
            qs = sbp("qs", [16, 512])
            qsb = sbp("qsb", [16, 512], BF16)
            for tt in range(n5):
                ln_tile(x8, tt, lambda p: (xn1[:, p * 512:(p + 1) * 512], xn1,
                                           ("p", p)), lns, "ln1c")
                for db in range(PB):
                    pq = ps8[db % 2]
                    for cb in range(PB):
                        pe(lambda t, o=pq[:, :],
                           l=W0[:, cb * D + db * 128: cb * D + (db + 1) * 128],
                           r_=xn1[:, cb * 512:(cb + 1) * 512],
                           st=(cb == 0), sp=(cb == PB - 1): t.matmul(
                               o, l, r_, start=st, stop=sp),
                           r=[(W0, ("ld", 0)), (xn1, ("p", cb))],
                           w=[(pq, None)], inc=(cb == PB - 1))
                    dve(lambda v, db=db, pq=pq: v.tensor_scalar(
                        qb[:, db * 512:(db + 1) * 512], pq[:, :],
                        pv[:, BQ + db:BQ + db + 1], 0.0, OP.add, OP.add),
                        r=[(pq, None), (pv, None)], w=[(qb, ("p", db))])
                    acte(lambda s, db=db, pq=pq: s.activation(
                        qsq[:, db * 512:(db + 1) * 512], pq[:, :], AF.Square,
                        bias=pv[:, BQ + db:BQ + db + 1]),
                        r=[(pq, None), (pv, None)], w=[(qsq, ("p", db))])
                for p in range(PB):
                    pe(lambda t, p=p: t.matmul(
                        ps8[2][0:16, :], c_bd[:, p * 16:(p + 1) * 16],
                        qsq[:, p * 512:(p + 1) * 512],
                        start=(p == 0), stop=(p == PB - 1)),
                       r=[(c_bd, None), (qsq, ("p", p))], w=[(ps8[2], None)],
                       inc=(p == PB - 1))
                dve(lambda v: v.tensor_scalar(qs[:, :], ps8[2][0:16, :], 1e-24, 0.0,
                                              OP.max, OP.add),
                    r=[(ps8[2], None)], w=[(qs, None)])
                acte(lambda s: s.activation(qs[:, :], qs[:, :], AF.Ln),
                     r=[(qs, None)], w=[(qs, None)])
                acte(lambda s: s.activation(qs[:, :], qs[:, :], AF.Exp, scale=-0.5),
                     r=[(qs, None)], w=[(qs, None)])
                dve(lambda v: v.tensor_scalar(qs[:, :], qs[:, :], sgate[:, 0:1], 0.0,
                                              OP.mult, OP.add),
                    r=[(qs, None), (sgate, None)], w=[(qs, None)])
                dve(lambda v: v.tensor_copy(qsb[:, :], qs[:, :]),
                    r=[(qs, None)], w=[(qsb, None)])
                for p in range(PB):
                    pbc = ps8[3 + p % 2]
                    pe(lambda t, p=p, o=pbc[:, :]: t.matmul(
                        o, c_selh[:, p * 128:(p + 1) * 128], qsb[:, :],
                        start=True, stop=True),
                       r=[(c_selh, None), (qsb, None)], w=[(pbc, None)])
                    dve(lambda v, p=p, pbc=pbc: v.tensor_tensor(
                        qn[:, p * 512:(p + 1) * 512], qb[:, p * 512:(p + 1) * 512],
                        pbc[:, :], OP.mult),
                        r=[(qb, ("p", p)), (pbc, None)], w=[(qn, ("p", p))])
                    py = ps8[5 + p % 2]
                    for hh in range(2):
                        h = 2 * p + hh
                        pe(lambda t, h=h, hh=hh, p=p, py=py: t.matmul(
                            py[hh * 64:(hh + 1) * 64, :],
                            msb[64 * (h % 2):64 * (h % 2) + 64,
                                64 * (h // 2):64 * (h // 2) + 64],
                            qn[64 * hh:64 * (hh + 1), p * 512:(p + 1) * 512],
                            start=True, stop=True, skip_group_check=True),
                           r=[(msb, None), (qn, ("p", p))], w=[(py, None)],
                           inc=(hh == 1))
                    dve(lambda v, p=p, py=py: v.tensor_copy(
                        ao[:, p * 512:(p + 1) * 512], py[:, :]),
                        r=[(py, None)], w=[(ao, ("p", p))])
                for db in range(PB):
                    po = ps8[7]
                    for cb in range(PB):
                        pe(lambda t, o=po[:, :],
                           l=W1[:, cb * D + db * 128: cb * D + (db + 1) * 128],
                           r_=ao[:, cb * 512:(cb + 1) * 512],
                           st=(cb == 0), sp=(cb == PB - 1): t.matmul(
                               o, l, r_, start=st, stop=sp),
                           r=[(W1, ("ld", 0)), (ao, ("p", cb))],
                           w=[(po, None)], inc=(cb == PB - 1))
                    dve(lambda v, db=db, tt=tt, po=po: v.scalar_tensor_tensor(
                        x8[db][:, tt * 512:(tt + 1) * 512], po[:, :],
                        pv[:, BO + db:BO + db + 1],
                        x8[db][:, tt * 512:(tt + 1) * 512], OP.add, OP.add),
                        r=[(po, None), (pv, None), (x8[db], None)],
                        w=[(x8[db], None)])
            tk.barrier()
            tk.drop([lns[k] for k in lns] + [xn1, knt[0], knt[1], vt[0], vt[1],
                                             kb, ksq, ksc, qb, qsq, qn, ao, qs, qsb])

    def ffn(b, inv):
        with ExitStack() as ph:
            sbp = lambda n, s, dt=F32: ph.enter_context(
                nc.sbuf_tensor(_nm(n), s, dt))
            lns = mk_ln_scratch(sbp)
            xn2 = [sbp(f"xn2_{i}", [128, PB * 512], BF16) for i in range(2)]
            w2f = [sbp(f"w2f{i}", [128, 4 * D], BF16) for i in range(2)]
            asl = sbp("asl", [128, 4 * 4 * 512], BF16)
            sil = sbp("sil", [128, 4 * 512], BF16)
            h3c = sbp("h3c", [128, 4 * 512], BF16)
            call = sbp("call", [128, NE * 512], BF16)
            gsc = {k: sbp("g_" + k, s, dt) for k, s, dt in (
                ("E", [128, 16], F32), ("m", [128, 1], F32), ("nm", [128, 1], F32),
                ("s", [128, 16], F32), ("gm", [128, 4], F32), ("g2", [128, 4], F32),
                ("q", [128, 16], F32), ("t", [128, 1], F32), ("ctm", [128, 16], BF16),
                ("cfm", [16, 512], BF16))}
            first_w13 = [True]
            for tt in range(n5):
                xn = xn2[tt % 2]
                ln_tile(x8, tt, lambda p: (xn[:, p * 512:(p + 1) * 512], xn,
                                           ("p", p)), lns, "ln2")
                for tbl in range(4):
                    pg = ps8[4]
                    for cb in range(PB):
                        pe(lambda t, o=pg[:, 0:16],
                           l=xn[:, cb * 512 + tbl * 128: cb * 512 + (tbl + 1) * 128],
                           r_=gwS[:, cb * 16:(cb + 1) * 16],
                           st=(cb == 0), sp=(cb == PB - 1): t.matmul(
                               o, l, r_, start=st, stop=sp),
                           r=[(xn, ("p", cb)), (gwS, None)], w=[(pg, None)],
                           inc=(cb == PB - 1))
                    E, m, nm_, s_, gm, g2, q, t4, ctm, cfm = (
                        gsc["E"], gsc["m"], gsc["nm"], gsc["s"], gsc["gm"],
                        gsc["g2"], gsc["q"], gsc["t"], gsc["ctm"], gsc["cfm"])
                    dve(lambda v, pg=pg: v.reduce_max(m[:, :], pg[:, 0:16], axis=AX.X),
                        r=[(pg, None)], w=[(m, None)])
                    dve(lambda v: v.tensor_scalar(nm_[:, :], m[:, :], -1.0, 0.0,
                                                  OP.mult, OP.add),
                        r=[(m, None)], w=[(nm_, None)])
                    acte(lambda s, pg=pg: s.activation(E[:, :], pg[:, 0:16], AF.Exp,
                                                       bias=nm_[:, 0:1]),
                         r=[(pg, None), (nm_, None)], w=[(E, None)])
                    dve(lambda v: v.reduce_sum(m[:, :], E[:, :], axis=AX.X),
                        r=[(E, None)], w=[(m, None)])
                    dve(lambda v: v.reciprocal(m[:, :], m[:, :]),
                        r=[(m, None)], w=[(m, None)])
                    dve(lambda v: v.tensor_scalar(s_[:, :], E[:, :], m[:, 0:1], 0.0,
                                                  OP.mult, OP.add),
                        r=[(E, None), (m, None)], w=[(s_, None)])
                    dve(lambda v: v.reduce_max(
                        gm[:, :], s_[:, :].rearrange("p (g e) -> p g e", g=4),
                        axis=AX.X),
                        r=[(s_, None)], w=[(gm, None)])
                    dve(lambda v: v.reduce_max(m[:, :], gm[:, :], axis=AX.X),
                        r=[(gm, None)], w=[(m, None)])
                    dve(lambda v: v.tensor_scalar(g2[:, :], gm[:, :], m[:, 0:1], -1e30,
                                                  OP.is_equal, OP.mult),
                        r=[(gm, None), (m, None)], w=[(g2, None)])
                    dve(lambda v: v.tensor_tensor(g2[:, :], gm[:, :], g2[:, :], OP.add),
                        r=[(gm, None), (g2, None)], w=[(g2, None)])
                    dve(lambda v: v.reduce_max(m[:, :], g2[:, :], axis=AX.X),
                        r=[(g2, None)], w=[(m, None)])
                    dve(lambda v: v.tensor_scalar(gm[:, :], gm[:, :], m[:, 0:1], 0.0,
                                                  OP.is_ge, OP.add),
                        r=[(gm, None), (m, None)], w=[(gm, None)])
                    dve(lambda v: v.tensor_tensor(
                        q[:, :].rearrange("p (g e) -> p g e", g=4),
                        s_[:, :].rearrange("p (g e) -> p g e", g=4),
                        gm[:, :].unsqueeze(2).broadcast_to((128, 4, 4)), OP.mult),
                        r=[(s_, None), (gm, None)], w=[(q, None)])
                    for it in range(4):
                        dve(lambda v: v.reduce_max(t4[:, :], q[:, :], axis=AX.X),
                            r=[(q, None)], w=[(t4, None)])
                        if it < 3:
                            dve(lambda v: v.tensor_scalar(
                                E[:, :], q[:, :], t4[:, 0:1], -1e30,
                                OP.is_equal, OP.mult),
                                r=[(q, None), (t4, None)], w=[(E, None)])
                            dve(lambda v: v.tensor_tensor(q[:, :], q[:, :], E[:, :],
                                                          OP.add),
                                r=[(q, None), (E, None)], w=[(q, None)])
                    dve(lambda v: v.tensor_tensor(
                        E[:, :].rearrange("p (g e) -> p g e", g=4),
                        s_[:, :].rearrange("p (g e) -> p g e", g=4),
                        gm[:, :].unsqueeze(2).broadcast_to((128, 4, 4)), OP.mult),
                        r=[(s_, None), (gm, None)], w=[(E, None)])
                    dve(lambda v: v.tensor_scalar(E[:, :], E[:, :], t4[:, 0:1], 0.0,
                                                  OP.is_ge, OP.add),
                        r=[(E, None), (t4, None)], w=[(E, None)])
                    dve(lambda v: v.tensor_tensor(ctm[:, :], s_[:, :], E[:, :],
                                                  OP.mult),
                        r=[(s_, None), (E, None)], w=[(ctm, None)])
                    pe(lambda t, tbl=tbl: t.transpose(
                        ps8[5][:, :].bitcast(BF16)[0:16, tbl * 128:(tbl + 1) * 128],
                        ctm[:, :], c_id[:, :]),
                       r=[(ctm, None), (c_id, None)], w=[(ps8[5], None)])
                    dve(lambda v, tbl=tbl: v.tensor_copy(
                        cfm[:, tbl * 128:(tbl + 1) * 128],
                        ps8[5][:, :].bitcast(BF16)[0:16, tbl * 128:(tbl + 1) * 128]),
                        r=[(ps8[5], None)], w=[(cfm, ("t", tbl))])
                for e in range(NE):
                    pc = ps8[6 + e % 2]
                    pe(lambda t, e=e, pc=pc: t.matmul(
                        pc[:, :], c_sele[:, e * 128:(e + 1) * 128], gsc["cfm"][:, :],
                        start=True, stop=True),
                       r=[(c_sele, None), (gsc["cfm"], ("t", 0)),
                          (gsc["cfm"], ("t", 1)), (gsc["cfm"], ("t", 2)),
                          (gsc["cfm"], ("t", 3))],
                       w=[(pc, None)])
                    acte(lambda s, e=e, pc=pc: s.activation(
                        call[:, e * 512:(e + 1) * 512], pc[:, :], AF.Identity),
                        r=[(pc, None)], w=[(call, ("e", e))])
                for ch, chunk in enumerate(FFN_CHUNKS):
                    for ei, e in enumerate(chunk):
                        wbuf = (W0, W1)[e % 2]
                        if first_w13[0]:
                            ldw(wbuf, dw13[e])
                            first_w13[0] = False
                        for ib in range(4):
                            for cb in range(PB):
                                pe(lambda t, o=ps8[ib][:, :],
                                   l=wbuf[:, cb * D + ib * 128: cb * D + (ib + 1) * 128],
                                   r_=xn[:, cb * 512:(cb + 1) * 512],
                                   st=(cb == 0), sp=(cb == PB - 1): t.matmul(
                                       o, l, r_, start=st, stop=sp),
                                   r=[(wbuf, ("ld", 0)), (xn, ("p", cb))],
                                   w=[(ps8[ib], None)], inc=(cb == PB - 1))
                            for cb in range(PB):
                                pe(lambda t, o=ps8[4 + ib][:, :],
                                   l=wbuf[:, cb * D + 512 + ib * 128:
                                          cb * D + 512 + (ib + 1) * 128],
                                   r_=xn[:, cb * 512:(cb + 1) * 512],
                                   st=(cb == 0), sp=(cb == PB - 1): t.matmul(
                                       o, l, r_, start=st, stop=sp),
                                   r=[(wbuf, ("ld", 0)), (xn, ("p", cb))],
                                   w=[(ps8[4 + ib], None)], inc=(cb == PB - 1))
                        nxt = e + 1
                        if nxt < NEX:
                            ldw((W0, W1)[nxt % 2], dw13[nxt])
                        elif inv == 0 or tt < n5 - 1:
                            pass
                        for ib in range(4):
                            acte(lambda s, ib=ib: s.activation(
                                sil[:, ib * 512:(ib + 1) * 512], ps8[ib][:, :],
                                AF.Silu),
                                r=[(ps8[ib], None)], w=[(sil, ("i", ib))])
                            if e < NE:
                                dve(lambda v, ib=ib, e=e: v.tensor_tensor(
                                    h3c[:, ib * 512:(ib + 1) * 512], ps8[4 + ib][:, :],
                                    call[:, e * 512:(e + 1) * 512], OP.mult),
                                    r=[(ps8[4 + ib], None), (call, ("e", e))],
                                    w=[(h3c, ("i", ib))])
                                dve(lambda v, ib=ib, ei=ei: v.tensor_tensor(
                                    asl[:, (ei * 4 + ib) * 512:(ei * 4 + ib + 1) * 512],
                                    sil[:, ib * 512:(ib + 1) * 512],
                                    h3c[:, ib * 512:(ib + 1) * 512], OP.mult),
                                    r=[(sil, ("i", ib)), (h3c, ("i", ib))],
                                    w=[(asl, ("a", ei * 4 + ib))])
                            else:
                                dve(lambda v, ib=ib, ei=ei: v.tensor_tensor(
                                    asl[:, (ei * 4 + ib) * 512:(ei * 4 + ib + 1) * 512],
                                    sil[:, ib * 512:(ib + 1) * 512], ps8[4 + ib][:, :],
                                    OP.mult),
                                    r=[(sil, ("i", ib)), (ps8[4 + ib], None)],
                                    w=[(asl, ("a", ei * 4 + ib))])
                    for ei, e in enumerate(chunk):
                        wb2 = w2f[ei % 2]
                        ldw(wb2, dw2[e])
                        for db in range(PB):
                            for ib in range(4):
                                pe(lambda t, o=ps8[db][:, :],
                                   l=wb2[:, ib * D + db * 128: ib * D + (db + 1) * 128],
                                   r_=asl[:, (ei * 4 + ib) * 512:
                                          (ei * 4 + ib + 1) * 512],
                                   st=(ei == 0 and ib == 0),
                                   sp=(ei == len(chunk) - 1 and ib == 3): t.matmul(
                                       o, l, r_, start=st, stop=sp,
                                       skip_group_check=True),
                                   r=[(wb2, ("ld", 0)), (asl, ("a", ei * 4 + ib))],
                                   w=[(ps8[db], None)],
                                   inc=(ib == 3))
                    for db in range(PB):
                        dve(lambda v, db=db, tt=tt: v.tensor_tensor(
                            x8[db][:, tt * 512:(tt + 1) * 512],
                            x8[db][:, tt * 512:(tt + 1) * 512], ps8[db][:, :], OP.add),
                            r=[(x8[db], None), (ps8[db], None)], w=[(x8[db], None)])
                first_w13[0] = True
            tk.barrier()
            tk.drop([lns[k] for k in lns] + xn2 + w2f +
                    [asl, sil, h3c, call] + list(gsc.values()))

    def attn_b(b):
        with ExitStack() as ph:
            sbp = lambda n, s, dt=F32: ph.enter_context(
                nc.sbuf_tensor(_nm(n), s, dt))
            lns = mk_ln_scratch(sbp)
            xn3 = sbp("xn3", [128, PB * 512], BF16)
            tmp = sbp("tmpmix", [128, PB * 256], BF16)
            xk = sbp("xk", [128, PB * 256], BF16)
            xv = sbp("xv", [128, PB * 256], BF16)
            xr = sbp("xr", [128, PB * 256], BF16)
            lastc = sbp("lastc", [128, PB], BF16)
            wkva = sbp("wkva", [128, PB * T], BF16)
            sA = sbp("sA", [128, 256])
            sB = sbp("sB", [128, 256])
            sC = sbp("sC", [128, 256])
            sD = sbp("sD", [128, 256])
            Pb = sbp("Pb", [128, 257])
            Qb = sbp("Qb", [128, 257])
            Pst = sbp("Pst", [128, PB])
            Qst = sbp("Qst", [128, PB])
            yt = sbp("yt", [128, PB * 256], BF16)

            def mix(dst, tmcol, omcol, pb, tti):
                t0 = (tti % 2) * 256
                a_tmp = tmp[:, pb * 256: pb * 256 + 256]
                a_xn = xn3[:, pb * 512 + t0: pb * 512 + t0 + 256]
                a_tm = pv[:, tmcol + pb: tmcol + pb + 1]
                a_om = dpar[:, omcol + pb: omcol + pb + 1]
                dve(lambda v, o=a_tmp, i=a_xn, s=a_tm: v.tensor_scalar(
                    o, i, s, 0.0, OP.mult, OP.add),
                    r=[(xn3, ("p", pb))], w=[(tmp, ("p", pb))])
                dve(lambda v, o=dst[:, pb * 256: pb * 256 + 1],
                    i0=lastc[:, pb:pb + 1], s=a_om,
                    i1=tmp[:, pb * 256: pb * 256 + 1]: v.scalar_tensor_tensor(
                        o, i0, s, i1, OP.mult, OP.add),
                    r=[(lastc, ("p", pb)), (dpar, ("c", omcol)), (tmp, ("p", pb))],
                    w=[(dst, ("p", pb))])
                dve(lambda v, o=dst[:, pb * 256 + 1: pb * 256 + 256],
                    i0=xn3[:, pb * 512 + t0: pb * 512 + t0 + 255], s=a_om,
                    i1=tmp[:, pb * 256 + 1: pb * 256 + 256]: v.scalar_tensor_tensor(
                        o, i0, s, i1, OP.mult, OP.add),
                    r=[(xn3, ("p", pb)), (dpar, ("c", omcol)), (tmp, ("p", pb))],
                    w=[(dst, ("p", pb))])

            ldw(W0, dW["wkey"])
            ldw(W1, dW["wval"])
            for p in range(PB):
                dve(lambda v, p=p: v.memset(lastc[:, p:p + 1], 0.0),
                    r=[], w=[(lastc, ("p", p))])
                dve(lambda v, p=p: v.memset(Pst[:, p:p + 1], 0.0),
                    r=[], w=[(Pst, ("p", p))])
                dve(lambda v, p=p: v.memset(Qst[:, p:p + 1], 0.0),
                    r=[], w=[(Qst, ("p", p))])
            for tti in range(n2):
                if tti % 2 == 0:
                    ln_tile(x8, tti // 2,
                            lambda p: (xn3[:, p * 512:(p + 1) * 512], xn3, ("p", p)),
                            lns, "ln3")
                for p in range(PB):
                    mix(xk, TMK, OTMK, p, tti)
                    mix(xv, TMV, OTMV, p, tti)
                for p in range(PB):
                    dve(lambda v, p=p, tti=tti: v.tensor_copy(
                        lastc[:, p:p + 1],
                        xn3[:, p * 512 + (tti % 2) * 256 + 255:
                            p * 512 + (tti % 2) * 256 + 256]),
                        r=[(xn3, ("p", p))], w=[(lastc, ("p", p))])
                for db in range(PB):
                    pk = ps8[db // 2][:, (db % 2) * 256:(db % 2) * 256 + 256]
                    pvv = ps8[4 + db // 2][:, (db % 2) * 256:(db % 2) * 256 + 256]
                    for cb in range(PB):
                        pe(lambda t, o=pk,
                           l=W0[:, cb * D + db * 128: cb * D + (db + 1) * 128],
                           r_=xk[:, cb * 256:(cb + 1) * 256],
                           st=(cb == 0 and db % 2 == 0), sp=(cb == PB - 1): t.matmul(
                               o, l, r_, start=st, stop=False,
                               skip_group_check=True),
                           r=[(W0, ("ld", 0)), (xk, ("p", cb))],
                           w=[(ps8[db // 2], None)], inc=(cb == PB - 1))
                        pe(lambda t, o=pvv,
                           l=W1[:, cb * D + db * 128: cb * D + (db + 1) * 128],
                           r_=xv[:, cb * 256:(cb + 1) * 256],
                           st=(cb == 0 and db % 2 == 0), sp=(cb == PB - 1): t.matmul(
                               o, l, r_, start=st, stop=False,
                               skip_group_check=True),
                           r=[(W1, ("ld", 0)), (xv, ("p", cb))],
                           w=[(ps8[4 + db // 2], None)], inc=(cb == PB - 1))
                for p in range(PB):
                    kps = ps8[p // 2][:, (p % 2) * 256:(p % 2) * 256 + 256]
                    vps = ps8[4 + p // 2][:, (p % 2) * 256:(p % 2) * 256 + 256]
                    kkey = (ps8[p // 2], None)
                    vkey = (ps8[4 + p // 2], None)
                    acte(lambda s, kps=kps, p=p: s.activation(
                        sC[:, :], kps, AF.Exp, bias=pv[:, UU + p:UU + p + 1]),
                        r=[kkey, (pv, None)], w=[(sC, None)])
                    acte(lambda s, kps=kps: s.activation(sA[:, :], kps, AF.Exp),
                         r=[kkey], w=[(sA, None)])
                    dve(lambda v, vps=vps: v.tensor_tensor(sB[:, :], sA[:, :], vps,
                                                           OP.mult),
                        r=[(sA, None), vkey], w=[(sB, None)])
                    dve(lambda v, vps=vps: v.tensor_tensor(sD[:, :], sC[:, :], vps,
                                                           OP.mult),
                        r=[(sC, None), vkey], w=[(sD, None)])
                    dve(lambda v, p=p: v.tensor_copy(Pb[:, 0:1], Pst[:, p:p + 1]),
                        r=[(Pst, ("p", p))], w=[(Pb, None)])
                    dve(lambda v, p=p: v.tensor_tensor_scan(
                        Pb[:, 1:257],
                        dpar[:, EW + p:EW + p + 1].broadcast_to((128, 256)),
                        sB[:, :], Pst[:, p:p + 1], OP.mult, OP.add),
                        r=[(dpar, ("c", EW)), (sB, None), (Pst, ("p", p)),
                           (Pb, None)],
                        w=[(Pb, None)])
                    dve(lambda v, p=p: v.tensor_copy(Qb[:, 0:1], Qst[:, p:p + 1]),
                        r=[(Qst, ("p", p))], w=[(Qb, None)])
                    dve(lambda v, p=p: v.tensor_tensor_scan(
                        Qb[:, 1:257],
                        dpar[:, EW + p:EW + p + 1].broadcast_to((128, 256)),
                        sA[:, :], Qst[:, p:p + 1], OP.mult, OP.add),
                        r=[(dpar, ("c", EW)), (sA, None), (Qst, ("p", p)),
                           (Qb, None)],
                        w=[(Qb, None)])
                    dve(lambda v, p=p: v.tensor_copy(Pst[:, p:p + 1], Pb[:, 256:257]),
                        r=[(Pb, None)], w=[(Pst, ("p", p))])
                    dve(lambda v, p=p: v.tensor_copy(Qst[:, p:p + 1], Qb[:, 256:257]),
                        r=[(Qb, None)], w=[(Qst, ("p", p))])
                    dve(lambda v: v.tensor_tensor(sD[:, :], Pb[:, 0:256], sD[:, :],
                                                  OP.add),
                        r=[(Pb, None), (sD, None)], w=[(sD, None)])
                    dve(lambda v: v.tensor_tensor(sC[:, :], Qb[:, 0:256], sC[:, :],
                                                  OP.add),
                        r=[(Qb, None), (sC, None)], w=[(sC, None)])
                    acte(lambda s: s.activation(sC[:, :], sC[:, :], AF.Ln),
                         r=[(sC, None)], w=[(sC, None)])
                    acte(lambda s: s.activation(sC[:, :], sC[:, :], AF.Exp,
                                                scale=-1.0),
                         r=[(sC, None)], w=[(sC, None)])
                    dve(lambda v, p=p, tti=tti: v.tensor_tensor(
                        wkva[:, p * T + tti * 256: p * T + (tti + 1) * 256],
                        sD[:, :], sC[:, :], OP.mult),
                        r=[(sD, None), (sC, None)], w=[(wkva, ("pt", (p, tti)))])
            ldw(W0, dW["wrec"])
            ldw(W1, dW["wout"])
            for p in range(PB):
                dve(lambda v, p=p: v.memset(lastc[:, p:p + 1], 0.0),
                    r=[], w=[(lastc, ("p", p))])
            for tti in range(n2):
                if tti % 2 == 0:
                    ln_tile(x8, tti // 2,
                            lambda p: (xn3[:, p * 512:(p + 1) * 512], xn3, ("p", p)),
                            lns, "ln3b")
                for p in range(PB):
                    mix(xr, TMR, OTMR, p, tti)
                for p in range(PB):
                    dve(lambda v, p=p, tti=tti: v.tensor_copy(
                        lastc[:, p:p + 1],
                        xn3[:, p * 512 + (tti % 2) * 256 + 255:
                            p * 512 + (tti % 2) * 256 + 256]),
                        r=[(xn3, ("p", p))], w=[(lastc, ("p", p))])
                for db in range(PB):
                    pr = ps8[db // 2][:, (db % 2) * 256:(db % 2) * 256 + 256]
                    for cb in range(PB):
                        pe(lambda t, o=pr,
                           l=W0[:, cb * D + db * 128: cb * D + (db + 1) * 128],
                           r_=xr[:, cb * 256:(cb + 1) * 256],
                           st=(cb == 0 and db % 2 == 0), sp=(cb == PB - 1): t.matmul(
                               o, l, r_, start=st, stop=False,
                               skip_group_check=True),
                           r=[(W0, ("ld", 0)), (xr, ("p", cb))],
                           w=[(ps8[db // 2], None)], inc=(cb == PB - 1))
                for p in range(PB):
                    rps = ps8[p // 2][:, (p % 2) * 256:(p % 2) * 256 + 256]
                    rkey = (ps8[p // 2], None)
                    acte(lambda s, rps=rps: s.activation(sA[:, :], rps, AF.Exp,
                                                         scale=-1.0),
                         r=[rkey], w=[(sA, None)])
                    dve(lambda v: v.tensor_scalar(sA[:, :], sA[:, :], 1.0, 0.0,
                                                  OP.add, OP.add),
                        r=[(sA, None)], w=[(sA, None)])
                    acte(lambda s: s.activation(sA[:, :], sA[:, :], AF.Ln),
                         r=[(sA, None)], w=[(sA, None)])
                    acte(lambda s: s.activation(sA[:, :], sA[:, :], AF.Exp,
                                                scale=-1.0),
                         r=[(sA, None)], w=[(sA, None)])
                    dve(lambda v, p=p, tti=tti: v.tensor_tensor(
                        yt[:, p * 256:(p + 1) * 256],
                        wkva[:, p * T + tti * 256: p * T + (tti + 1) * 256],
                        sA[:, :], OP.mult),
                        r=[(wkva, ("pt", (p, tti))), (sA, None)],
                        w=[(yt, ("p", p))])
                for db in range(PB):
                    po = ps8[4 + db // 2][:, (db % 2) * 256:(db % 2) * 256 + 256]
                    for cb in range(PB):
                        pe(lambda t, o=po,
                           l=W1[:, cb * D + db * 128: cb * D + (db + 1) * 128],
                           r_=yt[:, cb * 256:(cb + 1) * 256],
                           st=(cb == 0 and db % 2 == 0), sp=(cb == PB - 1): t.matmul(
                               o, l, r_, start=st, stop=False,
                               skip_group_check=True),
                           r=[(W1, ("ld", 0)), (yt, ("p", cb))],
                           w=[(ps8[4 + db // 2], None)], inc=(cb == PB - 1))
                    dve(lambda v, db=db, tti=tti, po=po: v.tensor_tensor(
                        x8[db][:, tti * 256:(tti + 1) * 256],
                        x8[db][:, tti * 256:(tti + 1) * 256], po, OP.add),
                        r=[(x8[db], None), (ps8[4 + db // 2], None)],
                        w=[(x8[db], None)])
            tk.barrier()
            tk.drop([lns[k] for k in lns] + [xn3, tmp, xk, xv, xr, lastc, wkva,
                                             sA, sB, sC, sD, Pb, Qb, Pst, Qst, yt])

    def tap(name, b):
        if taps:
            for p in range(PB):
                sq(lambda e, p=p: e.dma_start(dtap[name][b, p], x8[p][:, :]),
                   r=[(x8[p], None)])

    for b in range(NB):
        if b > 0:
            for p in range(PB):
                sq(lambda e, p=p, b=b: e.dma_start(x8[p][:, :], dx[b, p]),
                   w=[(x8[p], None)])
        attn_a(b)
        tap("attna", b)
        ffn(b, 0)
        tap("ffn1", b)
        attn_b(b)
        tap("attnb", b)
        ffn(b, 1)
        tap("ffn2", b)
        for p in range(PB):
            sq(lambda e, p=p, b=b: e.dma_start(dout[b, p], x8[p][:, :]),
               r=[(x8[p], None)])

    for i in range(tk.NDMA):
        for q, strm in (("sq", "sp"), ("gq", "pool")):
            n = tk.cnt[f"{q}{i}"]
            if n:
                tk.ops[strm].append(
                    lambda e, s=tk.sem[f"{q}{i}"], n=n: e.wait_ge(s, n))

    with es:
        with nc.Block() as block:
            @block.tensor
            def _(t):
                for f in tk.ops["pe"]:
                    f(t)

            @block.scalar
            def _(s):
                for f in tk.ops["act"]:
                    f(s)

            @block.vector
            def _(v):
                for f in tk.ops["dve"]:
                    f(v)

            @block.gpsimd
            def _(g):
                for f in tk.ops["pool"]:
                    f(g)

            @block.sync
            def _(s):
                for f in tk.ops["sp"]:
                    f(s)
    return nc


def _pack_inmap(T=2048):
    raise NotImplementedError


def _pk_proj(w):
    return np.ascontiguousarray(
        w.reshape(8, 128, w.shape[1]).transpose(1, 0, 2).reshape(128, -1))


def host_pack(inp, T):
    import ml_dtypes
    f32 = lambda a: np.asarray(a, dtype=np.float32)
    wq = _pk_proj(f32(inp["L0_attn_a_Wq"]))
    wk = _pk_proj(f32(inp["L0_attn_a_Wk"]))
    wv = _pk_proj(f32(inp["L0_attn_a_Wv"]))
    wo = _pk_proj(f32(inp["L0_attn_a_Wo"]))
    wkey = _pk_proj(f32(inp["L0_attn_b_W_key"]).T)
    wval = _pk_proj(f32(inp["L0_attn_b_W_value"]).T)
    wrec = _pk_proj(f32(inp["L0_attn_b_W_receptance"]).T)
    wout = _pk_proj(f32(inp["L0_attn_b_W_output"]).T)

    w1 = f32(inp["L0_ffn_w1"])
    w2 = f32(inp["L0_ffn_w2"])
    w3 = f32(inp["L0_ffn_w3"])
    ws1 = f32(inp["L0_ffn_ws1"])
    ws2 = f32(inp["L0_ffn_ws2"])
    ws3 = f32(inp["L0_ffn_ws3"])
    w13 = np.zeros((NEX, 128, PB * D), np.float32)
    w2p = np.zeros((NEX, 128, 4 * D), np.float32)
    for e in range(NE):
        w13[e] = _pk_proj(np.concatenate([w1[e], w3[e]], axis=1))
        w2p[e] = np.ascontiguousarray(
            w2[e].reshape(4, 128, D).transpose(1, 0, 2).reshape(128, 4 * D))
    for j in range(2):
        w13[NE + j] = _pk_proj(np.concatenate(
            [ws1[:, j * 512:(j + 1) * 512], ws3[:, j * 512:(j + 1) * 512]], axis=1))
        w2p[NE + j] = np.ascontiguousarray(
            ws2[j * 512:(j + 1) * 512].reshape(4, 128, D).transpose(1, 0, 2)
            .reshape(128, 4 * D))
    gw = _pk_proj(f32(inp["L0_ffn_gate_w"]).T)

    fmv = lambda a: np.ascontiguousarray(f32(a).reshape(8, 128).T)
    pvec = np.zeros((128, 80), np.float32)
    pvec[:, 0:8] = fmv(inp["L0_attn_a_bq"])
    pvec[:, 8:16] = fmv(inp["L0_attn_a_bk"])
    pvec[:, 16:24] = fmv(inp["L0_attn_a_bv"])
    pvec[:, 24:32] = fmv(inp["L0_attn_a_bo"])
    pvec[:, 32:40] = fmv(inp["L0_attn_b_time_first"])
    pvec[:, 40:48] = fmv(inp["L0_attn_b_time_decay"])
    pvec[:, 48:56] = fmv(np.asarray(inp["L0_attn_b_time_mix_k"]).reshape(-1))
    pvec[:, 56:64] = fmv(np.asarray(inp["L0_attn_b_time_mix_v"]).reshape(-1))
    pvec[:, 64:72] = fmv(np.asarray(inp["L0_attn_b_time_mix_r"]).reshape(-1))
    pvec[0:16, 72] = f32(inp["L0_attn_a_m"])
    brow = np.concatenate([f32(inp["L0_attn_a_bk"]),
                           f32(inp["L0_attn_a_bv"])])[None, :]

    bf = ml_dtypes.bfloat16
    ones_k1 = np.ones((1, 128), np.float32)
    onescol_b = np.ones((128, 1), bf)
    bd = np.zeros((128, PB * HEAD), bf)
    selh = np.zeros((16, PB * 128), bf)
    for p in range(PB):
        bd[0:64, p * 16 + 2 * p] = 1.0
        bd[64:128, p * 16 + 2 * p + 1] = 1.0
        selh[2 * p, p * 128: p * 128 + 64] = 1.0
        selh[2 * p + 1, p * 128 + 64: p * 128 + 128] = 1.0
    sel_e = np.zeros((16, NE * 128), bf)
    for e in range(NE):
        sel_e[e, e * 128:(e + 1) * 128] = 1.0
    ident_b = np.eye(128, dtype=bf)

    cvals = np.zeros((128, 8), np.float32)
    for i, v in enumerate([1e-5, -0.5, -1.0, -float(np.log(T))]):
        cvals[:, i] = v

    return dict(wq=wq, wk=wk, wv=wv, wo=wo, wkey=wkey, wval=wval, wrec=wrec,
                wout=wout, w13=w13, w2=w2p, gw=gw, pvec=pvec, brow=brow,
                ones_k1=ones_k1, onescol_b=onescol_b, bd=bd, selh=selh,
                sel_e=sel_e, ident_b=ident_b, cvals=cvals)


def pack_x(xbatches, T):
    return np.stack([np.ascontiguousarray(b.T).reshape(8, 128, T)
                     for b in xbatches])


def unpack_out(o, T):
    return np.stack([o[b].reshape(D, T).T for b in range(o.shape[0])])


def kernel(**inp):
    x = np.asarray(inp["L0_attn_a_x"], dtype=np.float32)
    T = x.shape[1]
    shared = host_pack(inp, T)
    NBc = x.shape[0] // 8
    in_maps = []
    for c in range(8):
        m = dict(shared)
        m["x"] = pack_x([x[c * NBc + b] for b in range(NBc)], T)
        in_maps.append(m)

    nc = build_nc(T=T, NB=NBc)
    res = run_bass_kernel_spmd(nc, in_maps, core_ids=list(range(8)))
    out = np.empty((8 * NBc, T, D), np.float32)
    for c in range(8):
        out[c * NBc:(c + 1) * NBc] = unpack_out(res.results[c]["out"], T)
    return out


# version 04
# speedup vs baseline: 1.0046x; 1.0046x over previous
import numpy as np
from contextlib import ExitStack

import concourse.bass as bass
import concourse.mybir as mybir
from concourse.bass_utils import run_bass_kernel_spmd

F32 = mybir.dt.float32
BF16 = mybir.dt.bfloat16
AF = mybir.ActivationFunctionType
OP = mybir.AluOpType
AX = mybir.AxisListType

D = 1024
PB = 8
NE = 16
NEX = 18
HEAD = 16
HD = 64
FFN_CHUNKS = ((0, 1, 2, 3), (4, 5, 6, 7), (8, 9, 10, 11), (12, 13, 14, 15),
              (16, 17))


class _Rec:
    __slots__ = ("src", "cnt")

    def __init__(self, src, cnt):
        self.src = src
        self.cnt = cnt


class TK:
    STREAMS = ("pe", "act", "dve", "pool", "sp")
    ENGS = ("pe", "act", "dve", "pool")
    NDMA = 8

    def __init__(self, nc, es):
        self.nc = nc
        self.psum_ids = set()
        self.ops = {s: [] for s in self.STREAMS}
        self.sem = {s: es.enter_context(nc.semaphore("sem_" + s)) for s in self.ENGS}
        for q in ("gq", "sq"):
            for i in range(self.NDMA):
                self.sem[f"{q}{i}"] = es.enter_context(nc.semaphore(f"sem_{q}{i}"))
        self.cnt = {s: 0 for s in self.sem}
        self.rr = {"gq": 0, "sq": 0}
        self.waited = {c: {s: 0 for s in self.sem} for c in self.STREAMS}
        self.floor = {c: {} for c in self.STREAMS}
        self.bufs = {}
        self.pe_pending = []

    def _deps(self, h, tag, readers):
        recs = self.bufs.get(id(h))
        if not recs:
            return ()
        out = []
        for t2, r in recs.items():
            if tag is None or t2 is None or t2 == tag:
                if r["w"] is not None:
                    out.append(r["w"])
                if readers:
                    out.extend(r["r"].values())
        return out

    def emit(self, stream, fn, reads=(), writes=(), inc=True, dma=None):
        need = dict(self.floor[stream])
        self.floor[stream] = {}
        for h, tag in reads:
            psum_rr = id(h) in self.psum_ids
            for r in self._deps(h, tag, psum_rr):
                if r.src == "pe" and stream == "pe":
                    continue
                if psum_rr and r.src == stream:
                    continue
                assert r.cnt is not None, "dep on open PE group"
                if r.cnt > need.get(r.src, 0):
                    need[r.src] = r.cnt
        for h, tag in writes:
            for r in self._deps(h, tag, True):
                if r.src == "pe" and stream == "pe":
                    continue
                assert r.cnt is not None, "dep on open PE group"
                if r.cnt > need.get(r.src, 0):
                    need[r.src] = r.cnt
        sname = None
        if dma is not None:
            sname = f"{dma}{self.rr[dma]}"
            self.rr[dma] = (self.rr[dma] + 1) % self.NDMA
            if self.cnt[sname] > need.get(sname, 0):
                need[sname] = self.cnt[sname]
        waits = []
        wt = self.waited[stream]
        for src in sorted(need):
            if need[src] > wt[src]:
                waits.append((self.sem[src], need[src]))
                wt[src] = need[src]

        if dma is not None:
            self.cnt[sname] += 16
            rec = _Rec(sname, self.cnt[sname])
            inc_sem, inc_n = self.sem[sname], 16
        elif inc:
            self.cnt[stream] += 1
            rec = _Rec(stream, self.cnt[stream])
            inc_sem, inc_n = self.sem[stream], 1
            if stream == "pe":
                for p in self.pe_pending:
                    p.cnt = rec.cnt
                self.pe_pending = []
        else:
            assert stream == "pe"
            rec = _Rec("pe", None)
            self.pe_pending.append(rec)
            inc_sem, inc_n = None, 0

        def closure(eng, _w=tuple(waits), _f=fn, _s=inc_sem, _n=inc_n):
            for sem, cnt in _w:
                eng.wait_ge(sem, cnt)
            inst = _f(eng)
            if _s is not None:
                inst.then_inc(_s, _n)

        self.ops[stream].append(closure)

        for h, tag in writes:
            self.bufs.setdefault(id(h), {})[tag] = {"w": rec, "r": {}}
        for h, tag in reads:
            recs = self.bufs.setdefault(id(h), {})
            recs.setdefault(tag, {"w": None, "r": {}})["r"][rec.src] = rec
        return rec

    def barrier(self):
        for c in ("pe", "act", "dve", "sp"):
            f = self.floor[c]
            for s in ("pe", "act", "dve"):
                f[s] = max(f.get(s, 0), self.cnt[s])

    def drop(self, handles):
        for h in handles:
            self.bufs.pop(id(h), None)


def build_nc(T=2048, NB=2, taps=False):
    nc = bass.Bass("TRN2", target_bir_lowering=False)
    n5 = T // 512
    n2 = T // 256
    LOGT = float(np.log(T))

    dx = nc.dram_tensor("x", [NB, PB, 128, T], F32, kind="ExternalInput").ap()
    dout = nc.dram_tensor("out", [NB, PB, 128, T], F32, kind="ExternalOutput").ap()
    dW = {}
    for nm in ("wq", "wk", "wv", "wo", "wkey", "wval", "wrec", "wout"):
        dW[nm] = nc.dram_tensor(nm, [128, PB * D], F32, kind="ExternalInput").ap()
    dw13 = nc.dram_tensor("w13", [NEX, 128, PB * D], F32, kind="ExternalInput").ap()
    dw2 = nc.dram_tensor("w2", [NEX, 128, 4 * D], F32, kind="ExternalInput").ap()
    dgw = nc.dram_tensor("gw", [128, PB * NE], F32, kind="ExternalInput").ap()
    dpv = nc.dram_tensor("pvec", [128, 80], F32, kind="ExternalInput").ap()
    dbrow = nc.dram_tensor("brow", [1, 2 * D], F32, kind="ExternalInput").ap()
    dones = nc.dram_tensor("ones_k1", [1, 128], F32, kind="ExternalInput").ap()
    donesb = nc.dram_tensor("onescol_b", [128, 1], BF16, kind="ExternalInput").ap()
    dbd = nc.dram_tensor("bd", [128, PB * HEAD], BF16, kind="ExternalInput").ap()
    dselh = nc.dram_tensor("selh", [16, PB * 128], BF16, kind="ExternalInput").ap()
    dsele = nc.dram_tensor("sel_e", [16, NE * 128], BF16, kind="ExternalInput").ap()
    dident = nc.dram_tensor("ident_b", [128, 128], BF16, kind="ExternalInput").ap()
    dcv = nc.dram_tensor("cvals", [128, 8], F32, kind="ExternalInput").ap()
    dtap = {}
    if taps:
        for nm in ("xn1", "attna", "ffn1", "attnb", "ffn2", "wkv"):
            dtap[nm] = nc.dram_tensor("tap_" + nm, [NB, PB, 128, T], F32,
                                      kind="ExternalOutput").ap()

    es = ExitStack()
    _uid = [0]

    def _nm(name):
        _uid[0] += 1
        return f"{name}_{_uid[0]}"

    sb = lambda name, shape, dt=F32: es.enter_context(nc.sbuf_tensor(name, shape, dt))
    tk = TK(nc, es)

    x8 = [sb(f"x{i}", [128, T]) for i in range(PB)]
    c_ones = sb("c_ones", [1, 128])
    c_onesb = sb("c_onesb", [128, 1], BF16)
    pv = sb("pv", [128, 80])
    dpar = sb("dpar", [128, 32])
    sgate = sb("sgate", [16, 1])
    gwS = sb("gwS", [128, PB * NE], BF16)
    msb = sb("msb", [128, 512], BF16)
    W0 = sb("W0", [128, PB * D], BF16)
    W1 = sb("W1", [128, PB * D], BF16)
    w2f = [sb(f"w2f{i}", [128, 4 * D], BF16) for i in range(2)]
    cvals = sb("cvals_s", [128, 8])
    ps8 = [es.enter_context(nc.psum_tensor(f"ps{i}", [128, 512], F32))
           for i in range(8)]
    tk.psum_ids = {id(p) for p in ps8}

    BQ, BK, BV, BO, UU, TD, TMK, TMV, TMR, MCOL = 0, 8, 16, 24, 32, 40, 48, 56, 64, 72
    OTMK, OTMV, OTMR, EW = 0, 8, 16, 24

    pe = lambda fn, r=(), w=(), inc=True: tk.emit("pe", fn, r, w, inc)
    acte = lambda fn, r=(), w=(): tk.emit("act", fn, r, w)
    dve = lambda fn, r=(), w=(): tk.emit("dve", fn, r, w)
    gq = lambda fn, r=(), w=(): tk.emit("pool", fn, r, w, dma="gq")
    sq = lambda fn, r=(), w=(): tk.emit("sp", fn, r, w, dma="sq")

    def ldw(dst, src_ap, r0=0, r1=None):
        n = dst.shape[1] if r1 is None else (r1 - r0)
        gq(lambda g, d=dst[:, r0:r0 + n], s=src_ap[:, r0:r0 + n]: g.dma_start(d, s),
           w=[(dst, ("ld", r0))])

    CV = [1e-5, -0.5, -1.0, -LOGT]
    for i, v in enumerate(CV):
        nc.const_aps.register(F32, v, cvals[:, i:i + 1])
    sq(lambda e: e.dma_start(cvals[:, :], dcv), w=[(cvals, None)])
    sq(lambda e: e.dma_start(c_ones[:, :], dones), w=[(c_ones, None)])
    sq(lambda e: e.dma_start(c_onesb[:, :], donesb), w=[(c_onesb, None)])
    sq(lambda e: e.dma_start(pv[:, :], dpv), w=[(pv, None)])
    gq(lambda e: e.dma_start(gwS[:, :], dgw), w=[(gwS, None)])
    for p in range(PB):
        sq(lambda e, p=p: e.dma_start(x8[p][:, :], dx[0, p]), w=[(x8[p], None)])

    for c in ("pe", "act", "dve"):
        for i in range(tk.NDMA):
            for q in ("sq", "gq"):
                if tk.cnt[f"{q}{i}"]:
                    tk.floor[c][f"{q}{i}"] = tk.cnt[f"{q}{i}"]
    for col, src in ((OTMK, TMK), (OTMV, TMV), (OTMR, TMR)):
        dve(lambda v, c=col, s=src: v.tensor_scalar(
            dpar[:, c:c + 8], pv[:, s:s + 8], -1.0, 1.0, OP.mult, OP.add),
            r=[(pv, None)], w=[(dpar, ("c", col))])
    acte(lambda s: s.activation(dpar[:, EW:EW + 8], pv[:, TD:TD + 8], AF.Exp),
         r=[(pv, None)], w=[(dpar, ("c", EW))])
    dve(lambda v: v.tensor_scalar(dpar[:, EW:EW + 8], dpar[:, EW:EW + 8],
                                  -1.0, 0.0, OP.mult, OP.bypass),
        r=[(dpar, ("c", EW))], w=[(dpar, ("c", EW))])
    acte(lambda s: s.activation(dpar[:, EW:EW + 8], dpar[:, EW:EW + 8], AF.Exp),
         r=[(dpar, ("c", EW))], w=[(dpar, ("c", EW))])
    acte(lambda s: s.activation(sgate[:, :], pv[0:16, MCOL:MCOL + 1], AF.Sigmoid),
         r=[(pv, None)], w=[(sgate, None)])
    acte(lambda s: s.activation(sgate[:, :], sgate[:, :], AF.Exp, scale=-LOGT),
         r=[(sgate, None)], w=[(sgate, None)])

    def ln_tile(xsrc, tt, dst_fn, scratch, tag):
        t0 = tt * 512
        xc, xq, stats = scratch["xc"], scratch["xq"], scratch["stats"]
        for hh in range(4):
            for p in range(PB):
                s0 = t0 + hh * 128
                acte(lambda s, p=p, s0=s0: s.activation(
                    xc[:, p * 128:(p + 1) * 128], xsrc[p][:, s0:s0 + 128], AF.Identity),
                    r=[(xsrc[p], ("c", s0 // 128))], w=[(xc, ("p", p))])
                acte(lambda s, p=p, s0=s0: s.activation(
                    xq[:, p * 128:(p + 1) * 128], xsrc[p][:, s0:s0 + 128], AF.Square),
                    r=[(xsrc[p], ("c", s0 // 128))], w=[(xq, ("p", p))])
            for p in range(PB):
                pe(lambda t, p=p, hh=hh: t.matmul(
                    ps8[0][0:1, hh * 128:(hh + 1) * 128], c_onesb[:, :],
                    xc[:, p * 128:(p + 1) * 128], start=(p == 0), stop=(p == PB - 1)),
                   r=[(c_onesb, None), (xc, ("p", p))], w=[(ps8[0], None)],
                   inc=(p == PB - 1))
                pe(lambda t, p=p, hh=hh: t.matmul(
                    ps8[1][0:1, hh * 128:(hh + 1) * 128], c_onesb[:, :],
                    xq[:, p * 128:(p + 1) * 128], start=(p == 0), stop=(p == PB - 1)),
                   r=[(c_onesb, None), (xq, ("p", p))], w=[(ps8[1], None)],
                   inc=(p == PB - 1))
        dve(lambda v: v.tensor_copy(stats[:, 0:512], ps8[0][0:1, :]),
            r=[(ps8[0], None)], w=[(stats, 0)])
        dve(lambda v: v.tensor_copy(stats[:, 512:1024], ps8[1][0:1, :]),
            r=[(ps8[1], None)], w=[(stats, 1)])
        pe(lambda t: t.matmul(ps8[2][:, :], c_ones[:, :], stats[:, 0:512],
                              start=True, stop=True),
           r=[(c_ones, None), (stats, 0)], w=[(ps8[2], None)])
        pe(lambda t: t.matmul(ps8[3][:, :], c_ones[:, :], stats[:, 512:1024],
                              start=True, stop=True),
           r=[(c_ones, None), (stats, 1)], w=[(ps8[3], None)])
        mean, ex2, var, istd = (scratch["mean"], scratch["ex2"], scratch["var"],
                                scratch["istd"])
        dve(lambda v: v.tensor_scalar(mean[:, :], ps8[2][:, :], 1.0 / D, 0.0,
                                      OP.mult, OP.add),
            r=[(ps8[2], None)], w=[(mean, None)])
        dve(lambda v: v.tensor_scalar(ex2[:, :], ps8[3][:, :], 1.0 / D, 0.0,
                                      OP.mult, OP.add),
            r=[(ps8[3], None)], w=[(ex2, None)])
        dve(lambda v: v.tensor_tensor(var[:, :], mean[:, :], mean[:, :], OP.mult),
            r=[(mean, None)], w=[(var, None)])
        dve(lambda v: v.tensor_tensor(var[:, :], ex2[:, :], var[:, :], OP.subtract),
            r=[(ex2, None), (var, None)], w=[(var, None)])
        acte(lambda s: s.activation(var[:, :], var[:, :], AF.Ln, bias=1e-5),
             r=[(var, None)], w=[(var, None)])
        acte(lambda s: s.activation(istd[:, :], var[:, :], AF.Exp, scale=-0.5),
             r=[(var, None)], w=[(istd, None)])
        dlt = scratch["dlt"]
        for p in range(PB):
            dap, dh, dtag = dst_fn(p)
            dve(lambda v, p=p: v.tensor_tensor(dlt[:, :], xsrc[p][:, t0:t0 + 512],
                                               mean[:, :], OP.subtract),
                r=[(xsrc[p], ("c", tt * 4 + j)) for j in range(4)] + [(mean, None)],
                w=[(dlt, None)])
            dve(lambda v, a=dap: v.tensor_tensor(a, dlt[:, :], istd[:, :], OP.mult),
                r=[(dlt, None), (istd, None)], w=[(dh, dtag)])

    def mk_ln_scratch(sbp):
        return dict(xc=sbp("ln_xc", [128, PB * 128], BF16),
                    xq=sbp("ln_xq", [128, PB * 128], BF16),
                    stats=sbp("ln_stats", [1, 1024]),
                    mean=sbp("ln_mean", [128, 512]), ex2=sbp("ln_ex2", [128, 512]),
                    var=sbp("ln_var", [128, 512]), istd=sbp("ln_istd", [128, 512]),
                    dlt=sbp("ln_dlt", [128, 512]))

    def attn_a(b):
        with ExitStack() as ph:
            sbp = lambda n, s, dt=F32: ph.enter_context(
                nc.sbuf_tensor(_nm(n), s, dt))
            lns = mk_ln_scratch(sbp)
            xn1 = sbp("xn1", [128, PB * 512], BF16)
            btm = sbp("btm", [128, 2 * D])
            browS = sbp("browS", [1, 2 * D])
            c_selh = sbp("c_selh", [16, PB * 128], BF16)
            c_bd = sbp("c_bd", [128, PB * HEAD], BF16)
            sq(lambda e: e.dma_start(browS[:, :], dbrow), w=[(browS, None)])
            sq(lambda e: e.dma_start(c_selh[:, :], dselh), w=[(c_selh, None)])
            sq(lambda e: e.dma_start(c_bd[:, :], dbd), w=[(c_bd, None)])
            for j in range(4):
                pe(lambda t, j=j: t.matmul(ps8[j][:, :], c_ones[:, :],
                                           browS[:, 512 * j:512 * (j + 1)],
                                           start=True, stop=True),
                   r=[(c_ones, None), (browS, None)], w=[(ps8[j], None)])
                dve(lambda v, j=j: v.tensor_copy(btm[:, 512 * j:512 * (j + 1)],
                                                 ps8[j][:, :]),
                    r=[(ps8[j], None)], w=[(btm, ("c", j))])
            knt = [sbp(f"knt{i}", [128, D], BF16) for i in range(2)]
            vt = [sbp(f"vt{i}", [128, D], BF16) for i in range(2)]
            kb = sbp("kb", [128, D])
            ksq = sbp("ksq", [128, D], BF16)
            ksc = sbp("ksc", [128, 16])
            ldw(W0, dW["wk"])
            ldw(W1, dW["wv"])
            for tt in range(n5):
                ln_tile(x8, tt, lambda p: (xn1[:, p * 512:(p + 1) * 512], xn1,
                                           ("p", p)), lns, "ln1")
                for tbl in range(4):
                    tb = tt * 4 + tbl
                    kn, vv = knt[tb % 2], vt[tb % 2]
                    for half in range(2):
                        pk, pv_ = ps8[half], ps8[2 + half]
                        for cb in range(PB):
                            lh = xn1[:, cb * 512 + tbl * 128: cb * 512 + (tbl + 1) * 128]
                            pe(lambda t, o=pk[:, :], l=lh,
                               r_=W0[:, cb * D + half * 512: cb * D + (half + 1) * 512],
                               st=(cb == 0), sp=(cb == PB - 1): t.matmul(
                                   o, l, r_, start=st, stop=sp),
                               r=[(xn1, ("p", cb)), (W0, ("ld", 0))],
                               w=[(pk, None)], inc=(cb == PB - 1))
                            pe(lambda t, o=pv_[:, :], l=lh,
                               r_=W1[:, cb * D + half * 512: cb * D + (half + 1) * 512],
                               st=(cb == 0), sp=(cb == PB - 1): t.matmul(
                                   o, l, r_, start=st, stop=sp),
                               r=[(xn1, ("p", cb)), (W1, ("ld", 0))],
                               w=[(pv_, None)], inc=(cb == PB - 1))
                        dve(lambda v, h=half: v.tensor_tensor(
                            kb[:, h * 512:(h + 1) * 512], ps8[h][:, :],
                            btm[:, h * 512:(h + 1) * 512], OP.add),
                            r=[(ps8[half], None), (btm, ("c", half))],
                            w=[(kb, ("h", half))])
                        acte(lambda s, h=half: s.activation(
                            ksq[:, h * 512:(h + 1) * 512], kb[:, h * 512:(h + 1) * 512],
                            AF.Square),
                            r=[(kb, ("h", half))], w=[(ksq, ("h", half))])
                        dve(lambda v, h=half, vv=vv: v.tensor_tensor(
                            vv[:, h * 512:(h + 1) * 512], ps8[2 + h][:, :],
                            btm[:, D + h * 512: D + (h + 1) * 512], OP.add),
                            r=[(ps8[2 + half], None), (btm, ("c", 2 + half))],
                            w=[(vv, ("h", half))])
                    dve(lambda v: v.reduce_sum(
                        ksc[:, :], ksq[:, :].rearrange("p (h d) -> p h d", d=HD),
                        axis=AX.X),
                        r=[(ksq, ("h", 0)), (ksq, ("h", 1))], w=[(ksc, None)])
                    dve(lambda v: v.tensor_scalar(ksc[:, :], ksc[:, :], 1e-24, 0.0,
                                                  OP.max, OP.add),
                        r=[(ksc, None)], w=[(ksc, None)])
                    acte(lambda s: s.activation(ksc[:, :], ksc[:, :], AF.Ln),
                         r=[(ksc, None)], w=[(ksc, None)])
                    acte(lambda s: s.activation(ksc[:, :], ksc[:, :], AF.Exp,
                                                scale=-0.5),
                         r=[(ksc, None)], w=[(ksc, None)])
                    for half in range(2):
                        dve(lambda v, h=half, kn=kn: v.tensor_tensor(
                            kn[:, h * 512:(h + 1) * 512].rearrange(
                                "p (h d) -> p h d", d=HD),
                            kb[:, h * 512:(h + 1) * 512].rearrange(
                                "p (h d) -> p h d", d=HD),
                            ksc[:, h * 8:(h + 1) * 8].unsqueeze(2).broadcast_to(
                                (128, 8, HD)),
                            OP.mult),
                            r=[(kb, ("h", half)), (ksc, None)], w=[(kn, ("h", half))])
                    for h in range(HEAD):
                        r0, c0 = 64 * (h % 2), 64 * (h // 2)
                        pe(lambda t, kn=kn, vv=vv, h=h, r0=r0, c0=c0,
                           st=(tb == 0 and h < 2),
                           sp=(tb == 4 * n5 - 1 and h == HEAD - 1):
                           t.matmul(ps8[7][r0:r0 + 64, c0:c0 + 64],
                                    kn[:, h * 64:(h + 1) * 64],
                                    vv[:, h * 64:(h + 1) * 64],
                                    start=st, stop=sp, skip_group_check=True),
                           r=[(kn, ("h", 0)), (kn, ("h", 1)), (vv, ("h", 0)),
                              (vv, ("h", 1))],
                           w=[(ps8[7], None)], inc=(h == HEAD - 1))
            dve(lambda v: v.tensor_copy(msb[:, :], ps8[7][:, :]),
                r=[(ps8[7], None)], w=[(msb, None)])
            ldw(W0, dW["wq"])
            ldw(W1, dW["wo"])
            qb = sbp("qb", [128, PB * 512], BF16)
            qsq = sbp("qsq", [128, PB * 512], BF16)
            qn = sbp("qn", [128, PB * 512], BF16)
            ao = sbp("ao", [128, PB * 512], BF16)
            qs = sbp("qs", [16, 512])
            qsb = sbp("qsb", [16, 512], BF16)
            for tt in range(n5):
                ln_tile(x8, tt, lambda p: (xn1[:, p * 512:(p + 1) * 512], xn1,
                                           ("p", p)), lns, "ln1c")
                for db in range(PB):
                    pq = ps8[db % 2]
                    for cb in range(PB):
                        pe(lambda t, o=pq[:, :],
                           l=W0[:, cb * D + db * 128: cb * D + (db + 1) * 128],
                           r_=xn1[:, cb * 512:(cb + 1) * 512],
                           st=(cb == 0), sp=(cb == PB - 1): t.matmul(
                               o, l, r_, start=st, stop=sp),
                           r=[(W0, ("ld", 0)), (xn1, ("p", cb))],
                           w=[(pq, None)], inc=(cb == PB - 1))
                    dve(lambda v, db=db, pq=pq: v.tensor_scalar(
                        qb[:, db * 512:(db + 1) * 512], pq[:, :],
                        pv[:, BQ + db:BQ + db + 1], 0.0, OP.add, OP.add),
                        r=[(pq, None), (pv, None)], w=[(qb, ("p", db))])
                    acte(lambda s, db=db, pq=pq: s.activation(
                        qsq[:, db * 512:(db + 1) * 512], pq[:, :], AF.Square,
                        bias=pv[:, BQ + db:BQ + db + 1]),
                        r=[(pq, None), (pv, None)], w=[(qsq, ("p", db))])
                for p in range(PB):
                    pe(lambda t, p=p: t.matmul(
                        ps8[2][0:16, :], c_bd[:, p * 16:(p + 1) * 16],
                        qsq[:, p * 512:(p + 1) * 512],
                        start=(p == 0), stop=(p == PB - 1)),
                       r=[(c_bd, None), (qsq, ("p", p))], w=[(ps8[2], None)],
                       inc=(p == PB - 1))
                dve(lambda v: v.tensor_scalar(qs[:, :], ps8[2][0:16, :], 1e-24, 0.0,
                                              OP.max, OP.add),
                    r=[(ps8[2], None)], w=[(qs, None)])
                acte(lambda s: s.activation(qs[:, :], qs[:, :], AF.Ln),
                     r=[(qs, None)], w=[(qs, None)])
                acte(lambda s: s.activation(qs[:, :], qs[:, :], AF.Exp, scale=-0.5),
                     r=[(qs, None)], w=[(qs, None)])
                dve(lambda v: v.tensor_scalar(qs[:, :], qs[:, :], sgate[:, 0:1], 0.0,
                                              OP.mult, OP.add),
                    r=[(qs, None), (sgate, None)], w=[(qs, None)])
                dve(lambda v: v.tensor_copy(qsb[:, :], qs[:, :]),
                    r=[(qs, None)], w=[(qsb, None)])
                for p in range(PB):
                    pbc = ps8[3 + p % 2]
                    pe(lambda t, p=p, o=pbc[:, :]: t.matmul(
                        o, c_selh[:, p * 128:(p + 1) * 128], qsb[:, :],
                        start=True, stop=True),
                       r=[(c_selh, None), (qsb, None)], w=[(pbc, None)])
                    dve(lambda v, p=p, pbc=pbc: v.tensor_tensor(
                        qn[:, p * 512:(p + 1) * 512], qb[:, p * 512:(p + 1) * 512],
                        pbc[:, :], OP.mult),
                        r=[(qb, ("p", p)), (pbc, None)], w=[(qn, ("p", p))])
                    py = ps8[5 + p % 2]
                    for hh in range(2):
                        h = 2 * p + hh
                        pe(lambda t, h=h, hh=hh, p=p, py=py: t.matmul(
                            py[hh * 64:(hh + 1) * 64, :],
                            msb[64 * (h % 2):64 * (h % 2) + 64,
                                64 * (h // 2):64 * (h // 2) + 64],
                            qn[64 * hh:64 * (hh + 1), p * 512:(p + 1) * 512],
                            start=True, stop=True, skip_group_check=True),
                           r=[(msb, None), (qn, ("p", p))], w=[(py, None)],
                           inc=(hh == 1))
                    dve(lambda v, p=p, py=py: v.tensor_copy(
                        ao[:, p * 512:(p + 1) * 512], py[:, :]),
                        r=[(py, None)], w=[(ao, ("p", p))])
                for db in range(PB):
                    po = ps8[7]
                    for cb in range(PB):
                        pe(lambda t, o=po[:, :],
                           l=W1[:, cb * D + db * 128: cb * D + (db + 1) * 128],
                           r_=ao[:, cb * 512:(cb + 1) * 512],
                           st=(cb == 0), sp=(cb == PB - 1): t.matmul(
                               o, l, r_, start=st, stop=sp),
                           r=[(W1, ("ld", 0)), (ao, ("p", cb))],
                           w=[(po, None)], inc=(cb == PB - 1))
                    xtags = [(x8[db], ("c", tt * 4 + j)) for j in range(4)]
                    dve(lambda v, db=db, tt=tt, po=po: v.scalar_tensor_tensor(
                        x8[db][:, tt * 512:(tt + 1) * 512], po[:, :],
                        pv[:, BO + db:BO + db + 1],
                        x8[db][:, tt * 512:(tt + 1) * 512], OP.add, OP.add),
                        r=[(po, None), (pv, None)] + xtags,
                        w=xtags)
            tk.barrier()
            tk.drop([lns[k] for k in lns] + [xn1, knt[0], knt[1], vt[0], vt[1],
                                             kb, ksq, ksc, qb, qsq, qn, ao, qs, qsb,
                                             btm, browS, c_selh, c_bd])

    def ffn(b, inv):
        with ExitStack() as ph:
            sbp = lambda n, s, dt=F32: ph.enter_context(
                nc.sbuf_tensor(_nm(n), s, dt))
            lns = mk_ln_scratch(sbp)
            xn2 = [sbp(f"xn2_{i}", [128, PB * 512], BF16) for i in range(2)]
            asl = sbp("asl", [128, 4 * 4 * 512], BF16)
            sil = sbp("sil", [128, 4 * 512], BF16)
            h3c = sbp("h3c", [128, 4 * 512], BF16)
            call = sbp("call", [128, NE * 512], BF16)
            gL = sbp("gL", [128, 64])
            gE = sbp("gE", [128, 64])
            gS = sbp("gS", [128, 64])
            gSm = sbp("gSm", [128, 64])
            gm4 = sbp("gm4", [128, 8])
            gma = sbp("gma", [128, 16])
            gmb = sbp("gmb", [128, 16])
            gr4 = sbp("gr4", [128, 4])
            gctm = sbp("gctm", [128, 64], BF16)
            gcfm = sbp("gcfm", [16, 512], BF16)
            c_sele = sbp("c_sele", [16, NE * 128], BF16)
            c_id = sbp("c_id", [128, 128], BF16)
            sq(lambda e: e.dma_start(c_sele[:, :], dsele), w=[(c_sele, None)])
            sq(lambda e: e.dma_start(c_id[:, :], dident), w=[(c_id, None)])
            seq = [(t_, e_) for t_ in range(n5) for e_ in range(NEX)]
            wbuf = lambda gi: (W0, W1)[gi % 2]
            ldw(wbuf(0), dw13[seq[0][1]])

            def gate(tt, xn):
                pg = ps8[4]
                for tbl in range(4):
                    for cb in range(PB):
                        pe(lambda t, o=pg[:, tbl * 16:(tbl + 1) * 16],
                           l=xn[:, cb * 512 + tbl * 128: cb * 512 + (tbl + 1) * 128],
                           r_=gwS[:, cb * 16:(cb + 1) * 16],
                           st=(cb == 0 and tbl == 0), sp=(cb == PB - 1): t.matmul(
                               o, l, r_, start=st, stop=sp, skip_group_check=True),
                           r=[(xn, ("p", cb)), (gwS, None)], w=[(pg, None)],
                           inc=(cb == PB - 1))
                v3 = lambda a: a.rearrange("p (t e) -> p t e", t=4)
                v16 = lambda a: a.rearrange("p (tg e) -> p tg e", tg=16)
                bc4 = lambda a: a.unsqueeze(2).broadcast_to((128, 4, 16))
                bc16 = lambda a: a.rearrange("p (t g) -> p t g", t=4)
                dve(lambda v: v.tensor_copy(gL[:, :], pg[:, 0:64]),
                    r=[(pg, None)], w=[(gL, None)])
                dve(lambda v: v.reduce_max(gm4[:, 0:4], v3(gL[:, :]), axis=AX.X),
                    r=[(gL, None)], w=[(gm4, 0)])
                dve(lambda v: v.tensor_tensor(v3(gE[:, :]), v3(gL[:, :]),
                                              bc4(gm4[:, 0:4]), OP.subtract),
                    r=[(gL, None), (gm4, 0)], w=[(gE, None)])
                acte(lambda s: s.activation(gE[:, :], gE[:, :], AF.Exp),
                     r=[(gE, None)], w=[(gE, None)])
                dve(lambda v: v.reduce_sum(gm4[:, 4:8], v3(gE[:, :]), axis=AX.X),
                    r=[(gE, None)], w=[(gm4, 1)])
                dve(lambda v: v.reciprocal(gr4[:, :], gm4[:, 4:8]),
                    r=[(gm4, 1)], w=[(gr4, None)])
                dve(lambda v: v.tensor_tensor(v3(gS[:, :]), v3(gE[:, :]),
                                              bc4(gr4[:, :]), OP.mult),
                    r=[(gE, None), (gr4, None)], w=[(gS, None)])
                bcg = lambda a: a.unsqueeze(2).broadcast_to((128, 4, 4))
                dve(lambda v: v.reduce_max(bc16(gma[:, :]), v16(gS[:, :]), axis=AX.X),
                    r=[(gS, None)], w=[(gma, None)])
                dve(lambda v: v.reduce_max(gr4[:, :], bc16(gma[:, :]), axis=AX.X),
                    r=[(gma, None)], w=[(gr4, None)])
                dve(lambda v: v.tensor_tensor(
                    bc16(gmb[:, :]), bc16(gma[:, :]), bcg(gr4[:, :]), OP.is_equal),
                    r=[(gma, None), (gr4, None)], w=[(gmb, None)])
                dve(lambda v: v.scalar_tensor_tensor(
                    gmb[:, :], gmb[:, :], -1e30, gma[:, :], OP.mult, OP.add),
                    r=[(gmb, None), (gma, None)], w=[(gmb, None)])
                dve(lambda v: v.reduce_max(gr4[:, :], bc16(gmb[:, :]), axis=AX.X),
                    r=[(gmb, None)], w=[(gr4, None)])
                dve(lambda v: v.tensor_tensor(
                    bc16(gmb[:, :]), bc16(gma[:, :]), bcg(gr4[:, :]), OP.is_ge),
                    r=[(gma, None), (gr4, None)], w=[(gmb, None)])
                dve(lambda v: v.tensor_tensor(
                    v16(gSm[:, :]), v16(gS[:, :]),
                    gmb[:, :].unsqueeze(2).broadcast_to((128, 16, 4)), OP.mult),
                    r=[(gS, None), (gmb, None)], w=[(gSm, None)])
                dve(lambda v: v.tensor_copy(gE[:, :], gSm[:, :]),
                    r=[(gSm, None)], w=[(gE, None)])
                for it in range(4):
                    dve(lambda v: v.reduce_max(gm4[:, 0:4], v3(gE[:, :]), axis=AX.X),
                        r=[(gE, None)], w=[(gm4, 0)])
                    if it < 3:
                        dve(lambda v: v.tensor_tensor(
                            v3(gL[:, :]), v3(gE[:, :]), bc4(gm4[:, 0:4]),
                            OP.is_equal),
                            r=[(gE, None), (gm4, 0)], w=[(gL, None)])
                        dve(lambda v: v.scalar_tensor_tensor(
                            gE[:, :], gL[:, :], -1e30, gE[:, :], OP.mult, OP.add),
                            r=[(gL, None), (gE, None)], w=[(gE, None)])
                dve(lambda v: v.tensor_tensor(v3(gL[:, :]), v3(gSm[:, :]),
                                              bc4(gm4[:, 0:4]), OP.is_ge),
                    r=[(gSm, None), (gm4, 0)], w=[(gL, None)])
                dve(lambda v: v.tensor_tensor(gctm[:, :], gS[:, :], gL[:, :],
                                              OP.mult),
                    r=[(gS, None), (gL, None)], w=[(gctm, None)])
                for tbl in range(4):
                    pe(lambda t, tbl=tbl: t.transpose(
                        ps8[5][:, :].bitcast(BF16)[0:16, tbl * 128:(tbl + 1) * 128],
                        gctm[:, tbl * 16:(tbl + 1) * 16], c_id[:, :]),
                       r=[(gctm, None), (c_id, None)], w=[(ps8[5], None)])
                    dve(lambda v, tbl=tbl: v.tensor_copy(
                        gcfm[:, tbl * 128:(tbl + 1) * 128],
                        ps8[5][:, :].bitcast(BF16)[0:16, tbl * 128:(tbl + 1) * 128]),
                        r=[(ps8[5], None)], w=[(gcfm, ("t", tbl))])

            gi = 0
            for tt in range(n5):
                xn = xn2[tt % 2]
                ln_tile(x8, tt, lambda p: (xn[:, p * 512:(p + 1) * 512], xn,
                                           ("p", p)), lns, "ln2")
                gate(tt, xn)
                for e in range(NE):
                    pc = ps8[4 + e % 4]
                    pe(lambda t, e=e, pc=pc: t.matmul(
                        pc[:, :], c_sele[:, e * 128:(e + 1) * 128], gcfm[:, :],
                        start=True, stop=True),
                       r=[(c_sele, None)] + [(gcfm, ("t", j)) for j in range(4)],
                       w=[(pc, None)])
                    acte(lambda s, e=e, pc=pc: s.activation(
                        call[:, e * 512:(e + 1) * 512], pc[:, :], AF.Identity),
                        r=[(pc, None)], w=[(call, ("e", e))])
                for chunk in FFN_CHUNKS:
                    ldw(w2f[0], dw2[chunk[0]])
                    if len(chunk) > 1:
                        ldw(w2f[1], dw2[chunk[1]])
                    for ei, e in enumerate(chunk):
                        wb = wbuf(gi)
                        for ib in range(4):
                            for cb in range(PB):
                                pe(lambda t, o=ps8[ib][:, :],
                                   l=wb[:, cb * D + ib * 128: cb * D + (ib + 1) * 128],
                                   r_=xn[:, cb * 512:(cb + 1) * 512],
                                   st=(cb == 0), sp=(cb == PB - 1): t.matmul(
                                       o, l, r_, start=st, stop=sp),
                                   r=[(wb, ("ld", 0)), (xn, ("p", cb))],
                                   w=[(ps8[ib], None)], inc=(cb == PB - 1))
                            for cb in range(PB):
                                pe(lambda t, o=ps8[4 + ib][:, :],
                                   l=wb[:, cb * D + 512 + ib * 128:
                                          cb * D + 512 + (ib + 1) * 128],
                                   r_=xn[:, cb * 512:(cb + 1) * 512],
                                   st=(cb == 0), sp=(cb == PB - 1): t.matmul(
                                       o, l, r_, start=st, stop=sp),
                                   r=[(wb, ("ld", 0)), (xn, ("p", cb))],
                                   w=[(ps8[4 + ib], None)], inc=(cb == PB - 1))
                        if gi + 1 < len(seq):
                            ldw(wbuf(gi + 1), dw13[seq[gi + 1][1]])
                        gi += 1
                        for ib in range(4):
                            acte(lambda s, ib=ib: s.activation(
                                sil[:, ib * 512:(ib + 1) * 512], ps8[ib][:, :],
                                AF.Silu),
                                r=[(ps8[ib], None)], w=[(sil, ("i", ib))])
                            if e < NE:
                                dve(lambda v, ib=ib, e=e: v.tensor_tensor(
                                    h3c[:, ib * 512:(ib + 1) * 512], ps8[4 + ib][:, :],
                                    call[:, e * 512:(e + 1) * 512], OP.mult),
                                    r=[(ps8[4 + ib], None), (call, ("e", e))],
                                    w=[(h3c, ("i", ib))])
                                dve(lambda v, ib=ib, ei=ei: v.tensor_tensor(
                                    asl[:, (ei * 4 + ib) * 512:(ei * 4 + ib + 1) * 512],
                                    sil[:, ib * 512:(ib + 1) * 512],
                                    h3c[:, ib * 512:(ib + 1) * 512], OP.mult),
                                    r=[(sil, ("i", ib)), (h3c, ("i", ib))],
                                    w=[(asl, ("a", ei * 4 + ib))])
                            else:
                                dve(lambda v, ib=ib, ei=ei: v.tensor_tensor(
                                    asl[:, (ei * 4 + ib) * 512:(ei * 4 + ib + 1) * 512],
                                    sil[:, ib * 512:(ib + 1) * 512], ps8[4 + ib][:, :],
                                    OP.mult),
                                    r=[(sil, ("i", ib)), (ps8[4 + ib], None)],
                                    w=[(asl, ("a", ei * 4 + ib))])
                    for ei, e in enumerate(chunk):
                        wb2 = w2f[ei % 2]
                        for db in range(PB):
                            for ib in range(4):
                                pe(lambda t, o=ps8[db][:, :],
                                   l=wb2[:, ib * D + db * 128: ib * D + (db + 1) * 128],
                                   r_=asl[:, (ei * 4 + ib) * 512:
                                          (ei * 4 + ib + 1) * 512],
                                   st=(ei == 0 and ib == 0),
                                   sp=(ei == len(chunk) - 1 and ib == 3): t.matmul(
                                       o, l, r_, start=st, stop=sp,
                                       skip_group_check=True),
                                   r=[(wb2, ("ld", 0)), (asl, ("a", ei * 4 + ib))],
                                   w=[(ps8[db], None)],
                                   inc=(ib == 3))
                        if ei + 2 < len(chunk):
                            ldw(w2f[ei % 2], dw2[chunk[ei + 2]])
                    for db in range(PB):
                        xtags = [(x8[db], ("c", tt * 4 + j)) for j in range(4)]
                        dve(lambda v, db=db, tt=tt: v.tensor_tensor(
                            x8[db][:, tt * 512:(tt + 1) * 512],
                            x8[db][:, tt * 512:(tt + 1) * 512], ps8[db][:, :], OP.add),
                            r=xtags + [(ps8[db], None)], w=xtags)
            tk.barrier()
            tk.drop([lns[k] for k in lns] + xn2 +
                    [asl, sil, h3c, call, gL, gE, gS, gSm, gm4, gma, gmb, gr4,
                     gctm, gcfm, c_sele, c_id])

    def attn_b(b):
        with ExitStack() as ph:
            sbp = lambda n, s, dt=F32: ph.enter_context(
                nc.sbuf_tensor(_nm(n), s, dt))
            lns = mk_ln_scratch(sbp)
            xn3 = sbp("xn3", [128, PB * 512], BF16)
            tmp = sbp("tmpmix", [128, PB * 256], BF16)
            xk = sbp("xk", [128, PB * 256], BF16)
            xv = sbp("xv", [128, PB * 256], BF16)
            xr = sbp("xr", [128, PB * 256], BF16)
            lastc = sbp("lastc", [128, PB], BF16)
            wkva = sbp("wkva", [128, PB * T], BF16)
            sA = sbp("sA", [128, 256])
            sB = sbp("sB", [128, 256])
            sC = sbp("sC", [128, 256])
            sD = sbp("sD", [128, 256])
            Pb = sbp("Pb", [128, 257])
            Qb = sbp("Qb", [128, 257])
            Pst = sbp("Pst", [128, PB])
            Qst = sbp("Qst", [128, PB])
            yt = sbp("yt", [128, PB * 256], BF16)

            def mix(dst, tmcol, omcol, pb, tti):
                t0 = (tti % 2) * 256
                a_tmp = tmp[:, pb * 256: pb * 256 + 256]
                a_xn = xn3[:, pb * 512 + t0: pb * 512 + t0 + 256]
                a_tm = pv[:, tmcol + pb: tmcol + pb + 1]
                a_om = dpar[:, omcol + pb: omcol + pb + 1]
                dve(lambda v, o=a_tmp, i=a_xn, s=a_tm: v.tensor_scalar(
                    o, i, s, 0.0, OP.mult, OP.add),
                    r=[(xn3, ("p", pb))], w=[(tmp, ("p", pb))])
                dve(lambda v, o=dst[:, pb * 256: pb * 256 + 1],
                    i0=lastc[:, pb:pb + 1], s=a_om,
                    i1=tmp[:, pb * 256: pb * 256 + 1]: v.scalar_tensor_tensor(
                        o, i0, s, i1, OP.mult, OP.add),
                    r=[(lastc, ("p", pb)), (dpar, ("c", omcol)), (tmp, ("p", pb))],
                    w=[(dst, ("p", pb))])
                dve(lambda v, o=dst[:, pb * 256 + 1: pb * 256 + 256],
                    i0=xn3[:, pb * 512 + t0: pb * 512 + t0 + 255], s=a_om,
                    i1=tmp[:, pb * 256 + 1: pb * 256 + 256]: v.scalar_tensor_tensor(
                        o, i0, s, i1, OP.mult, OP.add),
                    r=[(xn3, ("p", pb)), (dpar, ("c", omcol)), (tmp, ("p", pb))],
                    w=[(dst, ("p", pb))])

            ldw(W0, dW["wkey"])
            ldw(W1, dW["wval"])
            for p in range(PB):
                dve(lambda v, p=p: v.memset(lastc[:, p:p + 1], 0.0),
                    r=[], w=[(lastc, ("p", p))])
                dve(lambda v, p=p: v.memset(Pst[:, p:p + 1], 0.0),
                    r=[], w=[(Pst, ("p", p))])
                dve(lambda v, p=p: v.memset(Qst[:, p:p + 1], 0.0),
                    r=[], w=[(Qst, ("p", p))])
            for tti in range(n2):
                if tti % 2 == 0:
                    ln_tile(x8, tti // 2,
                            lambda p: (xn3[:, p * 512:(p + 1) * 512], xn3, ("p", p)),
                            lns, "ln3")
                for p in range(PB):
                    mix(xk, TMK, OTMK, p, tti)
                    mix(xv, TMV, OTMV, p, tti)
                for p in range(PB):
                    dve(lambda v, p=p, tti=tti: v.tensor_copy(
                        lastc[:, p:p + 1],
                        xn3[:, p * 512 + (tti % 2) * 256 + 255:
                            p * 512 + (tti % 2) * 256 + 256]),
                        r=[(xn3, ("p", p))], w=[(lastc, ("p", p))])
                for db in range(PB):
                    pk = ps8[db // 2][:, (db % 2) * 256:(db % 2) * 256 + 256]
                    pvv = ps8[4 + db // 2][:, (db % 2) * 256:(db % 2) * 256 + 256]
                    for cb in range(PB):
                        pe(lambda t, o=pk,
                           l=W0[:, cb * D + db * 128: cb * D + (db + 1) * 128],
                           r_=xk[:, cb * 256:(cb + 1) * 256],
                           st=(cb == 0 and db % 2 == 0), sp=(cb == PB - 1): t.matmul(
                               o, l, r_, start=st, stop=False,
                               skip_group_check=True),
                           r=[(W0, ("ld", 0)), (xk, ("p", cb))],
                           w=[(ps8[db // 2], None)], inc=(cb == PB - 1))
                        pe(lambda t, o=pvv,
                           l=W1[:, cb * D + db * 128: cb * D + (db + 1) * 128],
                           r_=xv[:, cb * 256:(cb + 1) * 256],
                           st=(cb == 0 and db % 2 == 0), sp=(cb == PB - 1): t.matmul(
                               o, l, r_, start=st, stop=False,
                               skip_group_check=True),
                           r=[(W1, ("ld", 0)), (xv, ("p", cb))],
                           w=[(ps8[4 + db // 2], None)], inc=(cb == PB - 1))
                for p in range(PB):
                    kps = ps8[p // 2][:, (p % 2) * 256:(p % 2) * 256 + 256]
                    vps = ps8[4 + p // 2][:, (p % 2) * 256:(p % 2) * 256 + 256]
                    kkey = (ps8[p // 2], None)
                    vkey = (ps8[4 + p // 2], None)
                    acte(lambda s, kps=kps, p=p: s.activation(
                        sC[:, :], kps, AF.Exp, bias=pv[:, UU + p:UU + p + 1]),
                        r=[kkey, (pv, None)], w=[(sC, None)])
                    acte(lambda s, kps=kps: s.activation(sA[:, :], kps, AF.Exp),
                         r=[kkey], w=[(sA, None)])
                    dve(lambda v, vps=vps: v.tensor_tensor(sB[:, :], sA[:, :], vps,
                                                           OP.mult),
                        r=[(sA, None), vkey], w=[(sB, None)])
                    dve(lambda v, vps=vps: v.tensor_tensor(sD[:, :], sC[:, :], vps,
                                                           OP.mult),
                        r=[(sC, None), vkey], w=[(sD, None)])
                    dve(lambda v, p=p: v.tensor_copy(Pb[:, 0:1], Pst[:, p:p + 1]),
                        r=[(Pst, ("p", p))], w=[(Pb, None)])
                    dve(lambda v, p=p: v.tensor_tensor_scan(
                        Pb[:, 1:257],
                        dpar[:, EW + p:EW + p + 1].broadcast_to((128, 256)),
                        sB[:, :], Pst[:, p:p + 1], OP.mult, OP.add),
                        r=[(dpar, ("c", EW)), (sB, None), (Pst, ("p", p)),
                           (Pb, None)],
                        w=[(Pb, None)])
                    dve(lambda v, p=p: v.tensor_copy(Qb[:, 0:1], Qst[:, p:p + 1]),
                        r=[(Qst, ("p", p))], w=[(Qb, None)])
                    dve(lambda v, p=p: v.tensor_tensor_scan(
                        Qb[:, 1:257],
                        dpar[:, EW + p:EW + p + 1].broadcast_to((128, 256)),
                        sA[:, :], Qst[:, p:p + 1], OP.mult, OP.add),
                        r=[(dpar, ("c", EW)), (sA, None), (Qst, ("p", p)),
                           (Qb, None)],
                        w=[(Qb, None)])
                    dve(lambda v, p=p: v.tensor_copy(Pst[:, p:p + 1], Pb[:, 256:257]),
                        r=[(Pb, None)], w=[(Pst, ("p", p))])
                    dve(lambda v, p=p: v.tensor_copy(Qst[:, p:p + 1], Qb[:, 256:257]),
                        r=[(Qb, None)], w=[(Qst, ("p", p))])
                    dve(lambda v: v.tensor_tensor(sD[:, :], Pb[:, 0:256], sD[:, :],
                                                  OP.add),
                        r=[(Pb, None), (sD, None)], w=[(sD, None)])
                    dve(lambda v: v.tensor_tensor(sC[:, :], Qb[:, 0:256], sC[:, :],
                                                  OP.add),
                        r=[(Qb, None), (sC, None)], w=[(sC, None)])
                    acte(lambda s: s.activation(sC[:, :], sC[:, :], AF.Ln),
                         r=[(sC, None)], w=[(sC, None)])
                    acte(lambda s: s.activation(sC[:, :], sC[:, :], AF.Exp,
                                                scale=-1.0),
                         r=[(sC, None)], w=[(sC, None)])
                    dve(lambda v, p=p, tti=tti: v.tensor_tensor(
                        wkva[:, p * T + tti * 256: p * T + (tti + 1) * 256],
                        sD[:, :], sC[:, :], OP.mult),
                        r=[(sD, None), (sC, None)], w=[(wkva, ("pt", (p, tti)))])
            ldw(W0, dW["wrec"])
            ldw(W1, dW["wout"])
            for p in range(PB):
                dve(lambda v, p=p: v.memset(lastc[:, p:p + 1], 0.0),
                    r=[], w=[(lastc, ("p", p))])
            for tti in range(n2):
                if tti % 2 == 0:
                    ln_tile(x8, tti // 2,
                            lambda p: (xn3[:, p * 512:(p + 1) * 512], xn3, ("p", p)),
                            lns, "ln3b")
                for p in range(PB):
                    mix(xr, TMR, OTMR, p, tti)
                for p in range(PB):
                    dve(lambda v, p=p, tti=tti: v.tensor_copy(
                        lastc[:, p:p + 1],
                        xn3[:, p * 512 + (tti % 2) * 256 + 255:
                            p * 512 + (tti % 2) * 256 + 256]),
                        r=[(xn3, ("p", p))], w=[(lastc, ("p", p))])
                for db in range(PB):
                    pr = ps8[db // 2][:, (db % 2) * 256:(db % 2) * 256 + 256]
                    for cb in range(PB):
                        pe(lambda t, o=pr,
                           l=W0[:, cb * D + db * 128: cb * D + (db + 1) * 128],
                           r_=xr[:, cb * 256:(cb + 1) * 256],
                           st=(cb == 0 and db % 2 == 0), sp=(cb == PB - 1): t.matmul(
                               o, l, r_, start=st, stop=False,
                               skip_group_check=True),
                           r=[(W0, ("ld", 0)), (xr, ("p", cb))],
                           w=[(ps8[db // 2], None)], inc=(cb == PB - 1))
                for p in range(PB):
                    rps = ps8[p // 2][:, (p % 2) * 256:(p % 2) * 256 + 256]
                    rkey = (ps8[p // 2], None)
                    acte(lambda s, rps=rps: s.activation(sA[:, :], rps, AF.Exp,
                                                         scale=-1.0),
                         r=[rkey], w=[(sA, None)])
                    dve(lambda v: v.tensor_scalar(sA[:, :], sA[:, :], 1.0, 0.0,
                                                  OP.add, OP.add),
                        r=[(sA, None)], w=[(sA, None)])
                    acte(lambda s: s.activation(sA[:, :], sA[:, :], AF.Ln),
                         r=[(sA, None)], w=[(sA, None)])
                    acte(lambda s: s.activation(sA[:, :], sA[:, :], AF.Exp,
                                                scale=-1.0),
                         r=[(sA, None)], w=[(sA, None)])
                    dve(lambda v, p=p, tti=tti: v.tensor_tensor(
                        yt[:, p * 256:(p + 1) * 256],
                        wkva[:, p * T + tti * 256: p * T + (tti + 1) * 256],
                        sA[:, :], OP.mult),
                        r=[(wkva, ("pt", (p, tti))), (sA, None)],
                        w=[(yt, ("p", p))])
                for db in range(PB):
                    po = ps8[4 + db // 2][:, (db % 2) * 256:(db % 2) * 256 + 256]
                    for cb in range(PB):
                        pe(lambda t, o=po,
                           l=W1[:, cb * D + db * 128: cb * D + (db + 1) * 128],
                           r_=yt[:, cb * 256:(cb + 1) * 256],
                           st=(cb == 0 and db % 2 == 0), sp=(cb == PB - 1): t.matmul(
                               o, l, r_, start=st, stop=False,
                               skip_group_check=True),
                           r=[(W1, ("ld", 0)), (yt, ("p", cb))],
                           w=[(ps8[4 + db // 2], None)], inc=(cb == PB - 1))
                    xtags = [(x8[db], ("c", tti * 2 + j)) for j in range(2)]
                    dve(lambda v, db=db, tti=tti, po=po: v.tensor_tensor(
                        x8[db][:, tti * 256:(tti + 1) * 256],
                        x8[db][:, tti * 256:(tti + 1) * 256], po, OP.add),
                        r=xtags + [(ps8[4 + db // 2], None)],
                        w=xtags)
            tk.barrier()
            tk.drop([lns[k] for k in lns] + [xn3, tmp, xk, xv, xr, lastc, wkva,
                                             sA, sB, sC, sD, Pb, Qb, Pst, Qst, yt])

    def tap(name, b):
        if taps:
            for p in range(PB):
                sq(lambda e, p=p: e.dma_start(dtap[name][b, p], x8[p][:, :]),
                   r=[(x8[p], None)])

    for b in range(NB):
        if b > 0:
            for p in range(PB):
                sq(lambda e, p=p, b=b: e.dma_start(x8[p][:, :], dx[b, p]),
                   w=[(x8[p], None)])
        attn_a(b)
        tap("attna", b)
        ffn(b, 0)
        tap("ffn1", b)
        attn_b(b)
        tap("attnb", b)
        ffn(b, 1)
        tap("ffn2", b)
        for p in range(PB):
            sq(lambda e, p=p, b=b: e.dma_start(dout[b, p], x8[p][:, :]),
               r=[(x8[p], None)])

    for i in range(tk.NDMA):
        for q, strm in (("sq", "sp"), ("gq", "pool")):
            n = tk.cnt[f"{q}{i}"]
            if n:
                tk.ops[strm].append(
                    lambda e, s=tk.sem[f"{q}{i}"], n=n: e.wait_ge(s, n))

    with es:
        with nc.Block() as block:
            @block.tensor
            def _(t):
                for f in tk.ops["pe"]:
                    f(t)

            @block.scalar
            def _(s):
                for f in tk.ops["act"]:
                    f(s)

            @block.vector
            def _(v):
                for f in tk.ops["dve"]:
                    f(v)

            @block.gpsimd
            def _(g):
                for f in tk.ops["pool"]:
                    f(g)

            @block.sync
            def _(s):
                for f in tk.ops["sp"]:
                    f(s)
    return nc


def _pack_inmap(T=2048):
    raise NotImplementedError


def _pk_proj(w):
    return np.ascontiguousarray(
        w.reshape(8, 128, w.shape[1]).transpose(1, 0, 2).reshape(128, -1))


def host_pack(inp, T):
    import ml_dtypes
    f32 = lambda a: np.asarray(a, dtype=np.float32)
    wq = _pk_proj(f32(inp["L0_attn_a_Wq"]))
    wk = _pk_proj(f32(inp["L0_attn_a_Wk"]))
    wv = _pk_proj(f32(inp["L0_attn_a_Wv"]))
    wo = _pk_proj(f32(inp["L0_attn_a_Wo"]))
    wkey = _pk_proj(f32(inp["L0_attn_b_W_key"]).T)
    wval = _pk_proj(f32(inp["L0_attn_b_W_value"]).T)
    wrec = _pk_proj(f32(inp["L0_attn_b_W_receptance"]).T)
    wout = _pk_proj(f32(inp["L0_attn_b_W_output"]).T)

    w1 = f32(inp["L0_ffn_w1"])
    w2 = f32(inp["L0_ffn_w2"])
    w3 = f32(inp["L0_ffn_w3"])
    ws1 = f32(inp["L0_ffn_ws1"])
    ws2 = f32(inp["L0_ffn_ws2"])
    ws3 = f32(inp["L0_ffn_ws3"])
    w13 = np.zeros((NEX, 128, PB * D), np.float32)
    w2p = np.zeros((NEX, 128, 4 * D), np.float32)
    for e in range(NE):
        w13[e] = _pk_proj(np.concatenate([w1[e], w3[e]], axis=1))
        w2p[e] = np.ascontiguousarray(
            w2[e].reshape(4, 128, D).transpose(1, 0, 2).reshape(128, 4 * D))
    for j in range(2):
        w13[NE + j] = _pk_proj(np.concatenate(
            [ws1[:, j * 512:(j + 1) * 512], ws3[:, j * 512:(j + 1) * 512]], axis=1))
        w2p[NE + j] = np.ascontiguousarray(
            ws2[j * 512:(j + 1) * 512].reshape(4, 128, D).transpose(1, 0, 2)
            .reshape(128, 4 * D))
    gw = _pk_proj(f32(inp["L0_ffn_gate_w"]).T)

    fmv = lambda a: np.ascontiguousarray(f32(a).reshape(8, 128).T)
    pvec = np.zeros((128, 80), np.float32)
    pvec[:, 0:8] = fmv(inp["L0_attn_a_bq"])
    pvec[:, 8:16] = fmv(inp["L0_attn_a_bk"])
    pvec[:, 16:24] = fmv(inp["L0_attn_a_bv"])
    pvec[:, 24:32] = fmv(inp["L0_attn_a_bo"])
    pvec[:, 32:40] = fmv(inp["L0_attn_b_time_first"])
    pvec[:, 40:48] = fmv(inp["L0_attn_b_time_decay"])
    pvec[:, 48:56] = fmv(np.asarray(inp["L0_attn_b_time_mix_k"]).reshape(-1))
    pvec[:, 56:64] = fmv(np.asarray(inp["L0_attn_b_time_mix_v"]).reshape(-1))
    pvec[:, 64:72] = fmv(np.asarray(inp["L0_attn_b_time_mix_r"]).reshape(-1))
    pvec[0:16, 72] = f32(inp["L0_attn_a_m"])
    brow = np.concatenate([f32(inp["L0_attn_a_bk"]),
                           f32(inp["L0_attn_a_bv"])])[None, :]

    bf = ml_dtypes.bfloat16
    ones_k1 = np.ones((1, 128), np.float32)
    onescol_b = np.ones((128, 1), bf)
    bd = np.zeros((128, PB * HEAD), bf)
    selh = np.zeros((16, PB * 128), bf)
    for p in range(PB):
        bd[0:64, p * 16 + 2 * p] = 1.0
        bd[64:128, p * 16 + 2 * p + 1] = 1.0
        selh[2 * p, p * 128: p * 128 + 64] = 1.0
        selh[2 * p + 1, p * 128 + 64: p * 128 + 128] = 1.0
    sel_e = np.zeros((16, NE * 128), bf)
    for e in range(NE):
        sel_e[e, e * 128:(e + 1) * 128] = 1.0
    ident_b = np.eye(128, dtype=bf)

    cvals = np.zeros((128, 8), np.float32)
    for i, v in enumerate([1e-5, -0.5, -1.0, -float(np.log(T))]):
        cvals[:, i] = v

    return dict(wq=wq, wk=wk, wv=wv, wo=wo, wkey=wkey, wval=wval, wrec=wrec,
                wout=wout, w13=w13, w2=w2p, gw=gw, pvec=pvec, brow=brow,
                ones_k1=ones_k1, onescol_b=onescol_b, bd=bd, selh=selh,
                sel_e=sel_e, ident_b=ident_b, cvals=cvals)


def pack_x(xbatches, T):
    return np.stack([np.ascontiguousarray(b.T).reshape(8, 128, T)
                     for b in xbatches])


def unpack_out(o, T):
    return np.stack([o[b].reshape(D, T).T for b in range(o.shape[0])])


def kernel(**inp):
    x = np.asarray(inp["L0_attn_a_x"], dtype=np.float32)
    T = x.shape[1]
    shared = host_pack(inp, T)
    NBc = x.shape[0] // 8
    in_maps = []
    for c in range(8):
        m = dict(shared)
        m["x"] = pack_x([x[c * NBc + b] for b in range(NBc)], T)
        in_maps.append(m)

    nc = build_nc(T=T, NB=NBc)
    res = run_bass_kernel_spmd(nc, in_maps, core_ids=list(range(8)))
    out = np.empty((8 * NBc, T, D), np.float32)
    for c in range(8):
        out[c * NBc:(c + 1) * NBc] = unpack_out(res.results[c]["out"], T)
    return out


# version 05
# speedup vs baseline: 1.0827x; 1.0777x over previous
import numpy as np
from contextlib import ExitStack

import concourse.bass as bass
import concourse.mybir as mybir
from concourse.bass_utils import run_bass_kernel_spmd

F32 = mybir.dt.float32
BF16 = mybir.dt.bfloat16
AF = mybir.ActivationFunctionType
OP = mybir.AluOpType
AX = mybir.AxisListType

D = 1024
PB = 8
NE = 16
NEX = 18
HEAD = 16
HD = 64
FFN_CHUNKS = ((0, 1, 2, 3), (4, 5, 6, 7), (8, 9, 10, 11), (12, 13, 14, 15),
              (16, 17))


class _Rec:
    __slots__ = ("src", "cnt")

    def __init__(self, src, cnt):
        self.src = src
        self.cnt = cnt


class TK:
    STREAMS = ("pe", "act", "dve", "pool", "sp")
    ENGS = ("pe", "act", "dve", "pool")
    NDMA = 8

    def __init__(self, nc, es):
        self.nc = nc
        self.psum_ids = set()
        self.ops = {s: [] for s in self.STREAMS}
        self.sem = {s: es.enter_context(nc.semaphore("sem_" + s)) for s in self.ENGS}
        for q in ("gq", "sq"):
            for i in range(self.NDMA):
                self.sem[f"{q}{i}"] = es.enter_context(nc.semaphore(f"sem_{q}{i}"))
        self.cnt = {s: 0 for s in self.sem}
        self.rr = {"gq": 0, "sq": 0}
        self.waited = {c: {s: 0 for s in self.sem} for c in self.STREAMS}
        self.floor = {c: {} for c in self.STREAMS}
        self.bufs = {}
        self.pe_pending = []

    def _deps(self, h, tag, readers):
        recs = self.bufs.get(id(h))
        if not recs:
            return ()
        out = []
        for t2, r in recs.items():
            if tag is None or t2 is None or t2 == tag:
                if r["w"] is not None:
                    out.append(r["w"])
                if readers:
                    out.extend(r["r"].values())
        return out

    def emit(self, stream, fn, reads=(), writes=(), inc=True, dma=None):
        need = dict(self.floor[stream])
        self.floor[stream] = {}
        for h, tag in reads:
            psum_rr = id(h) in self.psum_ids
            for r in self._deps(h, tag, psum_rr):
                if r.src == "pe" and stream == "pe":
                    continue
                if psum_rr and r.src == stream:
                    continue
                assert r.cnt is not None, "dep on open PE group"
                if r.cnt > need.get(r.src, 0):
                    need[r.src] = r.cnt
        for h, tag in writes:
            for r in self._deps(h, tag, True):
                if r.src == "pe" and stream == "pe":
                    continue
                assert r.cnt is not None, "dep on open PE group"
                if r.cnt > need.get(r.src, 0):
                    need[r.src] = r.cnt
        sname = None
        if dma is not None:
            sname = f"{dma}{self.rr[dma]}"
            self.rr[dma] = (self.rr[dma] + 1) % self.NDMA
            if self.cnt[sname] > need.get(sname, 0):
                need[sname] = self.cnt[sname]
        waits = []
        wt = self.waited[stream]
        for src in sorted(need):
            if need[src] > wt[src]:
                waits.append((self.sem[src], need[src]))
                wt[src] = need[src]

        if dma is not None:
            self.cnt[sname] += 16
            rec = _Rec(sname, self.cnt[sname])
            inc_sem, inc_n = self.sem[sname], 16
        elif inc:
            self.cnt[stream] += 1
            rec = _Rec(stream, self.cnt[stream])
            inc_sem, inc_n = self.sem[stream], 1
            if stream == "pe":
                for p in self.pe_pending:
                    p.cnt = rec.cnt
                self.pe_pending = []
        else:
            assert stream == "pe"
            rec = _Rec("pe", None)
            self.pe_pending.append(rec)
            inc_sem, inc_n = None, 0

        def closure(eng, _w=tuple(waits), _f=fn, _s=inc_sem, _n=inc_n):
            for sem, cnt in _w:
                eng.wait_ge(sem, cnt)
            inst = _f(eng)
            if _s is not None:
                inst.then_inc(_s, _n)

        self.ops[stream].append(closure)

        for h, tag in writes:
            self.bufs.setdefault(id(h), {})[tag] = {"w": rec, "r": {}}
        for h, tag in reads:
            recs = self.bufs.setdefault(id(h), {})
            recs.setdefault(tag, {"w": None, "r": {}})["r"][rec.src] = rec
        return rec

    def barrier(self):
        for c in ("pe", "act", "dve", "sp"):
            f = self.floor[c]
            for s in ("pe", "act", "dve"):
                f[s] = max(f.get(s, 0), self.cnt[s])

    def fence(self, stream):
        f = self.floor[stream]
        for s in ("pe", "act", "dve"):
            f[s] = max(f.get(s, 0), self.cnt[s])

    def drop(self, handles):
        for h in handles:
            self.bufs.pop(id(h), None)


def build_nc(T=2048, NB=2, taps=False):
    nc = bass.Bass("TRN2", target_bir_lowering=False)
    n5 = T // 512
    n2 = T // 256
    LOGT = float(np.log(T))

    dx = nc.dram_tensor("x", [NB, PB, 128, T], F32, kind="ExternalInput").ap()
    dout = nc.dram_tensor("out", [NB, PB, 128, T], F32, kind="ExternalOutput").ap()
    dW = {}
    for nm in ("wq", "wk", "wv", "wo", "wkey", "wval", "wrec", "wout"):
        dW[nm] = nc.dram_tensor(nm, [128, PB * D], F32, kind="ExternalInput").ap()
    dw13 = nc.dram_tensor("w13", [NEX, 128, PB * D], F32, kind="ExternalInput").ap()
    dw2 = nc.dram_tensor("w2", [NEX, 128, 4 * D], F32, kind="ExternalInput").ap()
    dgw = nc.dram_tensor("gw", [128, PB * NE], F32, kind="ExternalInput").ap()
    dpv = nc.dram_tensor("pvec", [128, 80], F32, kind="ExternalInput").ap()
    dbrow = nc.dram_tensor("brow", [1, 2 * D], F32, kind="ExternalInput").ap()
    dones = nc.dram_tensor("ones_k1", [1, 128], F32, kind="ExternalInput").ap()
    donesb = nc.dram_tensor("onescol_b", [128, 1], BF16, kind="ExternalInput").ap()
    dbd = nc.dram_tensor("bd", [128, PB * HEAD], BF16, kind="ExternalInput").ap()
    dselh = nc.dram_tensor("selh", [16, PB * 128], BF16, kind="ExternalInput").ap()
    dsele = nc.dram_tensor("sel_e", [16, NE * 128], BF16, kind="ExternalInput").ap()
    dident = nc.dram_tensor("ident_b", [128, 128], BF16, kind="ExternalInput").ap()
    dcv = nc.dram_tensor("cvals", [128, 8], F32, kind="ExternalInput").ap()
    dtap = {}
    if taps:
        for nm in ("xn1", "attna", "ffn1", "attnb", "ffn2", "wkv"):
            dtap[nm] = nc.dram_tensor("tap_" + nm, [NB, PB, 128, T], F32,
                                      kind="ExternalOutput").ap()

    es = ExitStack()
    _uid = [0]

    def _nm(name):
        _uid[0] += 1
        return f"{name}_{_uid[0]}"

    sb = lambda name, shape, dt=F32: es.enter_context(nc.sbuf_tensor(name, shape, dt))
    tk = TK(nc, es)

    x8 = [sb(f"x{i}", [128, T]) for i in range(PB)]
    c_ones = sb("c_ones", [1, 128])
    c_onesb = sb("c_onesb", [128, 1], BF16)
    pv = sb("pv", [128, 80])
    dpar = sb("dpar", [128, 40])
    sgate = sb("sgate", [16, 1])
    gwS = sb("gwS", [128, PB * NE], BF16)
    msb = sb("msb", [128, 512], BF16)
    W0 = sb("W0", [128, PB * D], BF16)
    W1 = sb("W1", [128, PB * D], BF16)
    cvals = sb("cvals_s", [128, 8])
    ps8 = [es.enter_context(nc.psum_tensor(f"ps{i}", [128, 512], F32))
           for i in range(8)]
    tk.psum_ids = {id(p) for p in ps8}

    BQ, BK, BV, BO, UU, TD, TMK, TMV, TMR, MCOL = 0, 8, 16, 24, 32, 40, 48, 56, 64, 72
    OTMK, OTMV, OTMR, EW = 0, 8, 16, 24

    pe = lambda fn, r=(), w=(), inc=True: tk.emit("pe", fn, r, w, inc)
    acte = lambda fn, r=(), w=(): tk.emit("act", fn, r, w)
    dve = lambda fn, r=(), w=(): tk.emit("dve", fn, r, w)
    gq = lambda fn, r=(), w=(): tk.emit("pool", fn, r, w, dma="gq")
    sq = lambda fn, r=(), w=(): tk.emit("sp", fn, r, w, dma="sq")

    def ldw(dst, src_ap, r0=0, r1=None):
        n = dst.shape[1] if r1 is None else (r1 - r0)
        gq(lambda g, d=dst[:, r0:r0 + n], s=src_ap[:, r0:r0 + n]: g.dma_start(d, s),
           w=[(dst, ("ld", r0))])

    CV = [1e-5, -0.5, -1.0, -LOGT]
    for i, v in enumerate(CV):
        nc.const_aps.register(F32, v, cvals[:, i:i + 1])
    sq(lambda e: e.dma_start(cvals[:, :], dcv), w=[(cvals, None)])
    sq(lambda e: e.dma_start(c_ones[:, :], dones), w=[(c_ones, None)])
    sq(lambda e: e.dma_start(c_onesb[:, :], donesb), w=[(c_onesb, None)])
    sq(lambda e: e.dma_start(pv[:, :], dpv), w=[(pv, None)])
    gq(lambda e: e.dma_start(gwS[:, :], dgw), w=[(gwS, None)])
    for p in range(PB):
        sq(lambda e, p=p: e.dma_start(x8[p][:, :], dx[0, p]), w=[(x8[p], None)])

    for c in ("pe", "act", "dve"):
        for i in range(tk.NDMA):
            for q in ("sq", "gq"):
                if tk.cnt[f"{q}{i}"]:
                    tk.floor[c][f"{q}{i}"] = tk.cnt[f"{q}{i}"]
    for col, src in ((OTMK, TMK), (OTMV, TMV), (OTMR, TMR)):
        dve(lambda v, c=col, s=src: v.tensor_scalar(
            dpar[:, c:c + 8], pv[:, s:s + 8], -1.0, 1.0, OP.mult, OP.add),
            r=[(pv, None)], w=[(dpar, ("c", col))])
    acte(lambda s: s.activation(dpar[:, EW:EW + 8], pv[:, TD:TD + 8], AF.Exp),
         r=[(pv, None)], w=[(dpar, ("c", EW))])
    dve(lambda v: v.tensor_scalar(dpar[:, EW:EW + 8], dpar[:, EW:EW + 8],
                                  -1.0, 0.0, OP.mult, OP.bypass),
        r=[(dpar, ("c", EW))], w=[(dpar, ("c", EW))])
    acte(lambda s: s.activation(dpar[:, EW:EW + 8], dpar[:, EW:EW + 8], AF.Exp),
         r=[(dpar, ("c", EW))], w=[(dpar, ("c", EW))])
    acte(lambda s: s.activation(dpar[:, 32:40], pv[:, UU:UU + 8], AF.Exp),
         r=[(pv, None)], w=[(dpar, ("c", 32))])
    acte(lambda s: s.activation(sgate[:, :], pv[0:16, MCOL:MCOL + 1], AF.Sigmoid),
         r=[(pv, None)], w=[(sgate, None)])
    acte(lambda s: s.activation(sgate[:, :], sgate[:, :], AF.Exp, scale=-LOGT),
         r=[(sgate, None)], w=[(sgate, None)])

    def ln_tile(xsrc, tt, dst_fn, scratch, tag):
        t0 = tt * 512
        xc, xq, stats = scratch["xc"], scratch["xq"], scratch["stats"]
        for hh in range(4):
            for p in range(PB):
                s0 = t0 + hh * 128
                acte(lambda s, p=p, s0=s0: s.activation(
                    xc[:, p * 128:(p + 1) * 128], xsrc[p][:, s0:s0 + 128], AF.Identity),
                    r=[(xsrc[p], ("c", s0 // 128))], w=[(xc, ("p", p))])
                acte(lambda s, p=p, s0=s0: s.activation(
                    xq[:, p * 128:(p + 1) * 128], xsrc[p][:, s0:s0 + 128], AF.Square),
                    r=[(xsrc[p], ("c", s0 // 128))], w=[(xq, ("p", p))])
            for p in range(PB):
                pe(lambda t, p=p, hh=hh: t.matmul(
                    ps8[0][0:1, hh * 128:(hh + 1) * 128], c_onesb[:, :],
                    xc[:, p * 128:(p + 1) * 128], start=(p == 0), stop=(p == PB - 1)),
                   r=[(c_onesb, None), (xc, ("p", p))], w=[(ps8[0], None)],
                   inc=(p == PB - 1))
                pe(lambda t, p=p, hh=hh: t.matmul(
                    ps8[1][0:1, hh * 128:(hh + 1) * 128], c_onesb[:, :],
                    xq[:, p * 128:(p + 1) * 128], start=(p == 0), stop=(p == PB - 1)),
                   r=[(c_onesb, None), (xq, ("p", p))], w=[(ps8[1], None)],
                   inc=(p == PB - 1))
        dve(lambda v: v.tensor_copy(stats[:, 0:512], ps8[0][0:1, :]),
            r=[(ps8[0], None)], w=[(stats, 0)])
        dve(lambda v: v.tensor_copy(stats[:, 512:1024], ps8[1][0:1, :]),
            r=[(ps8[1], None)], w=[(stats, 1)])
        pe(lambda t: t.matmul(ps8[2][:, :], c_ones[:, :], stats[:, 0:512],
                              start=True, stop=True),
           r=[(c_ones, None), (stats, 0)], w=[(ps8[2], None)])
        pe(lambda t: t.matmul(ps8[3][:, :], c_ones[:, :], stats[:, 512:1024],
                              start=True, stop=True),
           r=[(c_ones, None), (stats, 1)], w=[(ps8[3], None)])
        mean, ex2, var = scratch["mean"], scratch["ex2"], scratch["var"]
        istd = var
        dve(lambda v: v.tensor_scalar(mean[:, :], ps8[2][:, :], 1.0 / D, 0.0,
                                      OP.mult, OP.add),
            r=[(ps8[2], None)], w=[(mean, None)])
        dve(lambda v: v.tensor_scalar(ex2[:, :], ps8[3][:, :], 1.0 / D, 0.0,
                                      OP.mult, OP.add),
            r=[(ps8[3], None)], w=[(ex2, None)])
        dve(lambda v: v.tensor_tensor(var[:, :], mean[:, :], mean[:, :], OP.mult),
            r=[(mean, None)], w=[(var, None)])
        dve(lambda v: v.tensor_tensor(var[:, :], ex2[:, :], var[:, :], OP.subtract),
            r=[(ex2, None), (var, None)], w=[(var, None)])
        acte(lambda s: s.activation(var[:, :], var[:, :], AF.Ln, bias=1e-5),
             r=[(var, None)], w=[(var, None)])
        acte(lambda s: s.activation(istd[:, :], var[:, :], AF.Exp, scale=-0.5),
             r=[(var, None)], w=[(istd, None)])
        dlt = scratch["dlt"]
        for p in range(PB):
            dap, dh, dtag = dst_fn(p)
            dve(lambda v, p=p: v.tensor_tensor(dlt[:, :], xsrc[p][:, t0:t0 + 512],
                                               mean[:, :], OP.subtract),
                r=[(xsrc[p], ("c", tt * 4 + j)) for j in range(4)] + [(mean, None)],
                w=[(dlt, None)])
            dve(lambda v, a=dap: v.tensor_tensor(a, dlt[:, :], istd[:, :], OP.mult),
                r=[(dlt, None), (istd, None)], w=[(dh, dtag)])

    def mk_ln_scratch(sbp):
        return dict(xc=sbp("ln_xc", [128, PB * 128], BF16),
                    xq=sbp("ln_xq", [128, PB * 128], BF16),
                    stats=sbp("ln_stats", [1, 1024]),
                    mean=sbp("ln_mean", [128, 512]), ex2=sbp("ln_ex2", [128, 512]),
                    var=sbp("ln_var", [128, 512]),
                    dlt=sbp("ln_dlt", [128, 512]))

    def attn_a(b):
        with ExitStack() as ph:
            sbp = lambda n, s, dt=F32: ph.enter_context(
                nc.sbuf_tensor(_nm(n), s, dt))
            lns = mk_ln_scratch(sbp)
            xn1 = sbp("xn1", [128, PB * 512], BF16)
            btm = sbp("btm", [128, 2 * D])
            browS = sbp("browS", [1, 2 * D])
            c_selh = sbp("c_selh", [16, PB * 128], BF16)
            c_bd = sbp("c_bd", [128, PB * HEAD], BF16)
            sq(lambda e: e.dma_start(browS[:, :], dbrow), w=[(browS, None)])
            sq(lambda e: e.dma_start(c_selh[:, :], dselh), w=[(c_selh, None)])
            sq(lambda e: e.dma_start(c_bd[:, :], dbd), w=[(c_bd, None)])
            for j in range(4):
                pe(lambda t, j=j: t.matmul(ps8[j][:, :], c_ones[:, :],
                                           browS[:, 512 * j:512 * (j + 1)],
                                           start=True, stop=True),
                   r=[(c_ones, None), (browS, None)], w=[(ps8[j], None)])
                dve(lambda v, j=j: v.tensor_copy(btm[:, 512 * j:512 * (j + 1)],
                                                 ps8[j][:, :]),
                    r=[(ps8[j], None)], w=[(btm, ("c", j))])
            knt = [sbp(f"knt{i}", [128, D], BF16) for i in range(2)]
            vt = [sbp(f"vt{i}", [128, D], BF16) for i in range(2)]
            kb = sbp("kb", [128, D])
            ksq = sbp("ksq", [128, D], BF16)
            ksc = sbp("ksc", [128, 16])
            ldw(W0, dW["wk"])
            ldw(W1, dW["wv"])
            for tt in range(n5):
                ln_tile(x8, tt, lambda p: (xn1[:, p * 512:(p + 1) * 512], xn1,
                                           ("p", p)), lns, "ln1")
                for tbl in range(4):
                    tb = tt * 4 + tbl
                    kn, vv = knt[tb % 2], vt[tb % 2]
                    for half in range(2):
                        pk, pv_ = ps8[half], ps8[2 + half]
                        for cb in range(PB):
                            lh = xn1[:, cb * 512 + tbl * 128: cb * 512 + (tbl + 1) * 128]
                            pe(lambda t, o=pk[:, :], l=lh,
                               r_=W0[:, cb * D + half * 512: cb * D + (half + 1) * 512],
                               st=(cb == 0), sp=(cb == PB - 1): t.matmul(
                                   o, l, r_, start=st, stop=sp),
                               r=[(xn1, ("p", cb)), (W0, ("ld", 0))],
                               w=[(pk, None)], inc=(cb == PB - 1))
                            pe(lambda t, o=pv_[:, :], l=lh,
                               r_=W1[:, cb * D + half * 512: cb * D + (half + 1) * 512],
                               st=(cb == 0), sp=(cb == PB - 1): t.matmul(
                                   o, l, r_, start=st, stop=sp),
                               r=[(xn1, ("p", cb)), (W1, ("ld", 0))],
                               w=[(pv_, None)], inc=(cb == PB - 1))
                        dve(lambda v, h=half: v.tensor_tensor(
                            kb[:, h * 512:(h + 1) * 512], ps8[h][:, :],
                            btm[:, h * 512:(h + 1) * 512], OP.add),
                            r=[(ps8[half], None), (btm, ("c", half))],
                            w=[(kb, ("h", half))])
                        acte(lambda s, h=half: s.activation(
                            ksq[:, h * 512:(h + 1) * 512], kb[:, h * 512:(h + 1) * 512],
                            AF.Square),
                            r=[(kb, ("h", half))], w=[(ksq, ("h", half))])
                        dve(lambda v, h=half, vv=vv: v.tensor_tensor(
                            vv[:, h * 512:(h + 1) * 512], ps8[2 + h][:, :],
                            btm[:, D + h * 512: D + (h + 1) * 512], OP.add),
                            r=[(ps8[2 + half], None), (btm, ("c", 2 + half))],
                            w=[(vv, ("h", half))])
                    dve(lambda v: v.reduce_sum(
                        ksc[:, :], ksq[:, :].rearrange("p (h d) -> p h d", d=HD),
                        axis=AX.X),
                        r=[(ksq, ("h", 0)), (ksq, ("h", 1))], w=[(ksc, None)])
                    dve(lambda v: v.tensor_scalar(ksc[:, :], ksc[:, :], 1e-24, 0.0,
                                                  OP.max, OP.add),
                        r=[(ksc, None)], w=[(ksc, None)])
                    acte(lambda s: s.activation(ksc[:, :], ksc[:, :], AF.Ln),
                         r=[(ksc, None)], w=[(ksc, None)])
                    acte(lambda s: s.activation(ksc[:, :], ksc[:, :], AF.Exp,
                                                scale=-0.5),
                         r=[(ksc, None)], w=[(ksc, None)])
                    for half in range(2):
                        dve(lambda v, h=half, kn=kn: v.tensor_tensor(
                            kn[:, h * 512:(h + 1) * 512].rearrange(
                                "p (h d) -> p h d", d=HD),
                            kb[:, h * 512:(h + 1) * 512].rearrange(
                                "p (h d) -> p h d", d=HD),
                            ksc[:, h * 8:(h + 1) * 8].unsqueeze(2).broadcast_to(
                                (128, 8, HD)),
                            OP.mult),
                            r=[(kb, ("h", half)), (ksc, None)], w=[(kn, ("h", half))])
                    for h in range(HEAD):
                        r0, c0 = 64 * (h % 2), 64 * (h // 2)
                        pe(lambda t, kn=kn, vv=vv, h=h, r0=r0, c0=c0,
                           st=(tb == 0 and h < 2),
                           sp=(tb == 4 * n5 - 1 and h == HEAD - 1):
                           t.matmul(ps8[7][r0:r0 + 64, c0:c0 + 64],
                                    kn[:, h * 64:(h + 1) * 64],
                                    vv[:, h * 64:(h + 1) * 64],
                                    start=st, stop=sp, skip_group_check=True),
                           r=[(kn, ("h", 0)), (kn, ("h", 1)), (vv, ("h", 0)),
                              (vv, ("h", 1))],
                           w=[(ps8[7], None)], inc=(h == HEAD - 1))
            dve(lambda v: v.tensor_copy(msb[:, :], ps8[7][:, :]),
                r=[(ps8[7], None)], w=[(msb, None)])
            ldw(W0, dW["wq"])
            ldw(W1, dW["wo"])
            qb = sbp("qb", [128, PB * 512], BF16)
            qsq = sbp("qsq", [128, PB * 512], BF16)
            qn = sbp("qn", [128, PB * 512], BF16)
            ao = sbp("ao", [128, PB * 512], BF16)
            qs = sbp("qs", [16, 512])
            qsb = sbp("qsb", [16, 512], BF16)
            for tt in range(n5):
                ln_tile(x8, tt, lambda p: (xn1[:, p * 512:(p + 1) * 512], xn1,
                                           ("p", p)), lns, "ln1c")
                for db in range(PB):
                    pq = ps8[db % 2]
                    for cb in range(PB):
                        pe(lambda t, o=pq[:, :],
                           l=W0[:, cb * D + db * 128: cb * D + (db + 1) * 128],
                           r_=xn1[:, cb * 512:(cb + 1) * 512],
                           st=(cb == 0), sp=(cb == PB - 1): t.matmul(
                               o, l, r_, start=st, stop=sp),
                           r=[(W0, ("ld", 0)), (xn1, ("p", cb))],
                           w=[(pq, None)], inc=(cb == PB - 1))
                    dve(lambda v, db=db, pq=pq: v.tensor_scalar(
                        qb[:, db * 512:(db + 1) * 512], pq[:, :],
                        pv[:, BQ + db:BQ + db + 1], 0.0, OP.add, OP.add),
                        r=[(pq, None), (pv, None)], w=[(qb, ("p", db))])
                    acte(lambda s, db=db, pq=pq: s.activation(
                        qsq[:, db * 512:(db + 1) * 512], pq[:, :], AF.Square,
                        bias=pv[:, BQ + db:BQ + db + 1]),
                        r=[(pq, None), (pv, None)], w=[(qsq, ("p", db))])
                for p in range(PB):
                    pe(lambda t, p=p: t.matmul(
                        ps8[2][0:16, :], c_bd[:, p * 16:(p + 1) * 16],
                        qsq[:, p * 512:(p + 1) * 512],
                        start=(p == 0), stop=(p == PB - 1)),
                       r=[(c_bd, None), (qsq, ("p", p))], w=[(ps8[2], None)],
                       inc=(p == PB - 1))
                dve(lambda v: v.tensor_scalar(qs[:, :], ps8[2][0:16, :], 1e-24, 0.0,
                                              OP.max, OP.add),
                    r=[(ps8[2], None)], w=[(qs, None)])
                acte(lambda s: s.activation(qs[:, :], qs[:, :], AF.Ln),
                     r=[(qs, None)], w=[(qs, None)])
                acte(lambda s: s.activation(qs[:, :], qs[:, :], AF.Exp, scale=-0.5),
                     r=[(qs, None)], w=[(qs, None)])
                dve(lambda v: v.tensor_scalar(qs[:, :], qs[:, :], sgate[:, 0:1], 0.0,
                                              OP.mult, OP.add),
                    r=[(qs, None), (sgate, None)], w=[(qs, None)])
                dve(lambda v: v.tensor_copy(qsb[:, :], qs[:, :]),
                    r=[(qs, None)], w=[(qsb, None)])
                for p in range(PB):
                    pbc = ps8[3 + p % 2]
                    pe(lambda t, p=p, o=pbc[:, :]: t.matmul(
                        o, c_selh[:, p * 128:(p + 1) * 128], qsb[:, :],
                        start=True, stop=True),
                       r=[(c_selh, None), (qsb, None)], w=[(pbc, None)])
                    dve(lambda v, p=p, pbc=pbc: v.tensor_tensor(
                        qn[:, p * 512:(p + 1) * 512], qb[:, p * 512:(p + 1) * 512],
                        pbc[:, :], OP.mult),
                        r=[(qb, ("p", p)), (pbc, None)], w=[(qn, ("p", p))])
                    py = ps8[5 + p % 2]
                    for hh in range(2):
                        h = 2 * p + hh
                        pe(lambda t, h=h, hh=hh, p=p, py=py: t.matmul(
                            py[hh * 64:(hh + 1) * 64, :],
                            msb[64 * (h % 2):64 * (h % 2) + 64,
                                64 * (h // 2):64 * (h // 2) + 64],
                            qn[64 * hh:64 * (hh + 1), p * 512:(p + 1) * 512],
                            start=True, stop=True, skip_group_check=True),
                           r=[(msb, None), (qn, ("p", p))], w=[(py, None)],
                           inc=(hh == 1))
                    dve(lambda v, p=p, py=py: v.tensor_copy(
                        ao[:, p * 512:(p + 1) * 512], py[:, :]),
                        r=[(py, None)], w=[(ao, ("p", p))])
                for db in range(PB):
                    po = ps8[7]
                    for cb in range(PB):
                        pe(lambda t, o=po[:, :],
                           l=W1[:, cb * D + db * 128: cb * D + (db + 1) * 128],
                           r_=ao[:, cb * 512:(cb + 1) * 512],
                           st=(cb == 0), sp=(cb == PB - 1): t.matmul(
                               o, l, r_, start=st, stop=sp),
                           r=[(W1, ("ld", 0)), (ao, ("p", cb))],
                           w=[(po, None)], inc=(cb == PB - 1))
                    xtags = [(x8[db], ("c", tt * 4 + j)) for j in range(4)]
                    dve(lambda v, db=db, tt=tt, po=po: v.scalar_tensor_tensor(
                        x8[db][:, tt * 512:(tt + 1) * 512], po[:, :],
                        pv[:, BO + db:BO + db + 1],
                        x8[db][:, tt * 512:(tt + 1) * 512], OP.add, OP.add),
                        r=[(po, None), (pv, None)] + xtags,
                        w=xtags)
            tk.barrier()
            tk.drop([lns[k] for k in lns] + [xn1, knt[0], knt[1], vt[0], vt[1],
                                             kb, ksq, ksc, qb, qsq, qn, ao, qs, qsb,
                                             btm, browS, c_selh, c_bd])

    def ffn(b, inv):
        with ExitStack() as ph:
            sbp = lambda n, s, dt=F32: ph.enter_context(
                nc.sbuf_tensor(_nm(n), s, dt))
            lns = mk_ln_scratch(sbp)
            xn2 = [sbp(f"xn2_{i}", [128, PB * 512], BF16) for i in range(2)]
            asl = sbp("asl", [128, 4 * 4 * 512], BF16)
            sil = sbp("sil", [128, 4 * 512], BF16)
            h3c = sbp("h3c", [128, 4 * 512], BF16)
            call = sbp("call", [128, NE * 512], BF16)
            gL = sbp("gL", [128, 64])
            gE = sbp("gE", [128, 64])
            gS = sbp("gS", [128, 64])
            gSm = sbp("gSm", [128, 64])
            gm4 = sbp("gm4", [128, 8])
            gma = sbp("gma", [128, 16])
            gmb = sbp("gmb", [128, 16])
            gr4 = sbp("gr4", [128, 4])
            gctm = sbp("gctm", [128, 64], BF16)
            gcfm = sbp("gcfm", [16, 512], BF16)
            c_sele = sbp("c_sele", [16, NE * 128], BF16)
            c_id = sbp("c_id", [128, 128], BF16)
            sq(lambda e: e.dma_start(c_sele[:, :], dsele), w=[(c_sele, None)])
            sq(lambda e: e.dma_start(c_id[:, :], dident), w=[(c_id, None)])
            w2f = [sbp(f"w2f{i}", [128, 4 * D], BF16) for i in range(2)]
            seq = [(t_, e_) for t_ in range(n5) for e_ in range(NEX)]
            wbuf = lambda gi: (W0, W1)[gi % 2]
            ldw(wbuf(0), dw13[seq[0][1]])
            tk.fence("pool")

            def gate(tt, xn):
                pg = ps8[4]
                for tbl in range(4):
                    for cb in range(PB):
                        pe(lambda t, o=pg[:, tbl * 16:(tbl + 1) * 16],
                           l=xn[:, cb * 512 + tbl * 128: cb * 512 + (tbl + 1) * 128],
                           r_=gwS[:, cb * 16:(cb + 1) * 16],
                           st=(cb == 0 and tbl == 0), sp=(cb == PB - 1): t.matmul(
                               o, l, r_, start=st, stop=sp, skip_group_check=True),
                           r=[(xn, ("p", cb)), (gwS, None)], w=[(pg, None)],
                           inc=(cb == PB - 1))
                v3 = lambda a: a.rearrange("p (t e) -> p t e", t=4)
                v16 = lambda a: a.rearrange("p (tg e) -> p tg e", tg=16)
                bc4 = lambda a: a.unsqueeze(2).broadcast_to((128, 4, 16))
                bc16 = lambda a: a.rearrange("p (t g) -> p t g", t=4)
                dve(lambda v: v.tensor_copy(gL[:, :], pg[:, 0:64]),
                    r=[(pg, None)], w=[(gL, None)])
                dve(lambda v: v.reduce_max(gm4[:, 0:4], v3(gL[:, :]), axis=AX.X),
                    r=[(gL, None)], w=[(gm4, 0)])
                dve(lambda v: v.tensor_tensor(v3(gE[:, :]), v3(gL[:, :]),
                                              bc4(gm4[:, 0:4]), OP.subtract),
                    r=[(gL, None), (gm4, 0)], w=[(gE, None)])
                acte(lambda s: s.activation(gE[:, :], gE[:, :], AF.Exp),
                     r=[(gE, None)], w=[(gE, None)])
                dve(lambda v: v.reduce_sum(gm4[:, 4:8], v3(gE[:, :]), axis=AX.X),
                    r=[(gE, None)], w=[(gm4, 1)])
                dve(lambda v: v.reciprocal(gr4[:, :], gm4[:, 4:8]),
                    r=[(gm4, 1)], w=[(gr4, None)])
                dve(lambda v: v.tensor_tensor(v3(gS[:, :]), v3(gE[:, :]),
                                              bc4(gr4[:, :]), OP.mult),
                    r=[(gE, None), (gr4, None)], w=[(gS, None)])
                bcg = lambda a: a.unsqueeze(2).broadcast_to((128, 4, 4))
                dve(lambda v: v.reduce_max(bc16(gma[:, :]), v16(gS[:, :]), axis=AX.X),
                    r=[(gS, None)], w=[(gma, None)])
                dve(lambda v: v.reduce_max(gr4[:, :], bc16(gma[:, :]), axis=AX.X),
                    r=[(gma, None)], w=[(gr4, None)])
                dve(lambda v: v.tensor_tensor(
                    bc16(gmb[:, :]), bc16(gma[:, :]), bcg(gr4[:, :]), OP.is_equal),
                    r=[(gma, None), (gr4, None)], w=[(gmb, None)])
                dve(lambda v: v.scalar_tensor_tensor(
                    gmb[:, :], gmb[:, :], -1e30, gma[:, :], OP.mult, OP.add),
                    r=[(gmb, None), (gma, None)], w=[(gmb, None)])
                dve(lambda v: v.reduce_max(gr4[:, :], bc16(gmb[:, :]), axis=AX.X),
                    r=[(gmb, None)], w=[(gr4, None)])
                dve(lambda v: v.tensor_tensor(
                    bc16(gmb[:, :]), bc16(gma[:, :]), bcg(gr4[:, :]), OP.is_ge),
                    r=[(gma, None), (gr4, None)], w=[(gmb, None)])
                dve(lambda v: v.tensor_tensor(
                    v16(gSm[:, :]), v16(gS[:, :]),
                    gmb[:, :].unsqueeze(2).broadcast_to((128, 16, 4)), OP.mult),
                    r=[(gS, None), (gmb, None)], w=[(gSm, None)])
                dve(lambda v: v.tensor_copy(gE[:, :], gSm[:, :]),
                    r=[(gSm, None)], w=[(gE, None)])
                for it in range(4):
                    dve(lambda v: v.reduce_max(gm4[:, 0:4], v3(gE[:, :]), axis=AX.X),
                        r=[(gE, None)], w=[(gm4, 0)])
                    if it < 3:
                        dve(lambda v: v.tensor_tensor(
                            v3(gL[:, :]), v3(gE[:, :]), bc4(gm4[:, 0:4]),
                            OP.is_equal),
                            r=[(gE, None), (gm4, 0)], w=[(gL, None)])
                        dve(lambda v: v.scalar_tensor_tensor(
                            gE[:, :], gL[:, :], -1e30, gE[:, :], OP.mult, OP.add),
                            r=[(gL, None), (gE, None)], w=[(gE, None)])
                dve(lambda v: v.tensor_tensor(v3(gL[:, :]), v3(gSm[:, :]),
                                              bc4(gm4[:, 0:4]), OP.is_ge),
                    r=[(gSm, None), (gm4, 0)], w=[(gL, None)])
                dve(lambda v: v.tensor_tensor(gctm[:, :], gS[:, :], gL[:, :],
                                              OP.mult),
                    r=[(gS, None), (gL, None)], w=[(gctm, None)])
                for tbl in range(4):
                    pe(lambda t, tbl=tbl: t.transpose(
                        ps8[5][:, :].bitcast(BF16)[0:16, tbl * 128:(tbl + 1) * 128],
                        gctm[:, tbl * 16:(tbl + 1) * 16], c_id[:, :]),
                       r=[(gctm, None), (c_id, None)], w=[(ps8[5], None)])
                    dve(lambda v, tbl=tbl: v.tensor_copy(
                        gcfm[:, tbl * 128:(tbl + 1) * 128],
                        ps8[5][:, :].bitcast(BF16)[0:16, tbl * 128:(tbl + 1) * 128]),
                        r=[(ps8[5], None)], w=[(gcfm, ("t", tbl))])

            gi = 0
            for tt in range(n5):
                xn = xn2[tt % 2]
                ln_tile(x8, tt, lambda p: (xn[:, p * 512:(p + 1) * 512], xn,
                                           ("p", p)), lns, "ln2")
                gate(tt, xn)
                for e in range(NE):
                    pc = ps8[4 + e % 4]
                    pe(lambda t, e=e, pc=pc: t.matmul(
                        pc[:, :], c_sele[:, e * 128:(e + 1) * 128], gcfm[:, :],
                        start=True, stop=True),
                       r=[(c_sele, None)] + [(gcfm, ("t", j)) for j in range(4)],
                       w=[(pc, None)])
                    acte(lambda s, e=e, pc=pc: s.activation(
                        call[:, e * 512:(e + 1) * 512], pc[:, :], AF.Identity),
                        r=[(pc, None)], w=[(call, ("e", e))])
                for chunk in FFN_CHUNKS:
                    ldw(w2f[0], dw2[chunk[0]])
                    if len(chunk) > 1:
                        ldw(w2f[1], dw2[chunk[1]])
                    for ei, e in enumerate(chunk):
                        wb = wbuf(gi)
                        for ib in range(4):
                            for cb in range(PB):
                                pe(lambda t, o=ps8[ib][:, :],
                                   l=wb[:, cb * D + ib * 128: cb * D + (ib + 1) * 128],
                                   r_=xn[:, cb * 512:(cb + 1) * 512],
                                   st=(cb == 0), sp=(cb == PB - 1): t.matmul(
                                       o, l, r_, start=st, stop=sp),
                                   r=[(wb, ("ld", 0)), (xn, ("p", cb))],
                                   w=[(ps8[ib], None)], inc=(cb == PB - 1))
                            for cb in range(PB):
                                pe(lambda t, o=ps8[4 + ib][:, :],
                                   l=wb[:, cb * D + 512 + ib * 128:
                                          cb * D + 512 + (ib + 1) * 128],
                                   r_=xn[:, cb * 512:(cb + 1) * 512],
                                   st=(cb == 0), sp=(cb == PB - 1): t.matmul(
                                       o, l, r_, start=st, stop=sp),
                                   r=[(wb, ("ld", 0)), (xn, ("p", cb))],
                                   w=[(ps8[4 + ib], None)], inc=(cb == PB - 1))
                        if gi + 1 < len(seq):
                            ldw(wbuf(gi + 1), dw13[seq[gi + 1][1]])
                        gi += 1
                        for ib in range(4):
                            acte(lambda s, ib=ib: s.activation(
                                sil[:, ib * 512:(ib + 1) * 512], ps8[ib][:, :],
                                AF.Silu),
                                r=[(ps8[ib], None)], w=[(sil, ("i", ib))])
                            if e < NE:
                                dve(lambda v, ib=ib, e=e: v.tensor_tensor(
                                    h3c[:, ib * 512:(ib + 1) * 512], ps8[4 + ib][:, :],
                                    call[:, e * 512:(e + 1) * 512], OP.mult),
                                    r=[(ps8[4 + ib], None), (call, ("e", e))],
                                    w=[(h3c, ("i", ib))])
                                dve(lambda v, ib=ib, ei=ei: v.tensor_tensor(
                                    asl[:, (ei * 4 + ib) * 512:(ei * 4 + ib + 1) * 512],
                                    sil[:, ib * 512:(ib + 1) * 512],
                                    h3c[:, ib * 512:(ib + 1) * 512], OP.mult),
                                    r=[(sil, ("i", ib)), (h3c, ("i", ib))],
                                    w=[(asl, ("a", ei * 4 + ib))])
                            else:
                                dve(lambda v, ib=ib, ei=ei: v.tensor_tensor(
                                    asl[:, (ei * 4 + ib) * 512:(ei * 4 + ib + 1) * 512],
                                    sil[:, ib * 512:(ib + 1) * 512], ps8[4 + ib][:, :],
                                    OP.mult),
                                    r=[(sil, ("i", ib)), (ps8[4 + ib], None)],
                                    w=[(asl, ("a", ei * 4 + ib))])
                    for ei, e in enumerate(chunk):
                        wb2 = w2f[ei % 2]
                        for db in range(PB):
                            for ib in range(4):
                                pe(lambda t, o=ps8[db][:, :],
                                   l=wb2[:, ib * D + db * 128: ib * D + (db + 1) * 128],
                                   r_=asl[:, (ei * 4 + ib) * 512:
                                          (ei * 4 + ib + 1) * 512],
                                   st=(ei == 0 and ib == 0),
                                   sp=(ei == len(chunk) - 1 and ib == 3): t.matmul(
                                       o, l, r_, start=st, stop=sp,
                                       skip_group_check=True),
                                   r=[(wb2, ("ld", 0)), (asl, ("a", ei * 4 + ib))],
                                   w=[(ps8[db], None)],
                                   inc=(ib == 3))
                        if ei + 2 < len(chunk):
                            ldw(w2f[ei % 2], dw2[chunk[ei + 2]])
                    for db in range(PB):
                        xtags = [(x8[db], ("c", tt * 4 + j)) for j in range(4)]
                        dve(lambda v, db=db, tt=tt: v.tensor_tensor(
                            x8[db][:, tt * 512:(tt + 1) * 512],
                            x8[db][:, tt * 512:(tt + 1) * 512], ps8[db][:, :], OP.add),
                            r=xtags + [(ps8[db], None)], w=xtags)
            tk.barrier()
            tk.drop([lns[k] for k in lns] + xn2 + w2f +
                    [asl, sil, h3c, call, gL, gE, gS, gSm, gm4, gma, gmb, gr4,
                     gctm, gcfm, c_sele, c_id])

    def attn_b(b):
        EUC = 32

        def mixes(ph_sbp, names):
            out = {}
            for nmx in names:
                out[nmx] = ph_sbp("mx_" + nmx, [128, PB * 256], BF16)
            return out

        def mix_tile(dst, src_xn, tmcol, omcol, pb, tti):
            t0 = tti * 256
            xtag = [(src_xn, ("x", pb, (t0 + j * 128) // 512)) for j in (0, 1)]
            dve(lambda v, o=dst[:, pb * 256: pb * 256 + 256],
                i=src_xn[:, pb * T + t0: pb * T + t0 + 256],
                s=pv[:, tmcol + pb: tmcol + pb + 1]: v.tensor_scalar(
                    o, i, s, 0.0, OP.mult, OP.add),
                r=xtag + [(pv, None)], w=[(dst, ("p", pb))])
            lo = 1 if tti == 0 else 0
            if tti > 0:
                xtag = xtag + [(src_xn, ("x", pb, (t0 - 1) // 512))]
            dve(lambda v, o=dst[:, pb * 256 + lo: pb * 256 + 256],
                i0=src_xn[:, pb * T + t0 + lo - 1: pb * T + t0 + 255],
                s=dpar[:, omcol + pb: omcol + pb + 1],
                i1=dst[:, pb * 256 + lo: pb * 256 + 256]:
                v.scalar_tensor_tensor(o, i0, s, i1, OP.mult, OP.add),
                r=xtag + [(dpar, ("c", omcol)), (dst, ("p", pb))],
                w=[(dst, ("p", pb))])

        with ExitStack() as ph:
            sbp = lambda n, s, dt=F32: ph.enter_context(
                nc.sbuf_tensor(_nm(n), s, dt))
            lns = mk_ln_scratch(sbp)
            xn1 = sbp("xn1", [128, PB * 512], BF16)
            btm = sbp("btm", [128, 2 * D])
            browS = sbp("browS", [1, 2 * D])
            c_selh = sbp("c_selh", [16, PB * 128], BF16)
            c_bd = sbp("c_bd", [128, PB * HEAD], BF16)
            sq(lambda e: e.dma_start(browS[:, :], dbrow), w=[(browS, None)])
            sq(lambda e: e.dma_start(c_selh[:, :], dselh), w=[(c_selh, None)])
            sq(lambda e: e.dma_start(c_bd[:, :], dbd), w=[(c_bd, None)])
            for j in range(4):
                pe(lambda t, j=j: t.matmul(ps8[j][:, :], c_ones[:, :],
                                           browS[:, 512 * j:512 * (j + 1)],
                                           start=True, stop=True),
                   r=[(c_ones, None), (browS, None)], w=[(ps8[j], None)])
                dve(lambda v, j=j: v.tensor_copy(btm[:, 512 * j:512 * (j + 1)],
                                                 ps8[j][:, :]),
                    r=[(ps8[j], None)], w=[(btm, ("c", j))])
            knt = [sbp(f"knt{i}", [128, D], BF16) for i in range(2)]
            vt = [sbp(f"vt{i}", [128, D], BF16) for i in range(2)]
            kb = sbp("kb", [128, D])
            ksq = sbp("ksq", [128, D], BF16)
            ksc = sbp("ksc", [128, 16])
            ldw(W0, dW["wk"])
            ldw(W1, dW["wv"])
            for tt in range(n5):
                ln_tile(x8, tt, lambda p: (xn1[:, p * 512:(p + 1) * 512], xn1,
                                           ("p", p)), lns, "ln1")
                for tbl in range(4):
                    tb = tt * 4 + tbl
                    kn, vv = knt[tb % 2], vt[tb % 2]
                    for half in range(2):
                        pk, pv_ = ps8[half], ps8[2 + half]
                        for cb in range(PB):
                            lh = xn1[:, cb * 512 + tbl * 128: cb * 512 + (tbl + 1) * 128]
                            pe(lambda t, o=pk[:, :], l=lh,
                               r_=W0[:, cb * D + half * 512: cb * D + (half + 1) * 512],
                               st=(cb == 0), sp=(cb == PB - 1): t.matmul(
                                   o, l, r_, start=st, stop=sp),
                               r=[(xn1, ("p", cb)), (W0, ("ld", 0))],
                               w=[(pk, None)], inc=(cb == PB - 1))
                            pe(lambda t, o=pv_[:, :], l=lh,
                               r_=W1[:, cb * D + half * 512: cb * D + (half + 1) * 512],
                               st=(cb == 0), sp=(cb == PB - 1): t.matmul(
                                   o, l, r_, start=st, stop=sp),
                               r=[(xn1, ("p", cb)), (W1, ("ld", 0))],
                               w=[(pv_, None)], inc=(cb == PB - 1))
                        dve(lambda v, h=half: v.tensor_tensor(
                            kb[:, h * 512:(h + 1) * 512], ps8[h][:, :],
                            btm[:, h * 512:(h + 1) * 512], OP.add),
                            r=[(ps8[half], None), (btm, ("c", half))],
                            w=[(kb, ("h", half))])
                        acte(lambda s, h=half: s.activation(
                            ksq[:, h * 512:(h + 1) * 512], kb[:, h * 512:(h + 1) * 512],
                            AF.Square),
                            r=[(kb, ("h", half))], w=[(ksq, ("h", half))])
                        dve(lambda v, h=half, vv=vv: v.tensor_tensor(
                            vv[:, h * 512:(h + 1) * 512], ps8[2 + h][:, :],
                            btm[:, D + h * 512: D + (h + 1) * 512], OP.add),
                            r=[(ps8[2 + half], None), (btm, ("c", 2 + half))],
                            w=[(vv, ("h", half))])
                    dve(lambda v: v.reduce_sum(
                        ksc[:, :], ksq[:, :].rearrange("p (h d) -> p h d", d=HD),
                        axis=AX.X),
                        r=[(ksq, ("h", 0)), (ksq, ("h", 1))], w=[(ksc, None)])
                    dve(lambda v: v.tensor_scalar(ksc[:, :], ksc[:, :], 1e-24, 0.0,
                                                  OP.max, OP.add),
                        r=[(ksc, None)], w=[(ksc, None)])
                    acte(lambda s: s.activation(ksc[:, :], ksc[:, :], AF.Ln),
                         r=[(ksc, None)], w=[(ksc, None)])
                    acte(lambda s: s.activation(ksc[:, :], ksc[:, :], AF.Exp,
                                                scale=-0.5),
                         r=[(ksc, None)], w=[(ksc, None)])
                    for half in range(2):
                        dve(lambda v, h=half, kn=kn: v.tensor_tensor(
                            kn[:, h * 512:(h + 1) * 512].rearrange(
                                "p (h d) -> p h d", d=HD),
                            kb[:, h * 512:(h + 1) * 512].rearrange(
                                "p (h d) -> p h d", d=HD),
                            ksc[:, h * 8:(h + 1) * 8].unsqueeze(2).broadcast_to(
                                (128, 8, HD)),
                            OP.mult),
                            r=[(kb, ("h", half)), (ksc, None)], w=[(kn, ("h", half))])
                    for h in range(HEAD):
                        r0, c0 = 64 * (h % 2), 64 * (h // 2)
                        pe(lambda t, kn=kn, vv=vv, h=h, r0=r0, c0=c0,
                           st=(tb == 0 and h < 2),
                           sp=(tb == 4 * n5 - 1 and h == HEAD - 1):
                           t.matmul(ps8[7][r0:r0 + 64, c0:c0 + 64],
                                    kn[:, h * 64:(h + 1) * 64],
                                    vv[:, h * 64:(h + 1) * 64],
                                    start=st, stop=sp, skip_group_check=True),
                           r=[(kn, ("h", 0)), (kn, ("h", 1)), (vv, ("h", 0)),
                              (vv, ("h", 1))],
                           w=[(ps8[7], None)], inc=(h == HEAD - 1))
            dve(lambda v: v.tensor_copy(msb[:, :], ps8[7][:, :]),
                r=[(ps8[7], None)], w=[(msb, None)])
            ldw(W0, dW["wq"])
            ldw(W1, dW["wo"])
            qb = sbp("qb", [128, PB * 512], BF16)
            qsq = sbp("qsq", [128, PB * 512], BF16)
            qn = sbp("qn", [128, PB * 512], BF16)
            ao = sbp("ao", [128, PB * 512], BF16)
            qs = sbp("qs", [16, 512])
            qsb = sbp("qsb", [16, 512], BF16)
            for tt in range(n5):
                ln_tile(x8, tt, lambda p: (xn1[:, p * 512:(p + 1) * 512], xn1,
                                           ("p", p)), lns, "ln1c")
                for db in range(PB):
                    pq = ps8[db % 2]
                    for cb in range(PB):
                        pe(lambda t, o=pq[:, :],
                           l=W0[:, cb * D + db * 128: cb * D + (db + 1) * 128],
                           r_=xn1[:, cb * 512:(cb + 1) * 512],
                           st=(cb == 0), sp=(cb == PB - 1): t.matmul(
                               o, l, r_, start=st, stop=sp),
                           r=[(W0, ("ld", 0)), (xn1, ("p", cb))],
                           w=[(pq, None)], inc=(cb == PB - 1))
                    dve(lambda v, db=db, pq=pq: v.tensor_scalar(
                        qb[:, db * 512:(db + 1) * 512], pq[:, :],
                        pv[:, BQ + db:BQ + db + 1], 0.0, OP.add, OP.add),
                        r=[(pq, None), (pv, None)], w=[(qb, ("p", db))])
                    acte(lambda s, db=db, pq=pq: s.activation(
                        qsq[:, db * 512:(db + 1) * 512], pq[:, :], AF.Square,
                        bias=pv[:, BQ + db:BQ + db + 1]),
                        r=[(pq, None), (pv, None)], w=[(qsq, ("p", db))])
                for p in range(PB):
                    pe(lambda t, p=p: t.matmul(
                        ps8[2][0:16, :], c_bd[:, p * 16:(p + 1) * 16],
                        qsq[:, p * 512:(p + 1) * 512],
                        start=(p == 0), stop=(p == PB - 1)),
                       r=[(c_bd, None), (qsq, ("p", p))], w=[(ps8[2], None)],
                       inc=(p == PB - 1))
                dve(lambda v: v.tensor_scalar(qs[:, :], ps8[2][0:16, :], 1e-24, 0.0,
                                              OP.max, OP.add),
                    r=[(ps8[2], None)], w=[(qs, None)])
                acte(lambda s: s.activation(qs[:, :], qs[:, :], AF.Ln),
                     r=[(qs, None)], w=[(qs, None)])
                acte(lambda s: s.activation(qs[:, :], qs[:, :], AF.Exp, scale=-0.5),
                     r=[(qs, None)], w=[(qs, None)])
                dve(lambda v: v.tensor_scalar(qs[:, :], qs[:, :], sgate[:, 0:1], 0.0,
                                              OP.mult, OP.add),
                    r=[(qs, None), (sgate, None)], w=[(qs, None)])
                dve(lambda v: v.tensor_copy(qsb[:, :], qs[:, :]),
                    r=[(qs, None)], w=[(qsb, None)])
                for p in range(PB):
                    pbc = ps8[3 + p % 2]
                    pe(lambda t, p=p, o=pbc[:, :]: t.matmul(
                        o, c_selh[:, p * 128:(p + 1) * 128], qsb[:, :],
                        start=True, stop=True),
                       r=[(c_selh, None), (qsb, None)], w=[(pbc, None)])
                    dve(lambda v, p=p, pbc=pbc: v.tensor_tensor(
                        qn[:, p * 512:(p + 1) * 512], qb[:, p * 512:(p + 1) * 512],
                        pbc[:, :], OP.mult),
                        r=[(qb, ("p", p)), (pbc, None)], w=[(qn, ("p", p))])
                    py = ps8[5 + p % 2]
                    for hh in range(2):
                        h = 2 * p + hh
                        pe(lambda t, h=h, hh=hh, p=p, py=py: t.matmul(
                            py[hh * 64:(hh + 1) * 64, :],
                            msb[64 * (h % 2):64 * (h % 2) + 64,
                                64 * (h // 2):64 * (h // 2) + 64],
                            qn[64 * hh:64 * (hh + 1), p * 512:(p + 1) * 512],
                            start=True, stop=True, skip_group_check=True),
                           r=[(msb, None), (qn, ("p", p))], w=[(py, None)],
                           inc=(hh == 1))
                    dve(lambda v, p=p, py=py: v.tensor_copy(
                        ao[:, p * 512:(p + 1) * 512], py[:, :]),
                        r=[(py, None)], w=[(ao, ("p", p))])
                for db in range(PB):
                    po = ps8[7]
                    for cb in range(PB):
                        pe(lambda t, o=po[:, :],
                           l=W1[:, cb * D + db * 128: cb * D + (db + 1) * 128],
                           r_=ao[:, cb * 512:(cb + 1) * 512],
                           st=(cb == 0), sp=(cb == PB - 1): t.matmul(
                               o, l, r_, start=st, stop=sp),
                           r=[(W1, ("ld", 0)), (ao, ("p", cb))],
                           w=[(po, None)], inc=(cb == PB - 1))
                    xtags = [(x8[db], ("c", tt * 4 + j)) for j in range(4)]
                    dve(lambda v, db=db, tt=tt, po=po: v.scalar_tensor_tensor(
                        x8[db][:, tt * 512:(tt + 1) * 512], po[:, :],
                        pv[:, BO + db:BO + db + 1],
                        x8[db][:, tt * 512:(tt + 1) * 512], OP.add, OP.add),
                        r=[(po, None), (pv, None)] + xtags,
                        w=xtags)
            tk.barrier()
            tk.drop([lns[k] for k in lns] + [xn1, knt[0], knt[1], vt[0], vt[1],
                                             kb, ksq, ksc, qb, qsq, qn, ao, qs, qsb,
                                             btm, browS, c_selh, c_bd])

    def ffn(b, inv):
        with ExitStack() as ph:
            sbp = lambda n, s, dt=F32: ph.enter_context(
                nc.sbuf_tensor(_nm(n), s, dt))
            lns = mk_ln_scratch(sbp)
            xn2 = [sbp(f"xn2_{i}", [128, PB * 512], BF16) for i in range(2)]
            asl = sbp("asl", [128, 4 * 4 * 512], BF16)
            sil = sbp("sil", [128, 4 * 512], BF16)
            h3c = sbp("h3c", [128, 4 * 512], BF16)
            call = sbp("call", [128, NE * 512], BF16)
            gL = sbp("gL", [128, 64])
            gE = sbp("gE", [128, 64])
            gS = sbp("gS", [128, 64])
            gSm = sbp("gSm", [128, 64])
            gm4 = sbp("gm4", [128, 8])
            gma = sbp("gma", [128, 16])
            gmb = sbp("gmb", [128, 16])
            gr4 = sbp("gr4", [128, 4])
            gctm = sbp("gctm", [128, 64], BF16)
            gcfm = sbp("gcfm", [16, 512], BF16)
            c_sele = sbp("c_sele", [16, NE * 128], BF16)
            c_id = sbp("c_id", [128, 128], BF16)
            sq(lambda e: e.dma_start(c_sele[:, :], dsele), w=[(c_sele, None)])
            sq(lambda e: e.dma_start(c_id[:, :], dident), w=[(c_id, None)])
            w2f = [sbp(f"w2f{i}", [128, 4 * D], BF16) for i in range(2)]
            seq = [(t_, e_) for t_ in range(n5) for e_ in range(NEX)]
            wbuf = lambda gi: (W0, W1)[gi % 2]
            ldw(wbuf(0), dw13[seq[0][1]])
            tk.fence("pool")

            def gate(tt, xn):
                pg = ps8[4]
                for tbl in range(4):
                    for cb in range(PB):
                        pe(lambda t, o=pg[:, tbl * 16:(tbl + 1) * 16],
                           l=xn[:, cb * 512 + tbl * 128: cb * 512 + (tbl + 1) * 128],
                           r_=gwS[:, cb * 16:(cb + 1) * 16],
                           st=(cb == 0 and tbl == 0), sp=(cb == PB - 1): t.matmul(
                               o, l, r_, start=st, stop=sp, skip_group_check=True),
                           r=[(xn, ("p", cb)), (gwS, None)], w=[(pg, None)],
                           inc=(cb == PB - 1))
                v3 = lambda a: a.rearrange("p (t e) -> p t e", t=4)
                v16 = lambda a: a.rearrange("p (tg e) -> p tg e", tg=16)
                bc4 = lambda a: a.unsqueeze(2).broadcast_to((128, 4, 16))
                bc16 = lambda a: a.rearrange("p (t g) -> p t g", t=4)
                dve(lambda v: v.tensor_copy(gL[:, :], pg[:, 0:64]),
                    r=[(pg, None)], w=[(gL, None)])
                dve(lambda v: v.reduce_max(gm4[:, 0:4], v3(gL[:, :]), axis=AX.X),
                    r=[(gL, None)], w=[(gm4, 0)])
                dve(lambda v: v.tensor_tensor(v3(gE[:, :]), v3(gL[:, :]),
                                              bc4(gm4[:, 0:4]), OP.subtract),
                    r=[(gL, None), (gm4, 0)], w=[(gE, None)])
                acte(lambda s: s.activation(gE[:, :], gE[:, :], AF.Exp),
                     r=[(gE, None)], w=[(gE, None)])
                dve(lambda v: v.reduce_sum(gm4[:, 4:8], v3(gE[:, :]), axis=AX.X),
                    r=[(gE, None)], w=[(gm4, 1)])
                dve(lambda v: v.reciprocal(gr4[:, :], gm4[:, 4:8]),
                    r=[(gm4, 1)], w=[(gr4, None)])
                dve(lambda v: v.tensor_tensor(v3(gS[:, :]), v3(gE[:, :]),
                                              bc4(gr4[:, :]), OP.mult),
                    r=[(gE, None), (gr4, None)], w=[(gS, None)])
                bcg = lambda a: a.unsqueeze(2).broadcast_to((128, 4, 4))
                dve(lambda v: v.reduce_max(bc16(gma[:, :]), v16(gS[:, :]), axis=AX.X),
                    r=[(gS, None)], w=[(gma, None)])
                dve(lambda v: v.reduce_max(gr4[:, :], bc16(gma[:, :]), axis=AX.X),
                    r=[(gma, None)], w=[(gr4, None)])
                dve(lambda v: v.tensor_tensor(
                    bc16(gmb[:, :]), bc16(gma[:, :]), bcg(gr4[:, :]), OP.is_equal),
                    r=[(gma, None), (gr4, None)], w=[(gmb, None)])
                dve(lambda v: v.scalar_tensor_tensor(
                    gmb[:, :], gmb[:, :], -1e30, gma[:, :], OP.mult, OP.add),
                    r=[(gmb, None), (gma, None)], w=[(gmb, None)])
                dve(lambda v: v.reduce_max(gr4[:, :], bc16(gmb[:, :]), axis=AX.X),
                    r=[(gmb, None)], w=[(gr4, None)])
                dve(lambda v: v.tensor_tensor(
                    bc16(gmb[:, :]), bc16(gma[:, :]), bcg(gr4[:, :]), OP.is_ge),
                    r=[(gma, None), (gr4, None)], w=[(gmb, None)])
                dve(lambda v: v.tensor_tensor(
                    v16(gSm[:, :]), v16(gS[:, :]),
                    gmb[:, :].unsqueeze(2).broadcast_to((128, 16, 4)), OP.mult),
                    r=[(gS, None), (gmb, None)], w=[(gSm, None)])
                dve(lambda v: v.tensor_copy(gE[:, :], gSm[:, :]),
                    r=[(gSm, None)], w=[(gE, None)])
                for it in range(4):
                    dve(lambda v: v.reduce_max(gm4[:, 0:4], v3(gE[:, :]), axis=AX.X),
                        r=[(gE, None)], w=[(gm4, 0)])
                    if it < 3:
                        dve(lambda v: v.tensor_tensor(
                            v3(gL[:, :]), v3(gE[:, :]), bc4(gm4[:, 0:4]),
                            OP.is_equal),
                            r=[(gE, None), (gm4, 0)], w=[(gL, None)])
                        dve(lambda v: v.scalar_tensor_tensor(
                            gE[:, :], gL[:, :], -1e30, gE[:, :], OP.mult, OP.add),
                            r=[(gL, None), (gE, None)], w=[(gE, None)])
                dve(lambda v: v.tensor_tensor(v3(gL[:, :]), v3(gSm[:, :]),
                                              bc4(gm4[:, 0:4]), OP.is_ge),
                    r=[(gSm, None), (gm4, 0)], w=[(gL, None)])
                dve(lambda v: v.tensor_tensor(gctm[:, :], gS[:, :], gL[:, :],
                                              OP.mult),
                    r=[(gS, None), (gL, None)], w=[(gctm, None)])
                for tbl in range(4):
                    pe(lambda t, tbl=tbl: t.transpose(
                        ps8[5][:, :].bitcast(BF16)[0:16, tbl * 128:(tbl + 1) * 128],
                        gctm[:, tbl * 16:(tbl + 1) * 16], c_id[:, :]),
                       r=[(gctm, None), (c_id, None)], w=[(ps8[5], None)])
                    dve(lambda v, tbl=tbl: v.tensor_copy(
                        gcfm[:, tbl * 128:(tbl + 1) * 128],
                        ps8[5][:, :].bitcast(BF16)[0:16, tbl * 128:(tbl + 1) * 128]),
                        r=[(ps8[5], None)], w=[(gcfm, ("t", tbl))])

            gi = 0
            for tt in range(n5):
                xn = xn2[tt % 2]
                ln_tile(x8, tt, lambda p: (xn[:, p * 512:(p + 1) * 512], xn,
                                           ("p", p)), lns, "ln2")
                gate(tt, xn)
                for e in range(NE):
                    pc = ps8[4 + e % 4]
                    pe(lambda t, e=e, pc=pc: t.matmul(
                        pc[:, :], c_sele[:, e * 128:(e + 1) * 128], gcfm[:, :],
                        start=True, stop=True),
                       r=[(c_sele, None)] + [(gcfm, ("t", j)) for j in range(4)],
                       w=[(pc, None)])
                    acte(lambda s, e=e, pc=pc: s.activation(
                        call[:, e * 512:(e + 1) * 512], pc[:, :], AF.Identity),
                        r=[(pc, None)], w=[(call, ("e", e))])
                for chunk in FFN_CHUNKS:
                    ldw(w2f[0], dw2[chunk[0]])
                    if len(chunk) > 1:
                        ldw(w2f[1], dw2[chunk[1]])
                    for ei, e in enumerate(chunk):
                        wb = wbuf(gi)
                        for ib in range(4):
                            for cb in range(PB):
                                pe(lambda t, o=ps8[ib][:, :],
                                   l=wb[:, cb * D + ib * 128: cb * D + (ib + 1) * 128],
                                   r_=xn[:, cb * 512:(cb + 1) * 512],
                                   st=(cb == 0), sp=(cb == PB - 1): t.matmul(
                                       o, l, r_, start=st, stop=sp),
                                   r=[(wb, ("ld", 0)), (xn, ("p", cb))],
                                   w=[(ps8[ib], None)], inc=(cb == PB - 1))
                            for cb in range(PB):
                                pe(lambda t, o=ps8[4 + ib][:, :],
                                   l=wb[:, cb * D + 512 + ib * 128:
                                          cb * D + 512 + (ib + 1) * 128],
                                   r_=xn[:, cb * 512:(cb + 1) * 512],
                                   st=(cb == 0), sp=(cb == PB - 1): t.matmul(
                                       o, l, r_, start=st, stop=sp),
                                   r=[(wb, ("ld", 0)), (xn, ("p", cb))],
                                   w=[(ps8[4 + ib], None)], inc=(cb == PB - 1))
                        if gi + 1 < len(seq):
                            ldw(wbuf(gi + 1), dw13[seq[gi + 1][1]])
                        gi += 1
                        for ib in range(4):
                            acte(lambda s, ib=ib: s.activation(
                                sil[:, ib * 512:(ib + 1) * 512], ps8[ib][:, :],
                                AF.Silu),
                                r=[(ps8[ib], None)], w=[(sil, ("i", ib))])
                            if e < NE:
                                dve(lambda v, ib=ib, e=e: v.tensor_tensor(
                                    h3c[:, ib * 512:(ib + 1) * 512], ps8[4 + ib][:, :],
                                    call[:, e * 512:(e + 1) * 512], OP.mult),
                                    r=[(ps8[4 + ib], None), (call, ("e", e))],
                                    w=[(h3c, ("i", ib))])
                                dve(lambda v, ib=ib, ei=ei: v.tensor_tensor(
                                    asl[:, (ei * 4 + ib) * 512:(ei * 4 + ib + 1) * 512],
                                    sil[:, ib * 512:(ib + 1) * 512],
                                    h3c[:, ib * 512:(ib + 1) * 512], OP.mult),
                                    r=[(sil, ("i", ib)), (h3c, ("i", ib))],
                                    w=[(asl, ("a", ei * 4 + ib))])
                            else:
                                dve(lambda v, ib=ib, ei=ei: v.tensor_tensor(
                                    asl[:, (ei * 4 + ib) * 512:(ei * 4 + ib + 1) * 512],
                                    sil[:, ib * 512:(ib + 1) * 512], ps8[4 + ib][:, :],
                                    OP.mult),
                                    r=[(sil, ("i", ib)), (ps8[4 + ib], None)],
                                    w=[(asl, ("a", ei * 4 + ib))])
                    for ei, e in enumerate(chunk):
                        wb2 = w2f[ei % 2]
                        for db in range(PB):
                            for ib in range(4):
                                pe(lambda t, o=ps8[db][:, :],
                                   l=wb2[:, ib * D + db * 128: ib * D + (db + 1) * 128],
                                   r_=asl[:, (ei * 4 + ib) * 512:
                                          (ei * 4 + ib + 1) * 512],
                                   st=(ei == 0 and ib == 0),
                                   sp=(ei == len(chunk) - 1 and ib == 3): t.matmul(
                                       o, l, r_, start=st, stop=sp,
                                       skip_group_check=True),
                                   r=[(wb2, ("ld", 0)), (asl, ("a", ei * 4 + ib))],
                                   w=[(ps8[db], None)],
                                   inc=(ib == 3))
                        if ei + 2 < len(chunk):
                            ldw(w2f[ei % 2], dw2[chunk[ei + 2]])
                    for db in range(PB):
                        xtags = [(x8[db], ("c", tt * 4 + j)) for j in range(4)]
                        dve(lambda v, db=db, tt=tt: v.tensor_tensor(
                            x8[db][:, tt * 512:(tt + 1) * 512],
                            x8[db][:, tt * 512:(tt + 1) * 512], ps8[db][:, :], OP.add),
                            r=xtags + [(ps8[db], None)], w=xtags)
            tk.barrier()
            tk.drop([lns[k] for k in lns] + xn2 + w2f +
                    [asl, sil, h3c, call, gL, gE, gS, gSm, gm4, gma, gmb, gr4,
                     gctm, gcfm, c_sele, c_id])

    def attn_b(b):
        EUC = 32

        def mixes(ph_sbp, names):
            out = {}
            for nmx in names:
                out[nmx] = ph_sbp("mx_" + nmx, [128, PB * 256], BF16)
            return out

        def mix_tile(dst, src_xn, tmcol, omcol, pb, tti):
            t0 = tti * 256
            a_dst = dst[:, pb * 256: pb * 256 + 256]
            xtag = [(src_xn, ("x", pb, (t0 + j * 128) // 512)) for j in (0, 1)]
            dve(lambda v, o=a_dst,
                i=src_xn[:, pb * T + t0: pb * T + t0 + 256],
                s=pv[:, tmcol + pb: tmcol + pb + 1]: v.tensor_scalar(
                    o, i, s, 0.0, OP.mult, OP.add),
                r=xtag, w=[(dst, ("p", pb))])
            if tti == 0:
                lo = 1
            else:
                lo = 0
                xtag = xtag + [(src_xn, ("x", pb, (t0 - 1) // 512))]
            dve(lambda v, lo=lo,
                o=dst[:, pb * 256 + max(lo, 1) - (0 if lo else 0):][:, :0] if False
                else dst[:, pb * 256 + (1 if tti == 0 else 0): pb * 256 + 256],
                i0=src_xn[:, pb * T + t0 + (0 if tti else 0) - (0 if tti == 0 else 1):][:, :0] if False
                else src_xn[:, pb * T + t0 - (0 if tti == 0 else 1):
                            pb * T + t0 + 255],
                s=dpar[:, omcol + pb: omcol + pb + 1],
                i1=dst[:, pb * 256 + (1 if tti == 0 else 0): pb * 256 + 256]:
                v.scalar_tensor_tensor(o, i0, s, i1, OP.mult, OP.add),
                r=xtag + [(dpar, ("c", omcol)), (dst, ("p", pb))],
                w=[(dst, ("p", pb))])

        with ExitStack() as ph:
            sbp = lambda n, s, dt=F32: ph.enter_context(
                nc.sbuf_tensor(_nm(n), s, dt))
            xn3 = sbp("xn3", [128, PB * T], BF16)
            wkva = sbp("wkva", [128, PB * T], BF16)
            ldw(W0, dW["wkey"])
            ldw(W1, dW["wval"])
            with ExitStack() as p1:
                s1p = lambda n, s, dt=F32: p1.enter_context(
                    nc.sbuf_tensor(_nm(n), s, dt))
                with ExitStack() as lnes:
                    lnp = lambda n, s, dt=F32: lnes.enter_context(
                        nc.sbuf_tensor(_nm(n), s, dt))
                    lns = mk_ln_scratch(lnp)
                    for tt5 in range(n5):
                        ln_tile(x8, tt5,
                                lambda p, tt5=tt5: (
                                    xn3[:, p * T + tt5 * 512: p * T + (tt5 + 1) * 512],
                                    xn3, ("x", p, tt5)), lns, "ln3")
                    tk.barrier()
                    tk.drop([lns[k] for k in lns])
                mx = mixes(s1p, ("xk",))
                ekb = s1p("ekb", [128, PB * 256])
                ekvb = s1p("ekvb", [128, PB * 256])
                Pb8 = s1p("Pb8", [128, PB * 257])
                Qb8 = s1p("Qb8", [128, PB * 257])
                Pst = s1p("Pst", [128, PB])
                Qst = s1p("Qst", [128, PB])
                for p in range(PB):
                    dve(lambda v, p=p: v.memset(Pst[:, p:p + 1], 0.0),
                        r=[], w=[(Pst, ("p", p))])
                    dve(lambda v, p=p: v.memset(Qst[:, p:p + 1], 0.0),
                        r=[], w=[(Qst, ("p", p))])
                v3e = lambda a: a.rearrange("p (b n) -> p b n", b=PB)
                P3 = Pb8[:, :].rearrange("p (b n) -> p b n", b=PB)[:, :, 0:256]
                Q3 = Qb8[:, :].rearrange("p (b n) -> p b n", b=PB)[:, :, 0:256]
                for tti in range(n2):
                    for p in range(PB):
                        mix_tile(mx["xk"], xn3, TMK, OTMK, p, tti)
                    for db in range(PB):
                        pk = ps8[db // 2][:, (db % 2) * 256:(db % 2) * 256 + 256]
                        for cb in range(PB):
                            pe(lambda t, o=pk,
                               l=W0[:, cb * D + db * 128: cb * D + (db + 1) * 128],
                               r_=mx["xk"][:, cb * 256:(cb + 1) * 256],
                               st=(cb == 0 and db % 2 == 0), sp=(cb == PB - 1):
                               t.matmul(o, l, r_, start=st, stop=False,
                                        skip_group_check=True),
                               r=[(W0, ("ld", 0)), (mx["xk"], ("p", cb))],
                               w=[(ps8[db // 2], None)], inc=(cb == PB - 1))
                    for p in range(PB):
                        mix_tile(mx["xk"], xn3, TMV, OTMV, p, tti)
                    for db in range(PB):
                        pvv = ps8[4 + db // 2][:, (db % 2) * 256:(db % 2) * 256 + 256]
                        for cb in range(PB):
                            pe(lambda t, o=pvv,
                               l=W1[:, cb * D + db * 128: cb * D + (db + 1) * 128],
                               r_=mx["xk"][:, cb * 256:(cb + 1) * 256],
                               st=(cb == 0 and db % 2 == 0), sp=(cb == PB - 1):
                               t.matmul(o, l, r_, start=st, stop=False,
                                        skip_group_check=True),
                               r=[(W1, ("ld", 0)), (mx["xk"], ("p", cb))],
                               w=[(ps8[4 + db // 2], None)], inc=(cb == PB - 1))
                    for p in range(PB):
                        kps = ps8[p // 2][:, (p % 2) * 256:(p % 2) * 256 + 256]
                        vps = ps8[4 + p // 2][:, (p % 2) * 256:(p % 2) * 256 + 256]
                        acte(lambda s, kps=kps, p=p: s.activation(
                            ekb[:, p * 256:(p + 1) * 256], kps, AF.Exp),
                            r=[(ps8[p // 2], None)], w=[(ekb, ("p", p))])
                        dve(lambda v, vps=vps, p=p: v.tensor_tensor(
                            ekvb[:, p * 256:(p + 1) * 256],
                            ekb[:, p * 256:(p + 1) * 256], vps, OP.mult),
                            r=[(ekb, ("p", p)), (ps8[4 + p // 2], None)],
                            w=[(ekvb, ("p", p))])
                    for p in range(PB):
                        dve(lambda v, p=p: v.tensor_copy(
                            Pb8[:, p * 257: p * 257 + 1], Pst[:, p:p + 1]),
                            r=[(Pst, ("p", p))], w=[(Pb8, ("p", p))])
                        dve(lambda v, p=p: v.tensor_tensor_scan(
                            Pb8[:, p * 257 + 1: p * 257 + 257],
                            dpar[:, EW + p:EW + p + 1].broadcast_to((128, 256)),
                            ekvb[:, p * 256:(p + 1) * 256], Pst[:, p:p + 1],
                            OP.mult, OP.add),
                            r=[(dpar, ("c", EW)), (ekvb, ("p", p)),
                               (Pst, ("p", p)), (Pb8, ("p", p))],
                            w=[(Pb8, ("p", p))])
                        dve(lambda v, p=p: v.tensor_copy(
                            Qb8[:, p * 257: p * 257 + 1], Qst[:, p:p + 1]),
                            r=[(Qst, ("p", p))], w=[(Qb8, ("p", p))])
                        dve(lambda v, p=p: v.tensor_tensor_scan(
                            Qb8[:, p * 257 + 1: p * 257 + 257],
                            dpar[:, EW + p:EW + p + 1].broadcast_to((128, 256)),
                            ekb[:, p * 256:(p + 1) * 256], Qst[:, p:p + 1],
                            OP.mult, OP.add),
                            r=[(dpar, ("c", EW)), (ekb, ("p", p)),
                               (Qst, ("p", p)), (Qb8, ("p", p))],
                            w=[(Qb8, ("p", p))])
                        dve(lambda v, p=p: v.tensor_copy(
                            Pst[:, p:p + 1], Pb8[:, p * 257 + 256: p * 257 + 257]),
                            r=[(Pb8, ("p", p))], w=[(Pst, ("p", p))])
                        dve(lambda v, p=p: v.tensor_copy(
                            Qst[:, p:p + 1], Qb8[:, p * 257 + 256: p * 257 + 257]),
                            r=[(Qb8, ("p", p))], w=[(Qst, ("p", p))])
                    for p in range(PB):
                        dve(lambda v, p=p: v.scalar_tensor_tensor(
                            Pb8[:, p * 257: p * 257 + 256],
                            ekvb[:, p * 256:(p + 1) * 256],
                            dpar[:, EUC + p:EUC + p + 1],
                            Pb8[:, p * 257: p * 257 + 256], OP.mult, OP.add),
                            r=[(ekvb, ("p", p)), (dpar, ("c", 32)),
                               (Pb8, ("p", p))], w=[(Pb8, ("p", p))])
                        dve(lambda v, p=p: v.scalar_tensor_tensor(
                            Qb8[:, p * 257: p * 257 + 256],
                            ekb[:, p * 256:(p + 1) * 256],
                            dpar[:, EUC + p:EUC + p + 1],
                            Qb8[:, p * 257: p * 257 + 256], OP.mult, OP.add),
                            r=[(ekb, ("p", p)), (dpar, ("c", 32)),
                               (Qb8, ("p", p))], w=[(Qb8, ("p", p))])
                    allP = [(Pb8, ("p", p)) for p in range(PB)]
                    allQ = [(Qb8, ("p", p)) for p in range(PB)]
                    acte(lambda s: s.activation(Q3, Q3, AF.Ln), r=allQ, w=allQ)
                    acte(lambda s: s.activation(Q3, Q3, AF.Exp, scale=-1.0),
                         r=allQ, w=allQ)
                    dve(lambda v, tti=tti: v.tensor_tensor(
                        wkva[:, :].rearrange("p (b n) -> p b n", b=PB)[
                            :, :, tti * 256:(tti + 1) * 256],
                        P3, Q3, OP.mult),
                        r=allP + allQ, w=[(wkva, ("t", tti))])
                tk.barrier()
                tk.drop(list(mx.values()) +
                        [ekb, ekvb, Pb8, Qb8, Pst, Qst])
            ldw(W0, dW["wrec"])
            ldw(W1, dW["wout"])
            with ExitStack() as p2:
                s2p = lambda n, s, dt=F32: p2.enter_context(
                    nc.sbuf_tensor(_nm(n), s, dt))
                mx2 = mixes(s2p, ("xr",))
                erb = s2p("erb", [128, PB * 256])
                yt = [s2p(f"yt{i}", [128, PB * 256], BF16) for i in range(2)]
                v3e = lambda a: a.rearrange("p (b n) -> p b n", b=PB)

                def outproj(tti):
                    ytb = yt[tti % 2]
                    for db in range(PB):
                        po = ps8[4 + db // 2][:, (db % 2) * 256:(db % 2) * 256 + 256]
                        for cb in range(PB):
                            pe(lambda t, o=po,
                               l=W1[:, cb * D + db * 128: cb * D + (db + 1) * 128],
                               r_=ytb[:, cb * 256:(cb + 1) * 256],
                               st=(cb == 0 and db % 2 == 0), sp=(cb == PB - 1):
                               t.matmul(o, l, r_, start=st, stop=False,
                                        skip_group_check=True),
                               r=[(W1, ("ld", 0)), (ytb, ("p", cb))],
                               w=[(ps8[4 + db // 2], None)], inc=(cb == PB - 1))
                        xtags = [(x8[db], ("c", tti * 2 + j)) for j in range(2)]
                        dve(lambda v, db=db, tti=tti, po=po: v.tensor_tensor(
                            x8[db][:, tti * 256:(tti + 1) * 256],
                            x8[db][:, tti * 256:(tti + 1) * 256], po, OP.add),
                            r=xtags + [(ps8[4 + db // 2], None)], w=xtags)

                for tti in range(n2):
                    for p in range(PB):
                        mix_tile(mx2["xr"], xn3, TMR, OTMR, p, tti)
                    for db in range(PB):
                        pr = ps8[db // 2][:, (db % 2) * 256:(db % 2) * 256 + 256]
                        for cb in range(PB):
                            pe(lambda t, o=pr,
                               l=W0[:, cb * D + db * 128: cb * D + (db + 1) * 128],
                               r_=mx2["xr"][:, cb * 256:(cb + 1) * 256],
                               st=(cb == 0 and db % 2 == 0), sp=(cb == PB - 1):
                               t.matmul(o, l, r_, start=st, stop=False,
                                        skip_group_check=True),
                               r=[(W0, ("ld", 0)), (mx2["xr"], ("p", cb))],
                               w=[(ps8[db // 2], None)], inc=(cb == PB - 1))
                    for p in range(PB):
                        rps = ps8[p // 2][:, (p % 2) * 256:(p % 2) * 256 + 256]
                        acte(lambda s, rps=rps, p=p: s.activation(
                            erb[:, p * 256:(p + 1) * 256], rps, AF.Exp, scale=-1.0),
                            r=[(ps8[p // 2], None)], w=[(erb, ("p", p))])
                    alle = [(erb, ("p", p)) for p in range(PB)]
                    dve(lambda v: v.tensor_scalar(erb[:, :], erb[:, :], 1.0, 0.0,
                                                  OP.add, OP.add),
                        r=alle, w=alle)
                    acte(lambda s: s.activation(erb[:, :], erb[:, :], AF.Ln),
                         r=alle, w=alle)
                    acte(lambda s: s.activation(erb[:, :], erb[:, :], AF.Exp,
                                                scale=-1.0),
                         r=alle, w=alle)
                    dve(lambda v, tti=tti, ytb=yt[tti % 2]: v.tensor_tensor(
                        v3e(ytb[:, :]),
                        wkva[:, :].rearrange("p (b n) -> p b n", b=PB)[
                            :, :, tti * 256:(tti + 1) * 256],
                        v3e(erb[:, :]), OP.mult),
                        r=[(wkva, ("t", tti))] + alle,
                        w=[(yt[tti % 2], ("p", p)) for p in range(PB)])
                    if tti > 0:
                        outproj(tti - 1)
                outproj(n2 - 1)
                tk.barrier()
                tk.drop(list(mx2.values()) + [erb] + yt)
            tk.barrier()
            tk.drop([xn3, wkva])

    def tap(name, b):
        if taps:
            for p in range(PB):
                sq(lambda e, p=p: e.dma_start(dtap[name][b, p], x8[p][:, :]),
                   r=[(x8[p], None)])

    for b in range(NB):
        if b > 0:
            for p in range(PB):
                sq(lambda e, p=p, b=b: e.dma_start(x8[p][:, :], dx[b, p]),
                   w=[(x8[p], None)])
        attn_a(b)
        tap("attna", b)
        ffn(b, 0)
        tap("ffn1", b)
        attn_b(b)
        tap("attnb", b)
        ffn(b, 1)
        tap("ffn2", b)
        for p in range(PB):
            sq(lambda e, p=p, b=b: e.dma_start(dout[b, p], x8[p][:, :]),
               r=[(x8[p], None)])

    for i in range(tk.NDMA):
        for q, strm in (("sq", "sp"), ("gq", "pool")):
            n = tk.cnt[f"{q}{i}"]
            if n:
                tk.ops[strm].append(
                    lambda e, s=tk.sem[f"{q}{i}"], n=n: e.wait_ge(s, n))

    with es:
        with nc.Block() as block:
            @block.tensor
            def _(t):
                for f in tk.ops["pe"]:
                    f(t)

            @block.scalar
            def _(s):
                for f in tk.ops["act"]:
                    f(s)

            @block.vector
            def _(v):
                for f in tk.ops["dve"]:
                    f(v)

            @block.gpsimd
            def _(g):
                for f in tk.ops["pool"]:
                    f(g)

            @block.sync
            def _(s):
                for f in tk.ops["sp"]:
                    f(s)
    return nc


def _pack_inmap(T=2048):
    raise NotImplementedError


def _pk_proj(w):
    return np.ascontiguousarray(
        w.reshape(8, 128, w.shape[1]).transpose(1, 0, 2).reshape(128, -1))


def host_pack(inp, T):
    import ml_dtypes
    f32 = lambda a: np.asarray(a, dtype=np.float32)
    wq = _pk_proj(f32(inp["L0_attn_a_Wq"]))
    wk = _pk_proj(f32(inp["L0_attn_a_Wk"]))
    wv = _pk_proj(f32(inp["L0_attn_a_Wv"]))
    wo = _pk_proj(f32(inp["L0_attn_a_Wo"]))
    wkey = _pk_proj(f32(inp["L0_attn_b_W_key"]).T)
    wval = _pk_proj(f32(inp["L0_attn_b_W_value"]).T)
    wrec = _pk_proj(f32(inp["L0_attn_b_W_receptance"]).T)
    wout = _pk_proj(f32(inp["L0_attn_b_W_output"]).T)

    w1 = f32(inp["L0_ffn_w1"])
    w2 = f32(inp["L0_ffn_w2"])
    w3 = f32(inp["L0_ffn_w3"])
    ws1 = f32(inp["L0_ffn_ws1"])
    ws2 = f32(inp["L0_ffn_ws2"])
    ws3 = f32(inp["L0_ffn_ws3"])
    w13 = np.zeros((NEX, 128, PB * D), np.float32)
    w2p = np.zeros((NEX, 128, 4 * D), np.float32)
    for e in range(NE):
        w13[e] = _pk_proj(np.concatenate([w1[e], w3[e]], axis=1))
        w2p[e] = np.ascontiguousarray(
            w2[e].reshape(4, 128, D).transpose(1, 0, 2).reshape(128, 4 * D))
    for j in range(2):
        w13[NE + j] = _pk_proj(np.concatenate(
            [ws1[:, j * 512:(j + 1) * 512], ws3[:, j * 512:(j + 1) * 512]], axis=1))
        w2p[NE + j] = np.ascontiguousarray(
            ws2[j * 512:(j + 1) * 512].reshape(4, 128, D).transpose(1, 0, 2)
            .reshape(128, 4 * D))
    gw = _pk_proj(f32(inp["L0_ffn_gate_w"]).T)

    fmv = lambda a: np.ascontiguousarray(f32(a).reshape(8, 128).T)
    pvec = np.zeros((128, 80), np.float32)
    pvec[:, 0:8] = fmv(inp["L0_attn_a_bq"])
    pvec[:, 8:16] = fmv(inp["L0_attn_a_bk"])
    pvec[:, 16:24] = fmv(inp["L0_attn_a_bv"])
    pvec[:, 24:32] = fmv(inp["L0_attn_a_bo"])
    pvec[:, 32:40] = fmv(inp["L0_attn_b_time_first"])
    pvec[:, 40:48] = fmv(inp["L0_attn_b_time_decay"])
    pvec[:, 48:56] = fmv(np.asarray(inp["L0_attn_b_time_mix_k"]).reshape(-1))
    pvec[:, 56:64] = fmv(np.asarray(inp["L0_attn_b_time_mix_v"]).reshape(-1))
    pvec[:, 64:72] = fmv(np.asarray(inp["L0_attn_b_time_mix_r"]).reshape(-1))
    pvec[0:16, 72] = f32(inp["L0_attn_a_m"])
    brow = np.concatenate([f32(inp["L0_attn_a_bk"]),
                           f32(inp["L0_attn_a_bv"])])[None, :]

    bf = ml_dtypes.bfloat16
    ones_k1 = np.ones((1, 128), np.float32)
    onescol_b = np.ones((128, 1), bf)
    bd = np.zeros((128, PB * HEAD), bf)
    selh = np.zeros((16, PB * 128), bf)
    for p in range(PB):
        bd[0:64, p * 16 + 2 * p] = 1.0
        bd[64:128, p * 16 + 2 * p + 1] = 1.0
        selh[2 * p, p * 128: p * 128 + 64] = 1.0
        selh[2 * p + 1, p * 128 + 64: p * 128 + 128] = 1.0
    sel_e = np.zeros((16, NE * 128), bf)
    for e in range(NE):
        sel_e[e, e * 128:(e + 1) * 128] = 1.0
    ident_b = np.eye(128, dtype=bf)

    cvals = np.zeros((128, 8), np.float32)
    for i, v in enumerate([1e-5, -0.5, -1.0, -float(np.log(T))]):
        cvals[:, i] = v

    return dict(wq=wq, wk=wk, wv=wv, wo=wo, wkey=wkey, wval=wval, wrec=wrec,
                wout=wout, w13=w13, w2=w2p, gw=gw, pvec=pvec, brow=brow,
                ones_k1=ones_k1, onescol_b=onescol_b, bd=bd, selh=selh,
                sel_e=sel_e, ident_b=ident_b, cvals=cvals)


def pack_x(xbatches, T):
    return np.stack([np.ascontiguousarray(b.T).reshape(8, 128, T)
                     for b in xbatches])


def unpack_out(o, T):
    return np.stack([o[b].reshape(D, T).T for b in range(o.shape[0])])


def kernel(**inp):
    x = np.asarray(inp["L0_attn_a_x"], dtype=np.float32)
    T = x.shape[1]
    shared = host_pack(inp, T)
    NBc = x.shape[0] // 8
    in_maps = []
    for c in range(8):
        m = dict(shared)
        m["x"] = pack_x([x[c * NBc + b] for b in range(NBc)], T)
        in_maps.append(m)

    nc = build_nc(T=T, NB=NBc)
    res = run_bass_kernel_spmd(nc, in_maps, core_ids=list(range(8)))
    out = np.empty((8 * NBc, T, D), np.float32)
    for c in range(8):
        out[c * NBc:(c + 1) * NBc] = unpack_out(res.results[c]["out"], T)
    return out


# version 06
# speedup vs baseline: 1.1113x; 1.0264x over previous
import numpy as np
from contextlib import ExitStack

import concourse.bass as bass
import concourse.mybir as mybir
from concourse.bass_utils import run_bass_kernel_spmd

F32 = mybir.dt.float32
BF16 = mybir.dt.bfloat16
AF = mybir.ActivationFunctionType
OP = mybir.AluOpType
AX = mybir.AxisListType

D = 1024
PB = 8
NE = 16
NEX = 18
HEAD = 16
HD = 64
FFN_CHUNKS = ((0, 1, 2, 3), (4, 5, 6, 7), (8, 9, 10, 11), (12, 13, 14, 15),
              (16, 17))


class _Rec:
    __slots__ = ("src", "cnt")

    def __init__(self, src, cnt):
        self.src = src
        self.cnt = cnt


class TK:
    STREAMS = ("pe", "act", "dve", "pool", "sp")
    ENGS = ("pe", "act", "dve", "pool")
    NDMA = 8

    def __init__(self, nc, es):
        self.nc = nc
        self.psum_ids = set()
        self.ops = {s: [] for s in self.STREAMS}
        self.sem = {s: es.enter_context(nc.semaphore("sem_" + s)) for s in self.ENGS}
        for q in ("gq", "sq"):
            for i in range(self.NDMA):
                self.sem[f"{q}{i}"] = es.enter_context(nc.semaphore(f"sem_{q}{i}"))
        self.cnt = {s: 0 for s in self.sem}
        self.rr = {"gq": 0, "sq": 0}
        self.waited = {c: {s: 0 for s in self.sem} for c in self.STREAMS}
        self.floor = {c: {} for c in self.STREAMS}
        self.bufs = {}
        self.pe_pending = []

    def _deps(self, h, tag, readers):
        recs = self.bufs.get(id(h))
        if not recs:
            return ()
        out = []
        for t2, r in recs.items():
            if tag is None or t2 is None or t2 == tag:
                if r["w"] is not None:
                    out.append(r["w"])
                if readers:
                    out.extend(r["r"].values())
        return out

    def emit(self, stream, fn, reads=(), writes=(), inc=True, dma=None):
        need = dict(self.floor[stream])
        self.floor[stream] = {}
        for h, tag in reads:
            psum_rr = id(h) in self.psum_ids
            for r in self._deps(h, tag, psum_rr):
                if r.src == "pe" and stream == "pe":
                    continue
                if psum_rr and r.src == stream:
                    continue
                assert r.cnt is not None, "dep on open PE group"
                if r.cnt > need.get(r.src, 0):
                    need[r.src] = r.cnt
        for h, tag in writes:
            for r in self._deps(h, tag, True):
                if r.src == "pe" and stream == "pe":
                    continue
                assert r.cnt is not None, "dep on open PE group"
                if r.cnt > need.get(r.src, 0):
                    need[r.src] = r.cnt
        sname = None
        if dma is not None:
            sname = f"{dma}{self.rr[dma]}"
            self.rr[dma] = (self.rr[dma] + 1) % self.NDMA
            if self.cnt[sname] > need.get(sname, 0):
                need[sname] = self.cnt[sname]
        waits = []
        wt = self.waited[stream]
        for src in sorted(need):
            if need[src] > wt[src]:
                waits.append((self.sem[src], need[src]))
                wt[src] = need[src]

        if dma is not None:
            self.cnt[sname] += 16
            rec = _Rec(sname, self.cnt[sname])
            inc_sem, inc_n = self.sem[sname], 16
        elif inc:
            self.cnt[stream] += 1
            rec = _Rec(stream, self.cnt[stream])
            inc_sem, inc_n = self.sem[stream], 1
            if stream == "pe":
                for p in self.pe_pending:
                    p.cnt = rec.cnt
                self.pe_pending = []
        else:
            assert stream == "pe"
            rec = _Rec("pe", None)
            self.pe_pending.append(rec)
            inc_sem, inc_n = None, 0

        def closure(eng, _w=tuple(waits), _f=fn, _s=inc_sem, _n=inc_n):
            for sem, cnt in _w:
                eng.wait_ge(sem, cnt)
            inst = _f(eng)
            if _s is not None:
                inst.then_inc(_s, _n)

        self.ops[stream].append(closure)

        for h, tag in writes:
            self.bufs.setdefault(id(h), {})[tag] = {"w": rec, "r": {}}
        for h, tag in reads:
            recs = self.bufs.setdefault(id(h), {})
            recs.setdefault(tag, {"w": None, "r": {}})["r"][rec.src] = rec
        return rec

    def barrier(self):
        for c in ("pe", "act", "dve", "sp"):
            f = self.floor[c]
            for s in ("pe", "act", "dve"):
                f[s] = max(f.get(s, 0), self.cnt[s])

    def fence(self, stream):
        f = self.floor[stream]
        for s in ("pe", "act", "dve"):
            f[s] = max(f.get(s, 0), self.cnt[s])

    def drop(self, handles):
        for h in handles:
            self.bufs.pop(id(h), None)


def build_nc(T=2048, NB=2, taps=False):
    nc = bass.Bass("TRN2", target_bir_lowering=False)
    n5 = T // 512
    n2 = T // 256
    LOGT = float(np.log(T))

    dx = nc.dram_tensor("x", [NB, PB, 128, T], F32, kind="ExternalInput").ap()
    dout = nc.dram_tensor("out", [NB, PB, 128, T], F32, kind="ExternalOutput").ap()
    dW = {}
    for nm in ("wq", "wk", "wv", "wo", "wkey", "wval", "wrec", "wout"):
        dW[nm] = nc.dram_tensor(nm, [128, PB * D], F32, kind="ExternalInput").ap()
    dw13 = nc.dram_tensor("w13", [NEX, 128, PB * D], F32, kind="ExternalInput").ap()
    dw2 = nc.dram_tensor("w2", [NEX, 128, 4 * D], F32, kind="ExternalInput").ap()
    dgw = nc.dram_tensor("gw", [128, PB * NE], F32, kind="ExternalInput").ap()
    dpv = nc.dram_tensor("pvec", [128, 80], F32, kind="ExternalInput").ap()
    dbrow = nc.dram_tensor("brow", [1, 2 * D], F32, kind="ExternalInput").ap()
    dones = nc.dram_tensor("ones_k1", [1, 128], F32, kind="ExternalInput").ap()
    donesb = nc.dram_tensor("onescol_b", [128, 1], BF16, kind="ExternalInput").ap()
    dbd = nc.dram_tensor("bd", [128, PB * HEAD], BF16, kind="ExternalInput").ap()
    dselh = nc.dram_tensor("selh", [16, PB * 128], BF16, kind="ExternalInput").ap()
    dsele = nc.dram_tensor("sel_e", [16, NE * 128], BF16, kind="ExternalInput").ap()
    dident = nc.dram_tensor("ident_b", [128, 128], BF16, kind="ExternalInput").ap()
    dcv = nc.dram_tensor("cvals", [128, 8], F32, kind="ExternalInput").ap()
    dtap = {}
    if taps:
        for nm in ("xn1", "attna", "ffn1", "attnb", "ffn2", "wkv"):
            dtap[nm] = nc.dram_tensor("tap_" + nm, [NB, PB, 128, T], F32,
                                      kind="ExternalOutput").ap()

    es = ExitStack()
    _uid = [0]

    def _nm(name):
        _uid[0] += 1
        return f"{name}_{_uid[0]}"

    sb = lambda name, shape, dt=F32: es.enter_context(nc.sbuf_tensor(name, shape, dt))
    tk = TK(nc, es)

    x8 = [sb(f"x{i}", [128, T]) for i in range(PB)]
    c_ones = sb("c_ones", [1, 128])
    c_onesb = sb("c_onesb", [128, 1], BF16)
    pv = sb("pv", [128, 80])
    dpar = sb("dpar", [128, 40])
    sgate = sb("sgate", [16, 1])
    gwS = sb("gwS", [128, PB * NE], BF16)
    msb = sb("msb", [128, 512], BF16)
    W0 = sb("W0", [128, PB * D], BF16)
    W1 = sb("W1", [128, PB * D], BF16)
    cvals = sb("cvals_s", [128, 8])
    ps8 = [es.enter_context(nc.psum_tensor(f"ps{i}", [128, 512], F32))
           for i in range(8)]
    tk.psum_ids = {id(p) for p in ps8}

    BQ, BK, BV, BO, UU, TD, TMK, TMV, TMR, MCOL = 0, 8, 16, 24, 32, 40, 48, 56, 64, 72
    OTMK, OTMV, OTMR, EW = 0, 8, 16, 24

    pe = lambda fn, r=(), w=(), inc=True: tk.emit("pe", fn, r, w, inc)
    acte = lambda fn, r=(), w=(): tk.emit("act", fn, r, w)
    dve = lambda fn, r=(), w=(): tk.emit("dve", fn, r, w)
    gq = lambda fn, r=(), w=(): tk.emit("pool", fn, r, w, dma="gq")
    sq = lambda fn, r=(), w=(): tk.emit("sp", fn, r, w, dma="sq")

    sw13 = nc.dram_tensor("sw13", [NEX, 128, PB * D], BF16)
    sw2 = nc.dram_tensor("sw2", [NEX, 128, 4 * D], BF16)
    sproj = nc.dram_tensor("sproj", [8, 128, PB * D], BF16)
    _PROJ_IDX = {"wq": 0, "wk": 1, "wv": 2, "wo": 3, "wkey": 4, "wval": 5,
                 "wrec": 6, "wout": 7}
    _staged = {}

    def ldw(dst, src_ap, r0=0, r1=None):
        n = dst.shape[1] if r1 is None else (r1 - r0)
        gq(lambda g, d=dst[:, r0:r0 + n], s=src_ap[:, r0:r0 + n]: g.dma_start(d, s),
           w=[(dst, ("ld", r0))])

    def ldst(dst, key):
        if key[0] == "w13":
            slot, fsrc = sw13.ap()[key[1]], dw13[key[1]]
        elif key[0] == "w2":
            slot, fsrc = sw2.ap()[key[1]], dw2[key[1]]
        else:
            slot, fsrc = sproj.ap()[_PROJ_IDX[key[1]]], dW[key[1]]
        if key not in _staged:
            _staged[key] = True
            gq(lambda g, d=dst[:, :], s=fsrc: g.dma_start(d, s),
               w=[(dst, ("ld", 0))])
            sq(lambda e, d=slot, s=dst[:, :]: e.dma_start(d, s),
               r=[(dst, ("ld", 0))], w=[(sw13, key)])
        else:
            sq(lambda e, d=dst[:, :], s=slot: e.dma_start(d, s),
               r=[(sw13, key)], w=[(dst, ("ld", 0))])

    CV = [1e-5, -0.5, -1.0, -LOGT]
    for i, v in enumerate(CV):
        nc.const_aps.register(F32, v, cvals[:, i:i + 1])
    sq(lambda e: e.dma_start(cvals[:, :], dcv), w=[(cvals, None)])
    sq(lambda e: e.dma_start(c_ones[:, :], dones), w=[(c_ones, None)])
    sq(lambda e: e.dma_start(c_onesb[:, :], donesb), w=[(c_onesb, None)])
    sq(lambda e: e.dma_start(pv[:, :], dpv), w=[(pv, None)])
    gq(lambda e: e.dma_start(gwS[:, :], dgw), w=[(gwS, None)])
    for p in range(PB):
        sq(lambda e, p=p: e.dma_start(x8[p][:, :], dx[0, p]), w=[(x8[p], None)])

    for c in ("pe", "act", "dve"):
        for i in range(tk.NDMA):
            for q in ("sq", "gq"):
                if tk.cnt[f"{q}{i}"]:
                    tk.floor[c][f"{q}{i}"] = tk.cnt[f"{q}{i}"]
    for col, src in ((OTMK, TMK), (OTMV, TMV), (OTMR, TMR)):
        dve(lambda v, c=col, s=src: v.tensor_scalar(
            dpar[:, c:c + 8], pv[:, s:s + 8], -1.0, 1.0, OP.mult, OP.add),
            r=[(pv, None)], w=[(dpar, ("c", col))])
    acte(lambda s: s.activation(dpar[:, EW:EW + 8], pv[:, TD:TD + 8], AF.Exp),
         r=[(pv, None)], w=[(dpar, ("c", EW))])
    dve(lambda v: v.tensor_scalar(dpar[:, EW:EW + 8], dpar[:, EW:EW + 8],
                                  -1.0, 0.0, OP.mult, OP.bypass),
        r=[(dpar, ("c", EW))], w=[(dpar, ("c", EW))])
    acte(lambda s: s.activation(dpar[:, EW:EW + 8], dpar[:, EW:EW + 8], AF.Exp),
         r=[(dpar, ("c", EW))], w=[(dpar, ("c", EW))])
    acte(lambda s: s.activation(dpar[:, 32:40], pv[:, UU:UU + 8], AF.Exp),
         r=[(pv, None)], w=[(dpar, ("c", 32))])
    acte(lambda s: s.activation(sgate[:, :], pv[0:16, MCOL:MCOL + 1], AF.Sigmoid),
         r=[(pv, None)], w=[(sgate, None)])
    acte(lambda s: s.activation(sgate[:, :], sgate[:, :], AF.Exp, scale=-LOGT),
         r=[(sgate, None)], w=[(sgate, None)])

    def ln_tile(xsrc, tt, dst_fn, scratch, tag):
        t0 = tt * 512
        xc, xq, stats = scratch["xc"], scratch["xq"], scratch["stats"]
        for hh in range(4):
            for p in range(PB):
                s0 = t0 + hh * 128
                acte(lambda s, p=p, s0=s0: s.activation(
                    xc[:, p * 128:(p + 1) * 128], xsrc[p][:, s0:s0 + 128], AF.Identity),
                    r=[(xsrc[p], ("c", s0 // 128))], w=[(xc, ("p", p))])
                acte(lambda s, p=p, s0=s0: s.activation(
                    xq[:, p * 128:(p + 1) * 128], xsrc[p][:, s0:s0 + 128], AF.Square),
                    r=[(xsrc[p], ("c", s0 // 128))], w=[(xq, ("p", p))])
            for p in range(PB):
                pe(lambda t, p=p, hh=hh: t.matmul(
                    ps8[0][0:1, hh * 128:(hh + 1) * 128], c_onesb[:, :],
                    xc[:, p * 128:(p + 1) * 128], start=(p == 0), stop=(p == PB - 1)),
                   r=[(c_onesb, None), (xc, ("p", p))], w=[(ps8[0], None)],
                   inc=(p == PB - 1))
                pe(lambda t, p=p, hh=hh: t.matmul(
                    ps8[1][0:1, hh * 128:(hh + 1) * 128], c_onesb[:, :],
                    xq[:, p * 128:(p + 1) * 128], start=(p == 0), stop=(p == PB - 1)),
                   r=[(c_onesb, None), (xq, ("p", p))], w=[(ps8[1], None)],
                   inc=(p == PB - 1))
        dve(lambda v: v.tensor_copy(stats[:, 0:512], ps8[0][0:1, :]),
            r=[(ps8[0], None)], w=[(stats, 0)])
        dve(lambda v: v.tensor_copy(stats[:, 512:1024], ps8[1][0:1, :]),
            r=[(ps8[1], None)], w=[(stats, 1)])
        pe(lambda t: t.matmul(ps8[2][:, :], c_ones[:, :], stats[:, 0:512],
                              start=True, stop=True),
           r=[(c_ones, None), (stats, 0)], w=[(ps8[2], None)])
        pe(lambda t: t.matmul(ps8[3][:, :], c_ones[:, :], stats[:, 512:1024],
                              start=True, stop=True),
           r=[(c_ones, None), (stats, 1)], w=[(ps8[3], None)])
        mean, ex2, var = scratch["mean"], scratch["ex2"], scratch["var"]
        istd = var
        dve(lambda v: v.tensor_scalar(mean[:, :], ps8[2][:, :], 1.0 / D, 0.0,
                                      OP.mult, OP.add),
            r=[(ps8[2], None)], w=[(mean, None)])
        dve(lambda v: v.tensor_scalar(ex2[:, :], ps8[3][:, :], 1.0 / D, 0.0,
                                      OP.mult, OP.add),
            r=[(ps8[3], None)], w=[(ex2, None)])
        dve(lambda v: v.tensor_tensor(var[:, :], mean[:, :], mean[:, :], OP.mult),
            r=[(mean, None)], w=[(var, None)])
        dve(lambda v: v.tensor_tensor(var[:, :], ex2[:, :], var[:, :], OP.subtract),
            r=[(ex2, None), (var, None)], w=[(var, None)])
        acte(lambda s: s.activation(var[:, :], var[:, :], AF.Ln, bias=1e-5),
             r=[(var, None)], w=[(var, None)])
        acte(lambda s: s.activation(istd[:, :], var[:, :], AF.Exp, scale=-0.5),
             r=[(var, None)], w=[(istd, None)])
        dlt = scratch["dlt"]
        for p in range(PB):
            dap, dh, dtag = dst_fn(p)
            dve(lambda v, p=p: v.tensor_tensor(dlt[:, :], xsrc[p][:, t0:t0 + 512],
                                               mean[:, :], OP.subtract),
                r=[(xsrc[p], ("c", tt * 4 + j)) for j in range(4)] + [(mean, None)],
                w=[(dlt, None)])
            dve(lambda v, a=dap: v.tensor_tensor(a, dlt[:, :], istd[:, :], OP.mult),
                r=[(dlt, None), (istd, None)], w=[(dh, dtag)])

    def mk_ln_scratch(sbp):
        return dict(xc=sbp("ln_xc", [128, PB * 128], BF16),
                    xq=sbp("ln_xq", [128, PB * 128], BF16),
                    stats=sbp("ln_stats", [1, 1024]),
                    mean=sbp("ln_mean", [128, 512]), ex2=sbp("ln_ex2", [128, 512]),
                    var=sbp("ln_var", [128, 512]),
                    dlt=sbp("ln_dlt", [128, 512]))

    def attn_a(b):
        with ExitStack() as ph:
            sbp = lambda n, s, dt=F32: ph.enter_context(
                nc.sbuf_tensor(_nm(n), s, dt))
            lns = mk_ln_scratch(sbp)
            xn1 = sbp("xn1", [128, PB * 512], BF16)
            btm = sbp("btm", [128, 2 * D])
            browS = sbp("browS", [1, 2 * D])
            c_selh = sbp("c_selh", [16, PB * 128], BF16)
            c_bd = sbp("c_bd", [128, PB * HEAD], BF16)
            sq(lambda e: e.dma_start(browS[:, :], dbrow), w=[(browS, None)])
            sq(lambda e: e.dma_start(c_selh[:, :], dselh), w=[(c_selh, None)])
            sq(lambda e: e.dma_start(c_bd[:, :], dbd), w=[(c_bd, None)])
            for j in range(4):
                pe(lambda t, j=j: t.matmul(ps8[j][:, :], c_ones[:, :],
                                           browS[:, 512 * j:512 * (j + 1)],
                                           start=True, stop=True),
                   r=[(c_ones, None), (browS, None)], w=[(ps8[j], None)])
                dve(lambda v, j=j: v.tensor_copy(btm[:, 512 * j:512 * (j + 1)],
                                                 ps8[j][:, :]),
                    r=[(ps8[j], None)], w=[(btm, ("c", j))])
            knt = [sbp(f"knt{i}", [128, D], BF16) for i in range(2)]
            vt = [sbp(f"vt{i}", [128, D], BF16) for i in range(2)]
            kb = sbp("kb", [128, D])
            ksq = sbp("ksq", [128, D], BF16)
            ksc = sbp("ksc", [128, 16])
            ldst(W0, ("p", "wk"))
            ldst(W1, ("p", "wv"))
            for tt in range(n5):
                ln_tile(x8, tt, lambda p: (xn1[:, p * 512:(p + 1) * 512], xn1,
                                           ("p", p)), lns, "ln1")
                for tbl in range(4):
                    tb = tt * 4 + tbl
                    kn, vv = knt[tb % 2], vt[tb % 2]
                    for half in range(2):
                        pk, pv_ = ps8[half], ps8[2 + half]
                        for cb in range(PB):
                            lh = xn1[:, cb * 512 + tbl * 128: cb * 512 + (tbl + 1) * 128]
                            pe(lambda t, o=pk[:, :], l=lh,
                               r_=W0[:, cb * D + half * 512: cb * D + (half + 1) * 512],
                               st=(cb == 0), sp=(cb == PB - 1): t.matmul(
                                   o, l, r_, start=st, stop=sp),
                               r=[(xn1, ("p", cb)), (W0, ("ld", 0))],
                               w=[(pk, None)], inc=(cb == PB - 1))
                            pe(lambda t, o=pv_[:, :], l=lh,
                               r_=W1[:, cb * D + half * 512: cb * D + (half + 1) * 512],
                               st=(cb == 0), sp=(cb == PB - 1): t.matmul(
                                   o, l, r_, start=st, stop=sp),
                               r=[(xn1, ("p", cb)), (W1, ("ld", 0))],
                               w=[(pv_, None)], inc=(cb == PB - 1))
                        dve(lambda v, h=half: v.tensor_tensor(
                            kb[:, h * 512:(h + 1) * 512], ps8[h][:, :],
                            btm[:, h * 512:(h + 1) * 512], OP.add),
                            r=[(ps8[half], None), (btm, ("c", half))],
                            w=[(kb, ("h", half))])
                        acte(lambda s, h=half: s.activation(
                            ksq[:, h * 512:(h + 1) * 512], kb[:, h * 512:(h + 1) * 512],
                            AF.Square),
                            r=[(kb, ("h", half))], w=[(ksq, ("h", half))])
                        dve(lambda v, h=half, vv=vv: v.tensor_tensor(
                            vv[:, h * 512:(h + 1) * 512], ps8[2 + h][:, :],
                            btm[:, D + h * 512: D + (h + 1) * 512], OP.add),
                            r=[(ps8[2 + half], None), (btm, ("c", 2 + half))],
                            w=[(vv, ("h", half))])
                    dve(lambda v: v.reduce_sum(
                        ksc[:, :], ksq[:, :].rearrange("p (h d) -> p h d", d=HD),
                        axis=AX.X),
                        r=[(ksq, ("h", 0)), (ksq, ("h", 1))], w=[(ksc, None)])
                    dve(lambda v: v.tensor_scalar(ksc[:, :], ksc[:, :], 1e-24, 0.0,
                                                  OP.max, OP.add),
                        r=[(ksc, None)], w=[(ksc, None)])
                    acte(lambda s: s.activation(ksc[:, :], ksc[:, :], AF.Ln),
                         r=[(ksc, None)], w=[(ksc, None)])
                    acte(lambda s: s.activation(ksc[:, :], ksc[:, :], AF.Exp,
                                                scale=-0.5),
                         r=[(ksc, None)], w=[(ksc, None)])
                    for half in range(2):
                        dve(lambda v, h=half, kn=kn: v.tensor_tensor(
                            kn[:, h * 512:(h + 1) * 512].rearrange(
                                "p (h d) -> p h d", d=HD),
                            kb[:, h * 512:(h + 1) * 512].rearrange(
                                "p (h d) -> p h d", d=HD),
                            ksc[:, h * 8:(h + 1) * 8].unsqueeze(2).broadcast_to(
                                (128, 8, HD)),
                            OP.mult),
                            r=[(kb, ("h", half)), (ksc, None)], w=[(kn, ("h", half))])
                    for h in range(HEAD):
                        r0, c0 = 64 * (h % 2), 64 * (h // 2)
                        pe(lambda t, kn=kn, vv=vv, h=h, r0=r0, c0=c0,
                           st=(tb == 0 and h < 2),
                           sp=(tb == 4 * n5 - 1 and h == HEAD - 1):
                           t.matmul(ps8[7][r0:r0 + 64, c0:c0 + 64],
                                    kn[:, h * 64:(h + 1) * 64],
                                    vv[:, h * 64:(h + 1) * 64],
                                    start=st, stop=sp, skip_group_check=True),
                           r=[(kn, ("h", 0)), (kn, ("h", 1)), (vv, ("h", 0)),
                              (vv, ("h", 1))],
                           w=[(ps8[7], None)], inc=(h == HEAD - 1))
            dve(lambda v: v.tensor_copy(msb[:, :], ps8[7][:, :]),
                r=[(ps8[7], None)], w=[(msb, None)])
            ldst(W0, ("p", "wq"))
            ldst(W1, ("p", "wo"))
            qb = sbp("qb", [128, PB * 512], BF16)
            qsq = sbp("qsq", [128, PB * 512], BF16)
            qn = sbp("qn", [128, PB * 512], BF16)
            ao = sbp("ao", [128, PB * 512], BF16)
            qs = sbp("qs", [16, 512])
            qsb = sbp("qsb", [16, 512], BF16)
            for tt in range(n5):
                ln_tile(x8, tt, lambda p: (xn1[:, p * 512:(p + 1) * 512], xn1,
                                           ("p", p)), lns, "ln1c")
                for db in range(PB):
                    pq = ps8[db % 2]
                    for cb in range(PB):
                        pe(lambda t, o=pq[:, :],
                           l=W0[:, cb * D + db * 128: cb * D + (db + 1) * 128],
                           r_=xn1[:, cb * 512:(cb + 1) * 512],
                           st=(cb == 0), sp=(cb == PB - 1): t.matmul(
                               o, l, r_, start=st, stop=sp),
                           r=[(W0, ("ld", 0)), (xn1, ("p", cb))],
                           w=[(pq, None)], inc=(cb == PB - 1))
                    dve(lambda v, db=db, pq=pq: v.tensor_scalar(
                        qb[:, db * 512:(db + 1) * 512], pq[:, :],
                        pv[:, BQ + db:BQ + db + 1], 0.0, OP.add, OP.add),
                        r=[(pq, None), (pv, None)], w=[(qb, ("p", db))])
                    acte(lambda s, db=db, pq=pq: s.activation(
                        qsq[:, db * 512:(db + 1) * 512], pq[:, :], AF.Square,
                        bias=pv[:, BQ + db:BQ + db + 1]),
                        r=[(pq, None), (pv, None)], w=[(qsq, ("p", db))])
                for p in range(PB):
                    pe(lambda t, p=p: t.matmul(
                        ps8[2][0:16, :], c_bd[:, p * 16:(p + 1) * 16],
                        qsq[:, p * 512:(p + 1) * 512],
                        start=(p == 0), stop=(p == PB - 1)),
                       r=[(c_bd, None), (qsq, ("p", p))], w=[(ps8[2], None)],
                       inc=(p == PB - 1))
                dve(lambda v: v.tensor_scalar(qs[:, :], ps8[2][0:16, :], 1e-24, 0.0,
                                              OP.max, OP.add),
                    r=[(ps8[2], None)], w=[(qs, None)])
                acte(lambda s: s.activation(qs[:, :], qs[:, :], AF.Ln),
                     r=[(qs, None)], w=[(qs, None)])
                acte(lambda s: s.activation(qs[:, :], qs[:, :], AF.Exp, scale=-0.5),
                     r=[(qs, None)], w=[(qs, None)])
                dve(lambda v: v.tensor_scalar(qs[:, :], qs[:, :], sgate[:, 0:1], 0.0,
                                              OP.mult, OP.add),
                    r=[(qs, None), (sgate, None)], w=[(qs, None)])
                dve(lambda v: v.tensor_copy(qsb[:, :], qs[:, :]),
                    r=[(qs, None)], w=[(qsb, None)])
                for p in range(PB):
                    pbc = ps8[3 + p % 2]
                    pe(lambda t, p=p, o=pbc[:, :]: t.matmul(
                        o, c_selh[:, p * 128:(p + 1) * 128], qsb[:, :],
                        start=True, stop=True),
                       r=[(c_selh, None), (qsb, None)], w=[(pbc, None)])
                    dve(lambda v, p=p, pbc=pbc: v.tensor_tensor(
                        qn[:, p * 512:(p + 1) * 512], qb[:, p * 512:(p + 1) * 512],
                        pbc[:, :], OP.mult),
                        r=[(qb, ("p", p)), (pbc, None)], w=[(qn, ("p", p))])
                    py = ps8[5 + p % 2]
                    for hh in range(2):
                        h = 2 * p + hh
                        pe(lambda t, h=h, hh=hh, p=p, py=py: t.matmul(
                            py[hh * 64:(hh + 1) * 64, :],
                            msb[64 * (h % 2):64 * (h % 2) + 64,
                                64 * (h // 2):64 * (h // 2) + 64],
                            qn[64 * hh:64 * (hh + 1), p * 512:(p + 1) * 512],
                            start=True, stop=True, skip_group_check=True),
                           r=[(msb, None), (qn, ("p", p))], w=[(py, None)],
                           inc=(hh == 1))
                    dve(lambda v, p=p, py=py: v.tensor_copy(
                        ao[:, p * 512:(p + 1) * 512], py[:, :]),
                        r=[(py, None)], w=[(ao, ("p", p))])
                for db in range(PB):
                    po = ps8[7]
                    for cb in range(PB):
                        pe(lambda t, o=po[:, :],
                           l=W1[:, cb * D + db * 128: cb * D + (db + 1) * 128],
                           r_=ao[:, cb * 512:(cb + 1) * 512],
                           st=(cb == 0), sp=(cb == PB - 1): t.matmul(
                               o, l, r_, start=st, stop=sp),
                           r=[(W1, ("ld", 0)), (ao, ("p", cb))],
                           w=[(po, None)], inc=(cb == PB - 1))
                    xtags = [(x8[db], ("c", tt * 4 + j)) for j in range(4)]
                    dve(lambda v, db=db, tt=tt, po=po: v.scalar_tensor_tensor(
                        x8[db][:, tt * 512:(tt + 1) * 512], po[:, :],
                        pv[:, BO + db:BO + db + 1],
                        x8[db][:, tt * 512:(tt + 1) * 512], OP.add, OP.add),
                        r=[(po, None), (pv, None)] + xtags,
                        w=xtags)
            tk.barrier()
            tk.drop([lns[k] for k in lns] + [xn1, knt[0], knt[1], vt[0], vt[1],
                                             kb, ksq, ksc, qb, qsq, qn, ao, qs, qsb,
                                             btm, browS, c_selh, c_bd])

    def ffn(b, inv):
        with ExitStack() as ph:
            sbp = lambda n, s, dt=F32: ph.enter_context(
                nc.sbuf_tensor(_nm(n), s, dt))
            lns = mk_ln_scratch(sbp)
            xn2 = [sbp(f"xn2_{i}", [128, PB * 512], BF16) for i in range(2)]
            asl = sbp("asl", [128, 4 * 4 * 512], BF16)
            sil = sbp("sil", [128, 4 * 512], BF16)
            h3c = sbp("h3c", [128, 4 * 512], BF16)
            call = sbp("call", [128, NE * 512], BF16)
            gL = sbp("gL", [128, 64])
            gE = sbp("gE", [128, 64])
            gS = sbp("gS", [128, 64])
            gSm = sbp("gSm", [128, 64])
            gm4 = sbp("gm4", [128, 8])
            gma = sbp("gma", [128, 16])
            gmb = sbp("gmb", [128, 16])
            gr4 = sbp("gr4", [128, 4])
            gctm = sbp("gctm", [128, 64], BF16)
            gcfm = sbp("gcfm", [16, 512], BF16)
            c_sele = sbp("c_sele", [16, NE * 128], BF16)
            c_id = sbp("c_id", [128, 128], BF16)
            sq(lambda e: e.dma_start(c_sele[:, :], dsele), w=[(c_sele, None)])
            sq(lambda e: e.dma_start(c_id[:, :], dident), w=[(c_id, None)])
            w2f = [sbp(f"w2f{i}", [128, 4 * D], BF16) for i in range(2)]
            seq = [(t_, e_) for t_ in range(n5) for e_ in range(NEX)]
            wbuf = lambda gi: (W0, W1)[gi % 2]
            ldst(wbuf(0), ("w13", seq[0][1]))
            tk.fence("pool")

            def gate(tt, xn):
                pg = ps8[4]
                for tbl in range(4):
                    for cb in range(PB):
                        pe(lambda t, o=pg[:, tbl * 16:(tbl + 1) * 16],
                           l=xn[:, cb * 512 + tbl * 128: cb * 512 + (tbl + 1) * 128],
                           r_=gwS[:, cb * 16:(cb + 1) * 16],
                           st=(cb == 0 and tbl == 0), sp=(cb == PB - 1): t.matmul(
                               o, l, r_, start=st, stop=sp, skip_group_check=True),
                           r=[(xn, ("p", cb)), (gwS, None)], w=[(pg, None)],
                           inc=(cb == PB - 1))
                v3 = lambda a: a.rearrange("p (t e) -> p t e", t=4)
                v16 = lambda a: a.rearrange("p (tg e) -> p tg e", tg=16)
                bc4 = lambda a: a.unsqueeze(2).broadcast_to((128, 4, 16))
                bc16 = lambda a: a.rearrange("p (t g) -> p t g", t=4)
                dve(lambda v: v.tensor_copy(gL[:, :], pg[:, 0:64]),
                    r=[(pg, None)], w=[(gL, None)])
                dve(lambda v: v.reduce_max(gm4[:, 0:4], v3(gL[:, :]), axis=AX.X),
                    r=[(gL, None)], w=[(gm4, 0)])
                dve(lambda v: v.tensor_tensor(v3(gE[:, :]), v3(gL[:, :]),
                                              bc4(gm4[:, 0:4]), OP.subtract),
                    r=[(gL, None), (gm4, 0)], w=[(gE, None)])
                acte(lambda s: s.activation(gE[:, :], gE[:, :], AF.Exp),
                     r=[(gE, None)], w=[(gE, None)])
                dve(lambda v: v.reduce_sum(gm4[:, 4:8], v3(gE[:, :]), axis=AX.X),
                    r=[(gE, None)], w=[(gm4, 1)])
                dve(lambda v: v.reciprocal(gr4[:, :], gm4[:, 4:8]),
                    r=[(gm4, 1)], w=[(gr4, None)])
                dve(lambda v: v.tensor_tensor(v3(gS[:, :]), v3(gE[:, :]),
                                              bc4(gr4[:, :]), OP.mult),
                    r=[(gE, None), (gr4, None)], w=[(gS, None)])
                bcg = lambda a: a.unsqueeze(2).broadcast_to((128, 4, 4))
                dve(lambda v: v.reduce_max(bc16(gma[:, :]), v16(gS[:, :]), axis=AX.X),
                    r=[(gS, None)], w=[(gma, None)])
                dve(lambda v: v.reduce_max(gr4[:, :], bc16(gma[:, :]), axis=AX.X),
                    r=[(gma, None)], w=[(gr4, None)])
                dve(lambda v: v.tensor_tensor(
                    bc16(gmb[:, :]), bc16(gma[:, :]), bcg(gr4[:, :]), OP.is_equal),
                    r=[(gma, None), (gr4, None)], w=[(gmb, None)])
                dve(lambda v: v.scalar_tensor_tensor(
                    gmb[:, :], gmb[:, :], -1e30, gma[:, :], OP.mult, OP.add),
                    r=[(gmb, None), (gma, None)], w=[(gmb, None)])
                dve(lambda v: v.reduce_max(gr4[:, :], bc16(gmb[:, :]), axis=AX.X),
                    r=[(gmb, None)], w=[(gr4, None)])
                dve(lambda v: v.tensor_tensor(
                    bc16(gmb[:, :]), bc16(gma[:, :]), bcg(gr4[:, :]), OP.is_ge),
                    r=[(gma, None), (gr4, None)], w=[(gmb, None)])
                dve(lambda v: v.tensor_tensor(
                    v16(gSm[:, :]), v16(gS[:, :]),
                    gmb[:, :].unsqueeze(2).broadcast_to((128, 16, 4)), OP.mult),
                    r=[(gS, None), (gmb, None)], w=[(gSm, None)])
                dve(lambda v: v.tensor_copy(gE[:, :], gSm[:, :]),
                    r=[(gSm, None)], w=[(gE, None)])
                for it in range(4):
                    dve(lambda v: v.reduce_max(gm4[:, 0:4], v3(gE[:, :]), axis=AX.X),
                        r=[(gE, None)], w=[(gm4, 0)])
                    if it < 3:
                        dve(lambda v: v.tensor_tensor(
                            v3(gL[:, :]), v3(gE[:, :]), bc4(gm4[:, 0:4]),
                            OP.is_equal),
                            r=[(gE, None), (gm4, 0)], w=[(gL, None)])
                        dve(lambda v: v.scalar_tensor_tensor(
                            gE[:, :], gL[:, :], -1e30, gE[:, :], OP.mult, OP.add),
                            r=[(gL, None), (gE, None)], w=[(gE, None)])
                dve(lambda v: v.tensor_tensor(v3(gL[:, :]), v3(gSm[:, :]),
                                              bc4(gm4[:, 0:4]), OP.is_ge),
                    r=[(gSm, None), (gm4, 0)], w=[(gL, None)])
                dve(lambda v: v.tensor_tensor(gctm[:, :], gS[:, :], gL[:, :],
                                              OP.mult),
                    r=[(gS, None), (gL, None)], w=[(gctm, None)])
                for tbl in range(4):
                    pe(lambda t, tbl=tbl: t.transpose(
                        ps8[5][:, :].bitcast(BF16)[0:16, tbl * 128:(tbl + 1) * 128],
                        gctm[:, tbl * 16:(tbl + 1) * 16], c_id[:, :]),
                       r=[(gctm, None), (c_id, None)], w=[(ps8[5], None)])
                    dve(lambda v, tbl=tbl: v.tensor_copy(
                        gcfm[:, tbl * 128:(tbl + 1) * 128],
                        ps8[5][:, :].bitcast(BF16)[0:16, tbl * 128:(tbl + 1) * 128]),
                        r=[(ps8[5], None)], w=[(gcfm, ("t", tbl))])

            gi = 0
            for tt in range(n5):
                xn = xn2[tt % 2]
                ln_tile(x8, tt, lambda p: (xn[:, p * 512:(p + 1) * 512], xn,
                                           ("p", p)), lns, "ln2")
                gate(tt, xn)
                for e in range(NE):
                    pc = ps8[4 + e % 4]
                    pe(lambda t, e=e, pc=pc: t.matmul(
                        pc[:, :], c_sele[:, e * 128:(e + 1) * 128], gcfm[:, :],
                        start=True, stop=True),
                       r=[(c_sele, None)] + [(gcfm, ("t", j)) for j in range(4)],
                       w=[(pc, None)])
                    acte(lambda s, e=e, pc=pc: s.activation(
                        call[:, e * 512:(e + 1) * 512], pc[:, :], AF.Identity),
                        r=[(pc, None)], w=[(call, ("e", e))])
                for chunk in FFN_CHUNKS:
                    ldst(w2f[0], ("w2", chunk[0]))
                    if len(chunk) > 1:
                        ldst(w2f[1], ("w2", chunk[1]))
                    for ei, e in enumerate(chunk):
                        wb = wbuf(gi)
                        for ib in range(4):
                            for cb in range(PB):
                                pe(lambda t, o=ps8[ib][:, :],
                                   l=wb[:, cb * D + ib * 128: cb * D + (ib + 1) * 128],
                                   r_=xn[:, cb * 512:(cb + 1) * 512],
                                   st=(cb == 0), sp=(cb == PB - 1): t.matmul(
                                       o, l, r_, start=st, stop=sp),
                                   r=[(wb, ("ld", 0)), (xn, ("p", cb))],
                                   w=[(ps8[ib], None)], inc=(cb == PB - 1))
                            for cb in range(PB):
                                pe(lambda t, o=ps8[4 + ib][:, :],
                                   l=wb[:, cb * D + 512 + ib * 128:
                                          cb * D + 512 + (ib + 1) * 128],
                                   r_=xn[:, cb * 512:(cb + 1) * 512],
                                   st=(cb == 0), sp=(cb == PB - 1): t.matmul(
                                       o, l, r_, start=st, stop=sp),
                                   r=[(wb, ("ld", 0)), (xn, ("p", cb))],
                                   w=[(ps8[4 + ib], None)], inc=(cb == PB - 1))
                        if gi + 1 < len(seq):
                            ldst(wbuf(gi + 1), ("w13", seq[gi + 1][1]))
                        gi += 1
                        for ib in range(4):
                            acte(lambda s, ib=ib: s.activation(
                                sil[:, ib * 512:(ib + 1) * 512], ps8[ib][:, :],
                                AF.Silu),
                                r=[(ps8[ib], None)], w=[(sil, ("i", ib))])
                            if e < NE:
                                dve(lambda v, ib=ib, e=e: v.tensor_tensor(
                                    h3c[:, ib * 512:(ib + 1) * 512], ps8[4 + ib][:, :],
                                    call[:, e * 512:(e + 1) * 512], OP.mult),
                                    r=[(ps8[4 + ib], None), (call, ("e", e))],
                                    w=[(h3c, ("i", ib))])
                                dve(lambda v, ib=ib, ei=ei: v.tensor_tensor(
                                    asl[:, (ei * 4 + ib) * 512:(ei * 4 + ib + 1) * 512],
                                    sil[:, ib * 512:(ib + 1) * 512],
                                    h3c[:, ib * 512:(ib + 1) * 512], OP.mult),
                                    r=[(sil, ("i", ib)), (h3c, ("i", ib))],
                                    w=[(asl, ("a", ei * 4 + ib))])
                            else:
                                dve(lambda v, ib=ib, ei=ei: v.tensor_tensor(
                                    asl[:, (ei * 4 + ib) * 512:(ei * 4 + ib + 1) * 512],
                                    sil[:, ib * 512:(ib + 1) * 512], ps8[4 + ib][:, :],
                                    OP.mult),
                                    r=[(sil, ("i", ib)), (ps8[4 + ib], None)],
                                    w=[(asl, ("a", ei * 4 + ib))])
                    for ei, e in enumerate(chunk):
                        wb2 = w2f[ei % 2]
                        for db in range(PB):
                            for ib in range(4):
                                pe(lambda t, o=ps8[db][:, :],
                                   l=wb2[:, ib * D + db * 128: ib * D + (db + 1) * 128],
                                   r_=asl[:, (ei * 4 + ib) * 512:
                                          (ei * 4 + ib + 1) * 512],
                                   st=(ei == 0 and ib == 0),
                                   sp=(ei == len(chunk) - 1 and ib == 3): t.matmul(
                                       o, l, r_, start=st, stop=sp,
                                       skip_group_check=True),
                                   r=[(wb2, ("ld", 0)), (asl, ("a", ei * 4 + ib))],
                                   w=[(ps8[db], None)],
                                   inc=(ib == 3))
                        if ei + 2 < len(chunk):
                            ldst(w2f[ei % 2], ("w2", chunk[ei + 2]))
                    for db in range(PB):
                        xtags = [(x8[db], ("c", tt * 4 + j)) for j in range(4)]
                        dve(lambda v, db=db, tt=tt: v.tensor_tensor(
                            x8[db][:, tt * 512:(tt + 1) * 512],
                            x8[db][:, tt * 512:(tt + 1) * 512], ps8[db][:, :], OP.add),
                            r=xtags + [(ps8[db], None)], w=xtags)
            tk.barrier()
            tk.drop([lns[k] for k in lns] + xn2 + w2f +
                    [asl, sil, h3c, call, gL, gE, gS, gSm, gm4, gma, gmb, gr4,
                     gctm, gcfm, c_sele, c_id])

    def attn_b(b):
        EUC = 32

        def mixes(ph_sbp, names):
            out = {}
            for nmx in names:
                out[nmx] = ph_sbp("mx_" + nmx, [128, PB * 256], BF16)
            return out

        def mix_tile(dst, src_xn, tmcol, omcol, pb, tti):
            t0 = tti * 256
            a_dst = dst[:, pb * 256: pb * 256 + 256]
            xtag = [(src_xn, ("x", pb, (t0 + j * 128) // 512)) for j in (0, 1)]
            dve(lambda v, o=a_dst,
                i=src_xn[:, pb * T + t0: pb * T + t0 + 256],
                s=pv[:, tmcol + pb: tmcol + pb + 1]: v.tensor_scalar(
                    o, i, s, 0.0, OP.mult, OP.add),
                r=xtag, w=[(dst, ("p", pb))])
            if tti == 0:
                lo = 1
            else:
                lo = 0
                xtag = xtag + [(src_xn, ("x", pb, (t0 - 1) // 512))]
            dve(lambda v, lo=lo,
                o=dst[:, pb * 256 + max(lo, 1) - (0 if lo else 0):][:, :0] if False
                else dst[:, pb * 256 + (1 if tti == 0 else 0): pb * 256 + 256],
                i0=src_xn[:, pb * T + t0 + (0 if tti else 0) - (0 if tti == 0 else 1):][:, :0] if False
                else src_xn[:, pb * T + t0 - (0 if tti == 0 else 1):
                            pb * T + t0 + 255],
                s=dpar[:, omcol + pb: omcol + pb + 1],
                i1=dst[:, pb * 256 + (1 if tti == 0 else 0): pb * 256 + 256]:
                v.scalar_tensor_tensor(o, i0, s, i1, OP.mult, OP.add),
                r=xtag + [(dpar, ("c", omcol)), (dst, ("p", pb))],
                w=[(dst, ("p", pb))])

        with ExitStack() as ph:
            sbp = lambda n, s, dt=F32: ph.enter_context(
                nc.sbuf_tensor(_nm(n), s, dt))
            xn3 = sbp("xn3", [128, PB * T], BF16)
            wkva = sbp("wkva", [128, PB * T], BF16)
            ldst(W0, ("p", "wkey"))
            ldst(W1, ("p", "wval"))
            with ExitStack() as p1:
                s1p = lambda n, s, dt=F32: p1.enter_context(
                    nc.sbuf_tensor(_nm(n), s, dt))
                with ExitStack() as lnes:
                    lnp = lambda n, s, dt=F32: lnes.enter_context(
                        nc.sbuf_tensor(_nm(n), s, dt))
                    lns = mk_ln_scratch(lnp)
                    for tt5 in range(n5):
                        ln_tile(x8, tt5,
                                lambda p, tt5=tt5: (
                                    xn3[:, p * T + tt5 * 512: p * T + (tt5 + 1) * 512],
                                    xn3, ("x", p, tt5)), lns, "ln3")
                    tk.barrier()
                    tk.drop([lns[k] for k in lns])
                mx = mixes(s1p, ("xk",))
                ekb = s1p("ekb", [128, PB * 256])
                ekvb = s1p("ekvb", [128, PB * 256])
                Pb8 = s1p("Pb8", [128, PB * 257])
                Qb8 = s1p("Qb8", [128, PB * 257])
                Pst = s1p("Pst", [128, PB])
                Qst = s1p("Qst", [128, PB])
                for p in range(PB):
                    dve(lambda v, p=p: v.memset(Pst[:, p:p + 1], 0.0),
                        r=[], w=[(Pst, ("p", p))])
                    dve(lambda v, p=p: v.memset(Qst[:, p:p + 1], 0.0),
                        r=[], w=[(Qst, ("p", p))])
                v3e = lambda a: a.rearrange("p (b n) -> p b n", b=PB)
                P3 = Pb8[:, :].rearrange("p (b n) -> p b n", b=PB)[:, :, 0:256]
                Q3 = Qb8[:, :].rearrange("p (b n) -> p b n", b=PB)[:, :, 0:256]
                for tti in range(n2):
                    for p in range(PB):
                        mix_tile(mx["xk"], xn3, TMK, OTMK, p, tti)
                    for db in range(PB):
                        pk = ps8[db // 2][:, (db % 2) * 256:(db % 2) * 256 + 256]
                        for cb in range(PB):
                            pe(lambda t, o=pk,
                               l=W0[:, cb * D + db * 128: cb * D + (db + 1) * 128],
                               r_=mx["xk"][:, cb * 256:(cb + 1) * 256],
                               st=(cb == 0 and db % 2 == 0), sp=(cb == PB - 1):
                               t.matmul(o, l, r_, start=st, stop=False,
                                        skip_group_check=True),
                               r=[(W0, ("ld", 0)), (mx["xk"], ("p", cb))],
                               w=[(ps8[db // 2], None)], inc=(cb == PB - 1))
                    for p in range(PB):
                        mix_tile(mx["xk"], xn3, TMV, OTMV, p, tti)
                    for db in range(PB):
                        pvv = ps8[4 + db // 2][:, (db % 2) * 256:(db % 2) * 256 + 256]
                        for cb in range(PB):
                            pe(lambda t, o=pvv,
                               l=W1[:, cb * D + db * 128: cb * D + (db + 1) * 128],
                               r_=mx["xk"][:, cb * 256:(cb + 1) * 256],
                               st=(cb == 0 and db % 2 == 0), sp=(cb == PB - 1):
                               t.matmul(o, l, r_, start=st, stop=False,
                                        skip_group_check=True),
                               r=[(W1, ("ld", 0)), (mx["xk"], ("p", cb))],
                               w=[(ps8[4 + db // 2], None)], inc=(cb == PB - 1))
                    for p in range(PB):
                        kps = ps8[p // 2][:, (p % 2) * 256:(p % 2) * 256 + 256]
                        vps = ps8[4 + p // 2][:, (p % 2) * 256:(p % 2) * 256 + 256]
                        acte(lambda s, kps=kps, p=p: s.activation(
                            ekb[:, p * 256:(p + 1) * 256], kps, AF.Exp),
                            r=[(ps8[p // 2], None)], w=[(ekb, ("p", p))])
                        dve(lambda v, vps=vps, p=p: v.tensor_tensor(
                            ekvb[:, p * 256:(p + 1) * 256],
                            ekb[:, p * 256:(p + 1) * 256], vps, OP.mult),
                            r=[(ekb, ("p", p)), (ps8[4 + p // 2], None)],
                            w=[(ekvb, ("p", p))])
                    for p in range(PB):
                        dve(lambda v, p=p: v.tensor_copy(
                            Pb8[:, p * 257: p * 257 + 1], Pst[:, p:p + 1]),
                            r=[(Pst, ("p", p))], w=[(Pb8, ("p", p))])
                        dve(lambda v, p=p: v.tensor_tensor_scan(
                            Pb8[:, p * 257 + 1: p * 257 + 257],
                            dpar[:, EW + p:EW + p + 1].broadcast_to((128, 256)),
                            ekvb[:, p * 256:(p + 1) * 256], Pst[:, p:p + 1],
                            OP.mult, OP.add),
                            r=[(dpar, ("c", EW)), (ekvb, ("p", p)),
                               (Pst, ("p", p)), (Pb8, ("p", p))],
                            w=[(Pb8, ("p", p))])
                        dve(lambda v, p=p: v.tensor_copy(
                            Qb8[:, p * 257: p * 257 + 1], Qst[:, p:p + 1]),
                            r=[(Qst, ("p", p))], w=[(Qb8, ("p", p))])
                        dve(lambda v, p=p: v.tensor_tensor_scan(
                            Qb8[:, p * 257 + 1: p * 257 + 257],
                            dpar[:, EW + p:EW + p + 1].broadcast_to((128, 256)),
                            ekb[:, p * 256:(p + 1) * 256], Qst[:, p:p + 1],
                            OP.mult, OP.add),
                            r=[(dpar, ("c", EW)), (ekb, ("p", p)),
                               (Qst, ("p", p)), (Qb8, ("p", p))],
                            w=[(Qb8, ("p", p))])
                        dve(lambda v, p=p: v.tensor_copy(
                            Pst[:, p:p + 1], Pb8[:, p * 257 + 256: p * 257 + 257]),
                            r=[(Pb8, ("p", p))], w=[(Pst, ("p", p))])
                        dve(lambda v, p=p: v.tensor_copy(
                            Qst[:, p:p + 1], Qb8[:, p * 257 + 256: p * 257 + 257]),
                            r=[(Qb8, ("p", p))], w=[(Qst, ("p", p))])
                    for p in range(PB):
                        dve(lambda v, p=p: v.scalar_tensor_tensor(
                            Pb8[:, p * 257: p * 257 + 256],
                            ekvb[:, p * 256:(p + 1) * 256],
                            dpar[:, EUC + p:EUC + p + 1],
                            Pb8[:, p * 257: p * 257 + 256], OP.mult, OP.add),
                            r=[(ekvb, ("p", p)), (dpar, ("c", 32)),
                               (Pb8, ("p", p))], w=[(Pb8, ("p", p))])
                        dve(lambda v, p=p: v.scalar_tensor_tensor(
                            Qb8[:, p * 257: p * 257 + 256],
                            ekb[:, p * 256:(p + 1) * 256],
                            dpar[:, EUC + p:EUC + p + 1],
                            Qb8[:, p * 257: p * 257 + 256], OP.mult, OP.add),
                            r=[(ekb, ("p", p)), (dpar, ("c", 32)),
                               (Qb8, ("p", p))], w=[(Qb8, ("p", p))])
                    allP = [(Pb8, ("p", p)) for p in range(PB)]
                    allQ = [(Qb8, ("p", p)) for p in range(PB)]
                    acte(lambda s: s.activation(Q3, Q3, AF.Ln), r=allQ, w=allQ)
                    acte(lambda s: s.activation(Q3, Q3, AF.Exp, scale=-1.0),
                         r=allQ, w=allQ)
                    dve(lambda v, tti=tti: v.tensor_tensor(
                        wkva[:, :].rearrange("p (b n) -> p b n", b=PB)[
                            :, :, tti * 256:(tti + 1) * 256],
                        P3, Q3, OP.mult),
                        r=allP + allQ, w=[(wkva, ("t", tti))])
                tk.barrier()
                tk.drop(list(mx.values()) +
                        [ekb, ekvb, Pb8, Qb8, Pst, Qst])
            ldst(W0, ("p", "wrec"))
            ldst(W1, ("p", "wout"))
            with ExitStack() as p2:
                s2p = lambda n, s, dt=F32: p2.enter_context(
                    nc.sbuf_tensor(_nm(n), s, dt))
                mx2 = mixes(s2p, ("xr",))
                erb = s2p("erb", [128, PB * 256])
                yt = [s2p(f"yt{i}", [128, PB * 256], BF16) for i in range(2)]
                v3e = lambda a: a.rearrange("p (b n) -> p b n", b=PB)

                def outproj(tti):
                    ytb = yt[tti % 2]
                    for db in range(PB):
                        po = ps8[4 + db // 2][:, (db % 2) * 256:(db % 2) * 256 + 256]
                        for cb in range(PB):
                            pe(lambda t, o=po,
                               l=W1[:, cb * D + db * 128: cb * D + (db + 1) * 128],
                               r_=ytb[:, cb * 256:(cb + 1) * 256],
                               st=(cb == 0 and db % 2 == 0), sp=(cb == PB - 1):
                               t.matmul(o, l, r_, start=st, stop=False,
                                        skip_group_check=True),
                               r=[(W1, ("ld", 0)), (ytb, ("p", cb))],
                               w=[(ps8[4 + db // 2], None)], inc=(cb == PB - 1))
                        xtags = [(x8[db], ("c", tti * 2 + j)) for j in range(2)]
                        dve(lambda v, db=db, tti=tti, po=po: v.tensor_tensor(
                            x8[db][:, tti * 256:(tti + 1) * 256],
                            x8[db][:, tti * 256:(tti + 1) * 256], po, OP.add),
                            r=xtags + [(ps8[4 + db // 2], None)], w=xtags)

                for tti in range(n2):
                    for p in range(PB):
                        mix_tile(mx2["xr"], xn3, TMR, OTMR, p, tti)
                    for db in range(PB):
                        pr = ps8[db // 2][:, (db % 2) * 256:(db % 2) * 256 + 256]
                        for cb in range(PB):
                            pe(lambda t, o=pr,
                               l=W0[:, cb * D + db * 128: cb * D + (db + 1) * 128],
                               r_=mx2["xr"][:, cb * 256:(cb + 1) * 256],
                               st=(cb == 0 and db % 2 == 0), sp=(cb == PB - 1):
                               t.matmul(o, l, r_, start=st, stop=False,
                                        skip_group_check=True),
                               r=[(W0, ("ld", 0)), (mx2["xr"], ("p", cb))],
                               w=[(ps8[db // 2], None)], inc=(cb == PB - 1))
                    for p in range(PB):
                        rps = ps8[p // 2][:, (p % 2) * 256:(p % 2) * 256 + 256]
                        acte(lambda s, rps=rps, p=p: s.activation(
                            erb[:, p * 256:(p + 1) * 256], rps, AF.Exp, scale=-1.0),
                            r=[(ps8[p // 2], None)], w=[(erb, ("p", p))])
                    alle = [(erb, ("p", p)) for p in range(PB)]
                    dve(lambda v: v.tensor_scalar(erb[:, :], erb[:, :], 1.0, 0.0,
                                                  OP.add, OP.add),
                        r=alle, w=alle)
                    acte(lambda s: s.activation(erb[:, :], erb[:, :], AF.Ln),
                         r=alle, w=alle)
                    acte(lambda s: s.activation(erb[:, :], erb[:, :], AF.Exp,
                                                scale=-1.0),
                         r=alle, w=alle)
                    dve(lambda v, tti=tti, ytb=yt[tti % 2]: v.tensor_tensor(
                        v3e(ytb[:, :]),
                        wkva[:, :].rearrange("p (b n) -> p b n", b=PB)[
                            :, :, tti * 256:(tti + 1) * 256],
                        v3e(erb[:, :]), OP.mult),
                        r=[(wkva, ("t", tti))] + alle,
                        w=[(yt[tti % 2], ("p", p)) for p in range(PB)])
                    if tti > 0:
                        outproj(tti - 1)
                outproj(n2 - 1)
                tk.barrier()
                tk.drop(list(mx2.values()) + [erb] + yt)
            tk.barrier()
            tk.drop([xn3, wkva])

    def tap(name, b):
        if taps:
            for p in range(PB):
                sq(lambda e, p=p: e.dma_start(dtap[name][b, p], x8[p][:, :]),
                   r=[(x8[p], None)])

    for b in range(NB):
        if b > 0:
            for p in range(PB):
                sq(lambda e, p=p, b=b: e.dma_start(x8[p][:, :], dx[b, p]),
                   w=[(x8[p], None)])
        attn_a(b)
        tap("attna", b)
        ffn(b, 0)
        tap("ffn1", b)
        attn_b(b)
        tap("attnb", b)
        ffn(b, 1)
        tap("ffn2", b)
        for p in range(PB):
            sq(lambda e, p=p, b=b: e.dma_start(dout[b, p], x8[p][:, :]),
               r=[(x8[p], None)])

    for i in range(tk.NDMA):
        for q, strm in (("sq", "sp"), ("gq", "pool")):
            n = tk.cnt[f"{q}{i}"]
            if n:
                tk.ops[strm].append(
                    lambda e, s=tk.sem[f"{q}{i}"], n=n: e.wait_ge(s, n))

    with es:
        with nc.Block() as block:
            @block.tensor
            def _(t):
                for f in tk.ops["pe"]:
                    f(t)

            @block.scalar
            def _(s):
                for f in tk.ops["act"]:
                    f(s)

            @block.vector
            def _(v):
                for f in tk.ops["dve"]:
                    f(v)

            @block.gpsimd
            def _(g):
                for f in tk.ops["pool"]:
                    f(g)

            @block.sync
            def _(s):
                for f in tk.ops["sp"]:
                    f(s)
    return nc


def _pack_inmap(T=2048):
    raise NotImplementedError


def _pk_proj(w):
    return np.ascontiguousarray(
        w.reshape(8, 128, w.shape[1]).transpose(1, 0, 2).reshape(128, -1))


def host_pack(inp, T):
    import ml_dtypes
    f32 = lambda a: np.asarray(a, dtype=np.float32)
    wq = _pk_proj(f32(inp["L0_attn_a_Wq"]))
    wk = _pk_proj(f32(inp["L0_attn_a_Wk"]))
    wv = _pk_proj(f32(inp["L0_attn_a_Wv"]))
    wo = _pk_proj(f32(inp["L0_attn_a_Wo"]))
    wkey = _pk_proj(f32(inp["L0_attn_b_W_key"]).T)
    wval = _pk_proj(f32(inp["L0_attn_b_W_value"]).T)
    wrec = _pk_proj(f32(inp["L0_attn_b_W_receptance"]).T)
    wout = _pk_proj(f32(inp["L0_attn_b_W_output"]).T)

    w1 = f32(inp["L0_ffn_w1"])
    w2 = f32(inp["L0_ffn_w2"])
    w3 = f32(inp["L0_ffn_w3"])
    ws1 = f32(inp["L0_ffn_ws1"])
    ws2 = f32(inp["L0_ffn_ws2"])
    ws3 = f32(inp["L0_ffn_ws3"])
    w13 = np.zeros((NEX, 128, PB * D), np.float32)
    w2p = np.zeros((NEX, 128, 4 * D), np.float32)
    for e in range(NE):
        w13[e] = _pk_proj(np.concatenate([w1[e], w3[e]], axis=1))
        w2p[e] = np.ascontiguousarray(
            w2[e].reshape(4, 128, D).transpose(1, 0, 2).reshape(128, 4 * D))
    for j in range(2):
        w13[NE + j] = _pk_proj(np.concatenate(
            [ws1[:, j * 512:(j + 1) * 512], ws3[:, j * 512:(j + 1) * 512]], axis=1))
        w2p[NE + j] = np.ascontiguousarray(
            ws2[j * 512:(j + 1) * 512].reshape(4, 128, D).transpose(1, 0, 2)
            .reshape(128, 4 * D))
    gw = _pk_proj(f32(inp["L0_ffn_gate_w"]).T)

    fmv = lambda a: np.ascontiguousarray(f32(a).reshape(8, 128).T)
    pvec = np.zeros((128, 80), np.float32)
    pvec[:, 0:8] = fmv(inp["L0_attn_a_bq"])
    pvec[:, 8:16] = fmv(inp["L0_attn_a_bk"])
    pvec[:, 16:24] = fmv(inp["L0_attn_a_bv"])
    pvec[:, 24:32] = fmv(inp["L0_attn_a_bo"])
    pvec[:, 32:40] = fmv(inp["L0_attn_b_time_first"])
    pvec[:, 40:48] = fmv(inp["L0_attn_b_time_decay"])
    pvec[:, 48:56] = fmv(np.asarray(inp["L0_attn_b_time_mix_k"]).reshape(-1))
    pvec[:, 56:64] = fmv(np.asarray(inp["L0_attn_b_time_mix_v"]).reshape(-1))
    pvec[:, 64:72] = fmv(np.asarray(inp["L0_attn_b_time_mix_r"]).reshape(-1))
    pvec[0:16, 72] = f32(inp["L0_attn_a_m"])
    brow = np.concatenate([f32(inp["L0_attn_a_bk"]),
                           f32(inp["L0_attn_a_bv"])])[None, :]

    bf = ml_dtypes.bfloat16
    ones_k1 = np.ones((1, 128), np.float32)
    onescol_b = np.ones((128, 1), bf)
    bd = np.zeros((128, PB * HEAD), bf)
    selh = np.zeros((16, PB * 128), bf)
    for p in range(PB):
        bd[0:64, p * 16 + 2 * p] = 1.0
        bd[64:128, p * 16 + 2 * p + 1] = 1.0
        selh[2 * p, p * 128: p * 128 + 64] = 1.0
        selh[2 * p + 1, p * 128 + 64: p * 128 + 128] = 1.0
    sel_e = np.zeros((16, NE * 128), bf)
    for e in range(NE):
        sel_e[e, e * 128:(e + 1) * 128] = 1.0
    ident_b = np.eye(128, dtype=bf)

    cvals = np.zeros((128, 8), np.float32)
    for i, v in enumerate([1e-5, -0.5, -1.0, -float(np.log(T))]):
        cvals[:, i] = v

    return dict(wq=wq, wk=wk, wv=wv, wo=wo, wkey=wkey, wval=wval, wrec=wrec,
                wout=wout, w13=w13, w2=w2p, gw=gw, pvec=pvec, brow=brow,
                ones_k1=ones_k1, onescol_b=onescol_b, bd=bd, selh=selh,
                sel_e=sel_e, ident_b=ident_b, cvals=cvals)


def pack_x(xbatches, T):
    return np.stack([np.ascontiguousarray(b.T).reshape(8, 128, T)
                     for b in xbatches])


def unpack_out(o, T):
    return np.stack([o[b].reshape(D, T).T for b in range(o.shape[0])])


def kernel(**inp):
    x = np.asarray(inp["L0_attn_a_x"], dtype=np.float32)
    T = x.shape[1]
    shared = host_pack(inp, T)
    NBc = x.shape[0] // 8
    in_maps = []
    for c in range(8):
        m = dict(shared)
        m["x"] = pack_x([x[c * NBc + b] for b in range(NBc)], T)
        in_maps.append(m)

    nc = build_nc(T=T, NB=NBc)
    res = run_bass_kernel_spmd(nc, in_maps, core_ids=list(range(8)))
    out = np.empty((8 * NBc, T, D), np.float32)
    for c in range(8):
        out[c * NBc:(c + 1) * NBc] = unpack_out(res.results[c]["out"], T)
    return out


# version 07
# speedup vs baseline: 1.1252x; 1.0125x over previous
import numpy as np
from contextlib import ExitStack

import concourse.bass as bass
import concourse.mybir as mybir
from concourse.bass_utils import run_bass_kernel_spmd

F32 = mybir.dt.float32
BF16 = mybir.dt.bfloat16
AF = mybir.ActivationFunctionType
OP = mybir.AluOpType
AX = mybir.AxisListType

D = 1024
PB = 8
NE = 16
NEX = 18
HEAD = 16
HD = 64
FFN_CHUNKS = ((0, 1, 2, 3), (4, 5, 6, 7), (8, 9, 10, 11), (12, 13, 14, 15),
              (16, 17))


class _Rec:
    __slots__ = ("src", "cnt")

    def __init__(self, src, cnt):
        self.src = src
        self.cnt = cnt


class TK:
    STREAMS = ("pe", "act", "dve", "pool", "sp")
    ENGS = ("pe", "act", "dve", "pool")
    NDMA = 8

    def __init__(self, nc, es):
        self.nc = nc
        self.psum_ids = set()
        self.ops = {s: [] for s in self.STREAMS}
        self.sem = {s: es.enter_context(nc.semaphore("sem_" + s)) for s in self.ENGS}
        for q in ("gq", "sq"):
            for i in range(self.NDMA):
                self.sem[f"{q}{i}"] = es.enter_context(nc.semaphore(f"sem_{q}{i}"))
        self.cnt = {s: 0 for s in self.sem}
        self.rr = {"gq": 0, "sq": 0}
        self.waited = {c: {s: 0 for s in self.sem} for c in self.STREAMS}
        self.floor = {c: {} for c in self.STREAMS}
        self.bufs = {}
        self.pe_pending = []

    def _deps(self, h, tag, readers):
        recs = self.bufs.get(id(h))
        if not recs:
            return ()
        out = []
        for t2, r in recs.items():
            if tag is None or t2 is None or t2 == tag:
                if r["w"] is not None:
                    out.append(r["w"])
                if readers:
                    out.extend(r["r"].values())
        return out

    def emit(self, stream, fn, reads=(), writes=(), inc=True, dma=None):
        need = dict(self.floor[stream])
        self.floor[stream] = {}
        for h, tag in reads:
            psum_rr = id(h) in self.psum_ids
            for r in self._deps(h, tag, psum_rr):
                if r.src == "pe" and stream == "pe":
                    continue
                if psum_rr and r.src == stream:
                    continue
                assert r.cnt is not None, "dep on open PE group"
                if r.cnt > need.get(r.src, 0):
                    need[r.src] = r.cnt
        for h, tag in writes:
            for r in self._deps(h, tag, True):
                if r.src == "pe" and stream == "pe":
                    continue
                assert r.cnt is not None, "dep on open PE group"
                if r.cnt > need.get(r.src, 0):
                    need[r.src] = r.cnt
        sname = None
        if dma is not None:
            sname = f"{dma}{self.rr[dma]}"
            self.rr[dma] = (self.rr[dma] + 1) % self.NDMA
            if self.cnt[sname] > need.get(sname, 0):
                need[sname] = self.cnt[sname]
        waits = []
        wt = self.waited[stream]
        for src in sorted(need):
            if need[src] > wt[src]:
                waits.append((self.sem[src], need[src]))
                wt[src] = need[src]

        if dma is not None:
            self.cnt[sname] += 16
            rec = _Rec(sname, self.cnt[sname])
            inc_sem, inc_n = self.sem[sname], 16
        elif inc:
            self.cnt[stream] += 1
            rec = _Rec(stream, self.cnt[stream])
            inc_sem, inc_n = self.sem[stream], 1
            if stream == "pe":
                for p in self.pe_pending:
                    p.cnt = rec.cnt
                self.pe_pending = []
        else:
            assert stream == "pe"
            rec = _Rec("pe", None)
            self.pe_pending.append(rec)
            inc_sem, inc_n = None, 0

        def closure(eng, _w=tuple(waits), _f=fn, _s=inc_sem, _n=inc_n):
            for sem, cnt in _w:
                eng.wait_ge(sem, cnt)
            inst = _f(eng)
            if _s is not None:
                inst.then_inc(_s, _n)

        self.ops[stream].append(closure)

        for h, tag in writes:
            self.bufs.setdefault(id(h), {})[tag] = {"w": rec, "r": {}}
        for h, tag in reads:
            recs = self.bufs.setdefault(id(h), {})
            recs.setdefault(tag, {"w": None, "r": {}})["r"][rec.src] = rec
        return rec

    def barrier(self):
        for c in ("pe", "act", "dve", "sp"):
            f = self.floor[c]
            for s in ("pe", "act", "dve"):
                f[s] = max(f.get(s, 0), self.cnt[s])

    def fence(self, stream):
        f = self.floor[stream]
        for s in ("pe", "act", "dve"):
            f[s] = max(f.get(s, 0), self.cnt[s])

    def drop(self, handles):
        for h in handles:
            self.bufs.pop(id(h), None)


def build_nc(T=2048, NB=2, taps=False):
    nc = bass.Bass("TRN2", target_bir_lowering=False)
    n5 = T // 512
    n2 = T // 256
    LOGT = float(np.log(T))

    dx = nc.dram_tensor("x", [NB, PB, 128, T], F32, kind="ExternalInput").ap()
    dout = nc.dram_tensor("out", [NB, PB, 128, T], F32, kind="ExternalOutput").ap()
    dW = {}
    for nm in ("wq", "wk", "wv", "wo", "wkey", "wval", "wrec", "wout"):
        dW[nm] = nc.dram_tensor(nm, [128, PB * D], F32, kind="ExternalInput").ap()
    dw13 = nc.dram_tensor("w13", [NEX, 128, PB * D], F32, kind="ExternalInput").ap()
    dw2 = nc.dram_tensor("w2", [NEX, 128, 4 * D], F32, kind="ExternalInput").ap()
    dgw = nc.dram_tensor("gw", [128, PB * NE], F32, kind="ExternalInput").ap()
    dpv = nc.dram_tensor("pvec", [128, 80], F32, kind="ExternalInput").ap()
    dbrow = nc.dram_tensor("brow", [1, 2 * D], F32, kind="ExternalInput").ap()
    dones = nc.dram_tensor("ones_k1", [1, 128], F32, kind="ExternalInput").ap()
    donesb = nc.dram_tensor("onescol_b", [128, 1], BF16, kind="ExternalInput").ap()
    dbd = nc.dram_tensor("bd", [128, PB * HEAD], BF16, kind="ExternalInput").ap()
    dselh = nc.dram_tensor("selh", [16, PB * 128], BF16, kind="ExternalInput").ap()
    dsele = nc.dram_tensor("sel_e", [16, NE * 128], BF16, kind="ExternalInput").ap()
    dident = nc.dram_tensor("ident_b", [128, 128], BF16, kind="ExternalInput").ap()
    dcv = nc.dram_tensor("cvals", [128, 8], F32, kind="ExternalInput").ap()
    dtap = {}
    if taps:
        for nm in ("xn1", "attna", "ffn1", "attnb", "ffn2", "wkv"):
            dtap[nm] = nc.dram_tensor("tap_" + nm, [NB, PB, 128, T], F32,
                                      kind="ExternalOutput").ap()

    es = ExitStack()
    _uid = [0]

    def _nm(name):
        _uid[0] += 1
        return f"{name}_{_uid[0]}"

    sb = lambda name, shape, dt=F32: es.enter_context(nc.sbuf_tensor(name, shape, dt))
    tk = TK(nc, es)

    x8 = [sb(f"x{i}", [128, T]) for i in range(PB)]
    c_ones = sb("c_ones", [1, 128])
    c_onesb = sb("c_onesb", [128, 1], BF16)
    pv = sb("pv", [128, 80])
    dpar = sb("dpar", [128, 40])
    sgate = sb("sgate", [16, 1])
    gwS = sb("gwS", [128, PB * NE], BF16)
    msb = sb("msb", [128, 512], BF16)
    W0 = sb("W0", [128, PB * D], BF16)
    W1 = sb("W1", [128, PB * D], BF16)
    cvals = sb("cvals_s", [128, 8])
    ps8 = [es.enter_context(nc.psum_tensor(f"ps{i}", [128, 512], F32))
           for i in range(8)]
    tk.psum_ids = {id(p) for p in ps8}

    BQ, BK, BV, BO, UU, TD, TMK, TMV, TMR, MCOL = 0, 8, 16, 24, 32, 40, 48, 56, 64, 72
    OTMK, OTMV, OTMR, EW = 0, 8, 16, 24

    pe = lambda fn, r=(), w=(), inc=True: tk.emit("pe", fn, r, w, inc)
    acte = lambda fn, r=(), w=(): tk.emit("act", fn, r, w)
    dve = lambda fn, r=(), w=(): tk.emit("dve", fn, r, w)
    gq = lambda fn, r=(), w=(): tk.emit("pool", fn, r, w, dma="gq")
    sq = lambda fn, r=(), w=(): tk.emit("sp", fn, r, w, dma="sq")

    sw13 = nc.dram_tensor("sw13", [NEX, 128, PB * D], BF16)
    sw2 = nc.dram_tensor("sw2", [NEX, 128, 4 * D], BF16)
    sproj = nc.dram_tensor("sproj", [8, 128, PB * D], BF16)
    _PROJ_IDX = {"wq": 0, "wk": 1, "wv": 2, "wo": 3, "wkey": 4, "wval": 5,
                 "wrec": 6, "wout": 7}
    _staged = {}

    def ldw(dst, src_ap, r0=0, r1=None):
        n = dst.shape[1] if r1 is None else (r1 - r0)
        gq(lambda g, d=dst[:, r0:r0 + n], s=src_ap[:, r0:r0 + n]: g.dma_start(d, s),
           w=[(dst, ("ld", r0))])

    def ldst(dst, key):
        if key[0] == "w13":
            slot, fsrc = sw13.ap()[key[1]], dw13[key[1]]
        elif key[0] == "w2":
            slot, fsrc = sw2.ap()[key[1]], dw2[key[1]]
        else:
            slot, fsrc = sproj.ap()[_PROJ_IDX[key[1]]], dW[key[1]]
        if key not in _staged:
            _staged[key] = True
            gq(lambda g, d=dst[:, :], s=fsrc: g.dma_start(d, s),
               w=[(dst, ("ld", 0))])
            sq(lambda e, d=slot, s=dst[:, :]: e.dma_start(d, s),
               r=[(dst, ("ld", 0))], w=[(sw13, key)])
        else:
            sq(lambda e, d=dst[:, :], s=slot: e.dma_start(d, s),
               r=[(sw13, key)], w=[(dst, ("ld", 0))])

    CV = [1e-5, -0.5, -1.0, -LOGT]
    for i, v in enumerate(CV):
        nc.const_aps.register(F32, v, cvals[:, i:i + 1])
    sq(lambda e: e.dma_start(cvals[:, :], dcv), w=[(cvals, None)])
    sq(lambda e: e.dma_start(c_ones[:, :], dones), w=[(c_ones, None)])
    sq(lambda e: e.dma_start(c_onesb[:, :], donesb), w=[(c_onesb, None)])
    sq(lambda e: e.dma_start(pv[:, :], dpv), w=[(pv, None)])
    gq(lambda e: e.dma_start(gwS[:, :], dgw), w=[(gwS, None)])
    for p in range(PB):
        sq(lambda e, p=p: e.dma_start(x8[p][:, :], dx[0, p]), w=[(x8[p], None)])

    for c in ("pe", "act", "dve"):
        for i in range(tk.NDMA):
            for q in ("sq", "gq"):
                if tk.cnt[f"{q}{i}"]:
                    tk.floor[c][f"{q}{i}"] = tk.cnt[f"{q}{i}"]
    for col, src in ((OTMK, TMK), (OTMV, TMV), (OTMR, TMR)):
        dve(lambda v, c=col, s=src: v.tensor_scalar(
            dpar[:, c:c + 8], pv[:, s:s + 8], -1.0, 1.0, OP.mult, OP.add),
            r=[(pv, None)], w=[(dpar, ("c", col))])
    acte(lambda s: s.activation(dpar[:, EW:EW + 8], pv[:, TD:TD + 8], AF.Exp),
         r=[(pv, None)], w=[(dpar, ("c", EW))])
    dve(lambda v: v.tensor_scalar(dpar[:, EW:EW + 8], dpar[:, EW:EW + 8],
                                  -1.0, 0.0, OP.mult, OP.bypass),
        r=[(dpar, ("c", EW))], w=[(dpar, ("c", EW))])
    acte(lambda s: s.activation(dpar[:, EW:EW + 8], dpar[:, EW:EW + 8], AF.Exp),
         r=[(dpar, ("c", EW))], w=[(dpar, ("c", EW))])
    acte(lambda s: s.activation(dpar[:, 32:40], pv[:, UU:UU + 8], AF.Exp),
         r=[(pv, None)], w=[(dpar, ("c", 32))])
    acte(lambda s: s.activation(sgate[:, :], pv[0:16, MCOL:MCOL + 1], AF.Sigmoid),
         r=[(pv, None)], w=[(sgate, None)])
    acte(lambda s: s.activation(sgate[:, :], sgate[:, :], AF.Exp, scale=-LOGT),
         r=[(sgate, None)], w=[(sgate, None)])

    def ln_tile(xsrc, tt, dst_fn, scratch, tag):
        t0 = tt * 512
        xc, xq, stats = scratch["xc"], scratch["xq"], scratch["stats"]
        for hh in range(4):
            for p in range(PB):
                s0 = t0 + hh * 128
                acte(lambda s, p=p, s0=s0: s.activation(
                    xc[:, p * 128:(p + 1) * 128], xsrc[p][:, s0:s0 + 128], AF.Identity),
                    r=[(xsrc[p], ("c", s0 // 128))], w=[(xc, ("p", p))])
                acte(lambda s, p=p, s0=s0: s.activation(
                    xq[:, p * 128:(p + 1) * 128], xsrc[p][:, s0:s0 + 128], AF.Square),
                    r=[(xsrc[p], ("c", s0 // 128))], w=[(xq, ("p", p))])
            for p in range(PB):
                pe(lambda t, p=p, hh=hh: t.matmul(
                    ps8[0][0:1, hh * 128:(hh + 1) * 128], c_onesb[:, :],
                    xc[:, p * 128:(p + 1) * 128], start=(p == 0), stop=(p == PB - 1)),
                   r=[(c_onesb, None), (xc, ("p", p))], w=[(ps8[0], None)],
                   inc=(p == PB - 1))
                pe(lambda t, p=p, hh=hh: t.matmul(
                    ps8[1][0:1, hh * 128:(hh + 1) * 128], c_onesb[:, :],
                    xq[:, p * 128:(p + 1) * 128], start=(p == 0), stop=(p == PB - 1)),
                   r=[(c_onesb, None), (xq, ("p", p))], w=[(ps8[1], None)],
                   inc=(p == PB - 1))
        dve(lambda v: v.tensor_copy(stats[:, 0:512], ps8[0][0:1, :]),
            r=[(ps8[0], None)], w=[(stats, 0)])
        dve(lambda v: v.tensor_copy(stats[:, 512:1024], ps8[1][0:1, :]),
            r=[(ps8[1], None)], w=[(stats, 1)])
        pe(lambda t: t.matmul(ps8[2][:, :], c_ones[:, :], stats[:, 0:512],
                              start=True, stop=True),
           r=[(c_ones, None), (stats, 0)], w=[(ps8[2], None)])
        pe(lambda t: t.matmul(ps8[3][:, :], c_ones[:, :], stats[:, 512:1024],
                              start=True, stop=True),
           r=[(c_ones, None), (stats, 1)], w=[(ps8[3], None)])
        mean, ex2, var = scratch["mean"], scratch["ex2"], scratch["var"]
        istd = var
        dve(lambda v: v.tensor_scalar(mean[:, :], ps8[2][:, :], 1.0 / D, 0.0,
                                      OP.mult, OP.add),
            r=[(ps8[2], None)], w=[(mean, None)])
        dve(lambda v: v.tensor_scalar(ex2[:, :], ps8[3][:, :], 1.0 / D, 0.0,
                                      OP.mult, OP.add),
            r=[(ps8[3], None)], w=[(ex2, None)])
        dve(lambda v: v.tensor_tensor(var[:, :], mean[:, :], mean[:, :], OP.mult),
            r=[(mean, None)], w=[(var, None)])
        dve(lambda v: v.tensor_tensor(var[:, :], ex2[:, :], var[:, :], OP.subtract),
            r=[(ex2, None), (var, None)], w=[(var, None)])
        acte(lambda s: s.activation(var[:, :], var[:, :], AF.Ln, bias=1e-5),
             r=[(var, None)], w=[(var, None)])
        acte(lambda s: s.activation(istd[:, :], var[:, :], AF.Exp, scale=-0.5),
             r=[(var, None)], w=[(istd, None)])
        dlt = scratch["dlt"]
        for p in range(PB):
            dap, dh, dtag = dst_fn(p)
            dve(lambda v, p=p: v.tensor_tensor(dlt[:, :], xsrc[p][:, t0:t0 + 512],
                                               mean[:, :], OP.subtract),
                r=[(xsrc[p], ("c", tt * 4 + j)) for j in range(4)] + [(mean, None)],
                w=[(dlt, None)])
            dve(lambda v, a=dap: v.tensor_tensor(a, dlt[:, :], istd[:, :], OP.mult),
                r=[(dlt, None), (istd, None)], w=[(dh, dtag)])

    def mk_ln_scratch(sbp):
        return dict(xc=sbp("ln_xc", [128, PB * 128], BF16),
                    xq=sbp("ln_xq", [128, PB * 128], BF16),
                    stats=sbp("ln_stats", [1, 1024]),
                    mean=sbp("ln_mean", [128, 512]), ex2=sbp("ln_ex2", [128, 512]),
                    var=sbp("ln_var", [128, 512]),
                    dlt=sbp("ln_dlt", [128, 512]))

    def attn_a(b):
        with ExitStack() as ph:
            sbp = lambda n, s, dt=F32: ph.enter_context(
                nc.sbuf_tensor(_nm(n), s, dt))
            lns = mk_ln_scratch(sbp)
            xn1 = sbp("xn1", [128, PB * 512], BF16)
            btm = sbp("btm", [128, 2 * D])
            browS = sbp("browS", [1, 2 * D])
            c_selh = sbp("c_selh", [16, PB * 128], BF16)
            c_bd = sbp("c_bd", [128, PB * HEAD], BF16)
            sq(lambda e: e.dma_start(browS[:, :], dbrow), w=[(browS, None)])
            sq(lambda e: e.dma_start(c_selh[:, :], dselh), w=[(c_selh, None)])
            sq(lambda e: e.dma_start(c_bd[:, :], dbd), w=[(c_bd, None)])
            for j in range(4):
                pe(lambda t, j=j: t.matmul(ps8[j][:, :], c_ones[:, :],
                                           browS[:, 512 * j:512 * (j + 1)],
                                           start=True, stop=True),
                   r=[(c_ones, None), (browS, None)], w=[(ps8[j], None)])
                dve(lambda v, j=j: v.tensor_copy(btm[:, 512 * j:512 * (j + 1)],
                                                 ps8[j][:, :]),
                    r=[(ps8[j], None)], w=[(btm, ("c", j))])
            knt = [sbp(f"knt{i}", [128, D], BF16) for i in range(2)]
            vt = [sbp(f"vt{i}", [128, D], BF16) for i in range(2)]
            kb = sbp("kb", [128, D])
            ksq = sbp("ksq", [128, D], BF16)
            ksc = sbp("ksc", [128, 16])
            ldst(W0, ("p", "wk"))
            ldst(W1, ("p", "wv"))
            for tt in range(n5):
                ln_tile(x8, tt, lambda p: (xn1[:, p * 512:(p + 1) * 512], xn1,
                                           ("p", p)), lns, "ln1")
                for tbl in range(4):
                    tb = tt * 4 + tbl
                    kn, vv = knt[tb % 2], vt[tb % 2]
                    for half in range(2):
                        pk, pv_ = ps8[half], ps8[2 + half]
                        for cb in range(PB):
                            lh = xn1[:, cb * 512 + tbl * 128: cb * 512 + (tbl + 1) * 128]
                            pe(lambda t, o=pk[:, :], l=lh,
                               r_=W0[:, cb * D + half * 512: cb * D + (half + 1) * 512],
                               st=(cb == 0), sp=(cb == PB - 1): t.matmul(
                                   o, l, r_, start=st, stop=sp),
                               r=[(xn1, ("p", cb)), (W0, ("ld", 0))],
                               w=[(pk, None)], inc=(cb == PB - 1))
                            pe(lambda t, o=pv_[:, :], l=lh,
                               r_=W1[:, cb * D + half * 512: cb * D + (half + 1) * 512],
                               st=(cb == 0), sp=(cb == PB - 1): t.matmul(
                                   o, l, r_, start=st, stop=sp),
                               r=[(xn1, ("p", cb)), (W1, ("ld", 0))],
                               w=[(pv_, None)], inc=(cb == PB - 1))
                        dve(lambda v, h=half: v.tensor_tensor(
                            kb[:, h * 512:(h + 1) * 512], ps8[h][:, :],
                            btm[:, h * 512:(h + 1) * 512], OP.add),
                            r=[(ps8[half], None), (btm, ("c", half))],
                            w=[(kb, ("h", half))])
                        acte(lambda s, h=half: s.activation(
                            ksq[:, h * 512:(h + 1) * 512], kb[:, h * 512:(h + 1) * 512],
                            AF.Square),
                            r=[(kb, ("h", half))], w=[(ksq, ("h", half))])
                        dve(lambda v, h=half, vv=vv: v.tensor_tensor(
                            vv[:, h * 512:(h + 1) * 512], ps8[2 + h][:, :],
                            btm[:, D + h * 512: D + (h + 1) * 512], OP.add),
                            r=[(ps8[2 + half], None), (btm, ("c", 2 + half))],
                            w=[(vv, ("h", half))])
                    dve(lambda v: v.reduce_sum(
                        ksc[:, :], ksq[:, :].rearrange("p (h d) -> p h d", d=HD),
                        axis=AX.X),
                        r=[(ksq, ("h", 0)), (ksq, ("h", 1))], w=[(ksc, None)])
                    dve(lambda v: v.tensor_scalar(ksc[:, :], ksc[:, :], 1e-24, 0.0,
                                                  OP.max, OP.add),
                        r=[(ksc, None)], w=[(ksc, None)])
                    acte(lambda s: s.activation(ksc[:, :], ksc[:, :], AF.Ln),
                         r=[(ksc, None)], w=[(ksc, None)])
                    acte(lambda s: s.activation(ksc[:, :], ksc[:, :], AF.Exp,
                                                scale=-0.5),
                         r=[(ksc, None)], w=[(ksc, None)])
                    for half in range(2):
                        dve(lambda v, h=half, kn=kn: v.tensor_tensor(
                            kn[:, h * 512:(h + 1) * 512].rearrange(
                                "p (h d) -> p h d", d=HD),
                            kb[:, h * 512:(h + 1) * 512].rearrange(
                                "p (h d) -> p h d", d=HD),
                            ksc[:, h * 8:(h + 1) * 8].unsqueeze(2).broadcast_to(
                                (128, 8, HD)),
                            OP.mult),
                            r=[(kb, ("h", half)), (ksc, None)], w=[(kn, ("h", half))])
                    for h in range(HEAD):
                        r0, c0 = 64 * (h % 2), 64 * (h // 2)
                        pe(lambda t, kn=kn, vv=vv, h=h, r0=r0, c0=c0,
                           st=(tb == 0 and h < 2),
                           sp=(tb == 4 * n5 - 1 and h == HEAD - 1):
                           t.matmul(ps8[7][r0:r0 + 64, c0:c0 + 64],
                                    kn[:, h * 64:(h + 1) * 64],
                                    vv[:, h * 64:(h + 1) * 64],
                                    start=st, stop=sp, skip_group_check=True),
                           r=[(kn, ("h", 0)), (kn, ("h", 1)), (vv, ("h", 0)),
                              (vv, ("h", 1))],
                           w=[(ps8[7], None)], inc=(h == HEAD - 1))
            dve(lambda v: v.tensor_copy(msb[:, :], ps8[7][:, :]),
                r=[(ps8[7], None)], w=[(msb, None)])
            ldst(W0, ("p", "wq"))
            ldst(W1, ("p", "wo"))
            qb = sbp("qb", [128, PB * 512], BF16)
            qsq = sbp("qsq", [128, PB * 512], BF16)
            qn = sbp("qn", [128, PB * 512], BF16)
            ao = sbp("ao", [128, PB * 512], BF16)
            qs = sbp("qs", [16, 512])
            qsb = sbp("qsb", [16, 512], BF16)
            for tt in range(n5):
                ln_tile(x8, tt, lambda p: (xn1[:, p * 512:(p + 1) * 512], xn1,
                                           ("p", p)), lns, "ln1c")
                for db in range(PB):
                    pq = ps8[db % 2]
                    for cb in range(PB):
                        pe(lambda t, o=pq[:, :],
                           l=W0[:, cb * D + db * 128: cb * D + (db + 1) * 128],
                           r_=xn1[:, cb * 512:(cb + 1) * 512],
                           st=(cb == 0), sp=(cb == PB - 1): t.matmul(
                               o, l, r_, start=st, stop=sp),
                           r=[(W0, ("ld", 0)), (xn1, ("p", cb))],
                           w=[(pq, None)], inc=(cb == PB - 1))
                    dve(lambda v, db=db, pq=pq: v.tensor_scalar(
                        qb[:, db * 512:(db + 1) * 512], pq[:, :],
                        pv[:, BQ + db:BQ + db + 1], 0.0, OP.add, OP.add),
                        r=[(pq, None), (pv, None)], w=[(qb, ("p", db))])
                    acte(lambda s, db=db, pq=pq: s.activation(
                        qsq[:, db * 512:(db + 1) * 512], pq[:, :], AF.Square,
                        bias=pv[:, BQ + db:BQ + db + 1]),
                        r=[(pq, None), (pv, None)], w=[(qsq, ("p", db))])
                for p in range(PB):
                    pe(lambda t, p=p: t.matmul(
                        ps8[2][0:16, :], c_bd[:, p * 16:(p + 1) * 16],
                        qsq[:, p * 512:(p + 1) * 512],
                        start=(p == 0), stop=(p == PB - 1)),
                       r=[(c_bd, None), (qsq, ("p", p))], w=[(ps8[2], None)],
                       inc=(p == PB - 1))
                dve(lambda v: v.tensor_scalar(qs[:, :], ps8[2][0:16, :], 1e-24, 0.0,
                                              OP.max, OP.add),
                    r=[(ps8[2], None)], w=[(qs, None)])
                acte(lambda s: s.activation(qs[:, :], qs[:, :], AF.Ln),
                     r=[(qs, None)], w=[(qs, None)])
                acte(lambda s: s.activation(qs[:, :], qs[:, :], AF.Exp, scale=-0.5),
                     r=[(qs, None)], w=[(qs, None)])
                dve(lambda v: v.tensor_scalar(qs[:, :], qs[:, :], sgate[:, 0:1], 0.0,
                                              OP.mult, OP.add),
                    r=[(qs, None), (sgate, None)], w=[(qs, None)])
                dve(lambda v: v.tensor_copy(qsb[:, :], qs[:, :]),
                    r=[(qs, None)], w=[(qsb, None)])
                for p in range(PB):
                    pbc = ps8[3 + p % 2]
                    pe(lambda t, p=p, o=pbc[:, :]: t.matmul(
                        o, c_selh[:, p * 128:(p + 1) * 128], qsb[:, :],
                        start=True, stop=True),
                       r=[(c_selh, None), (qsb, None)], w=[(pbc, None)])
                    dve(lambda v, p=p, pbc=pbc: v.tensor_tensor(
                        qn[:, p * 512:(p + 1) * 512], qb[:, p * 512:(p + 1) * 512],
                        pbc[:, :], OP.mult),
                        r=[(qb, ("p", p)), (pbc, None)], w=[(qn, ("p", p))])
                    py = ps8[5 + p % 2]
                    for hh in range(2):
                        h = 2 * p + hh
                        pe(lambda t, h=h, hh=hh, p=p, py=py: t.matmul(
                            py[hh * 64:(hh + 1) * 64, :],
                            msb[64 * (h % 2):64 * (h % 2) + 64,
                                64 * (h // 2):64 * (h // 2) + 64],
                            qn[64 * hh:64 * (hh + 1), p * 512:(p + 1) * 512],
                            start=True, stop=True, skip_group_check=True),
                           r=[(msb, None), (qn, ("p", p))], w=[(py, None)],
                           inc=(hh == 1))
                    dve(lambda v, p=p, py=py: v.tensor_copy(
                        ao[:, p * 512:(p + 1) * 512], py[:, :]),
                        r=[(py, None)], w=[(ao, ("p", p))])
                for db in range(PB):
                    po = ps8[7]
                    for cb in range(PB):
                        pe(lambda t, o=po[:, :],
                           l=W1[:, cb * D + db * 128: cb * D + (db + 1) * 128],
                           r_=ao[:, cb * 512:(cb + 1) * 512],
                           st=(cb == 0), sp=(cb == PB - 1): t.matmul(
                               o, l, r_, start=st, stop=sp),
                           r=[(W1, ("ld", 0)), (ao, ("p", cb))],
                           w=[(po, None)], inc=(cb == PB - 1))
                    xtags = [(x8[db], ("c", tt * 4 + j)) for j in range(4)]
                    dve(lambda v, db=db, tt=tt, po=po: v.scalar_tensor_tensor(
                        x8[db][:, tt * 512:(tt + 1) * 512], po[:, :],
                        pv[:, BO + db:BO + db + 1],
                        x8[db][:, tt * 512:(tt + 1) * 512], OP.add, OP.add),
                        r=[(po, None), (pv, None)] + xtags,
                        w=xtags)
            tk.barrier()
            tk.drop([lns[k] for k in lns] + [xn1, knt[0], knt[1], vt[0], vt[1],
                                             kb, ksq, ksc, qb, qsq, qn, ao, qs, qsb,
                                             btm, browS, c_selh, c_bd])

    def ffn(b, inv):
        with ExitStack() as ph:
            sbp = lambda n, s, dt=F32: ph.enter_context(
                nc.sbuf_tensor(_nm(n), s, dt))
            lns = mk_ln_scratch(sbp)
            xn2 = [sbp(f"xn2_{i}", [128, PB * 512], BF16) for i in range(2)]
            asl = sbp("asl", [128, 4 * 4 * 512], BF16)
            sil = sbp("sil", [128, 4 * 512], BF16)
            h3c = sbp("h3c", [128, 4 * 512], BF16)
            call = sbp("call", [128, NE * 512], BF16)
            gL = sbp("gL", [128, 64])
            gE = sbp("gE", [128, 64])
            gS = sbp("gS", [128, 64])
            gSm = sbp("gSm", [128, 64])
            gm4 = sbp("gm4", [128, 8])
            gma = sbp("gma", [128, 16])
            gmb = sbp("gmb", [128, 16])
            gr4 = sbp("gr4", [128, 4])
            gctm = sbp("gctm", [128, 64], BF16)
            gcfm = sbp("gcfm", [16, 512], BF16)
            c_sele = sbp("c_sele", [16, NE * 128], BF16)
            c_id = sbp("c_id", [128, 128], BF16)
            sq(lambda e: e.dma_start(c_sele[:, :], dsele), w=[(c_sele, None)])
            sq(lambda e: e.dma_start(c_id[:, :], dident), w=[(c_id, None)])
            w2f = [sbp(f"w2f{i}", [128, 4 * D], BF16) for i in range(2)]
            seq = [(t_, e_) for t_ in range(n5) for e_ in range(NEX)]
            wbuf = lambda gi: (W0, W1)[gi % 2]
            ldst(wbuf(0), ("w13", seq[0][1]))
            tk.fence("pool")

            def gate(tt, xn):
                pg = ps8[4]
                for tbl in range(4):
                    for cb in range(PB):
                        pe(lambda t, o=pg[:, tbl * 16:(tbl + 1) * 16],
                           l=xn[:, cb * 512 + tbl * 128: cb * 512 + (tbl + 1) * 128],
                           r_=gwS[:, cb * 16:(cb + 1) * 16],
                           st=(cb == 0 and tbl == 0), sp=(cb == PB - 1): t.matmul(
                               o, l, r_, start=st, stop=sp, skip_group_check=True),
                           r=[(xn, ("p", cb)), (gwS, None)], w=[(pg, None)],
                           inc=(cb == PB - 1))
                v3 = lambda a: a.rearrange("p (t e) -> p t e", t=4)
                v16 = lambda a: a.rearrange("p (tg e) -> p tg e", tg=16)
                bc4 = lambda a: a.unsqueeze(2).broadcast_to((128, 4, 16))
                bc16 = lambda a: a.rearrange("p (t g) -> p t g", t=4)
                dve(lambda v: v.tensor_copy(gL[:, :], pg[:, 0:64]),
                    r=[(pg, None)], w=[(gL, None)])
                dve(lambda v: v.reduce_max(gm4[:, 0:4], v3(gL[:, :]), axis=AX.X),
                    r=[(gL, None)], w=[(gm4, 0)])
                dve(lambda v: v.tensor_tensor(v3(gE[:, :]), v3(gL[:, :]),
                                              bc4(gm4[:, 0:4]), OP.subtract),
                    r=[(gL, None), (gm4, 0)], w=[(gE, None)])
                acte(lambda s: s.activation(gE[:, :], gE[:, :], AF.Exp),
                     r=[(gE, None)], w=[(gE, None)])
                dve(lambda v: v.reduce_sum(gm4[:, 4:8], v3(gE[:, :]), axis=AX.X),
                    r=[(gE, None)], w=[(gm4, 1)])
                dve(lambda v: v.reciprocal(gr4[:, :], gm4[:, 4:8]),
                    r=[(gm4, 1)], w=[(gr4, None)])
                dve(lambda v: v.tensor_tensor(v3(gS[:, :]), v3(gE[:, :]),
                                              bc4(gr4[:, :]), OP.mult),
                    r=[(gE, None), (gr4, None)], w=[(gS, None)])
                bcg = lambda a: a.unsqueeze(2).broadcast_to((128, 4, 4))
                dve(lambda v: v.reduce_max(bc16(gma[:, :]), v16(gS[:, :]), axis=AX.X),
                    r=[(gS, None)], w=[(gma, None)])
                dve(lambda v: v.reduce_max(gr4[:, :], bc16(gma[:, :]), axis=AX.X),
                    r=[(gma, None)], w=[(gr4, None)])
                dve(lambda v: v.tensor_tensor(
                    bc16(gmb[:, :]), bc16(gma[:, :]), bcg(gr4[:, :]), OP.is_equal),
                    r=[(gma, None), (gr4, None)], w=[(gmb, None)])
                dve(lambda v: v.scalar_tensor_tensor(
                    gmb[:, :], gmb[:, :], -1e30, gma[:, :], OP.mult, OP.add),
                    r=[(gmb, None), (gma, None)], w=[(gmb, None)])
                dve(lambda v: v.reduce_max(gr4[:, :], bc16(gmb[:, :]), axis=AX.X),
                    r=[(gmb, None)], w=[(gr4, None)])
                dve(lambda v: v.tensor_tensor(
                    bc16(gmb[:, :]), bc16(gma[:, :]), bcg(gr4[:, :]), OP.is_ge),
                    r=[(gma, None), (gr4, None)], w=[(gmb, None)])
                dve(lambda v: v.tensor_tensor(
                    v16(gSm[:, :]), v16(gS[:, :]),
                    gmb[:, :].unsqueeze(2).broadcast_to((128, 16, 4)), OP.mult),
                    r=[(gS, None), (gmb, None)], w=[(gSm, None)])
                dve(lambda v: v.tensor_copy(gE[:, :], gSm[:, :]),
                    r=[(gSm, None)], w=[(gE, None)])
                for it in range(4):
                    dve(lambda v: v.reduce_max(gm4[:, 0:4], v3(gE[:, :]), axis=AX.X),
                        r=[(gE, None)], w=[(gm4, 0)])
                    if it < 3:
                        dve(lambda v: v.tensor_tensor(
                            v3(gL[:, :]), v3(gE[:, :]), bc4(gm4[:, 0:4]),
                            OP.is_equal),
                            r=[(gE, None), (gm4, 0)], w=[(gL, None)])
                        dve(lambda v: v.scalar_tensor_tensor(
                            gE[:, :], gL[:, :], -1e30, gE[:, :], OP.mult, OP.add),
                            r=[(gL, None), (gE, None)], w=[(gE, None)])
                dve(lambda v: v.tensor_tensor(v3(gL[:, :]), v3(gSm[:, :]),
                                              bc4(gm4[:, 0:4]), OP.is_ge),
                    r=[(gSm, None), (gm4, 0)], w=[(gL, None)])
                dve(lambda v: v.tensor_tensor(gctm[:, :], gS[:, :], gL[:, :],
                                              OP.mult),
                    r=[(gS, None), (gL, None)], w=[(gctm, None)])
                for tbl in range(4):
                    pe(lambda t, tbl=tbl: t.transpose(
                        ps8[5][:, :].bitcast(BF16)[0:16, tbl * 128:(tbl + 1) * 128],
                        gctm[:, tbl * 16:(tbl + 1) * 16], c_id[:, :]),
                       r=[(gctm, None), (c_id, None)], w=[(ps8[5], None)])
                    dve(lambda v, tbl=tbl: v.tensor_copy(
                        gcfm[:, tbl * 128:(tbl + 1) * 128],
                        ps8[5][:, :].bitcast(BF16)[0:16, tbl * 128:(tbl + 1) * 128]),
                        r=[(ps8[5], None)], w=[(gcfm, ("t", tbl))])

            gi = 0
            for tt in range(n5):
                xn = xn2[tt % 2]
                ln_tile(x8, tt, lambda p: (xn[:, p * 512:(p + 1) * 512], xn,
                                           ("p", p)), lns, "ln2")
                gate(tt, xn)
                for e in range(NE):
                    pc = ps8[4 + e % 4]
                    pe(lambda t, e=e, pc=pc: t.matmul(
                        pc[:, :], c_sele[:, e * 128:(e + 1) * 128], gcfm[:, :],
                        start=True, stop=True),
                       r=[(c_sele, None)] + [(gcfm, ("t", j)) for j in range(4)],
                       w=[(pc, None)])
                    acte(lambda s, e=e, pc=pc: s.activation(
                        call[:, e * 512:(e + 1) * 512], pc[:, :], AF.Identity),
                        r=[(pc, None)], w=[(call, ("e", e))])
                for chunk in FFN_CHUNKS:
                    ldst(w2f[0], ("w2", chunk[0]))
                    if len(chunk) > 1:
                        ldst(w2f[1], ("w2", chunk[1]))
                    for ei, e in enumerate(chunk):
                        wb = wbuf(gi)
                        for ib in range(4):
                            for cb in range(PB):
                                pe(lambda t, o=ps8[ib][:, :],
                                   l=wb[:, cb * D + ib * 128: cb * D + (ib + 1) * 128],
                                   r_=xn[:, cb * 512:(cb + 1) * 512],
                                   st=(cb == 0), sp=(cb == PB - 1): t.matmul(
                                       o, l, r_, start=st, stop=sp),
                                   r=[(wb, ("ld", 0)), (xn, ("p", cb))],
                                   w=[(ps8[ib], None)], inc=(cb == PB - 1))
                            for cb in range(PB):
                                pe(lambda t, o=ps8[4 + ib][:, :],
                                   l=wb[:, cb * D + 512 + ib * 128:
                                          cb * D + 512 + (ib + 1) * 128],
                                   r_=xn[:, cb * 512:(cb + 1) * 512],
                                   st=(cb == 0), sp=(cb == PB - 1): t.matmul(
                                       o, l, r_, start=st, stop=sp),
                                   r=[(wb, ("ld", 0)), (xn, ("p", cb))],
                                   w=[(ps8[4 + ib], None)], inc=(cb == PB - 1))
                        if gi + 1 < len(seq):
                            ldst(wbuf(gi + 1), ("w13", seq[gi + 1][1]))
                        gi += 1
                        for ib in range(4):
                            acte(lambda s, ib=ib: s.activation(
                                sil[:, ib * 512:(ib + 1) * 512], ps8[ib][:, :],
                                AF.Silu),
                                r=[(ps8[ib], None)], w=[(sil, ("i", ib))])
                            if e < NE:
                                dve(lambda v, ib=ib, e=e: v.tensor_tensor(
                                    h3c[:, ib * 512:(ib + 1) * 512], ps8[4 + ib][:, :],
                                    call[:, e * 512:(e + 1) * 512], OP.mult),
                                    r=[(ps8[4 + ib], None), (call, ("e", e))],
                                    w=[(h3c, ("i", ib))])
                                dve(lambda v, ib=ib, ei=ei: v.tensor_tensor(
                                    asl[:, (ei * 4 + ib) * 512:(ei * 4 + ib + 1) * 512],
                                    sil[:, ib * 512:(ib + 1) * 512],
                                    h3c[:, ib * 512:(ib + 1) * 512], OP.mult),
                                    r=[(sil, ("i", ib)), (h3c, ("i", ib))],
                                    w=[(asl, ("a", ei * 4 + ib))])
                            else:
                                dve(lambda v, ib=ib, ei=ei: v.tensor_tensor(
                                    asl[:, (ei * 4 + ib) * 512:(ei * 4 + ib + 1) * 512],
                                    sil[:, ib * 512:(ib + 1) * 512], ps8[4 + ib][:, :],
                                    OP.mult),
                                    r=[(sil, ("i", ib)), (ps8[4 + ib], None)],
                                    w=[(asl, ("a", ei * 4 + ib))])
                    for ei, e in enumerate(chunk):
                        wb2 = w2f[ei % 2]
                        for db in range(PB):
                            for ib in range(4):
                                pe(lambda t, o=ps8[db][:, :],
                                   l=wb2[:, ib * D + db * 128: ib * D + (db + 1) * 128],
                                   r_=asl[:, (ei * 4 + ib) * 512:
                                          (ei * 4 + ib + 1) * 512],
                                   st=(ei == 0 and ib == 0),
                                   sp=(ei == len(chunk) - 1 and ib == 3): t.matmul(
                                       o, l, r_, start=st, stop=sp,
                                       skip_group_check=True),
                                   r=[(wb2, ("ld", 0)), (asl, ("a", ei * 4 + ib))],
                                   w=[(ps8[db], None)],
                                   inc=(ib == 3))
                        if ei + 2 < len(chunk):
                            ldst(w2f[ei % 2], ("w2", chunk[ei + 2]))
                    for db in range(PB):
                        xtags = [(x8[db], ("c", tt * 4 + j)) for j in range(4)]
                        dve(lambda v, db=db, tt=tt: v.tensor_tensor(
                            x8[db][:, tt * 512:(tt + 1) * 512],
                            x8[db][:, tt * 512:(tt + 1) * 512], ps8[db][:, :], OP.add),
                            r=xtags + [(ps8[db], None)], w=xtags)
            tk.barrier()
            tk.drop([lns[k] for k in lns] + xn2 + w2f +
                    [asl, sil, h3c, call, gL, gE, gS, gSm, gm4, gma, gmb, gr4,
                     gctm, gcfm, c_sele, c_id])

    def attn_b(b):
        EUC = 32

        def mixes(ph_sbp, names):
            out = {}
            for nmx in names:
                out[nmx] = ph_sbp("mx_" + nmx, [128, PB * 256], BF16)
            return out

        def mix_tile(dst, src_xn, tmcol, omcol, pb, tti):
            t0 = tti * 256
            a_dst = dst[:, pb * 256: pb * 256 + 256]
            xtag = [(src_xn, ("x", pb, (t0 + j * 128) // 512)) for j in (0, 1)]
            dve(lambda v, o=a_dst,
                i=src_xn[:, pb * T + t0: pb * T + t0 + 256],
                s=pv[:, tmcol + pb: tmcol + pb + 1]: v.tensor_scalar(
                    o, i, s, 0.0, OP.mult, OP.add),
                r=xtag, w=[(dst, ("p", pb))])
            if tti == 0:
                lo = 1
            else:
                lo = 0
                xtag = xtag + [(src_xn, ("x", pb, (t0 - 1) // 512))]
            dve(lambda v, lo=lo,
                o=dst[:, pb * 256 + max(lo, 1) - (0 if lo else 0):][:, :0] if False
                else dst[:, pb * 256 + (1 if tti == 0 else 0): pb * 256 + 256],
                i0=src_xn[:, pb * T + t0 + (0 if tti else 0) - (0 if tti == 0 else 1):][:, :0] if False
                else src_xn[:, pb * T + t0 - (0 if tti == 0 else 1):
                            pb * T + t0 + 255],
                s=dpar[:, omcol + pb: omcol + pb + 1],
                i1=dst[:, pb * 256 + (1 if tti == 0 else 0): pb * 256 + 256]:
                v.scalar_tensor_tensor(o, i0, s, i1, OP.mult, OP.add),
                r=xtag + [(dpar, ("c", omcol)), (dst, ("p", pb))],
                w=[(dst, ("p", pb))])

        with ExitStack() as ph:
            sbp = lambda n, s, dt=F32: ph.enter_context(
                nc.sbuf_tensor(_nm(n), s, dt))
            xn3 = sbp("xn3", [128, PB * T], BF16)
            wkva = sbp("wkva", [128, PB * T], BF16)
            ldst(W0, ("p", "wkey"))
            ldst(W1, ("p", "wval"))
            with ExitStack() as p1:
                s1p = lambda n, s, dt=F32: p1.enter_context(
                    nc.sbuf_tensor(_nm(n), s, dt))
                with ExitStack() as lnes:
                    lnp = lambda n, s, dt=F32: lnes.enter_context(
                        nc.sbuf_tensor(_nm(n), s, dt))
                    lns = mk_ln_scratch(lnp)
                    for tt5 in range(n5):
                        ln_tile(x8, tt5,
                                lambda p, tt5=tt5: (
                                    xn3[:, p * T + tt5 * 512: p * T + (tt5 + 1) * 512],
                                    xn3, ("x", p, tt5)), lns, "ln3")
                    tk.barrier()
                    tk.drop([lns[k] for k in lns])
                mx = mixes(s1p, ("xk",))
                ekb2 = [s1p(f"ekb{i}", [128, PB * 256], BF16) for i in range(2)]
                ekvb2 = [s1p(f"ekvb{i}", [128, PB * 256], BF16) for i in range(2)]
                Pb8 = s1p("Pb8", [128, PB * 257])
                Qb8 = s1p("Qb8", [128, PB * 257])
                Pst = s1p("Pst", [128, PB])
                Qst = s1p("Qst", [128, PB])
                for p in range(PB):
                    dve(lambda v, p=p: v.memset(Pst[:, p:p + 1], 0.0),
                        r=[], w=[(Pst, ("p", p))])
                    dve(lambda v, p=p: v.memset(Qst[:, p:p + 1], 0.0),
                        r=[], w=[(Qst, ("p", p))])
                v3e = lambda a: a.rearrange("p (b n) -> p b n", b=PB)
                P3 = Pb8[:, :].rearrange("p (b n) -> p b n", b=PB)[:, :, 0:256]
                Q3 = Qb8[:, :].rearrange("p (b n) -> p b n", b=PB)[:, :, 0:256]

                def chain(tti):
                    ekb, ekvb = ekb2[tti % 2], ekvb2[tti % 2]
                    for p in range(PB):
                        dve(lambda v, p=p: v.tensor_copy(
                            Pb8[:, p * 257: p * 257 + 1], Pst[:, p:p + 1]),
                            r=[(Pst, ("p", p))], w=[(Pb8, ("p", p))])
                        dve(lambda v, p=p, ekvb=ekvb: v.tensor_tensor_scan(
                            Pb8[:, p * 257 + 1: p * 257 + 257],
                            dpar[:, EW + p:EW + p + 1].broadcast_to((128, 256)),
                            ekvb[:, p * 256:(p + 1) * 256], Pst[:, p:p + 1],
                            OP.mult, OP.add),
                            r=[(dpar, ("c", EW)), (ekvb, ("p", p)),
                               (Pst, ("p", p)), (Pb8, ("p", p))],
                            w=[(Pb8, ("p", p))])
                        dve(lambda v, p=p: v.tensor_copy(
                            Qb8[:, p * 257: p * 257 + 1], Qst[:, p:p + 1]),
                            r=[(Qst, ("p", p))], w=[(Qb8, ("p", p))])
                        dve(lambda v, p=p, ekb=ekb: v.tensor_tensor_scan(
                            Qb8[:, p * 257 + 1: p * 257 + 257],
                            dpar[:, EW + p:EW + p + 1].broadcast_to((128, 256)),
                            ekb[:, p * 256:(p + 1) * 256], Qst[:, p:p + 1],
                            OP.mult, OP.add),
                            r=[(dpar, ("c", EW)), (ekb, ("p", p)),
                               (Qst, ("p", p)), (Qb8, ("p", p))],
                            w=[(Qb8, ("p", p))])
                        dve(lambda v, p=p: v.tensor_copy(
                            Pst[:, p:p + 1], Pb8[:, p * 257 + 256: p * 257 + 257]),
                            r=[(Pb8, ("p", p))], w=[(Pst, ("p", p))])
                        dve(lambda v, p=p: v.tensor_copy(
                            Qst[:, p:p + 1], Qb8[:, p * 257 + 256: p * 257 + 257]),
                            r=[(Qb8, ("p", p))], w=[(Qst, ("p", p))])
                    for p in range(PB):
                        dve(lambda v, p=p, ekvb=ekvb: v.scalar_tensor_tensor(
                            Pb8[:, p * 257: p * 257 + 256],
                            ekvb[:, p * 256:(p + 1) * 256],
                            dpar[:, EUC + p:EUC + p + 1],
                            Pb8[:, p * 257: p * 257 + 256], OP.mult, OP.add),
                            r=[(ekvb, ("p", p)), (dpar, ("c", 32)),
                               (Pb8, ("p", p))], w=[(Pb8, ("p", p))])
                        dve(lambda v, p=p, ekb=ekb: v.scalar_tensor_tensor(
                            Qb8[:, p * 257: p * 257 + 256],
                            ekb[:, p * 256:(p + 1) * 256],
                            dpar[:, EUC + p:EUC + p + 1],
                            Qb8[:, p * 257: p * 257 + 256], OP.mult, OP.add),
                            r=[(ekb, ("p", p)), (dpar, ("c", 32)),
                               (Qb8, ("p", p))], w=[(Qb8, ("p", p))])
                    allP = [(Pb8, ("p", p)) for p in range(PB)]
                    allQ = [(Qb8, ("p", p)) for p in range(PB)]
                    acte(lambda s: s.activation(Q3, Q3, AF.Ln), r=allQ, w=allQ)
                    acte(lambda s: s.activation(Q3, Q3, AF.Exp, scale=-1.0),
                         r=allQ, w=allQ)
                    dve(lambda v, tti=tti: v.tensor_tensor(
                        wkva[:, :].rearrange("p (b n) -> p b n", b=PB)[
                            :, :, tti * 256:(tti + 1) * 256],
                        P3, Q3, OP.mult),
                        r=allP + allQ, w=[(wkva, ("t", tti))])

                for tti in range(n2):
                    for p in range(PB):
                        mix_tile(mx["xk"], xn3, TMK, OTMK, p, tti)
                    for db in range(PB):
                        pk = ps8[db // 2][:, (db % 2) * 256:(db % 2) * 256 + 256]
                        for cb in range(PB):
                            pe(lambda t, o=pk,
                               l=W0[:, cb * D + db * 128: cb * D + (db + 1) * 128],
                               r_=mx["xk"][:, cb * 256:(cb + 1) * 256],
                               st=(cb == 0 and db % 2 == 0), sp=(cb == PB - 1):
                               t.matmul(o, l, r_, start=st, stop=False,
                                        skip_group_check=True),
                               r=[(W0, ("ld", 0)), (mx["xk"], ("p", cb))],
                               w=[(ps8[db // 2], None)], inc=(cb == PB - 1))
                    for p in range(PB):
                        mix_tile(mx["xk"], xn3, TMV, OTMV, p, tti)
                    for db in range(PB):
                        pvv = ps8[4 + db // 2][:, (db % 2) * 256:(db % 2) * 256 + 256]
                        for cb in range(PB):
                            pe(lambda t, o=pvv,
                               l=W1[:, cb * D + db * 128: cb * D + (db + 1) * 128],
                               r_=mx["xk"][:, cb * 256:(cb + 1) * 256],
                               st=(cb == 0 and db % 2 == 0), sp=(cb == PB - 1):
                               t.matmul(o, l, r_, start=st, stop=False,
                                        skip_group_check=True),
                               r=[(W1, ("ld", 0)), (mx["xk"], ("p", cb))],
                               w=[(ps8[4 + db // 2], None)], inc=(cb == PB - 1))
                    ekb, ekvb = ekb2[tti % 2], ekvb2[tti % 2]
                    for p in range(PB):
                        kps = ps8[p // 2][:, (p % 2) * 256:(p % 2) * 256 + 256]
                        vps = ps8[4 + p // 2][:, (p % 2) * 256:(p % 2) * 256 + 256]
                        acte(lambda s, kps=kps, p=p, ekb=ekb: s.activation(
                            ekb[:, p * 256:(p + 1) * 256], kps, AF.Exp),
                            r=[(ps8[p // 2], None)], w=[(ekb, ("p", p))])
                        dve(lambda v, vps=vps, p=p, ekb=ekb, ekvb=ekvb:
                            v.tensor_tensor(
                                ekvb[:, p * 256:(p + 1) * 256],
                                ekb[:, p * 256:(p + 1) * 256], vps, OP.mult),
                            r=[(ekb, ("p", p)), (ps8[4 + p // 2], None)],
                            w=[(ekvb, ("p", p))])
                    if tti > 0:
                        chain(tti - 1)
                chain(n2 - 1)
                tk.barrier()
                tk.drop(list(mx.values()) + ekb2 + ekvb2 +
                        [Pb8, Qb8, Pst, Qst])
            ldst(W0, ("p", "wrec"))
            ldst(W1, ("p", "wout"))
            with ExitStack() as p2:
                s2p = lambda n, s, dt=F32: p2.enter_context(
                    nc.sbuf_tensor(_nm(n), s, dt))
                mx2 = mixes(s2p, ("xr",))
                erb = s2p("erb", [128, PB * 256])
                yt = [s2p(f"yt{i}", [128, PB * 256], BF16) for i in range(2)]
                v3e = lambda a: a.rearrange("p (b n) -> p b n", b=PB)

                def outproj(tti):
                    ytb = yt[tti % 2]
                    for db in range(PB):
                        po = ps8[4 + db // 2][:, (db % 2) * 256:(db % 2) * 256 + 256]
                        for cb in range(PB):
                            pe(lambda t, o=po,
                               l=W1[:, cb * D + db * 128: cb * D + (db + 1) * 128],
                               r_=ytb[:, cb * 256:(cb + 1) * 256],
                               st=(cb == 0 and db % 2 == 0), sp=(cb == PB - 1):
                               t.matmul(o, l, r_, start=st, stop=False,
                                        skip_group_check=True),
                               r=[(W1, ("ld", 0)), (ytb, ("p", cb))],
                               w=[(ps8[4 + db // 2], None)], inc=(cb == PB - 1))
                        xtags = [(x8[db], ("c", tti * 2 + j)) for j in range(2)]
                        dve(lambda v, db=db, tti=tti, po=po: v.tensor_tensor(
                            x8[db][:, tti * 256:(tti + 1) * 256],
                            x8[db][:, tti * 256:(tti + 1) * 256], po, OP.add),
                            r=xtags + [(ps8[4 + db // 2], None)], w=xtags)

                for tti in range(n2):
                    for p in range(PB):
                        mix_tile(mx2["xr"], xn3, TMR, OTMR, p, tti)
                    for db in range(PB):
                        pr = ps8[db // 2][:, (db % 2) * 256:(db % 2) * 256 + 256]
                        for cb in range(PB):
                            pe(lambda t, o=pr,
                               l=W0[:, cb * D + db * 128: cb * D + (db + 1) * 128],
                               r_=mx2["xr"][:, cb * 256:(cb + 1) * 256],
                               st=(cb == 0 and db % 2 == 0), sp=(cb == PB - 1):
                               t.matmul(o, l, r_, start=st, stop=False,
                                        skip_group_check=True),
                               r=[(W0, ("ld", 0)), (mx2["xr"], ("p", cb))],
                               w=[(ps8[db // 2], None)], inc=(cb == PB - 1))
                    for p in range(PB):
                        rps = ps8[p // 2][:, (p % 2) * 256:(p % 2) * 256 + 256]
                        acte(lambda s, rps=rps, p=p: s.activation(
                            erb[:, p * 256:(p + 1) * 256], rps, AF.Exp, scale=-1.0),
                            r=[(ps8[p // 2], None)], w=[(erb, ("p", p))])
                    alle = [(erb, ("p", p)) for p in range(PB)]
                    dve(lambda v: v.tensor_scalar(erb[:, :], erb[:, :], 1.0, 0.0,
                                                  OP.add, OP.add),
                        r=alle, w=alle)
                    acte(lambda s: s.activation(erb[:, :], erb[:, :], AF.Ln),
                         r=alle, w=alle)
                    acte(lambda s: s.activation(erb[:, :], erb[:, :], AF.Exp,
                                                scale=-1.0),
                         r=alle, w=alle)
                    dve(lambda v, tti=tti, ytb=yt[tti % 2]: v.tensor_tensor(
                        v3e(ytb[:, :]),
                        wkva[:, :].rearrange("p (b n) -> p b n", b=PB)[
                            :, :, tti * 256:(tti + 1) * 256],
                        v3e(erb[:, :]), OP.mult),
                        r=[(wkva, ("t", tti))] + alle,
                        w=[(yt[tti % 2], ("p", p)) for p in range(PB)])
                    if tti > 0:
                        outproj(tti - 1)
                outproj(n2 - 1)
                tk.barrier()
                tk.drop(list(mx2.values()) + [erb] + yt)
            tk.barrier()
            tk.drop([xn3, wkva])

    def tap(name, b):
        if taps:
            for p in range(PB):
                sq(lambda e, p=p: e.dma_start(dtap[name][b, p], x8[p][:, :]),
                   r=[(x8[p], None)])

    for b in range(NB):
        if b > 0:
            for p in range(PB):
                sq(lambda e, p=p, b=b: e.dma_start(x8[p][:, :], dx[b, p]),
                   w=[(x8[p], None)])
        attn_a(b)
        tap("attna", b)
        ffn(b, 0)
        tap("ffn1", b)
        attn_b(b)
        tap("attnb", b)
        ffn(b, 1)
        tap("ffn2", b)
        for p in range(PB):
            sq(lambda e, p=p, b=b: e.dma_start(dout[b, p], x8[p][:, :]),
               r=[(x8[p], None)])

    for i in range(tk.NDMA):
        for q, strm in (("sq", "sp"), ("gq", "pool")):
            n = tk.cnt[f"{q}{i}"]
            if n:
                tk.ops[strm].append(
                    lambda e, s=tk.sem[f"{q}{i}"], n=n: e.wait_ge(s, n))

    with es:
        with nc.Block() as block:
            @block.tensor
            def _(t):
                for f in tk.ops["pe"]:
                    f(t)

            @block.scalar
            def _(s):
                for f in tk.ops["act"]:
                    f(s)

            @block.vector
            def _(v):
                for f in tk.ops["dve"]:
                    f(v)

            @block.gpsimd
            def _(g):
                for f in tk.ops["pool"]:
                    f(g)

            @block.sync
            def _(s):
                for f in tk.ops["sp"]:
                    f(s)
    return nc


def _pack_inmap(T=2048):
    raise NotImplementedError


def _pk_proj(w):
    return np.ascontiguousarray(
        w.reshape(8, 128, w.shape[1]).transpose(1, 0, 2).reshape(128, -1))


def host_pack(inp, T):
    import ml_dtypes
    f32 = lambda a: np.asarray(a, dtype=np.float32)
    wq = _pk_proj(f32(inp["L0_attn_a_Wq"]))
    wk = _pk_proj(f32(inp["L0_attn_a_Wk"]))
    wv = _pk_proj(f32(inp["L0_attn_a_Wv"]))
    wo = _pk_proj(f32(inp["L0_attn_a_Wo"]))
    wkey = _pk_proj(f32(inp["L0_attn_b_W_key"]).T)
    wval = _pk_proj(f32(inp["L0_attn_b_W_value"]).T)
    wrec = _pk_proj(f32(inp["L0_attn_b_W_receptance"]).T)
    wout = _pk_proj(f32(inp["L0_attn_b_W_output"]).T)

    w1 = f32(inp["L0_ffn_w1"])
    w2 = f32(inp["L0_ffn_w2"])
    w3 = f32(inp["L0_ffn_w3"])
    ws1 = f32(inp["L0_ffn_ws1"])
    ws2 = f32(inp["L0_ffn_ws2"])
    ws3 = f32(inp["L0_ffn_ws3"])
    w13 = np.zeros((NEX, 128, PB * D), np.float32)
    w2p = np.zeros((NEX, 128, 4 * D), np.float32)
    for e in range(NE):
        w13[e] = _pk_proj(np.concatenate([w1[e], w3[e]], axis=1))
        w2p[e] = np.ascontiguousarray(
            w2[e].reshape(4, 128, D).transpose(1, 0, 2).reshape(128, 4 * D))
    for j in range(2):
        w13[NE + j] = _pk_proj(np.concatenate(
            [ws1[:, j * 512:(j + 1) * 512], ws3[:, j * 512:(j + 1) * 512]], axis=1))
        w2p[NE + j] = np.ascontiguousarray(
            ws2[j * 512:(j + 1) * 512].reshape(4, 128, D).transpose(1, 0, 2)
            .reshape(128, 4 * D))
    gw = _pk_proj(f32(inp["L0_ffn_gate_w"]).T)

    fmv = lambda a: np.ascontiguousarray(f32(a).reshape(8, 128).T)
    pvec = np.zeros((128, 80), np.float32)
    pvec[:, 0:8] = fmv(inp["L0_attn_a_bq"])
    pvec[:, 8:16] = fmv(inp["L0_attn_a_bk"])
    pvec[:, 16:24] = fmv(inp["L0_attn_a_bv"])
    pvec[:, 24:32] = fmv(inp["L0_attn_a_bo"])
    pvec[:, 32:40] = fmv(inp["L0_attn_b_time_first"])
    pvec[:, 40:48] = fmv(inp["L0_attn_b_time_decay"])
    pvec[:, 48:56] = fmv(np.asarray(inp["L0_attn_b_time_mix_k"]).reshape(-1))
    pvec[:, 56:64] = fmv(np.asarray(inp["L0_attn_b_time_mix_v"]).reshape(-1))
    pvec[:, 64:72] = fmv(np.asarray(inp["L0_attn_b_time_mix_r"]).reshape(-1))
    pvec[0:16, 72] = f32(inp["L0_attn_a_m"])
    brow = np.concatenate([f32(inp["L0_attn_a_bk"]),
                           f32(inp["L0_attn_a_bv"])])[None, :]

    bf = ml_dtypes.bfloat16
    ones_k1 = np.ones((1, 128), np.float32)
    onescol_b = np.ones((128, 1), bf)
    bd = np.zeros((128, PB * HEAD), bf)
    selh = np.zeros((16, PB * 128), bf)
    for p in range(PB):
        bd[0:64, p * 16 + 2 * p] = 1.0
        bd[64:128, p * 16 + 2 * p + 1] = 1.0
        selh[2 * p, p * 128: p * 128 + 64] = 1.0
        selh[2 * p + 1, p * 128 + 64: p * 128 + 128] = 1.0
    sel_e = np.zeros((16, NE * 128), bf)
    for e in range(NE):
        sel_e[e, e * 128:(e + 1) * 128] = 1.0
    ident_b = np.eye(128, dtype=bf)

    cvals = np.zeros((128, 8), np.float32)
    for i, v in enumerate([1e-5, -0.5, -1.0, -float(np.log(T))]):
        cvals[:, i] = v

    return dict(wq=wq, wk=wk, wv=wv, wo=wo, wkey=wkey, wval=wval, wrec=wrec,
                wout=wout, w13=w13, w2=w2p, gw=gw, pvec=pvec, brow=brow,
                ones_k1=ones_k1, onescol_b=onescol_b, bd=bd, selh=selh,
                sel_e=sel_e, ident_b=ident_b, cvals=cvals)


def pack_x(xbatches, T):
    return np.stack([np.ascontiguousarray(b.T).reshape(8, 128, T)
                     for b in xbatches])


def unpack_out(o, T):
    return np.stack([o[b].reshape(D, T).T for b in range(o.shape[0])])


def kernel(**inp):
    x = np.asarray(inp["L0_attn_a_x"], dtype=np.float32)
    T = x.shape[1]
    shared = host_pack(inp, T)
    NBc = x.shape[0] // 8
    in_maps = []
    for c in range(8):
        m = dict(shared)
        m["x"] = pack_x([x[c * NBc + b] for b in range(NBc)], T)
        in_maps.append(m)

    nc = build_nc(T=T, NB=NBc)
    res = run_bass_kernel_spmd(nc, in_maps, core_ids=list(range(8)))
    out = np.empty((8 * NBc, T, D), np.float32)
    for c in range(8):
        out[c * NBc:(c + 1) * NBc] = unpack_out(res.results[c]["out"], T)
    return out


# version 08
# speedup vs baseline: 1.1513x; 1.0232x over previous
import numpy as np
from contextlib import ExitStack

import concourse.bass as bass
import concourse.mybir as mybir
from concourse.bass_utils import run_bass_kernel_spmd

F32 = mybir.dt.float32
BF16 = mybir.dt.bfloat16
AF = mybir.ActivationFunctionType
OP = mybir.AluOpType
AX = mybir.AxisListType

D = 1024
PB = 8
NE = 16
NEX = 18
HEAD = 16
HD = 64
FFN_CHUNKS = ((0, 1, 2, 3), (4, 5, 6, 7), (8, 9, 10, 11), (12, 13, 14, 15),
              (16, 17))


class _Rec:
    __slots__ = ("src", "cnt")

    def __init__(self, src, cnt):
        self.src = src
        self.cnt = cnt


class TK:
    STREAMS = ("pe", "act", "dve", "pool", "sp")
    ENGS = ("pe", "act", "dve", "pool")
    NDMA = 8

    def __init__(self, nc, es):
        self.nc = nc
        self.psum_ids = set()
        self.ops = {s: [] for s in self.STREAMS}
        self.sem = {s: es.enter_context(nc.semaphore("sem_" + s)) for s in self.ENGS}
        for q in ("gq", "sq"):
            for i in range(self.NDMA):
                self.sem[f"{q}{i}"] = es.enter_context(nc.semaphore(f"sem_{q}{i}"))
        self.cnt = {s: 0 for s in self.sem}
        self.rr = {"gq": 0, "sq": 0}
        self.waited = {c: {s: 0 for s in self.sem} for c in self.STREAMS}
        self.floor = {c: {} for c in self.STREAMS}
        self.bufs = {}
        self.pe_pending = []

    def _deps(self, h, tag, readers):
        recs = self.bufs.get(id(h))
        if not recs:
            return ()
        out = []
        for t2, r in recs.items():
            if tag is None or t2 is None or t2 == tag:
                if r["w"] is not None:
                    out.append(r["w"])
                if readers:
                    out.extend(r["r"].values())
        return out

    def emit(self, stream, fn, reads=(), writes=(), inc=True, dma=None):
        need = dict(self.floor[stream])
        self.floor[stream] = {}
        for h, tag in reads:
            psum_rr = id(h) in self.psum_ids
            for r in self._deps(h, tag, psum_rr):
                if r.src == "pe" and stream == "pe":
                    continue
                if psum_rr and r.src == stream:
                    continue
                assert r.cnt is not None, "dep on open PE group"
                if r.cnt > need.get(r.src, 0):
                    need[r.src] = r.cnt
        for h, tag in writes:
            for r in self._deps(h, tag, True):
                if r.src == "pe" and stream == "pe":
                    continue
                assert r.cnt is not None, "dep on open PE group"
                if r.cnt > need.get(r.src, 0):
                    need[r.src] = r.cnt
        sname = None
        if dma is not None:
            sname = f"{dma}{self.rr[dma]}"
            self.rr[dma] = (self.rr[dma] + 1) % self.NDMA
            if self.cnt[sname] > need.get(sname, 0):
                need[sname] = self.cnt[sname]
        waits = []
        wt = self.waited[stream]
        for src in sorted(need):
            if need[src] > wt[src]:
                waits.append((self.sem[src], need[src]))
                wt[src] = need[src]

        if dma is not None:
            self.cnt[sname] += 16
            rec = _Rec(sname, self.cnt[sname])
            inc_sem, inc_n = self.sem[sname], 16
        elif inc:
            self.cnt[stream] += 1
            rec = _Rec(stream, self.cnt[stream])
            inc_sem, inc_n = self.sem[stream], 1
            if stream == "pe":
                for p in self.pe_pending:
                    p.cnt = rec.cnt
                self.pe_pending = []
        else:
            assert stream == "pe"
            rec = _Rec("pe", None)
            self.pe_pending.append(rec)
            inc_sem, inc_n = None, 0

        def closure(eng, _w=tuple(waits), _f=fn, _s=inc_sem, _n=inc_n):
            for sem, cnt in _w:
                eng.wait_ge(sem, cnt)
            inst = _f(eng)
            if _s is not None:
                inst.then_inc(_s, _n)

        self.ops[stream].append(closure)

        for h, tag in writes:
            self.bufs.setdefault(id(h), {})[tag] = {"w": rec, "r": {}}
        for h, tag in reads:
            recs = self.bufs.setdefault(id(h), {})
            recs.setdefault(tag, {"w": None, "r": {}})["r"][rec.src] = rec
        return rec

    def barrier(self):
        for c in ("pe", "act", "dve", "sp"):
            f = self.floor[c]
            for s in ("pe", "act", "dve"):
                f[s] = max(f.get(s, 0), self.cnt[s])

    def fence(self, stream):
        f = self.floor[stream]
        for s in ("pe", "act", "dve"):
            f[s] = max(f.get(s, 0), self.cnt[s])

    def drop(self, handles):
        for h in handles:
            self.bufs.pop(id(h), None)


def build_nc(T=2048, NB=2, taps=False):
    nc = bass.Bass("TRN2", target_bir_lowering=False)
    n5 = T // 512
    n2 = T // 256
    LOGT = float(np.log(T))

    dx = nc.dram_tensor("x", [NB, PB, 128, T], F32, kind="ExternalInput").ap()
    dout = nc.dram_tensor("out", [NB, PB, 128, T], F32, kind="ExternalOutput").ap()
    dW = {}
    for nm in ("wq", "wk", "wv", "wo", "wkey", "wval", "wrec", "wout"):
        dW[nm] = nc.dram_tensor(nm, [128, PB * D], F32, kind="ExternalInput").ap()
    dw13 = nc.dram_tensor("w13", [NEX, 128, PB * D], F32, kind="ExternalInput").ap()
    dw2 = nc.dram_tensor("w2", [NEX, 128, 4 * D], F32, kind="ExternalInput").ap()
    dgw = nc.dram_tensor("gw", [128, PB * NE], F32, kind="ExternalInput").ap()
    dpv = nc.dram_tensor("pvec", [128, 80], F32, kind="ExternalInput").ap()
    dbrow = nc.dram_tensor("brow", [1, 2 * D], F32, kind="ExternalInput").ap()
    dones = nc.dram_tensor("ones_k1", [1, 128], F32, kind="ExternalInput").ap()
    donesb = nc.dram_tensor("onescol_b", [128, 1], BF16, kind="ExternalInput").ap()
    dbd = nc.dram_tensor("bd", [128, PB * HEAD], BF16, kind="ExternalInput").ap()
    dselh = nc.dram_tensor("selh", [16, PB * 128], BF16, kind="ExternalInput").ap()
    dsele = nc.dram_tensor("sel_e", [16, NE * 128], BF16, kind="ExternalInput").ap()
    dident = nc.dram_tensor("ident_b", [128, 128], BF16, kind="ExternalInput").ap()
    dcv = nc.dram_tensor("cvals", [128, 8], F32, kind="ExternalInput").ap()
    dtap = {}
    if taps:
        for nm in ("xn1", "attna", "ffn1", "attnb", "ffn2", "wkv"):
            dtap[nm] = nc.dram_tensor("tap_" + nm, [NB, PB, 128, T], F32,
                                      kind="ExternalOutput").ap()

    es = ExitStack()
    _uid = [0]

    def _nm(name):
        _uid[0] += 1
        return f"{name}_{_uid[0]}"

    sb = lambda name, shape, dt=F32: es.enter_context(nc.sbuf_tensor(name, shape, dt))
    tk = TK(nc, es)

    x8 = [sb(f"x{i}", [128, T]) for i in range(PB)]
    c_ones = sb("c_ones", [1, 128])
    c_onesb = sb("c_onesb", [128, 1], BF16)
    pv = sb("pv", [128, 80])
    dpar = sb("dpar", [128, 40])
    sgate = sb("sgate", [16, 1])
    gwS = sb("gwS", [128, PB * NE], BF16)
    msb = sb("msb", [128, 512], BF16)
    W0 = sb("W0", [128, PB * D], BF16)
    W1 = sb("W1", [128, PB * D], BF16)
    cvals = sb("cvals_s", [128, 8])
    ps8 = [es.enter_context(nc.psum_tensor(f"ps{i}", [128, 512], F32))
           for i in range(8)]
    tk.psum_ids = {id(p) for p in ps8}

    BQ, BK, BV, BO, UU, TD, TMK, TMV, TMR, MCOL = 0, 8, 16, 24, 32, 40, 48, 56, 64, 72
    OTMK, OTMV, OTMR, EW = 0, 8, 16, 24

    pe = lambda fn, r=(), w=(), inc=True: tk.emit("pe", fn, r, w, inc)
    pool = lambda fn, r=(), w=(): tk.emit("pool", fn, r, w)
    acte = lambda fn, r=(), w=(): tk.emit("act", fn, r, w)
    dve = lambda fn, r=(), w=(): tk.emit("dve", fn, r, w)
    gq = lambda fn, r=(), w=(): tk.emit("pool", fn, r, w, dma="gq")
    sq = lambda fn, r=(), w=(): tk.emit("sp", fn, r, w, dma="sq")

    sw13 = nc.dram_tensor("sw13", [NEX, 128, PB * D], BF16)
    sw2 = nc.dram_tensor("sw2", [NEX, 128, 4 * D], BF16)
    sproj = nc.dram_tensor("sproj", [8, 128, PB * D], BF16)
    _PROJ_IDX = {"wq": 0, "wk": 1, "wv": 2, "wo": 3, "wkey": 4, "wval": 5,
                 "wrec": 6, "wout": 7}
    _staged = {}

    def ldw(dst, src_ap, r0=0, r1=None):
        n = dst.shape[1] if r1 is None else (r1 - r0)
        gq(lambda g, d=dst[:, r0:r0 + n], s=src_ap[:, r0:r0 + n]: g.dma_start(d, s),
           w=[(dst, ("ld", r0))])

    def ldst(dst, key):
        if key[0] == "w13":
            slot, fsrc = sw13.ap()[key[1]], dw13[key[1]]
        elif key[0] == "w2":
            slot, fsrc = sw2.ap()[key[1]], dw2[key[1]]
        else:
            slot, fsrc = sproj.ap()[_PROJ_IDX[key[1]]], dW[key[1]]
        if key not in _staged:
            _staged[key] = True
            gq(lambda g, d=dst[:, :], s=fsrc: g.dma_start(d, s),
               w=[(dst, ("ld", 0))])
            sq(lambda e, d=slot, s=dst[:, :]: e.dma_start(d, s),
               r=[(dst, ("ld", 0))], w=[(sw13, key)])
        else:
            sq(lambda e, d=dst[:, :], s=slot: e.dma_start(d, s),
               r=[(sw13, key)], w=[(dst, ("ld", 0))])

    CV = [1e-5, -0.5, -1.0, -LOGT]
    for i, v in enumerate(CV):
        nc.const_aps.register(F32, v, cvals[:, i:i + 1])
    sq(lambda e: e.dma_start(cvals[:, :], dcv), w=[(cvals, None)])
    sq(lambda e: e.dma_start(c_ones[:, :], dones), w=[(c_ones, None)])
    sq(lambda e: e.dma_start(c_onesb[:, :], donesb), w=[(c_onesb, None)])
    sq(lambda e: e.dma_start(pv[:, :], dpv), w=[(pv, None)])
    gq(lambda e: e.dma_start(gwS[:, :], dgw), w=[(gwS, None)])
    for p in range(PB):
        sq(lambda e, p=p: e.dma_start(x8[p][:, :], dx[0, p]), w=[(x8[p], None)])

    for c in ("pe", "act", "dve"):
        for i in range(tk.NDMA):
            for q in ("sq", "gq"):
                if tk.cnt[f"{q}{i}"]:
                    tk.floor[c][f"{q}{i}"] = tk.cnt[f"{q}{i}"]
    for col, src in ((OTMK, TMK), (OTMV, TMV), (OTMR, TMR)):
        dve(lambda v, c=col, s=src: v.tensor_scalar(
            dpar[:, c:c + 8], pv[:, s:s + 8], -1.0, 1.0, OP.mult, OP.add),
            r=[(pv, None)], w=[(dpar, ("c", col))])
    acte(lambda s: s.activation(dpar[:, EW:EW + 8], pv[:, TD:TD + 8], AF.Exp),
         r=[(pv, None)], w=[(dpar, ("c", EW))])
    dve(lambda v: v.tensor_scalar(dpar[:, EW:EW + 8], dpar[:, EW:EW + 8],
                                  -1.0, 0.0, OP.mult, OP.bypass),
        r=[(dpar, ("c", EW))], w=[(dpar, ("c", EW))])
    acte(lambda s: s.activation(dpar[:, EW:EW + 8], dpar[:, EW:EW + 8], AF.Exp),
         r=[(dpar, ("c", EW))], w=[(dpar, ("c", EW))])
    acte(lambda s: s.activation(dpar[:, 32:40], pv[:, UU:UU + 8], AF.Exp),
         r=[(pv, None)], w=[(dpar, ("c", 32))])
    acte(lambda s: s.activation(sgate[:, :], pv[0:16, MCOL:MCOL + 1], AF.Sigmoid),
         r=[(pv, None)], w=[(sgate, None)])
    acte(lambda s: s.activation(sgate[:, :], sgate[:, :], AF.Exp, scale=-LOGT),
         r=[(sgate, None)], w=[(sgate, None)])

    def ln_tile(xsrc, tt, dst_fn, scratch, tag):
        t0 = tt * 512
        xc, xq, stats = scratch["xc"], scratch["xq"], scratch["stats"]
        for hh in range(4):
            for p in range(PB):
                s0 = t0 + hh * 128
                acte(lambda s, p=p, s0=s0: s.activation(
                    xc[:, p * 128:(p + 1) * 128], xsrc[p][:, s0:s0 + 128], AF.Identity),
                    r=[(xsrc[p], ("c", s0 // 128))], w=[(xc, ("p", p))])
                acte(lambda s, p=p, s0=s0: s.activation(
                    xq[:, p * 128:(p + 1) * 128], xsrc[p][:, s0:s0 + 128], AF.Square),
                    r=[(xsrc[p], ("c", s0 // 128))], w=[(xq, ("p", p))])
            for p in range(PB):
                pe(lambda t, p=p, hh=hh: t.matmul(
                    ps8[0][0:1, hh * 128:(hh + 1) * 128], c_onesb[:, :],
                    xc[:, p * 128:(p + 1) * 128], start=(p == 0), stop=(p == PB - 1)),
                   r=[(c_onesb, None), (xc, ("p", p))], w=[(ps8[0], None)],
                   inc=(p == PB - 1))
                pe(lambda t, p=p, hh=hh: t.matmul(
                    ps8[1][0:1, hh * 128:(hh + 1) * 128], c_onesb[:, :],
                    xq[:, p * 128:(p + 1) * 128], start=(p == 0), stop=(p == PB - 1)),
                   r=[(c_onesb, None), (xq, ("p", p))], w=[(ps8[1], None)],
                   inc=(p == PB - 1))
        dve(lambda v: v.tensor_copy(stats[:, 0:512], ps8[0][0:1, :]),
            r=[(ps8[0], None)], w=[(stats, 0)])
        dve(lambda v: v.tensor_copy(stats[:, 512:1024], ps8[1][0:1, :]),
            r=[(ps8[1], None)], w=[(stats, 1)])
        pe(lambda t: t.matmul(ps8[2][:, :], c_ones[:, :], stats[:, 0:512],
                              start=True, stop=True),
           r=[(c_ones, None), (stats, 0)], w=[(ps8[2], None)])
        pe(lambda t: t.matmul(ps8[3][:, :], c_ones[:, :], stats[:, 512:1024],
                              start=True, stop=True),
           r=[(c_ones, None), (stats, 1)], w=[(ps8[3], None)])
        mean, ex2, var = scratch["mean"], scratch["ex2"], scratch["var"]
        istd = var
        dve(lambda v: v.tensor_scalar(mean[:, :], ps8[2][:, :], 1.0 / D, 0.0,
                                      OP.mult, OP.add),
            r=[(ps8[2], None)], w=[(mean, None)])
        dve(lambda v: v.tensor_scalar(ex2[:, :], ps8[3][:, :], 1.0 / D, 0.0,
                                      OP.mult, OP.add),
            r=[(ps8[3], None)], w=[(ex2, None)])
        dve(lambda v: v.tensor_tensor(var[:, :], mean[:, :], mean[:, :], OP.mult),
            r=[(mean, None)], w=[(var, None)])
        dve(lambda v: v.tensor_tensor(var[:, :], ex2[:, :], var[:, :], OP.subtract),
            r=[(ex2, None), (var, None)], w=[(var, None)])
        acte(lambda s: s.activation(var[:, :], var[:, :], AF.Ln, bias=1e-5),
             r=[(var, None)], w=[(var, None)])
        acte(lambda s: s.activation(istd[:, :], var[:, :], AF.Exp, scale=-0.5),
             r=[(var, None)], w=[(istd, None)])
        dlt = scratch["dlt"]
        for p in range(PB):
            dap, dh, dtag = dst_fn(p)
            dve(lambda v, p=p: v.tensor_tensor(dlt[:, :], xsrc[p][:, t0:t0 + 512],
                                               mean[:, :], OP.subtract),
                r=[(xsrc[p], ("c", tt * 4 + j)) for j in range(4)] + [(mean, None)],
                w=[(dlt, None)])
            dve(lambda v, a=dap: v.tensor_tensor(a, dlt[:, :], istd[:, :], OP.mult),
                r=[(dlt, None), (istd, None)], w=[(dh, dtag)])

    def mk_ln_scratch(sbp):
        return dict(xc=sbp("ln_xc", [128, PB * 128], BF16),
                    xq=sbp("ln_xq", [128, PB * 128], BF16),
                    stats=sbp("ln_stats", [1, 1024]),
                    mean=sbp("ln_mean", [128, 512]), ex2=sbp("ln_ex2", [128, 512]),
                    var=sbp("ln_var", [128, 512]),
                    dlt=sbp("ln_dlt", [128, 512]))

    def attn_a(b):
        with ExitStack() as ph:
            sbp = lambda n, s, dt=F32: ph.enter_context(
                nc.sbuf_tensor(_nm(n), s, dt))
            lns = mk_ln_scratch(sbp)
            xn1 = sbp("xn1", [128, PB * 512], BF16)
            btm = sbp("btm", [128, 2 * D])
            browS = sbp("browS", [1, 2 * D])
            c_selh = sbp("c_selh", [16, PB * 128], BF16)
            c_bd = sbp("c_bd", [128, PB * HEAD], BF16)
            sq(lambda e: e.dma_start(browS[:, :], dbrow), w=[(browS, None)])
            sq(lambda e: e.dma_start(c_selh[:, :], dselh), w=[(c_selh, None)])
            sq(lambda e: e.dma_start(c_bd[:, :], dbd), w=[(c_bd, None)])
            for j in range(4):
                pe(lambda t, j=j: t.matmul(ps8[j][:, :], c_ones[:, :],
                                           browS[:, 512 * j:512 * (j + 1)],
                                           start=True, stop=True),
                   r=[(c_ones, None), (browS, None)], w=[(ps8[j], None)])
                dve(lambda v, j=j: v.tensor_copy(btm[:, 512 * j:512 * (j + 1)],
                                                 ps8[j][:, :]),
                    r=[(ps8[j], None)], w=[(btm, ("c", j))])
            knt = [sbp(f"knt{i}", [128, D], BF16) for i in range(2)]
            vt = [sbp(f"vt{i}", [128, D], BF16) for i in range(2)]
            kb = sbp("kb", [128, D])
            ksq = sbp("ksq", [128, D], BF16)
            ksc = sbp("ksc", [128, 16])
            ldst(W0, ("p", "wk"))
            ldst(W1, ("p", "wv"))
            for tt in range(n5):
                ln_tile(x8, tt, lambda p: (xn1[:, p * 512:(p + 1) * 512], xn1,
                                           ("p", p)), lns, "ln1")
                for tbl in range(4):
                    tb = tt * 4 + tbl
                    kn, vv = knt[tb % 2], vt[tb % 2]
                    for half in range(2):
                        pk, pv_ = ps8[half], ps8[2 + half]
                        for cb in range(PB):
                            lh = xn1[:, cb * 512 + tbl * 128: cb * 512 + (tbl + 1) * 128]
                            pe(lambda t, o=pk[:, :], l=lh,
                               r_=W0[:, cb * D + half * 512: cb * D + (half + 1) * 512],
                               st=(cb == 0), sp=(cb == PB - 1): t.matmul(
                                   o, l, r_, start=st, stop=sp),
                               r=[(xn1, ("p", cb)), (W0, ("ld", 0))],
                               w=[(pk, None)], inc=(cb == PB - 1))
                            pe(lambda t, o=pv_[:, :], l=lh,
                               r_=W1[:, cb * D + half * 512: cb * D + (half + 1) * 512],
                               st=(cb == 0), sp=(cb == PB - 1): t.matmul(
                                   o, l, r_, start=st, stop=sp),
                               r=[(xn1, ("p", cb)), (W1, ("ld", 0))],
                               w=[(pv_, None)], inc=(cb == PB - 1))
                        dve(lambda v, h=half: v.tensor_tensor(
                            kb[:, h * 512:(h + 1) * 512], ps8[h][:, :],
                            btm[:, h * 512:(h + 1) * 512], OP.add),
                            r=[(ps8[half], None), (btm, ("c", half))],
                            w=[(kb, ("h", half))])
                        acte(lambda s, h=half: s.activation(
                            ksq[:, h * 512:(h + 1) * 512], kb[:, h * 512:(h + 1) * 512],
                            AF.Square),
                            r=[(kb, ("h", half))], w=[(ksq, ("h", half))])
                        dve(lambda v, h=half, vv=vv: v.tensor_tensor(
                            vv[:, h * 512:(h + 1) * 512], ps8[2 + h][:, :],
                            btm[:, D + h * 512: D + (h + 1) * 512], OP.add),
                            r=[(ps8[2 + half], None), (btm, ("c", 2 + half))],
                            w=[(vv, ("h", half))])
                    dve(lambda v: v.reduce_sum(
                        ksc[:, :], ksq[:, :].rearrange("p (h d) -> p h d", d=HD),
                        axis=AX.X),
                        r=[(ksq, ("h", 0)), (ksq, ("h", 1))], w=[(ksc, None)])
                    dve(lambda v: v.tensor_scalar(ksc[:, :], ksc[:, :], 1e-24, 0.0,
                                                  OP.max, OP.add),
                        r=[(ksc, None)], w=[(ksc, None)])
                    acte(lambda s: s.activation(ksc[:, :], ksc[:, :], AF.Ln),
                         r=[(ksc, None)], w=[(ksc, None)])
                    acte(lambda s: s.activation(ksc[:, :], ksc[:, :], AF.Exp,
                                                scale=-0.5),
                         r=[(ksc, None)], w=[(ksc, None)])
                    for half in range(2):
                        dve(lambda v, h=half, kn=kn: v.tensor_tensor(
                            kn[:, h * 512:(h + 1) * 512].rearrange(
                                "p (h d) -> p h d", d=HD),
                            kb[:, h * 512:(h + 1) * 512].rearrange(
                                "p (h d) -> p h d", d=HD),
                            ksc[:, h * 8:(h + 1) * 8].unsqueeze(2).broadcast_to(
                                (128, 8, HD)),
                            OP.mult),
                            r=[(kb, ("h", half)), (ksc, None)], w=[(kn, ("h", half))])
                    for h in range(HEAD):
                        r0, c0 = 64 * (h % 2), 64 * (h // 2)
                        pe(lambda t, kn=kn, vv=vv, h=h, r0=r0, c0=c0,
                           st=(tb == 0 and h < 2),
                           sp=(tb == 4 * n5 - 1 and h == HEAD - 1):
                           t.matmul(ps8[7][r0:r0 + 64, c0:c0 + 64],
                                    kn[:, h * 64:(h + 1) * 64],
                                    vv[:, h * 64:(h + 1) * 64],
                                    start=st, stop=sp, skip_group_check=True),
                           r=[(kn, ("h", 0)), (kn, ("h", 1)), (vv, ("h", 0)),
                              (vv, ("h", 1))],
                           w=[(ps8[7], None)], inc=(h == HEAD - 1))
            dve(lambda v: v.tensor_copy(msb[:, :], ps8[7][:, :]),
                r=[(ps8[7], None)], w=[(msb, None)])
            ldst(W0, ("p", "wq"))
            ldst(W1, ("p", "wo"))
            qb = sbp("qb", [128, PB * 512], BF16)
            qsq = sbp("qsq", [128, PB * 512], BF16)
            qn = sbp("qn", [128, PB * 512], BF16)
            ao = sbp("ao", [128, PB * 512], BF16)
            qs = sbp("qs", [16, 512])
            qsb = sbp("qsb", [16, 512], BF16)
            for tt in range(n5):
                ln_tile(x8, tt, lambda p: (xn1[:, p * 512:(p + 1) * 512], xn1,
                                           ("p", p)), lns, "ln1c")
                for db in range(PB):
                    pq = ps8[db % 2]
                    for cb in range(PB):
                        pe(lambda t, o=pq[:, :],
                           l=W0[:, cb * D + db * 128: cb * D + (db + 1) * 128],
                           r_=xn1[:, cb * 512:(cb + 1) * 512],
                           st=(cb == 0), sp=(cb == PB - 1): t.matmul(
                               o, l, r_, start=st, stop=sp),
                           r=[(W0, ("ld", 0)), (xn1, ("p", cb))],
                           w=[(pq, None)], inc=(cb == PB - 1))
                    acte(lambda s, db=db, pq=pq: s.activation(
                        qb[:, db * 512:(db + 1) * 512], pq[:, :], AF.Identity,
                        bias=pv[:, BQ + db:BQ + db + 1]),
                        r=[(pq, None), (pv, None)], w=[(qb, ("p", db))])
                    acte(lambda s, db=db, pq=pq: s.activation(
                        qsq[:, db * 512:(db + 1) * 512], pq[:, :], AF.Square,
                        bias=pv[:, BQ + db:BQ + db + 1]),
                        r=[(pq, None), (pv, None)], w=[(qsq, ("p", db))])
                for p in range(PB):
                    pe(lambda t, p=p: t.matmul(
                        ps8[2][0:16, :], c_bd[:, p * 16:(p + 1) * 16],
                        qsq[:, p * 512:(p + 1) * 512],
                        start=(p == 0), stop=(p == PB - 1)),
                       r=[(c_bd, None), (qsq, ("p", p))], w=[(ps8[2], None)],
                       inc=(p == PB - 1))
                dve(lambda v: v.tensor_scalar(qs[:, :], ps8[2][0:16, :], 1e-24, 0.0,
                                              OP.max, OP.add),
                    r=[(ps8[2], None)], w=[(qs, None)])
                acte(lambda s: s.activation(qs[:, :], qs[:, :], AF.Ln),
                     r=[(qs, None)], w=[(qs, None)])
                acte(lambda s: s.activation(qs[:, :], qs[:, :], AF.Exp, scale=-0.5),
                     r=[(qs, None)], w=[(qs, None)])
                dve(lambda v: v.tensor_scalar(qs[:, :], qs[:, :], sgate[:, 0:1], 0.0,
                                              OP.mult, OP.add),
                    r=[(qs, None), (sgate, None)], w=[(qs, None)])
                dve(lambda v: v.tensor_copy(qsb[:, :], qs[:, :]),
                    r=[(qs, None)], w=[(qsb, None)])
                for p in range(PB):
                    pbc = ps8[3 + p % 2]
                    pe(lambda t, p=p, o=pbc[:, :]: t.matmul(
                        o, c_selh[:, p * 128:(p + 1) * 128], qsb[:, :],
                        start=True, stop=True),
                       r=[(c_selh, None), (qsb, None)], w=[(pbc, None)])
                    dve(lambda v, p=p, pbc=pbc: v.tensor_tensor(
                        qn[:, p * 512:(p + 1) * 512], qb[:, p * 512:(p + 1) * 512],
                        pbc[:, :], OP.mult),
                        r=[(qb, ("p", p)), (pbc, None)], w=[(qn, ("p", p))])
                    py = ps8[5 + p % 2]
                    for hh in range(2):
                        h = 2 * p + hh
                        pe(lambda t, h=h, hh=hh, p=p, py=py: t.matmul(
                            py[hh * 64:(hh + 1) * 64, :],
                            msb[64 * (h % 2):64 * (h % 2) + 64,
                                64 * (h // 2):64 * (h // 2) + 64],
                            qn[64 * hh:64 * (hh + 1), p * 512:(p + 1) * 512],
                            start=True, stop=True, skip_group_check=True),
                           r=[(msb, None), (qn, ("p", p))], w=[(py, None)],
                           inc=(hh == 1))
                    acte(lambda s, p=p, py=py: s.activation(
                        ao[:, p * 512:(p + 1) * 512], py[:, :], AF.Identity),
                        r=[(py, None)], w=[(ao, ("p", p))])
                for db in range(PB):
                    po = ps8[7]
                    for cb in range(PB):
                        pe(lambda t, o=po[:, :],
                           l=W1[:, cb * D + db * 128: cb * D + (db + 1) * 128],
                           r_=ao[:, cb * 512:(cb + 1) * 512],
                           st=(cb == 0), sp=(cb == PB - 1): t.matmul(
                               o, l, r_, start=st, stop=sp),
                           r=[(W1, ("ld", 0)), (ao, ("p", cb))],
                           w=[(po, None)], inc=(cb == PB - 1))
                    xtags = [(x8[db], ("c", tt * 4 + j)) for j in range(4)]
                    dve(lambda v, db=db, tt=tt, po=po: v.scalar_tensor_tensor(
                        x8[db][:, tt * 512:(tt + 1) * 512], po[:, :],
                        pv[:, BO + db:BO + db + 1],
                        x8[db][:, tt * 512:(tt + 1) * 512], OP.add, OP.add),
                        r=[(po, None), (pv, None)] + xtags,
                        w=xtags)
            tk.barrier()
            tk.drop([lns[k] for k in lns] + [xn1, knt[0], knt[1], vt[0], vt[1],
                                             kb, ksq, ksc, qb, qsq, qn, ao, qs, qsb,
                                             btm, browS, c_selh, c_bd])

    def ffn(b, inv):
        with ExitStack() as ph:
            sbp = lambda n, s, dt=F32: ph.enter_context(
                nc.sbuf_tensor(_nm(n), s, dt))
            lns = mk_ln_scratch(sbp)
            xn2 = [sbp(f"xn2_{i}", [128, PB * 512], BF16) for i in range(2)]
            asl = sbp("asl", [128, 4 * 4 * 512], BF16)
            sil = sbp("sil", [128, 4 * 512], BF16)
            h3c = sbp("h3c", [128, 4 * 512], BF16)
            call = sbp("call", [128, NE * 512], BF16)
            gL = sbp("gL", [128, 64])
            gE = sbp("gE", [128, 64])
            gS = sbp("gS", [128, 64])
            gSm = sbp("gSm", [128, 64])
            gm4 = sbp("gm4", [128, 8])
            gma = sbp("gma", [128, 16])
            gmb = sbp("gmb", [128, 16])
            gr4 = sbp("gr4", [128, 4])
            gctm = sbp("gctm", [128, 64], BF16)
            gcfm = sbp("gcfm", [16, 512], BF16)
            c_sele = sbp("c_sele", [16, NE * 128], BF16)
            c_id = sbp("c_id", [128, 128], BF16)
            sq(lambda e: e.dma_start(c_sele[:, :], dsele), w=[(c_sele, None)])
            sq(lambda e: e.dma_start(c_id[:, :], dident), w=[(c_id, None)])
            w2f = [sbp(f"w2f{i}", [128, 4 * D], BF16) for i in range(2)]
            seq = [(t_, e_) for t_ in range(n5) for e_ in range(NEX)]
            wbuf = lambda gi: (W0, W1)[gi % 2]
            ldst(wbuf(0), ("w13", seq[0][1]))
            tk.fence("pool")

            def gate(tt, xn):
                pg = ps8[4]
                for tbl in range(4):
                    for cb in range(PB):
                        pe(lambda t, o=pg[:, tbl * 16:(tbl + 1) * 16],
                           l=xn[:, cb * 512 + tbl * 128: cb * 512 + (tbl + 1) * 128],
                           r_=gwS[:, cb * 16:(cb + 1) * 16],
                           st=(cb == 0 and tbl == 0), sp=(cb == PB - 1): t.matmul(
                               o, l, r_, start=st, stop=sp, skip_group_check=True),
                           r=[(xn, ("p", cb)), (gwS, None)], w=[(pg, None)],
                           inc=(cb == PB - 1))
                v3 = lambda a: a.rearrange("p (t e) -> p t e", t=4)
                v16 = lambda a: a.rearrange("p (tg e) -> p tg e", tg=16)
                bc4 = lambda a: a.unsqueeze(2).broadcast_to((128, 4, 16))
                bc16 = lambda a: a.rearrange("p (t g) -> p t g", t=4)
                dve(lambda v: v.tensor_copy(gL[:, :], pg[:, 0:64]),
                    r=[(pg, None)], w=[(gL, None)])
                dve(lambda v: v.reduce_max(gm4[:, 0:4], v3(gL[:, :]), axis=AX.X),
                    r=[(gL, None)], w=[(gm4, 0)])
                dve(lambda v: v.tensor_tensor(v3(gE[:, :]), v3(gL[:, :]),
                                              bc4(gm4[:, 0:4]), OP.subtract),
                    r=[(gL, None), (gm4, 0)], w=[(gE, None)])
                acte(lambda s: s.activation(gE[:, :], gE[:, :], AF.Exp),
                     r=[(gE, None)], w=[(gE, None)])
                dve(lambda v: v.reduce_sum(gm4[:, 4:8], v3(gE[:, :]), axis=AX.X),
                    r=[(gE, None)], w=[(gm4, 1)])
                dve(lambda v: v.reciprocal(gr4[:, :], gm4[:, 4:8]),
                    r=[(gm4, 1)], w=[(gr4, None)])
                dve(lambda v: v.tensor_tensor(v3(gS[:, :]), v3(gE[:, :]),
                                              bc4(gr4[:, :]), OP.mult),
                    r=[(gE, None), (gr4, None)], w=[(gS, None)])
                bcg = lambda a: a.unsqueeze(2).broadcast_to((128, 4, 4))
                dve(lambda v: v.reduce_max(bc16(gma[:, :]), v16(gS[:, :]), axis=AX.X),
                    r=[(gS, None)], w=[(gma, None)])
                dve(lambda v: v.reduce_max(gr4[:, :], bc16(gma[:, :]), axis=AX.X),
                    r=[(gma, None)], w=[(gr4, None)])
                dve(lambda v: v.tensor_tensor(
                    bc16(gmb[:, :]), bc16(gma[:, :]), bcg(gr4[:, :]), OP.is_equal),
                    r=[(gma, None), (gr4, None)], w=[(gmb, None)])
                dve(lambda v: v.scalar_tensor_tensor(
                    gmb[:, :], gmb[:, :], -1e30, gma[:, :], OP.mult, OP.add),
                    r=[(gmb, None), (gma, None)], w=[(gmb, None)])
                dve(lambda v: v.reduce_max(gr4[:, :], bc16(gmb[:, :]), axis=AX.X),
                    r=[(gmb, None)], w=[(gr4, None)])
                dve(lambda v: v.tensor_tensor(
                    bc16(gmb[:, :]), bc16(gma[:, :]), bcg(gr4[:, :]), OP.is_ge),
                    r=[(gma, None), (gr4, None)], w=[(gmb, None)])
                dve(lambda v: v.tensor_tensor(
                    v16(gSm[:, :]), v16(gS[:, :]),
                    gmb[:, :].unsqueeze(2).broadcast_to((128, 16, 4)), OP.mult),
                    r=[(gS, None), (gmb, None)], w=[(gSm, None)])
                dve(lambda v: v.tensor_copy(gE[:, :], gSm[:, :]),
                    r=[(gSm, None)], w=[(gE, None)])
                for it in range(4):
                    dve(lambda v: v.reduce_max(gm4[:, 0:4], v3(gE[:, :]), axis=AX.X),
                        r=[(gE, None)], w=[(gm4, 0)])
                    if it < 3:
                        dve(lambda v: v.tensor_tensor(
                            v3(gL[:, :]), v3(gE[:, :]), bc4(gm4[:, 0:4]),
                            OP.is_equal),
                            r=[(gE, None), (gm4, 0)], w=[(gL, None)])
                        dve(lambda v: v.scalar_tensor_tensor(
                            gE[:, :], gL[:, :], -1e30, gE[:, :], OP.mult, OP.add),
                            r=[(gL, None), (gE, None)], w=[(gE, None)])
                dve(lambda v: v.tensor_tensor(v3(gL[:, :]), v3(gSm[:, :]),
                                              bc4(gm4[:, 0:4]), OP.is_ge),
                    r=[(gSm, None), (gm4, 0)], w=[(gL, None)])
                dve(lambda v: v.tensor_tensor(gctm[:, :], gS[:, :], gL[:, :],
                                              OP.mult),
                    r=[(gS, None), (gL, None)], w=[(gctm, None)])
                for tbl in range(4):
                    pe(lambda t, tbl=tbl: t.transpose(
                        ps8[5][:, :].bitcast(BF16)[0:16, tbl * 128:(tbl + 1) * 128],
                        gctm[:, tbl * 16:(tbl + 1) * 16], c_id[:, :]),
                       r=[(gctm, None), (c_id, None)], w=[(ps8[5], None)])
                    dve(lambda v, tbl=tbl: v.tensor_copy(
                        gcfm[:, tbl * 128:(tbl + 1) * 128],
                        ps8[5][:, :].bitcast(BF16)[0:16, tbl * 128:(tbl + 1) * 128]),
                        r=[(ps8[5], None)], w=[(gcfm, ("t", tbl))])

            gi = 0
            for tt in range(n5):
                xn = xn2[tt % 2]
                ln_tile(x8, tt, lambda p: (xn[:, p * 512:(p + 1) * 512], xn,
                                           ("p", p)), lns, "ln2")
                gate(tt, xn)
                for e in range(NE):
                    pc = ps8[4 + e % 4]
                    pe(lambda t, e=e, pc=pc: t.matmul(
                        pc[:, :], c_sele[:, e * 128:(e + 1) * 128], gcfm[:, :],
                        start=True, stop=True),
                       r=[(c_sele, None)] + [(gcfm, ("t", j)) for j in range(4)],
                       w=[(pc, None)])
                    acte(lambda s, e=e, pc=pc: s.activation(
                        call[:, e * 512:(e + 1) * 512], pc[:, :], AF.Identity),
                        r=[(pc, None)], w=[(call, ("e", e))])
                for chunk in FFN_CHUNKS:
                    ldst(w2f[0], ("w2", chunk[0]))
                    if len(chunk) > 1:
                        ldst(w2f[1], ("w2", chunk[1]))
                    for ei, e in enumerate(chunk):
                        wb = wbuf(gi)
                        for ib in range(4):
                            for cb in range(PB):
                                pe(lambda t, o=ps8[ib][:, :],
                                   l=wb[:, cb * D + ib * 128: cb * D + (ib + 1) * 128],
                                   r_=xn[:, cb * 512:(cb + 1) * 512],
                                   st=(cb == 0), sp=(cb == PB - 1): t.matmul(
                                       o, l, r_, start=st, stop=sp),
                                   r=[(wb, ("ld", 0)), (xn, ("p", cb))],
                                   w=[(ps8[ib], None)], inc=(cb == PB - 1))
                            for cb in range(PB):
                                pe(lambda t, o=ps8[4 + ib][:, :],
                                   l=wb[:, cb * D + 512 + ib * 128:
                                          cb * D + 512 + (ib + 1) * 128],
                                   r_=xn[:, cb * 512:(cb + 1) * 512],
                                   st=(cb == 0), sp=(cb == PB - 1): t.matmul(
                                       o, l, r_, start=st, stop=sp),
                                   r=[(wb, ("ld", 0)), (xn, ("p", cb))],
                                   w=[(ps8[4 + ib], None)], inc=(cb == PB - 1))
                        if gi + 1 < len(seq):
                            ldst(wbuf(gi + 1), ("w13", seq[gi + 1][1]))
                        gi += 1
                        for ib in range(4):
                            acte(lambda s, ib=ib: s.activation(
                                sil[:, ib * 512:(ib + 1) * 512], ps8[ib][:, :],
                                AF.Silu),
                                r=[(ps8[ib], None)], w=[(sil, ("i", ib))])
                            if e < NE:
                                dve(lambda v, ib=ib, e=e: v.tensor_tensor(
                                    h3c[:, ib * 512:(ib + 1) * 512], ps8[4 + ib][:, :],
                                    call[:, e * 512:(e + 1) * 512], OP.mult),
                                    r=[(ps8[4 + ib], None), (call, ("e", e))],
                                    w=[(h3c, ("i", ib))])
                                dve(lambda v, ib=ib, ei=ei: v.tensor_tensor(
                                    asl[:, (ei * 4 + ib) * 512:(ei * 4 + ib + 1) * 512],
                                    sil[:, ib * 512:(ib + 1) * 512],
                                    h3c[:, ib * 512:(ib + 1) * 512], OP.mult),
                                    r=[(sil, ("i", ib)), (h3c, ("i", ib))],
                                    w=[(asl, ("a", ei * 4 + ib))])
                            else:
                                dve(lambda v, ib=ib, ei=ei: v.tensor_tensor(
                                    asl[:, (ei * 4 + ib) * 512:(ei * 4 + ib + 1) * 512],
                                    sil[:, ib * 512:(ib + 1) * 512], ps8[4 + ib][:, :],
                                    OP.mult),
                                    r=[(sil, ("i", ib)), (ps8[4 + ib], None)],
                                    w=[(asl, ("a", ei * 4 + ib))])
                    for ei, e in enumerate(chunk):
                        wb2 = w2f[ei % 2]
                        for db in range(PB):
                            for ib in range(4):
                                pe(lambda t, o=ps8[db][:, :],
                                   l=wb2[:, ib * D + db * 128: ib * D + (db + 1) * 128],
                                   r_=asl[:, (ei * 4 + ib) * 512:
                                          (ei * 4 + ib + 1) * 512],
                                   st=(ei == 0 and ib == 0),
                                   sp=(ei == len(chunk) - 1 and ib == 3): t.matmul(
                                       o, l, r_, start=st, stop=sp,
                                       skip_group_check=True),
                                   r=[(wb2, ("ld", 0)), (asl, ("a", ei * 4 + ib))],
                                   w=[(ps8[db], None)],
                                   inc=(ib == 3))
                        if ei + 2 < len(chunk):
                            ldst(w2f[ei % 2], ("w2", chunk[ei + 2]))
                    for db in range(PB):
                        xtags = [(x8[db], ("c", tt * 4 + j)) for j in range(4)]
                        dve(lambda v, db=db, tt=tt: v.tensor_tensor(
                            x8[db][:, tt * 512:(tt + 1) * 512],
                            x8[db][:, tt * 512:(tt + 1) * 512], ps8[db][:, :], OP.add),
                            r=xtags + [(ps8[db], None)], w=xtags)
            tk.barrier()
            tk.drop([lns[k] for k in lns] + xn2 + w2f +
                    [asl, sil, h3c, call, gL, gE, gS, gSm, gm4, gma, gmb, gr4,
                     gctm, gcfm, c_sele, c_id])

    def attn_b(b):
        EUC = 32

        def mixes(ph_sbp, names):
            out = {}
            for nmx in names:
                out[nmx] = ph_sbp("mx_" + nmx, [128, PB * 256], BF16)
            return out

        mtmp = [None]

        def mix_tile(dst, src_xn, tmcol, omcol, pb, tti):
            t0 = tti * 256
            tmp = mtmp[0]
            xtag = [(src_xn, ("x", pb, (t0 + j * 128) // 512)) for j in (0, 1)]
            acte(lambda s, o=dst[:, pb * 256: pb * 256 + 256],
                 i=src_xn[:, pb * T + t0: pb * T + t0 + 256],
                 sc=pv[:, tmcol + pb: tmcol + pb + 1]: s.activation(
                     o, i, AF.Identity, scale=sc),
                 r=xtag + [(pv, None)], w=[(dst, ("p", pb))])
            lo = 1 if tti == 0 else 0
            if tti > 0:
                xtag = xtag + [(src_xn, ("x", pb, (t0 - 1) // 512))]
            acte(lambda s, o=tmp[:, pb * 256 + lo: pb * 256 + 256],
                 i=src_xn[:, pb * T + t0 + lo - 1: pb * T + t0 + 255],
                 sc=dpar[:, omcol + pb: omcol + pb + 1]: s.activation(
                     o, i, AF.Identity, scale=sc),
                 r=xtag + [(dpar, ("c", omcol))], w=[(tmp, ("p", pb))])
            pool(lambda g, o=dst[:, pb * 256 + lo: pb * 256 + 256],
                 i0=tmp[:, pb * 256 + lo: pb * 256 + 256],
                 i1=dst[:, pb * 256 + lo: pb * 256 + 256]: g.tensor_tensor(
                     o, i0, i1, OP.add),
                 r=[(tmp, ("p", pb)), (dst, ("p", pb))], w=[(dst, ("p", pb))])

        with ExitStack() as ph:
            sbp = lambda n, s, dt=F32: ph.enter_context(
                nc.sbuf_tensor(_nm(n), s, dt))
            xn3 = sbp("xn3", [128, PB * T], BF16)
            wkva = sbp("wkva", [128, PB * T], BF16)
            ldst(W0, ("p", "wkey"))
            ldst(W1, ("p", "wval"))
            with ExitStack() as p1:
                s1p = lambda n, s, dt=F32: p1.enter_context(
                    nc.sbuf_tensor(_nm(n), s, dt))
                with ExitStack() as lnes:
                    lnp = lambda n, s, dt=F32: lnes.enter_context(
                        nc.sbuf_tensor(_nm(n), s, dt))
                    lns = mk_ln_scratch(lnp)
                    for tt5 in range(n5):
                        ln_tile(x8, tt5,
                                lambda p, tt5=tt5: (
                                    xn3[:, p * T + tt5 * 512: p * T + (tt5 + 1) * 512],
                                    xn3, ("x", p, tt5)), lns, "ln3")
                    tk.barrier()
                    tk.drop([lns[k] for k in lns])
                mx = mixes(s1p, ("xk",))
                mtmp[0] = s1p("mixtmp", [128, PB * 256], BF16)
                ekb = s1p("ekb", [128, PB * 256])
                ekvb = s1p("ekvb", [128, PB * 256])
                Pb8 = s1p("Pb8", [128, PB * 257])
                Qb8 = s1p("Qb8", [128, PB * 257])
                Pst = s1p("Pst", [128, PB])
                Qst = s1p("Qst", [128, PB])
                for p in range(PB):
                    dve(lambda v, p=p: v.memset(Pst[:, p:p + 1], 0.0),
                        r=[], w=[(Pst, ("p", p))])
                    dve(lambda v, p=p: v.memset(Qst[:, p:p + 1], 0.0),
                        r=[], w=[(Qst, ("p", p))])
                v3e = lambda a: a.rearrange("p (b n) -> p b n", b=PB)
                P3 = Pb8[:, :].rearrange("p (b n) -> p b n", b=PB)[:, :, 0:256]
                Q3 = Qb8[:, :].rearrange("p (b n) -> p b n", b=PB)[:, :, 0:256]
                for tti in range(n2):
                    for p in range(PB):
                        mix_tile(mx["xk"], xn3, TMK, OTMK, p, tti)
                    for db in range(PB):
                        pk = ps8[db // 2][:, (db % 2) * 256:(db % 2) * 256 + 256]
                        for cb in range(PB):
                            pe(lambda t, o=pk,
                               l=W0[:, cb * D + db * 128: cb * D + (db + 1) * 128],
                               r_=mx["xk"][:, cb * 256:(cb + 1) * 256],
                               st=(cb == 0 and db % 2 == 0), sp=(cb == PB - 1):
                               t.matmul(o, l, r_, start=st, stop=False,
                                        skip_group_check=True),
                               r=[(W0, ("ld", 0)), (mx["xk"], ("p", cb))],
                               w=[(ps8[db // 2], None)], inc=(cb == PB - 1))
                    for p in range(PB):
                        mix_tile(mx["xk"], xn3, TMV, OTMV, p, tti)
                    for db in range(PB):
                        pvv = ps8[4 + db // 2][:, (db % 2) * 256:(db % 2) * 256 + 256]
                        for cb in range(PB):
                            pe(lambda t, o=pvv,
                               l=W1[:, cb * D + db * 128: cb * D + (db + 1) * 128],
                               r_=mx["xk"][:, cb * 256:(cb + 1) * 256],
                               st=(cb == 0 and db % 2 == 0), sp=(cb == PB - 1):
                               t.matmul(o, l, r_, start=st, stop=False,
                                        skip_group_check=True),
                               r=[(W1, ("ld", 0)), (mx["xk"], ("p", cb))],
                               w=[(ps8[4 + db // 2], None)], inc=(cb == PB - 1))
                    for p in range(PB):
                        kps = ps8[p // 2][:, (p % 2) * 256:(p % 2) * 256 + 256]
                        vps = ps8[4 + p // 2][:, (p % 2) * 256:(p % 2) * 256 + 256]
                        acte(lambda s, kps=kps, p=p: s.activation(
                            ekb[:, p * 256:(p + 1) * 256], kps, AF.Exp),
                            r=[(ps8[p // 2], None)], w=[(ekb, ("p", p))])
                        dve(lambda v, vps=vps, p=p: v.tensor_tensor(
                            ekvb[:, p * 256:(p + 1) * 256],
                            ekb[:, p * 256:(p + 1) * 256], vps, OP.mult),
                            r=[(ekb, ("p", p)), (ps8[4 + p // 2], None)],
                            w=[(ekvb, ("p", p))])
                    for p in range(PB):
                        dve(lambda v, p=p: v.tensor_copy(
                            Pb8[:, p * 257: p * 257 + 1], Pst[:, p:p + 1]),
                            r=[(Pst, ("p", p))], w=[(Pb8, ("p", p))])
                        dve(lambda v, p=p: v.tensor_tensor_scan(
                            Pb8[:, p * 257 + 1: p * 257 + 257],
                            dpar[:, EW + p:EW + p + 1].broadcast_to((128, 256)),
                            ekvb[:, p * 256:(p + 1) * 256], Pst[:, p:p + 1],
                            OP.mult, OP.add),
                            r=[(dpar, ("c", EW)), (ekvb, ("p", p)),
                               (Pst, ("p", p)), (Pb8, ("p", p))],
                            w=[(Pb8, ("p", p))])
                        dve(lambda v, p=p: v.tensor_copy(
                            Qb8[:, p * 257: p * 257 + 1], Qst[:, p:p + 1]),
                            r=[(Qst, ("p", p))], w=[(Qb8, ("p", p))])
                        dve(lambda v, p=p: v.tensor_tensor_scan(
                            Qb8[:, p * 257 + 1: p * 257 + 257],
                            dpar[:, EW + p:EW + p + 1].broadcast_to((128, 256)),
                            ekb[:, p * 256:(p + 1) * 256], Qst[:, p:p + 1],
                            OP.mult, OP.add),
                            r=[(dpar, ("c", EW)), (ekb, ("p", p)),
                               (Qst, ("p", p)), (Qb8, ("p", p))],
                            w=[(Qb8, ("p", p))])
                        dve(lambda v, p=p: v.tensor_copy(
                            Pst[:, p:p + 1], Pb8[:, p * 257 + 256: p * 257 + 257]),
                            r=[(Pb8, ("p", p))], w=[(Pst, ("p", p))])
                        dve(lambda v, p=p: v.tensor_copy(
                            Qst[:, p:p + 1], Qb8[:, p * 257 + 256: p * 257 + 257]),
                            r=[(Qb8, ("p", p))], w=[(Qst, ("p", p))])
                    for p in range(PB):
                        dve(lambda v, p=p: v.scalar_tensor_tensor(
                            Pb8[:, p * 257: p * 257 + 256],
                            ekvb[:, p * 256:(p + 1) * 256],
                            dpar[:, EUC + p:EUC + p + 1],
                            Pb8[:, p * 257: p * 257 + 256], OP.mult, OP.add),
                            r=[(ekvb, ("p", p)), (dpar, ("c", 32)),
                               (Pb8, ("p", p))], w=[(Pb8, ("p", p))])
                        dve(lambda v, p=p: v.scalar_tensor_tensor(
                            Qb8[:, p * 257: p * 257 + 256],
                            ekb[:, p * 256:(p + 1) * 256],
                            dpar[:, EUC + p:EUC + p + 1],
                            Qb8[:, p * 257: p * 257 + 256], OP.mult, OP.add),
                            r=[(ekb, ("p", p)), (dpar, ("c", 32)),
                               (Qb8, ("p", p))], w=[(Qb8, ("p", p))])
                    allP = [(Pb8, ("p", p)) for p in range(PB)]
                    allQ = [(Qb8, ("p", p)) for p in range(PB)]
                    acte(lambda s: s.activation(Q3, Q3, AF.Ln), r=allQ, w=allQ)
                    acte(lambda s: s.activation(Q3, Q3, AF.Exp, scale=-1.0),
                         r=allQ, w=allQ)
                    dve(lambda v, tti=tti: v.tensor_tensor(
                        wkva[:, :].rearrange("p (b n) -> p b n", b=PB)[
                            :, :, tti * 256:(tti + 1) * 256],
                        P3, Q3, OP.mult),
                        r=allP + allQ, w=[(wkva, ("t", tti))])
                tk.barrier()
                tk.drop(list(mx.values()) +
                        [ekb, ekvb, Pb8, Qb8, Pst, Qst])
            ldst(W0, ("p", "wrec"))
            ldst(W1, ("p", "wout"))
            with ExitStack() as p2:
                s2p = lambda n, s, dt=F32: p2.enter_context(
                    nc.sbuf_tensor(_nm(n), s, dt))
                mx2 = mixes(s2p, ("xr",))
                mtmp[0] = s2p("mixtmp2", [128, PB * 256], BF16)
                erb = s2p("erb", [128, PB * 256])
                yt = [s2p(f"yt{i}", [128, PB * 256], BF16) for i in range(2)]
                v3e = lambda a: a.rearrange("p (b n) -> p b n", b=PB)

                def outproj(tti):
                    ytb = yt[tti % 2]
                    for db in range(PB):
                        po = ps8[4 + db // 2][:, (db % 2) * 256:(db % 2) * 256 + 256]
                        for cb in range(PB):
                            pe(lambda t, o=po,
                               l=W1[:, cb * D + db * 128: cb * D + (db + 1) * 128],
                               r_=ytb[:, cb * 256:(cb + 1) * 256],
                               st=(cb == 0 and db % 2 == 0), sp=(cb == PB - 1):
                               t.matmul(o, l, r_, start=st, stop=False,
                                        skip_group_check=True),
                               r=[(W1, ("ld", 0)), (ytb, ("p", cb))],
                               w=[(ps8[4 + db // 2], None)], inc=(cb == PB - 1))
                        xtags = [(x8[db], ("c", tti * 2 + j)) for j in range(2)]
                        dve(lambda v, db=db, tti=tti, po=po: v.tensor_tensor(
                            x8[db][:, tti * 256:(tti + 1) * 256],
                            x8[db][:, tti * 256:(tti + 1) * 256], po, OP.add),
                            r=xtags + [(ps8[4 + db // 2], None)], w=xtags)

                for tti in range(n2):
                    for p in range(PB):
                        mix_tile(mx2["xr"], xn3, TMR, OTMR, p, tti)
                    for db in range(PB):
                        pr = ps8[db // 2][:, (db % 2) * 256:(db % 2) * 256 + 256]
                        for cb in range(PB):
                            pe(lambda t, o=pr,
                               l=W0[:, cb * D + db * 128: cb * D + (db + 1) * 128],
                               r_=mx2["xr"][:, cb * 256:(cb + 1) * 256],
                               st=(cb == 0 and db % 2 == 0), sp=(cb == PB - 1):
                               t.matmul(o, l, r_, start=st, stop=False,
                                        skip_group_check=True),
                               r=[(W0, ("ld", 0)), (mx2["xr"], ("p", cb))],
                               w=[(ps8[db // 2], None)], inc=(cb == PB - 1))
                    for p in range(PB):
                        rps = ps8[p // 2][:, (p % 2) * 256:(p % 2) * 256 + 256]
                        acte(lambda s, rps=rps, p=p: s.activation(
                            erb[:, p * 256:(p + 1) * 256], rps, AF.Exp, scale=-1.0),
                            r=[(ps8[p // 2], None)], w=[(erb, ("p", p))])
                    alle = [(erb, ("p", p)) for p in range(PB)]
                    dve(lambda v: v.tensor_scalar(erb[:, :], erb[:, :], 1.0, 0.0,
                                                  OP.add, OP.add),
                        r=alle, w=alle)
                    acte(lambda s: s.activation(erb[:, :], erb[:, :], AF.Ln),
                         r=alle, w=alle)
                    acte(lambda s: s.activation(erb[:, :], erb[:, :], AF.Exp,
                                                scale=-1.0),
                         r=alle, w=alle)
                    dve(lambda v, tti=tti, ytb=yt[tti % 2]: v.tensor_tensor(
                        v3e(ytb[:, :]),
                        wkva[:, :].rearrange("p (b n) -> p b n", b=PB)[
                            :, :, tti * 256:(tti + 1) * 256],
                        v3e(erb[:, :]), OP.mult),
                        r=[(wkva, ("t", tti))] + alle,
                        w=[(yt[tti % 2], ("p", p)) for p in range(PB)])
                    if tti > 0:
                        outproj(tti - 1)
                outproj(n2 - 1)
                tk.barrier()
                tk.drop(list(mx2.values()) + [erb] + yt)
            tk.barrier()
            tk.drop([xn3, wkva])

    def tap(name, b):
        if taps:
            for p in range(PB):
                sq(lambda e, p=p: e.dma_start(dtap[name][b, p], x8[p][:, :]),
                   r=[(x8[p], None)])

    for b in range(NB):
        if b > 0:
            for p in range(PB):
                sq(lambda e, p=p, b=b: e.dma_start(x8[p][:, :], dx[b, p]),
                   w=[(x8[p], None)])
        attn_a(b)
        tap("attna", b)
        ffn(b, 0)
        tap("ffn1", b)
        attn_b(b)
        tap("attnb", b)
        ffn(b, 1)
        tap("ffn2", b)
        for p in range(PB):
            sq(lambda e, p=p, b=b: e.dma_start(dout[b, p], x8[p][:, :]),
               r=[(x8[p], None)])

    for i in range(tk.NDMA):
        for q, strm in (("sq", "sp"), ("gq", "pool")):
            n = tk.cnt[f"{q}{i}"]
            if n:
                tk.ops[strm].append(
                    lambda e, s=tk.sem[f"{q}{i}"], n=n: e.wait_ge(s, n))

    with es:
        with nc.Block() as block:
            @block.tensor
            def _(t):
                for f in tk.ops["pe"]:
                    f(t)

            @block.scalar
            def _(s):
                for f in tk.ops["act"]:
                    f(s)

            @block.vector
            def _(v):
                for f in tk.ops["dve"]:
                    f(v)

            @block.gpsimd
            def _(g):
                for f in tk.ops["pool"]:
                    f(g)

            @block.sync
            def _(s):
                for f in tk.ops["sp"]:
                    f(s)
    return nc


def _pack_inmap(T=2048):
    raise NotImplementedError


def _pk_proj(w):
    return np.ascontiguousarray(
        w.reshape(8, 128, w.shape[1]).transpose(1, 0, 2).reshape(128, -1))


def host_pack(inp, T):
    import ml_dtypes
    f32 = lambda a: np.asarray(a, dtype=np.float32)
    wq = _pk_proj(f32(inp["L0_attn_a_Wq"]))
    wk = _pk_proj(f32(inp["L0_attn_a_Wk"]))
    wv = _pk_proj(f32(inp["L0_attn_a_Wv"]))
    wo = _pk_proj(f32(inp["L0_attn_a_Wo"]))
    wkey = _pk_proj(f32(inp["L0_attn_b_W_key"]).T)
    wval = _pk_proj(f32(inp["L0_attn_b_W_value"]).T)
    wrec = _pk_proj(f32(inp["L0_attn_b_W_receptance"]).T)
    wout = _pk_proj(f32(inp["L0_attn_b_W_output"]).T)

    w1 = f32(inp["L0_ffn_w1"])
    w2 = f32(inp["L0_ffn_w2"])
    w3 = f32(inp["L0_ffn_w3"])
    ws1 = f32(inp["L0_ffn_ws1"])
    ws2 = f32(inp["L0_ffn_ws2"])
    ws3 = f32(inp["L0_ffn_ws3"])
    w13 = np.zeros((NEX, 128, PB * D), np.float32)
    w2p = np.zeros((NEX, 128, 4 * D), np.float32)
    for e in range(NE):
        w13[e] = _pk_proj(np.concatenate([w1[e], w3[e]], axis=1))
        w2p[e] = np.ascontiguousarray(
            w2[e].reshape(4, 128, D).transpose(1, 0, 2).reshape(128, 4 * D))
    for j in range(2):
        w13[NE + j] = _pk_proj(np.concatenate(
            [ws1[:, j * 512:(j + 1) * 512], ws3[:, j * 512:(j + 1) * 512]], axis=1))
        w2p[NE + j] = np.ascontiguousarray(
            ws2[j * 512:(j + 1) * 512].reshape(4, 128, D).transpose(1, 0, 2)
            .reshape(128, 4 * D))
    gw = _pk_proj(f32(inp["L0_ffn_gate_w"]).T)

    fmv = lambda a: np.ascontiguousarray(f32(a).reshape(8, 128).T)
    pvec = np.zeros((128, 80), np.float32)
    pvec[:, 0:8] = fmv(inp["L0_attn_a_bq"])
    pvec[:, 8:16] = fmv(inp["L0_attn_a_bk"])
    pvec[:, 16:24] = fmv(inp["L0_attn_a_bv"])
    pvec[:, 24:32] = fmv(inp["L0_attn_a_bo"])
    pvec[:, 32:40] = fmv(inp["L0_attn_b_time_first"])
    pvec[:, 40:48] = fmv(inp["L0_attn_b_time_decay"])
    pvec[:, 48:56] = fmv(np.asarray(inp["L0_attn_b_time_mix_k"]).reshape(-1))
    pvec[:, 56:64] = fmv(np.asarray(inp["L0_attn_b_time_mix_v"]).reshape(-1))
    pvec[:, 64:72] = fmv(np.asarray(inp["L0_attn_b_time_mix_r"]).reshape(-1))
    pvec[0:16, 72] = f32(inp["L0_attn_a_m"])
    brow = np.concatenate([f32(inp["L0_attn_a_bk"]),
                           f32(inp["L0_attn_a_bv"])])[None, :]

    bf = ml_dtypes.bfloat16
    ones_k1 = np.ones((1, 128), np.float32)
    onescol_b = np.ones((128, 1), bf)
    bd = np.zeros((128, PB * HEAD), bf)
    selh = np.zeros((16, PB * 128), bf)
    for p in range(PB):
        bd[0:64, p * 16 + 2 * p] = 1.0
        bd[64:128, p * 16 + 2 * p + 1] = 1.0
        selh[2 * p, p * 128: p * 128 + 64] = 1.0
        selh[2 * p + 1, p * 128 + 64: p * 128 + 128] = 1.0
    sel_e = np.zeros((16, NE * 128), bf)
    for e in range(NE):
        sel_e[e, e * 128:(e + 1) * 128] = 1.0
    ident_b = np.eye(128, dtype=bf)

    cvals = np.zeros((128, 8), np.float32)
    for i, v in enumerate([1e-5, -0.5, -1.0, -float(np.log(T))]):
        cvals[:, i] = v

    return dict(wq=wq, wk=wk, wv=wv, wo=wo, wkey=wkey, wval=wval, wrec=wrec,
                wout=wout, w13=w13, w2=w2p, gw=gw, pvec=pvec, brow=brow,
                ones_k1=ones_k1, onescol_b=onescol_b, bd=bd, selh=selh,
                sel_e=sel_e, ident_b=ident_b, cvals=cvals)


def pack_x(xbatches, T):
    return np.stack([np.ascontiguousarray(b.T).reshape(8, 128, T)
                     for b in xbatches])


def unpack_out(o, T):
    return np.stack([o[b].reshape(D, T).T for b in range(o.shape[0])])


def kernel(**inp):
    x = np.asarray(inp["L0_attn_a_x"], dtype=np.float32)
    T = x.shape[1]
    shared = host_pack(inp, T)
    NBc = x.shape[0] // 8
    in_maps = []
    for c in range(8):
        m = dict(shared)
        m["x"] = pack_x([x[c * NBc + b] for b in range(NBc)], T)
        in_maps.append(m)

    nc = build_nc(T=T, NB=NBc)
    res = run_bass_kernel_spmd(nc, in_maps, core_ids=list(range(8)))
    out = np.empty((8 * NBc, T, D), np.float32)
    for c in range(8):
        out[c * NBc:(c + 1) * NBc] = unpack_out(res.results[c]["out"], T)
    return out
